# version 04
# baseline (speedup 1.0000x reference)
import numpy as np
import ml_dtypes
from contextlib import ExitStack
import concourse.bass as bass
import concourse.mybir as mybir
from concourse.bass_utils import run_bass_kernel_spmd

F32 = mybir.dt.float32
BF16 = mybir.dt.bfloat16
AF = mybir.ActivationFunctionType
ALU = mybir.AluOpType
AX = mybir.AxisListType.X

D = 1024
S = 2048
BPC = 2
NTOK = BPC * S
DC = 8
TT = 512
NT = NTOK // TT
NTS = S // TT
EPS1 = 1e-5
EPS2 = 1e-6
NH = 16
SC_IN = float(64 ** -0.5)
SC_MLA = float(1.0 / np.sqrt(np.float32(96.0)))
NE = 8
HID = 2048
NEG = -1e30

PHASES = "ABC"
DEBUG = ()
N_CORES = 8


def _bd(*mats):
    n = sum(m.shape[0] for m in mats)
    m_ = sum(m.shape[1] for m in mats)
    out = np.zeros((n, m_), np.float32)
    r = c = 0
    for m in mats:
        out[r:r + m.shape[0], c:c + m.shape[1]] = m
        r += m.shape[0]; c += m.shape[1]
    return out


def _rope_tables():
    j = np.arange(0, 64, 2, dtype=np.float64) / 64.0
    invf = 1.0 / (10000.0 ** j)
    t = np.arange(S, dtype=np.float64)
    ang = t[None, :] * invf[:, None]
    c64 = np.concatenate([np.cos(ang), np.cos(ang)], axis=0).astype(np.float32)
    s64 = np.concatenate([np.sin(ang), np.sin(ang)], axis=0).astype(np.float32)
    c2i = np.concatenate([c64, c64], axis=0)
    s2i = np.concatenate([s64, s64], axis=0)
    B = np.zeros((64, 64), np.float32)
    for i in range(32):
        B[i, i + 32] = -1.0
        B[i + 32, i] = 1.0
    b2T = np.ascontiguousarray(_bd(B, B).T)
    i2 = np.arange(0, 32, 2, dtype=np.float64) / 32.0
    invm = 10000.0 ** (-i2)
    angm = t[None, :] * invm[:, None]
    c32 = np.repeat(np.cos(angm), 2, axis=0).astype(np.float32)
    s32 = np.repeat(np.sin(angm), 2, axis=0).astype(np.float32)
    cm2 = np.concatenate([c32, c32], axis=0)
    sm2 = np.concatenate([s32, s32], axis=0)
    A = np.zeros((32, 32), np.float32)
    for i in range(16):
        A[2 * i, 2 * i + 1] = -1.0
        A[2 * i + 1, 2 * i] = 1.0
    a2T = np.ascontiguousarray(_bd(A, A).T)
    a1T = np.ascontiguousarray(A.T)
    return c2i, s2i, b2T, cm2, sm2, a2T, a1T


def _prepare(inputs):
    f = np.float32
    C = {}
    def ct(x):
        return np.ascontiguousarray(np.asarray(x, dtype=f))
    C["qkvT"] = ct(np.asarray(inputs["L0_ffn_qkv_w"], f).T)
    C["outT"] = ct(np.asarray(inputs["L0_ffn_out_w"], f).T)
    C["w1T"] = ct(np.asarray(inputs["L0_ffn_w1"], f).transpose(0, 2, 1))
    C["w2T"] = ct(np.asarray(inputs["L0_ffn_w2"], f).transpose(0, 2, 1))
    C["w3T"] = ct(np.asarray(inputs["L0_ffn_w3"], f).transpose(0, 2, 1))
    C["gateT"] = ct(np.asarray(inputs["L0_ffn_gate_w"], f).T)
    for k in ("W_DQ", "W_UQ", "W_QR", "W_DKV", "W_UK", "W_UV", "W_KR", "W_O"):
        C["m_" + k] = ct(inputs["L0_attn_" + k])
    qb = np.asarray(inputs["L0_ffn_qkv_b"], f)
    C["qkvb_c"] = ct(qb.reshape(24, 128).T)
    C["vbr"] = ct(qb[2048:].reshape(1, 1024))
    C["outb_c"] = ct(np.asarray(inputs["L0_ffn_out_b"], f).reshape(8, 128).T)
    C["n1c"] = ct(np.asarray(inputs["L0_ffn_norm1_w"], f).reshape(8, 128).T)
    C["n2c"] = ct(np.asarray(inputs["L0_ffn_norm2_w"], f).reshape(8, 128).T)
    C["n1r"] = ct(np.asarray(inputs["L0_ffn_norm1_w"], f).reshape(1, 1024))
    C["b1c"] = ct(np.asarray(inputs["L0_ffn_b1"], f).reshape(8, 16, 128).transpose(2, 0, 1))
    C["b2c"] = ct(np.asarray(inputs["L0_ffn_b2"], f).reshape(8, 16, 128).transpose(2, 0, 1))
    C["b3r"] = ct(np.asarray(inputs["L0_ffn_b3"], f).reshape(1, 8192))
    c2i, s2i, b2T, cm2, sm2, a2T, a1T = _rope_tables()
    C["c2i"], C["s2i"], C["b2T"] = c2i, s2i, b2T
    C["cm2"], C["sm2"], C["a2T"], C["a1T"] = cm2, sm2, a2T, a1T
    C["identF"] = np.eye(128, dtype=f)
    m4 = np.zeros((4, 128, 512), f)
    tri = np.fromfunction(lambda k, q: (k > q), (128, 128))
    for r in range(4):
        for s_ in range(4):
            if s_ < r:
                m4[r][:, s_ * 128:(s_ + 1) * 128] = NEG
            elif s_ == r:
                m4[r][:, s_ * 128:(s_ + 1) * 128] = np.where(tri, NEG, 0.0)
    C["mask4"] = np.ascontiguousarray(m4.transpose(1, 0, 2).reshape(128, 4 * 512))
    x = np.asarray(inputs["L0_attn_x"], f)
    xc = [np.ascontiguousarray(x[2 * c:2 * c + 2].reshape(NTOK, D).T)
          for c in range(N_CORES)]
    return C, xc


_UID = [0]


def _un(name):
    _UID[0] += 1
    return f"{name}_{_UID[0]}"


class Sem:
    __slots__ = ("h", "n")
    def __init__(self, h):
        self.h = h
        self.n = 0
    def inc(self, inst, k=1):
        inst.then_inc(self.h, k)
        self.n += k
        return self.n


class Env:
    pass


DRAM_SPECS = [
    ("qkvT", [D, 3072]), ("outT", [D, D]),
    ("w1T", [NE, D, HID]), ("w2T", [NE, D, HID]), ("w3T", [NE, HID, D]),
    ("gateT", [D, 8]),
    ("m_W_DQ", [D, 256]), ("m_W_UQ", [256, D]), ("m_W_QR", [256, 512]),
    ("m_W_DKV", [D, 256]), ("m_W_UK", [256, D]), ("m_W_UV", [256, D]),
    ("m_W_KR", [D, 32]), ("m_W_O", [D, D]),
    ("qkvb_c", [128, 24]), ("vbr", [1, 1024]), ("outb_c", [128, 8]),
    ("n1c", [128, 8]), ("n2c", [128, 8]), ("n1r", [1, 1024]),
    ("b1c", [128, 8, 16]), ("b2c", [128, 8, 16]), ("b3r", [1, 8192]),
    ("c2i", [128, S]), ("s2i", [128, S]), ("b2T", [128, 128]),
    ("cm2", [64, S]), ("sm2", [64, S]), ("a2T", [64, 64]), ("a1T", [32, 32]),
    ("identF", [128, 128]), ("mask4", [128, 4 * 512]),
]

ENGS = ("tensor", "vector", "scalar", "gpsimd", "sync")


def _build():
    nc = bass.Bass("TRN2", target_bir_lowering=False)
    e = Env()
    e.nc = nc
    dt = nc.dram_tensor
    e.x0 = dt("x_fm", [D, NTOK], F32, kind="ExternalInput").ap()
    e.wk = {name: dt(name, shape, F32, kind="ExternalInput").ap()
            for name, shape in DRAM_SPECS}

    def scratch(name, dtype):
        kind = "ExternalOutput" if name in DEBUG else "Internal"
        return dt(name, [D, NTOK], dtype, kind=kind).ap()
    e.x1s = scratch("x1s", F32)
    e.x2s = scratch("x2s", F32)
    e.us = scratch("us", F32)
    e.u2s = scratch("u2s", BF16)
    e.u1s = scratch("u1s", BF16) if "u1s" in DEBUG else None
    e.yfm = dt("y_fm", [D, NTOK], F32, kind="ExternalOutput").ap()
    if "rows" in DEBUG:
        e.rows_dbg = dt("rows_dbg", [4, NTOK], BF16, kind="ExternalOutput").ap()
        e.rowsf_dbg = dt("rowsf_dbg", [2, NTOK], F32, kind="ExternalOutput").ap()
    else:
        e.rows_dbg = None
        e.rowsf_dbg = None
    e.qk_dbg = (dt("qk_dbg", [128, 4 * 8 * S], BF16, kind="ExternalOutput").ap()
                if "qk" in DEBUG else None)
    e.attf_dbg = (dt("attf_dbg", [128, 8 * NTOK], BF16, kind="ExternalOutput").ap()
                  if "attf" in DEBUG else None)

    def dtile(ap, t, w=TT):
        return ap.rearrange("(dc p) f -> p dc f", p=128)[:, :, t * w:(t + 1) * w]
    e.dtile = dtile

    with ExitStack() as ST:
        en = ST.enter_context
        for s_ in ("ldc", "ldg", "ldx", "ldx2", "ldw", "sq", "pes", "row", "dve", "act",
                   "pool", "stw", "acc", "bar", "gat", "ldu", "exs"):
            setattr(e, s_, Sem(en(nc.semaphore(s_))))
        sb = nc.sbuf_tensor
        e.identF = en(sb("c_identF", [128, 128], F32))
        e.onesF = en(sb("c_onesF", [1, 128], F32))
        e.onesB = en(sb("c_onesB", [1, 128], BF16))
        e.oneC = en(sb("c_oneC", [128, 1], BF16))
        e.b2T = en(sb("c_b2T", [128, 128], BF16))
        e.a2T = en(sb("c_a2T", [64, 64], BF16))
        e.a1T = en(sb("c_a1T", [32, 32], BF16))
        e.qkvb = en(sb("c_qkvb", [128, 24], F32))
        e.vbr = en(sb("c_vbr", [1, 1024], BF16))
        e.outb = en(sb("c_outb", [128, 8], F32))
        e.n1c = en(sb("c_n1c", [128, 8], F32))
        e.n2c = en(sb("c_n2c", [128, 8], F32))
        e.n1r = en(sb("c_n1r", [1, 1024], BF16))
        e.b1c = en(sb("c_b1c", [128, 8, 16], F32))
        e.b2c = en(sb("c_b2c", [128, 8, 16], F32))
        e.gateT = en(sb("c_gateT", [128, 8, 8], BF16))
        e.onesC1 = en(sb("c_onesC1", [128, 1], BF16))
        e.ceps1 = en(sb("c_eps1", [1, 1], F32))
        e.ceps2 = en(sb("c_eps2", [1, 1], F32))

        with nc.Block() as block:
            _emit_const_loads(block, e)
            out_a = e.x1s if ("B" in PHASES or "C" in PHASES) else e.yfm
            _emit_ffn(block, e, e.x0, out_a)
            if "B" in PHASES:
                _barrier(block, e)
                out_b = e.x2s if "C" in PHASES else e.yfm
                _emit_mla(block, e, out_a, out_b)
            if "C" in PHASES:
                _barrier(block, e)
                _emit_ffn(block, e, e.x2s, e.yfm)
            _emit_final(block, e)
    return nc


def _barrier(block, e):
    bs = e.bar
    caps = [(s_, getattr(e, s_).n) for s_ in
            ("ldc", "ldg", "ldx", "ldx2", "ldw", "ldu", "stw", "acc")]

    @block.gpsimd
    def _(g):
        for s_, n_ in caps:
            g.wait_ge(getattr(e, s_).h, n_)

    @block.sync
    def _(sy):
        for s_, n_ in caps:
            sy.wait_ge(getattr(e, s_).h, n_)
    for name in ENGS:
        def _f(eng, bs=bs):
            bs.inc(eng.nop(), 1)
        getattr(block, name)(_f)
    tgt = bs.n
    for name in ENGS:
        def _f(eng, tgt=tgt, bs=bs):
            eng.wait_ge(bs.h, tgt)
        getattr(block, name)(_f)


def _emit_const_loads(block, e):
    wk = e.wk
    ldc = e.ldc

    @block.sync
    def _(s_):
        for src, dst in [
            (wk["identF"], e.identF[:]), (wk["qkvb_c"], e.qkvb[:]),
            (wk["outb_c"], e.outb[:]), (wk["n1c"], e.n1c[:]), (wk["n2c"], e.n2c[:]),
            (wk["b1c"], e.b1c[:]), (wk["b2c"], e.b2c[:]),
        ]:
            ldc.inc(s_.dma_start(out=dst, in_=src), 16)

    @block.gpsimd
    def _(g):
        for src, dst in [
            (wk["b2T"], e.b2T[:]), (wk["a2T"], e.a2T[:]), (wk["a1T"], e.a1T[:]),
            (wk["n1r"], e.n1r[:]),
            (wk["vbr"], e.vbr[:]),
            (wk["gateT"].rearrange("(dc p) x -> p dc x", p=128), e.gateT[:]),
        ]:
            e.ldg.inc(g.dma_start(out=dst, in_=src), 16)

    @block.vector
    def _(v):
        v.memset(e.onesF[:], 1.0)
        v.memset(e.onesB[:], 1.0)
        v.memset(e.onesC1[:], 1.0)
        v.memset(e.ceps1[:], EPS1)
        e.row.inc(v.memset(e.ceps2[:], EPS2), 1)
        e.dve.inc(v.memset(e.oneC[:], 1.0 / 1024.0), 1)
    e.ldc_done = ldc.n
    e.ldg_done = e.ldg.n
    e.memset_done = e.dve.n


def _row_chain(block, e, ops):
    for eng_name, fn in ops:
        tgt = e.row.n
        def _f(eng, fn=fn, tgt=tgt):
            eng.wait_ge(e.row.h, tgt)
            with e.nc.allow_low_precision(reason="per-token scale rows in bf16"):
                e.row.inc(fn(eng), 1)
        getattr(block, eng_name)(_f)


def _emit_stats(block, e, x_in, with_mean):
    nc = e.nc
    with ExitStack() as ST:
        en = ST.enter_context
        xb = [en(nc.sbuf_tensor(_un(f"st_xb{i}"), [128, DC, TT], BF16)) for i in range(2)]
        sq = [en(nc.sbuf_tensor(_un(f"st_sq{i}"), [128, DC, TT], BF16)) for i in range(2)]
        pm = [en(nc.psum_tensor(_un(f"st_pm{i}"), [1, TT], F32)) for i in range(2)]
        pms = [en(nc.psum_tensor(_un(f"st_pms{i}"), [1, TT], F32)) for i in range(2)]
        b_ldx = e.ldg.n
        b_sq = e.sq.n
        b_pe = e.pes.n
        b_row = e.row.n
        for t in range(NT):
            @block.gpsimd
            def _(g, t=t):
                if t >= 2:
                    g.wait_ge(e.pes.h, b_pe + (t - 1))
                e.ldg.inc(g.dma_start(out=xb[t % 2][:], in_=e.dtile(x_in, t)), 16)

            @block.scalar
            def _(sc, t=t):
                sc.wait_ge(e.ldg.h, b_ldx + 16 * (t + 1))
                if t >= 2:
                    sc.wait_ge(e.pes.h, b_pe + (t - 1))
                e.sq.inc(sc.activation(sq[t % 2].ap(), xb[t % 2].ap(), AF.Square,
                                       bias=0.0, scale=1.0), 1)

            @block.tensor
            def _(tt_, t=t):
                tt_.wait_ge(e.dve.h, e.memset_done)
                tt_.wait_ge(e.sq.h, b_sq + (t + 1))
                if t >= 2:
                    tt_.wait_ge(e.row.h, b_row + 2 * (t - 1))
                if with_mean:
                    for dc in range(DC):
                        tt_.matmul(pm[t % 2][:], e.oneC[:], xb[t % 2][:, dc, :],
                                   start=(dc == 0), stop=(dc == DC - 1))
                for dc in range(DC):
                    mm = tt_.matmul(pms[t % 2][:], e.oneC[:], sq[t % 2][:, dc, :],
                                    start=(dc == 0), stop=(dc == DC - 1))
                e.pes.inc(mm, 1)

            @block.vector
            def _(v, t=t):
                v.wait_ge(e.pes.h, b_pe + (t + 1))
                if with_mean:
                    v.tensor_copy(e.r_m[:, t * TT:(t + 1) * TT], pm[t % 2][:])
                e.row.inc(v.tensor_copy(e.r_ms[:, t * TT:(t + 1) * TT],
                                        pms[t % 2][:]), 2)


def _emit_ln_rows(block, e, compose_rms):
    ops = [
        ("scalar", lambda sc: sc.activation(e.rs1[:], e.r_m[:], AF.Square, bias=0.0, scale=1.0)),
        ("vector", lambda v: v.tensor_tensor(e.rs2[:], e.r_ms[:], e.rs1[:], ALU.subtract)),
        ("scalar", lambda sc: sc.activation(e.rs1[:], e.rs2[:], AF.Sqrt, bias=e.ceps1[0:1, 0:1], scale=1.0)),
        ("vector", lambda v: v.reciprocal(e.r_a[:], e.rs1[:])),
        ("vector", lambda v: v.scalar_tensor_tensor(e.r_b[:], e.r_m[:], -1.0, e.r_a[:], ALU.mult, ALU.mult)),
    ]
    if compose_rms:
        ops += [
            ("scalar", lambda sc: sc.activation(e.rs1[:], e.r_a[:], AF.Square, bias=0.0, scale=1.0)),
            ("vector", lambda v: v.tensor_tensor(e.rs3[:], e.rs2[:], e.rs1[:], ALU.mult)),
            ("scalar", lambda sc: sc.activation(e.rs1[:], e.rs3[:], AF.Sqrt, bias=e.ceps2[0:1, 0:1], scale=1.0)),
            ("vector", lambda v: v.reciprocal(e.rs3[:], e.rs1[:])),
            ("vector", lambda v: v.tensor_tensor(e.r_A[:], e.r_a[:], e.rs3[:], ALU.mult)),
            ("vector", lambda v: v.scalar_tensor_tensor(e.r_B[:], e.r_m[:], -1.0, e.r_A[:], ALU.mult, ALU.mult)),
        ]
    _row_chain(block, e, ops)


def _emit_rms_row(block, e):
    _row_chain(block, e, [
        ("scalar", lambda sc: sc.activation(e.r_B[:], e.r_ms[:], AF.Sqrt, bias=e.ceps2[0:1, 0:1], scale=1.0)),
        ("vector", lambda v: v.reciprocal(e.r_A[:], e.r_B[:])),
    ])


def _emit_ffn(block, e, x_in, x_out):
    nc = e.nc
    with ExitStack() as SF:
        enf = SF.enter_context
        for rn in ("r_A", "r_B", "r_ms", "r_a", "r_b"):
            setattr(e, rn, enf(nc.sbuf_tensor(_un("f_") + rn, [1, NTOK], BF16))[:])

        with ExitStack() as SS:
            ens = SS.enter_context
            for rn in ("r_m", "rs1", "rs2", "rs3"):
                setattr(e, rn, ens(nc.sbuf_tensor(_un("f_") + rn, [1, NTOK], F32))[:])
            _emit_stats(block, e, x_in, with_mean=True)
            _emit_ln_rows(block, e, compose_rms=True)
            if e.rows_dbg is not None:
                rt_ = e.row.n

                @block.sync
                def _(sy, rt_=rt_):
                    sy.wait_ge(e.row.h, rt_)
                    for i, r in enumerate((e.r_A, e.r_B, e.r_a, e.r_b)):
                        e.stw.inc(sy.dma_start(out=e.rows_dbg[i:i + 1, :], in_=r), 16)

                @block.gpsimd
                def _(g, rt_=rt_):
                    g.wait_ge(e.row.h, rt_)
                    for i, r in enumerate((e.r_m, e.r_ms)):
                        e.stw.inc(g.dma_start(out=e.rowsf_dbg[i:i + 1, :], in_=r), 16)
        for s_ in range(BPC):
            _emit_inner_seq(block, e, x_in, s_)
        _emit_rms_row(block, e)
        with ExitStack() as SC:
            enc = SC.enter_context
            cT = [enc(nc.sbuf_tensor(_un(f"f_cT{i}"), [97, NTOK], BF16)) for i in range(3)]
            e.combR = [cT[i // 3][(i % 3) * 32:(i % 3) * 32 + 1, :] for i in range(NE)]
            base_done = _emit_u2_gate_base(block, e, x_in, x_out)
            _emit_moe(block, e, x_out, base_done)
            _drain_phase(block, e)
        SF.pop_all().close()


def _drain_phase(block, e):
    _barrier(block, e)


def _emit_inner_seq(block, e, x_in, s_):
    _barrier(block, e)
    nc = e.nc
    wk = e.wk
    t0 = s_ * NTS
    row_ready = e.row.n
    with ExitStack() as ST:
        en = ST.enter_context
        qrope = en(nc.sbuf_tensor(_un("i_qrope"), [128, 8, S], BF16))
        krope = en(nc.sbuf_tensor(_un("i_krope"), [128, 8, S], BF16))
        vtm = en(nc.sbuf_tensor(_un("i_vtm"), [128, 16, 16, 65], BF16))
        nbc = en(nc.sbuf_tensor(_un("i_nbc"), [128, 16], F32))

        with ExitStack() as SP:
            enp = SP.enter_context
            u1f = enp(nc.sbuf_tensor(_un("i_u1f"), [128, DC, S], BF16))
            qkraw = [enp(nc.sbuf_tensor(_un(f"i_qkraw{i}"), [128, TT], BF16)) for i in range(2)]
            rt = [enp(nc.sbuf_tensor(_un(f"i_rt{i}"), [128, TT], F32)) for i in range(2)]
            pGB = enp(nc.psum_tensor(_un("i_pGB"), [128, TT], F32))
            pBW = [enp(nc.psum_tensor(_un(f"i_pBW{i}"), [128, TT], F32)) for i in range(1)]
            pqk = [enp(nc.psum_tensor(_un(f"i_pqk{i}"), [128, TT], F32)) for i in range(2)]
            pbq = [enp(nc.psum_tensor(_un(f"i_pbq{i}"), [128, TT], F32)) for i in range(2)]
            pvv = [enp(nc.psum_tensor(_un(f"i_pvv{i}"), [128, TT], F32)) for i in range(2)]


            @block.vector
            def _(v):
                v.wait_ge(e.ldc.h, e.ldc_done)
                v.wait_ge(e.ldg.h, e.ldg_done)
                e.dve.inc(v.memset(vtm.ap()[:, :, :, 64:65], 1.0), 1)
            vones = e.dve.n

            SX = ExitStack()
            xs = SX.enter_context(nc.sbuf_tensor(_un("i_xs"), [128, DC, TT], F32))
            xs_free = {}
            u1_done = {}
            for ts in range(NTS):
                t = t0 + ts
                sl = slice(t * TT, (t + 1) * TT)
                ssl = slice(ts * TT, (ts + 1) * TT)
                ldx_t = e.ldx.n + 16

                @block.sync
                def _(sy, ts=ts, t=t):
                    if ts >= 1:
                        sy.wait_ge(e.dve.h, xs_free[ts - 1])
                    e.ldx.inc(sy.dma_start(out=xs[:], in_=e.dtile(x_in, t)), 16)

                @block.tensor
                def _(tt_, ts=ts, sl=sl):
                    tt_.wait_ge(e.row.h, row_ready)
                    if ts >= 1:
                        tt_.wait_ge(e.dve.h, u1_done[ts - 1])
                    e.pes.inc(tt_.matmul(pGB[:], e.onesB[:], e.r_A[:, sl],
                                         start=True, stop=True), 1)
                for dc in range(DC):
                    @block.tensor
                    def _(tt_, dc=dc, ts=ts, sl=sl):
                        if dc >= 1:
                            tt_.wait_ge(e.dve.h, bw_used[dc - 1])
                        elif ts >= 1:
                            tt_.wait_ge(e.dve.h, u1_done[ts - 1])
                        e.pes.inc(tt_.matmul(pBW[0][:],
                                             e.n1r[0:1, dc * 128:(dc + 1) * 128],
                                             e.r_B[:, sl], start=True, stop=True), 1)
                    if dc == 0:
                        bw_used = {}
                    bw_pe = e.pes.n

                    @block.vector
                    def _(v, dc=dc, ts=ts, ssl=ssl, bw_pe=bw_pe, ldx_t=ldx_t):
                        v.wait_ge(e.pes.h, bw_pe)
                        v.wait_ge(e.ldx.h, ldx_t)
                        e.dve.inc(v.tensor_tensor(rt[dc % 2][:], xs[:, dc, :],
                                                  pGB[:], ALU.mult), 1)
                        v.wait_ge(e.dve.h, e.dve.n)
                        e.dve.inc(v.scalar_tensor_tensor(
                            u1f[:, dc, ssl], rt[dc % 2][:], e.n1c[:, dc:dc + 1],
                            pBW[0][:], ALU.mult, ALU.add), 1)
                        bw_used[dc] = e.dve.n
                xs_free[ts] = e.dve.n
                u1_done[ts] = e.dve.n
            u1_all = e.dve.n
            SX.pop_all().close()

            if e.u1s is not None:
                @block.sync
                def _(sy, u1_all=u1_all):
                    sy.wait_ge(e.dve.h, u1_all)
                    for ts in range(NTS):
                        e.stw.inc(sy.dma_start(
                            out=e.dtile(e.u1s, t0 + ts),
                            in_=u1f[:, :, ts * TT:(ts + 1) * TT]), 16)

            with ExitStack() as SV:
                wv = SV.enter_context(nc.sbuf_tensor(_un("i_wv"), [128, DC, 512], BF16))
                tbl_done = e.ldw.n
                v_evac = {}
                vh_used = {}
                for hh in range(2):
                    @block.gpsimd
                    def _(g, hh=hh, u1_all=u1_all):
                        g.wait_ge(e.ldc.h, e.ldc_done)
                        g.wait_ge(e.ldg.h, e.ldg_done)
                        g.wait_ge(e.dve.h, u1_all)
                        if hh >= 1:
                            g.wait_ge(e.pes.h, vh_used[hh - 1])
                        e.ldw.inc(g.dma_start(
                            out=wv[:],
                            in_=wk["qkvT"][:, 2048 + hh * 512:2048 + (hh + 1) * 512]
                            .rearrange("(dc p) f -> p dc f", p=128)), 16)
                    wvl = e.ldw.n
                    for ts in range(NTS):
                        for st_ in range(4):
                            ai = hh * 16 + ts * 4 + st_

                            @block.tensor
                            def _(tt_, st_=st_, hh=hh, ts=ts, ai=ai,
                                  u1_all=u1_all, wvl=wvl):
                                tt_.wait_ge(e.dve.h, u1_all)
                                tt_.wait_ge(e.ldw.h, wvl)
                                if ai >= 2:
                                    tt_.wait_ge(e.act.h, v_evac[ai - 2])
                                for dc in range(DC):
                                    tt_.matmul(pvv[ai % 2][:],
                                               u1f[:, dc, ts * TT + st_ * 128:ts * TT + (st_ + 1) * 128],
                                               wv[:, dc, :],
                                               start=(dc == 0), stop=False)
                                mm = tt_.matmul(pvv[ai % 2][:], e.onesB[:],
                                                e.vbr[0:1, hh * 512:(hh + 1) * 512],
                                                start=False, stop=True)
                                e.pes.inc(mm, 1)
                                if ts == NTS - 1 and st_ == 3:
                                    vh_used[hh] = e.pes.n
                            pe_v = e.pes.n

                            @block.vector
                            def _(v, st_=st_, hh=hh, ts=ts, ai=ai, pe_v=pe_v,
                                  vones=vones):
                                v.wait_ge(e.pes.h, pe_v)
                                v.wait_ge(e.dve.h, vones)
                                kt = ts * 4 + st_
                                e.act.inc(v.tensor_copy(
                                    vtm.ap()[:, kt, hh * 8:(hh + 1) * 8, 0:64],
                                    pvv[ai % 2][:]), 1)
                                v_evac[ai] = e.act.n

                v_done_pe = e.pes.n
                v_done_act = e.act.n
                SV.pop_all().close()

            with ExitStack() as SQ:
                enq = SQ.enter_context
                c2i = enq(nc.sbuf_tensor(_un("i_c2i"), [128, S], BF16))
                s2i = enq(nc.sbuf_tensor(_un("i_s2i"), [128, S], BF16))
                wc = [enq(nc.sbuf_tensor(_un(f"i_wc{i}"), [128, DC, 128], BF16)) for i in range(2)]
                sqn = enq(nc.sbuf_tensor(_un("i_sqn"), [128, TT], BF16))
                qn2 = enq(nc.sbuf_tensor(_un("i_qn2"), [1, 64], F32))
                kn2s = enq(nc.sbuf_tensor(_un("i_kn2s"), [1, 64], F32))
                qm2 = enq(nc.sbuf_tensor(_un("i_qm2"), [1, 16], F32))
                km2 = enq(nc.sbuf_tensor(_un("i_km2"), [1, 16], F32))
                b2r = enq(nc.sbuf_tensor(_un("i_b2r"), [1, 16], F32))
                negbr = enq(nc.sbuf_tensor(_un("i_negbr"), [1, 16], F32))

                @block.gpsimd
                def _(g):
                    g.wait_ge(e.ldc.h, e.ldc_done)
                    g.wait_ge(e.ldg.h, e.ldg_done)
                    g.wait_ge(e.pes.h, v_done_pe)
                    e.ldw.inc(g.dma_start(out=c2i[:], in_=wk["c2i"]), 16)
                    e.ldw.inc(g.dma_start(out=s2i[:], in_=wk["s2i"]), 16)
                tbl2_done = e.ldw.n

                qk_evac = {}
                bq_pe = {}
                rope_done = {}
                wc_used = {}
                sqn_t = {}
                nrm_pe = {}
                nrm_red = {}
                for c in range(16):
                    @block.gpsimd
                    def _(g, c=c):
                        if c >= 2:
                            g.wait_ge(e.pes.h, wc_used[c - 2])
                        e.ldw.inc(g.dma_start(
                            out=wc[c % 2][:],
                            in_=wk["qkvT"][:, c * 128:(c + 1) * 128]
                            .rearrange("(dc p) f -> p dc f", p=128)), 16)
                    wcl = e.ldw.n
                    for ts in range(NTS):
                        ac = c * 4 + ts
                        ssl = slice(ts * TT, (ts + 1) * TT)

                        @block.tensor
                        def _(tt_, c=c, ts=ts, ssl=ssl, ac=ac, wcl=wcl,
                              u1_all=u1_all):
                            tt_.wait_ge(e.dve.h, u1_all)
                            tt_.wait_ge(e.ldw.h, wcl)
                            if ac >= 2:
                                tt_.wait_ge(e.act.h, qk_evac[ac - 2])
                            for dc in range(DC):
                                mm = tt_.matmul(pqk[ac % 2][:], wc[c % 2][:, dc, :],
                                                u1f[:, dc, ssl],
                                                start=(dc == 0), stop=(dc == DC - 1))
                            e.pes.inc(mm, 1)
                            if ts == NTS - 1:
                                wc_used[c] = e.pes.n
                        qk_pe = e.pes.n

                        @block.vector
                        def _(v, c=c, qk_pe=qk_pe, ac=ac):
                            v.wait_ge(e.pes.h, qk_pe)
                            if ac >= 2:
                                v.wait_ge(e.pes.h, bq_pe[ac - 2])
                                v.wait_ge(e.dve.h, rope_done[ac - 2])
                                v.wait_ge(e.pool.h, sqn_t[ac - 2])
                            e.act.inc(v.tensor_scalar(qkraw[ac % 2][:], pqk[ac % 2][:],
                                                      e.qkvb[:, c:c + 1], None,
                                                      ALU.add), 1)
                            qk_evac[ac] = e.act.n

                        @block.tensor
                        def _(tt_, ac=ac):
                            tt_.wait_ge(e.act.h, qk_evac[ac])
                            if ac >= 2:
                                tt_.wait_ge(e.dve.h, rope_done[ac - 2])
                            e.pes.inc(tt_.matmul(pbq[ac % 2][:], e.b2T[:], qkraw[ac % 2][:],
                                                 start=True, stop=True), 1)
                            bq_pe[ac] = e.pes.n

                        @block.vector
                        def _(v, c=c, ssl=ssl, ac=ac):
                            dst = qrope if c < 8 else krope
                            ci = c % 8
                            v.wait_ge(e.pes.h, bq_pe[ac])
                            e.dve.inc(v.tensor_tensor(rt[0][:], qkraw[ac % 2][:],
                                                      c2i[:, ssl], ALU.mult), 1)
                            e.dve.inc(v.tensor_tensor(rt[1][:], pbq[ac % 2][:],
                                                      s2i[:, ssl], ALU.mult), 1)
                            v.wait_ge(e.dve.h, e.dve.n)
                            e.dve.inc(v.tensor_tensor(dst[:, ci, ssl], rt[0][:], rt[1][:],
                                                      ALU.add), 1)
                            rope_done[ac] = e.dve.n

                        @block.gpsimd
                        def _(g, ac=ac):
                            g.wait_ge(e.act.h, qk_evac[ac])
                            if ac >= 1:
                                g.wait_ge(e.pes.h, nrm_pe[ac - 1])
                            e.pool.inc(g.tensor_tensor(sqn[:], qkraw[ac % 2][:],
                                                       qkraw[ac % 2][:], ALU.mult), 1)
                        sqn_t[ac] = e.pool.n

                        @block.tensor
                        def _(tt_, ac=ac):
                            tt_.wait_ge(e.pool.h, sqn_t[ac])
                            if ac < 2:
                                tt_.wait_ge(e.act.h, v_done_act)
                            else:
                                tt_.wait_ge(e.dve.h, nrm_red[ac - 2])
                            tt_.matmul(pvv[ac % 2][0:1, :], e.onesC1[0:64, 0:1],
                                       sqn[0:64, :], start=True, stop=True)
                            mm = tt_.matmul(pvv[ac % 2][32:33, :],
                                            e.onesC1[64:128, 0:1],
                                            sqn[64:128, :], start=True, stop=True)
                            e.pes.inc(mm, 1)
                        nrm_pe[ac] = e.pes.n

                        @block.vector
                        def _(v, ac=ac, c=c, ts=ts):
                            v.wait_ge(e.pes.h, nrm_pe[ac])
                            dst = qn2 if c < 8 else kn2s
                            h0 = (c % 8) * 2
                            e.dve.inc(v.reduce_max(
                                dst[0:1, h0 * 4 + ts:h0 * 4 + ts + 1],
                                pvv[ac % 2][0:1, :], axis=AX), 1)
                            e.dve.inc(v.reduce_max(
                                dst[0:1, (h0 + 1) * 4 + ts:(h0 + 1) * 4 + ts + 1],
                                pvv[ac % 2][32:33, :], axis=AX), 1)
                        nrm_red[ac] = e.dve.n

                bnd0 = e.dve.n

                @block.vector
                def _(v, bnd0=bnd0):
                    v.wait_ge(e.dve.h, bnd0)
                    for h in range(16):
                        e.dve.inc(v.reduce_max(qm2[0:1, h:h + 1],
                                               qn2[0:1, h * 4:h * 4 + 4], axis=AX), 1)
                        e.dve.inc(v.reduce_max(km2[0:1, h:h + 1],
                                               kn2s[0:1, h * 4:h * 4 + 4], axis=AX), 1)
                    v.wait_ge(e.dve.h, e.dve.n)
                    e.dve.inc(v.tensor_tensor(b2r[:], qm2[:], km2[:], ALU.mult), 1)
                bq_t = e.dve.n

                @block.scalar
                def _(sc, bq_t=bq_t):
                    sc.wait_ge(e.dve.h, bq_t)
                    e.sq.inc(sc.activation(b2r[:], b2r[:], AF.Sqrt,
                                           bias=0.0, scale=1.0), 1)
                br_t = e.sq.n

                @block.vector
                def _(v, br_t=br_t):
                    v.wait_ge(e.sq.h, br_t)
                    e.dve.inc(v.tensor_scalar(negbr[:], b2r[:], -SC_IN, None,
                                              ALU.mult), 1)
                nb_t = e.dve.n

                @block.tensor
                def _(tt_, nb_t=nb_t):
                    tt_.wait_ge(e.dve.h, nb_t)
                    for h in range(16):
                        mm = tt_.matmul(pGB[:, h:h + 1], e.onesF[:],
                                        negbr[0:1, h:h + 1], start=True, stop=True)
                    e.pes.inc(mm, 1)
                nbp = e.pes.n

                @block.vector
                def _(v, nbp=nbp):
                    v.wait_ge(e.pes.h, nbp)
                    e.dve.inc(v.tensor_copy(nbc[:], pGB[:, 0:16]), 1)
                e.nbc_ready = e.dve.n
                SQ.pop_all().close()
            proj_pe = e.pes.n
            proj_act = e.act.n
            proj_dve = e.dve.n
            SP.pop_all().close()

        if e.qk_dbg is not None:
            @block.sync
            def _(sy, s_=s_, proj_dve=proj_dve):
                sy.wait_ge(e.dve.h, proj_dve)
                off = s_ * 2 * 8 * S
                e.stw.inc(sy.dma_start(
                    out=e.qk_dbg[:, off:off + 8 * S],
                    in_=qrope.ap().rearrange("p c f -> p (c f)")), 16)
                e.stw.inc(sy.dma_start(
                    out=e.qk_dbg[:, off + 8 * S:off + 16 * S],
                    in_=krope.ap().rearrange("p c f -> p (c f)")), 16)

        _emit_inner_attention(block, e, x_in, s_, qrope, krope, vtm, nbc,
                              proj_pe, proj_act, proj_dve)
        ST.pop_all().close()


def _emit_inner_attention(block, e, x_in, s_, qrope, krope, vtm, nbc,
                          proj_pe, proj_act, proj_dve):
    _barrier(block, e)
    nc = e.nc
    wk = e.wk
    t0 = s_ * NTS
    row_ready = e.row.n
    with ExitStack() as ST:
        en = ST.enter_context
        outw = en(nc.sbuf_tensor(_un("i_outw"), [128, DC, D], BF16))
        attf = en(nc.sbuf_tensor(_un("i_attf"), [128, 8, TT], BF16))
        psb = [en(nc.sbuf_tensor(_un(f"i_psb{i}"), [128, 1024], BF16)) for i in range(6)]
        recb = en(nc.sbuf_tensor(_un("i_recb"), [1, TT], BF16))
        rbs = en(nc.sbuf_tensor(_un("i_rbs"), [64, TT], BF16))
        axs = en(nc.sbuf_tensor(_un("i_axs"), [128, DC, TT], F32))
        umb2 = [en(nc.sbuf_tensor(_un(f"i_umb{i}"), [128, TT], F32)) for i in range(2)]
        sqb = [en(nc.sbuf_tensor(_un(f"i_sqb{i}"), [128, TT], BF16)) for i in range(2)]
        sqm_pe = {}
        pSt = [en(nc.psum_tensor(_un(f"a_pSt{i}"), [128, 1024], F32)) for i in range(3)]
        pPV = en(nc.psum_tensor(_un("a_pPV"), [65, TT], F32))
        pMi = en(nc.psum_tensor(_un("a_pMi"), [128, TT], F32))

        @block.gpsimd
        def _(g):
            e.ldw.inc(g.dma_start(
                out=outw[:], in_=wk["outT"].rearrange("(dc p) f -> p dc f", p=128)), 16)
        outw_done = e.ldw.n

        pv_free = ("dve", proj_dve)
        pmi_guard = ("dve", proj_dve)
        attf_free = ("pes", proj_pe)
        xs_guard = ("dve", proj_dve)
        umb_guard = [("stw", e.stw.n), ("sq", e.sq.n)]
        recb_prev = proj_dve
        rbs_guard = proj_dve
        st_extra = ("dve", proj_dve)
        pv_t = {}
        st_t = {}
        kpg = 0
        exs_base = e.exs.n

        def semh(name):
            return getattr(e, name).h

        NPAIR = 8
        exp_jobs = []
        for q5 in range(NTS):
            qsl = slice(q5 * TT, (q5 + 1) * TT)
            for h in range(NH):
                hc = h // 2
                hb = (h % 2) * 64
                kp0 = kpg
                g0, g1 = st_extra

                @block.tensor
                def _(tt_, hb=hb, hc=hc, qsl=qsl, kp0=kp0, g0=g0, g1=g1,
                      first=(q5 == 0 and h == 0)):
                    if first:
                        tt_.wait_ge(e.act.h, proj_act)
                        tt_.wait_ge(e.dve.h, proj_dve)
                    tt_.wait_ge(semh(g0), g1)
                    for kp in range(NPAIR):
                        kk = kp0 + kp
                        if kk >= 3:
                            tt_.wait_ge(e.exs.h, exs_base + kk - 2)
                        for j in (0, 1):
                            kc = kp * 2 + j
                            mm = tt_.matmul(pSt[kk % 3][:, j * 512:(j + 1) * 512],
                                            krope.ap()[hb:hb + 64, hc, kc * 128:(kc + 1) * 128],
                                            qrope.ap()[hb:hb + 64, hc, qsl],
                                            start=True, stop=True)
                        e.pes.inc(mm, 1)
                        st_t[kp0 + kp] = e.pes.n
                st_last = e.pes.n
                for kp in range(NPAIR):
                    pv_t[kp0 + kp] = st_last + kp + 1

                exp_jobs.append((h, kp0))
                if h == NH - 1:
                    @block.scalar
                    def _(sc, jobs=tuple(exp_jobs)):
                        sc.wait_ge(e.dve.h, e.nbc_ready)
                        for (hh, hp0) in jobs:
                            for kp in range(NPAIR):
                                kk = hp0 + kp
                                sc.wait_ge(e.pes.h, st_t[kk])
                                if kk >= 6:
                                    sc.wait_ge(e.pes.h, pv_t[kk - 6])
                                sc.activation(psb[kk % 6][:], pSt[kk % 3][:],
                                              AF.Exp, bias=nbc[:, hh:hh + 1],
                                              scale=SC_IN).then_inc(e.exs.h, 1)
                                e.exs.n += 1
                    exp_jobs = []

                g0, g1 = pv_free

                @block.tensor
                def _(tt_, h=h, kp0=kp0, g0=g0, g1=g1):
                    for kp in range(NPAIR):
                        kk = kp0 + kp
                        tt_.wait_ge(e.exs.h, exs_base + kk + 1)
                        if kp == 0:
                            tt_.wait_ge(semh(g0), g1)
                        mm = tt_.matmul(pPV[:], vtm.ap()[:, kp * 2, h, :],
                                        psb[kk % 6][:, 0:512],
                                        start=(kp == 0), stop=False)
                        mm = tt_.matmul(pPV[:], vtm.ap()[:, kp * 2 + 1, h, :],
                                        psb[kk % 6][:, 512:1024],
                                        start=False, stop=(kp == NPAIR - 1))
                        e.pes.inc(mm, 1)
                        assert e.pes.n == pv_t[kk]
                kpg += NPAIR
                pv_all = e.pes.n

                @block.vector
                def _(v, pv_all=pv_all, recb_prev=recb_prev):
                    v.wait_ge(e.pes.h, pv_all)
                    v.wait_ge(e.dve.h, recb_prev)
                    e.dve.inc(v.tensor_copy(recb[:], pPV[64:65, :]), 1)
                recb_done = e.dve.n
                recb_prev = recb_done
                g0, g1 = pmi_guard

                @block.tensor
                def _(tt_, recb_done=recb_done, g0=g0, g1=g1):
                    tt_.wait_ge(e.dve.h, recb_done)
                    tt_.wait_ge(semh(g0), g1)
                    e.pes.inc(tt_.matmul(pMi[0:64, :], e.onesB[0:1, 0:64], recb[:],
                                         start=True, stop=True), 1)
                rb_pe = e.pes.n

                @block.vector
                def _(v, rb_pe=rb_pe, rbs_guard=rbs_guard, h=h,
                      g0=attf_free[0], g1=attf_free[1]):
                    v.wait_ge(e.pes.h, rb_pe)
                    v.wait_ge(e.dve.h, rbs_guard)
                    with e.nc.allow_low_precision(reason="softmax recip bf16"):
                        e.dve.inc(v.reciprocal(rbs[:], pMi[0:64, :]), 1)
                    v.wait_ge(e.dve.h, e.dve.n)
                    if h == 0:
                        v.wait_ge(semh(g0), g1)
                    hc2 = h // 2
                    hb2 = (h % 2) * 64
                    e.dve.inc(v.tensor_tensor(attf.ap()[hb2:hb2 + 64, hc2, :],
                                              pPV[0:64, :], rbs[:], ALU.mult), 1)
                pmi_guard = ("dve", e.dve.n)
                pv_free = ("dve", e.dve.n)
                rbs_guard = e.dve.n

            t = t0 + q5
            sl = slice(t * TT, (t + 1) * TT)
            attf_all = e.dve.n
            ldx_t = e.ldx2.n + 16
            g0, g1 = xs_guard

            @block.sync
            def _(sy, t=t, g0=g0, g1=g1):
                sy.wait_ge(semh(g0), g1)
                e.ldx2.inc(sy.dma_start(out=axs[:], in_=e.dtile(x_in, t)), 16)

            gm0, gm1 = pmi_guard
            gu0 = [umb_guard[0], umb_guard[1]]
            wo_pe = {}

            @block.tensor
            def _(tt_, attf_all=attf_all, sl=sl, gm0=gm0, gm1=gm1):
                tt_.wait_ge(e.dve.h, attf_all)
                tt_.wait_ge(e.ldw.h, outw_done)
                tt_.wait_ge(e.row.h, row_ready)
                tt_.wait_ge(semh(gm0), gm1)
                e.pes.inc(tt_.matmul(pMi[:], e.onesB[:], e.r_a[:, sl],
                                     start=True, stop=True), 1)
            wo_used = {}
            sq_done = {}
            dma_done = {}
            for dc in range(DC):
                @block.tensor
                def _(tt_, dc=dc, sl=sl):
                    if dc >= 2:
                        tt_.wait_ge(e.dve.h, wo_used[dc - 2])
                    for hv in range(8):
                        tt_.matmul(pSt[dc % 2][:, 0:512],
                                   outw[:, hv, dc * 128:(dc + 1) * 128],
                                   attf.ap()[:, hv, :],
                                   start=(hv == 0), stop=False)
                    mm = tt_.matmul(pSt[dc % 2][:, 0:512], e.onesB[:],
                                    e.r_b[:, sl], start=False, stop=True)
                    e.pes.inc(mm, 1)
                wo_pe[dc] = e.pes.n

                @block.vector
                def _(v, dc=dc, ldx_t=ldx_t, gu0=gu0):
                    v.wait_ge(e.ldx2.h, ldx_t)
                    v.wait_ge(e.pes.h, wo_pe[dc])
                    if dc == 0:
                        for gg0, gg1 in gu0:
                            v.wait_ge(semh(gg0), gg1)
                    if dc >= 2:
                        v.wait_ge(e.stw.h, dma_done[dc - 2])
                        v.wait_ge(e.sq.h, sq_done[dc - 2])
                    e.dve.inc(v.tensor_tensor(umb2[dc % 2][:], axs[:, dc, :],
                                              pMi[:], ALU.mult), 1)
                    v.wait_ge(e.dve.h, e.dve.n)
                    e.dve.inc(v.scalar_tensor_tensor(
                        umb2[dc % 2][:], pSt[dc % 2][:, 0:512], e.outb[:, dc:dc + 1],
                        umb2[dc % 2][:], ALU.add, ALU.add), 1)
                    wo_used[dc] = e.dve.n
                um_d = e.dve.n

                @block.sync
                def _(sy, dc=dc, t=t, um_d=um_d):
                    sy.wait_ge(e.dve.h, um_d)
                    e.stw.inc(sy.dma_start(
                        out=e.dtile(e.us, t)[:, dc:dc + 1, :],
                        in_=umb2[dc % 2][:]), 16)
                dma_done[dc] = e.stw.n

                @block.gpsimd
                def _(g, dc=dc, um_d=um_d):
                    g.wait_ge(e.dve.h, um_d)
                    if dc >= 2:
                        g.wait_ge(e.pes.h, sqm_pe[dc - 2])
                    e.sq.inc(g.tensor_tensor(sqb[dc % 2][:], umb2[dc % 2][:],
                                             umb2[dc % 2][:], ALU.mult), 1)
                sq_done[dc] = e.sq.n

                @block.tensor
                def _(tt_, dc=dc, gpv=pv_free):
                    tt_.wait_ge(e.sq.h, sq_done[dc])
                    if dc == 0:
                        tt_.wait_ge(semh(gpv[0]), gpv[1])
                    mm = tt_.matmul(pPV[0:1, 0:TT], e.oneC[:], sqb[dc % 2][:],
                                    start=(dc == 0), stop=(dc == DC - 1))
                    e.pes.inc(mm, 1)
                sqm_pe[dc] = e.pes.n
            um_done = e.dve.n
            attf_free = ("pes", wo_pe[DC - 1])
            xs_guard = ("dve", um_done)
            st_extra = ("dve", um_done)
            pmi_guard = ("dve", um_done)
            umb_guard = [("stw", dma_done[DC - 1]), ("sq", sq_done[DC - 1])]
            msq_pe = e.pes.n

            @block.vector
            def _(v, t=t, msq_pe=msq_pe):
                v.wait_ge(e.pes.h, msq_pe)
                e.row.inc(v.tensor_copy(e.r_ms[:, t * TT:(t + 1) * TT],
                                        pPV[0:1, 0:TT]), 1)
            pv_free = ("row", e.row.n)
        ST.pop_all().close()


def _emit_u2_gate_base(block, e, x_in, x_out):
    _barrier(block, e)
    nc = e.nc
    row_r = e.row.n
    with ExitStack() as ST:
        en = ST.enter_context
        uss = [en(nc.sbuf_tensor(_un(f"g_uss{i}"), [128, DC, TT], F32)) for i in range(2)]
        xss = [en(nc.sbuf_tensor(_un(f"g_xss{i}"), [128, DC, TT], F32)) for i in range(2)]
        u2b = [en(nc.sbuf_tensor(_un(f"g_u2b{i}"), [128, DC, TT], BF16)) for i in range(2)]
        x1b = [en(nc.sbuf_tensor(_un(f"g_x1b{i}"), [128, DC, TT], F32)) for i in range(2)]
        lgt = en(nc.sbuf_tensor(_un("g_lgt"), [128, 8], F32))
        top = en(nc.sbuf_tensor(_un("g_top"), [128, 8], F32))
        gw = en(nc.sbuf_tensor(_un("g_gw"), [128, 6], F32))
        msk = en(nc.sbuf_tensor(_un("g_msk"), [128, 4, 8], F32))
        logit = en(nc.sbuf_tensor(_un("g_logit"), [8, NTOK], F32))
        pR2 = [en(nc.psum_tensor(_un(f"g_pR2{i}"), [128, TT], F32)) for i in range(2)]
        plg = [en(nc.psum_tensor(_un(f"g_plg{i}"), [8, TT], F32)) for i in range(2)]
        ptr = en(nc.psum_tensor(_un("g_ptr"), [128, 8], F32))
        prow = en(nc.psum_tensor(_un("g_prow"), [1, 128], F32))
        b_ldx = e.ldx.n
        b_ldx2 = e.ldx2.n
        tile_done = {}
        lg_evac = {}
        for t in range(NT):
            @block.sync
            def _(sy, t=t):
                if t >= 2:
                    sy.wait_ge(e.dve.h, tile_done[t - 2])
                e.ldx.inc(sy.dma_start(out=uss[t % 2][:], in_=e.dtile(e.us, t)), 16)
                e.ldx2.inc(sy.dma_start(out=xss[t % 2][:], in_=e.dtile(x_in, t)), 16)

            @block.tensor
            def _(tt_, t=t, row_r=row_r):
                tt_.wait_ge(e.row.h, row_r)
                if t >= 2:
                    tt_.wait_ge(e.dve.h, tile_done[t - 2])
                e.pes.inc(tt_.matmul(pR2[t % 2][:], e.onesB[:],
                                     e.r_A[:, t * TT:(t + 1) * TT],
                                     start=True, stop=True), 1)
            pe_r2 = e.pes.n

            @block.vector
            def _(v, t=t, pe_r2=pe_r2):
                v.wait_ge(e.ldx.h, b_ldx + 16 * (t + 1))
                v.wait_ge(e.ldx2.h, b_ldx2 + 16 * (t + 1))
                v.wait_ge(e.pes.h, pe_r2)
                for dc in range(DC):
                    e.dve.inc(v.scalar_tensor_tensor(
                        u2b[t % 2][:, dc, :], uss[t % 2][:, dc, :],
                        e.n2c[:, dc:dc + 1], pR2[t % 2][:], ALU.mult, ALU.mult), 1)
                    e.dve.inc(v.scalar_tensor_tensor(
                        x1b[t % 2][:, dc, :], uss[t % 2][:, dc, :], 0.5,
                        xss[t % 2][:, dc, :], ALU.mult, ALU.add), 1)
                tile_done[t] = e.dve.n
            td = tile_done[t]

            @block.sync
            def _(sy, t=t, td=td):
                sy.wait_ge(e.dve.h, td)
                e.stw.inc(sy.dma_start(out=e.dtile(e.u2s, t), in_=u2b[t % 2][:]), 16)
                e.stw.inc(sy.dma_start(out=e.dtile(x_out, t), in_=x1b[t % 2][:]), 16)

            @block.tensor
            def _(tt_, t=t, td=td):
                tt_.wait_ge(e.dve.h, td)
                tt_.wait_ge(e.ldc.h, e.ldc_done)
                tt_.wait_ge(e.ldg.h, e.ldg_done)
                if t >= 2:
                    tt_.wait_ge(e.act.h, lg_evac[t - 2])
                for dc in range(DC):
                    mm = tt_.matmul(plg[t % 2][:], e.gateT[:, dc, :],
                                    u2b[t % 2][:, dc, :],
                                    start=(dc == 0), stop=(dc == DC - 1))
                e.pes.inc(mm, 1)
            lg_pe = e.pes.n

            @block.vector
            def _(v, t=t, lg_pe=lg_pe):
                v.wait_ge(e.pes.h, lg_pe)
                e.act.inc(v.tensor_copy(logit[:, t * TT:(t + 1) * TT],
                                        plg[t % 2][:]), 1)
                lg_evac[t] = e.act.n
        lg_all = e.act.n
        base_done = e.stw.n

        row_evac = e.gat.n
        for i in range(NTOK // 128):
            i128 = slice(i * 128, (i + 1) * 128)

            @block.tensor
            def _(tt_, i=i, i128=i128, lg_all=lg_all, row_evac=row_evac):
                tt_.wait_ge(e.act.h, lg_all)
                tt_.wait_ge(e.gat.h, row_evac)
                if i >= 1:
                    tt_.wait_ge(e.dve.h, comb_prev)
                e.pes.inc(tt_.transpose(ptr[:], logit[:, i128],
                                        e.identF[0:8, 0:8]), 1)
            trd = e.pes.n

            @block.vector
            def _(v, trd=trd):
                v.wait_ge(e.pes.h, trd)
                e.dve.inc(v.tensor_copy(lgt[:], ptr[:]), 1)
                v.wait_ge(e.dve.h, e.dve.n)
                e.dve.inc(v.max(out=top[:], in_=lgt[:]), 1)
                v.wait_ge(e.dve.h, e.dve.n)
                e.dve.inc(v.tensor_tensor(gw[:, 0:1], top[:, 1:2], top[:, 0:1],
                                          ALU.subtract), 1)
            gd1 = e.dve.n

            @block.scalar
            def _(sc, gd1=gd1):
                sc.wait_ge(e.dve.h, gd1)
                e.act.inc(sc.activation(gw[:, 1:2], gw[:, 0:1], AF.Exp,
                                        bias=0.0, scale=1.0), 1)
            gd2 = e.act.n

            @block.vector
            def _(v, gd2=gd2):
                v.wait_ge(e.act.h, gd2)
                e.dve.inc(v.tensor_scalar(gw[:, 2:3], gw[:, 1:2], 1.0, None,
                                          ALU.add), 1)
                v.wait_ge(e.dve.h, e.dve.n)
                e.dve.inc(v.reciprocal(gw[:, 3:4], gw[:, 2:3]), 1)
                v.wait_ge(e.dve.h, e.dve.n)
                e.dve.inc(v.tensor_scalar(gw[:, 4:5], gw[:, 3:4], 0.5, None,
                                          ALU.mult), 1)
                v.wait_ge(e.dve.h, e.dve.n)
                e.dve.inc(v.tensor_tensor(gw[:, 5:6], gw[:, 1:2], gw[:, 4:5],
                                          ALU.mult), 1)
                e.dve.inc(v.tensor_scalar(msk[:, 0, :], lgt[:], top[:, 0:1], None,
                                          ALU.is_equal), 1)
                e.dve.inc(v.tensor_scalar(msk[:, 2, :], lgt[:], top[:, 1:2], None,
                                          ALU.is_equal), 1)
                v.wait_ge(e.dve.h, e.dve.n)
                e.dve.inc(v.tensor_scalar(msk[:, 1, :], msk[:, 0, :], -1.0, 1.0,
                                          ALU.mult, ALU.add), 1)
                v.wait_ge(e.dve.h, e.dve.n)
                e.dve.inc(v.tensor_tensor(msk[:, 2, :], msk[:, 2, :],
                                          msk[:, 1, :], ALU.mult), 1)
                v.wait_ge(e.dve.h, e.dve.n)
                e.dve.inc(v.tensor_scalar(msk[:, 3, :], msk[:, 0, :], gw[:, 4:5],
                                          None, ALU.mult), 1)
                v.wait_ge(e.dve.h, e.dve.n)
                e.dve.inc(v.scalar_tensor_tensor(msk[:, 3, :], msk[:, 2, :],
                                                 gw[:, 5:6], msk[:, 3, :],
                                                 ALU.mult, ALU.add), 1)
            comb_done = e.dve.n
            comb_prev = comb_done

            for ex in range(NE):
                @block.tensor
                def _(tt_, ex=ex, comb_done=comb_done, row_evac=row_evac):
                    tt_.wait_ge(e.dve.h, comb_done)
                    tt_.wait_ge(e.gat.h, row_evac)
                    e.pes.inc(tt_.transpose(prow[:], msk.ap()[:, 3, ex:ex + 1],
                                            e.identF[:]), 1)
                tr1 = e.pes.n

                @block.vector
                def _(v, ex=ex, i=i, tr1=tr1):
                    v.wait_ge(e.pes.h, tr1)
                    e.gat.inc(v.tensor_copy(e.combR[ex][:, i * 128:(i + 1) * 128],
                                            prow[:]), 1)
                row_evac = e.gat.n
        ST.pop_all().close()
    return base_done


def _emit_moe(block, e, x_out, base_done):
    _barrier(block, e)
    nc = e.nc
    wk = e.wk
    comb_ready = e.gat.n
    with ExitStack() as ST:
        en = ST.enter_context
        w12 = [en(nc.sbuf_tensor(_un(f"m_w12{i}"), [128, DC, 2048], BF16)) for i in range(2)]
        w3s = [en(nc.sbuf_tensor(_un(f"m_w3s{i}"), [128, DC, 1024], BF16)) for i in range(2)]
        u2t = [en(nc.sbuf_tensor(_un(f"m_u2t{i}"), [128, DC, TT], BF16)) for i in range(2)]
        s1b = [en(nc.sbuf_tensor(_un(f"m_s1b{i}"), [128, TT], BF16)) for i in range(2)]
        s2b = [en(nc.sbuf_tensor(_un(f"m_s2b{i}"), [128, TT], BF16)) for i in range(2)]
        hsb = en(nc.sbuf_tensor(_un("m_hsb"), [128, DC, TT], BF16))
        ybf = [en(nc.sbuf_tensor(_un(f"m_ybf{i}"), [128, TT], F32)) for i in range(2)]
        cbs = en(nc.sbuf_tensor(_un("m_cbs"), [128, TT], BF16))
        b3u = [en(nc.sbuf_tensor(_un(f"m_b3u{i}"), [97, 1024], BF16)) for i in range(2)]
        onesB3 = en(nc.sbuf_tensor(_un("m_onesB3"), [97, 128], BF16))
        pH1 = [en(nc.psum_tensor(_un(f"m_pH1{i}"), [128, TT], F32)) for i in range(2)]
        pH2 = [en(nc.psum_tensor(_un(f"m_pH2{i}"), [128, TT], F32)) for i in range(2)]
        pY = [en(nc.psum_tensor(_un(f"m_pY{i}"), [128, TT], F32)) for i in range(2)]
        pcb = en(nc.psum_tensor(_un("m_pcb"), [128, TT], F32))

        @block.vector
        def _(v):
            e.dve.inc(v.memset(onesB3[:], 1.0), 1)
        ob3_done = e.dve.n

        w12_used = {}
        w3_used = {}
        u2_used = {}
        silu_t = {}
        s2g_t = {}
        h_used = {}
        y_acc = {}
        y_evac = {}
        hsb_used = {}
        cb_guard = ("act", e.act.n)

        def semh(name):
            return getattr(e, name).h

        ui = -1
        for ex in range(NE):
            for hf in range(2):
                ui += 1

                @block.gpsimd
                def _(g, ex=ex, hf=hf, ui=ui):
                    if ui >= 2:
                        g.wait_ge(e.pes.h, w12_used[ui - 2])
                    else:
                        g.wait_ge(e.ldc.h, e.ldc_done)
                        g.wait_ge(e.ldg.h, e.ldg_done)
                    e.ldw.inc(g.dma_start(
                        out=w12[ui % 2][:, :, 0:1024],
                        in_=wk["w1T"][ex, :, hf * 1024:(hf + 1) * 1024]
                        .rearrange("(dc p) f -> p dc f", p=128)), 16)
                    e.ldw.inc(g.dma_start(
                        out=w12[ui % 2][:, :, 1024:2048],
                        in_=wk["w2T"][ex, :, hf * 1024:(hf + 1) * 1024]
                        .rearrange("(dc p) f -> p dc f", p=128)), 16)
                ldw12 = e.ldw.n

                @block.gpsimd
                def _(g, ex=ex, hf=hf, ui=ui):
                    if ui >= 2:
                        g.wait_ge(e.pes.h, w3_used[ui - 2])
                    e.ldw.inc(g.dma_start(
                        out=w3s[ui % 2][:],
                        in_=wk["w3T"][ex, hf * 1024:(hf + 1) * 1024, :]
                        .rearrange("(hc p) f -> p hc f", p=128)), 16)
                    for rb in (0, 32, 64):
                        e.ldw.inc(g.dma_start(
                            out=b3u[ui % 2][rb:rb + 1, :],
                            in_=wk["b3r"][0:1, ex * 1024:(ex + 1) * 1024]), 16)
                ldw3 = e.ldw.n

                for t in range(NT):
                    ti = ui * NT + t

                    @block.sync
                    def _(sy, t=t, ti=ti):
                        if ti >= 2:
                            sy.wait_ge(e.pes.h, u2_used[ti - 2])
                        e.ldu.inc(sy.dma_start(out=u2t[t % 2][:],
                                               in_=e.dtile(e.u2s, t)), 16)
                    ldu_t = e.ldu.n

                    g0, g1 = cb_guard

                    @block.tensor
                    def _(tt_, ex=ex, t=t, g0=g0, g1=g1):
                        tt_.wait_ge(e.gat.h, comb_ready)
                        tt_.wait_ge(semh(g0), g1)
                        tt_.wait_ge(e.dve.h, ob3_done)
                        rb = (ex % 3) * 32
                        e.pes.inc(tt_.matmul(
                            pcb[:], onesB3[rb:rb + 1, 0:128],
                            e.combR[ex][:, t * TT:(t + 1) * TT],
                            start=True, stop=True), 1)
                    cb_pe = e.pes.n

                    @block.vector
                    def _(v, cb_pe=cb_pe):
                        v.wait_ge(e.pes.h, cb_pe)
                        e.act.inc(v.tensor_copy(cbs[:], pcb[:]), 1)
                    cbs_done = e.act.n
                    cb_guard = ("act", cbs_done)

                    exs0 = e.exs.n
                    gat0 = e.gat.n
                    pool0 = e.pool.n
                    for hcq in range(DC):
                        hci = ti * DC + hcq
                        silu_t[hci] = exs0 + hcq + 1
                        s2g_t[hci] = gat0 + hcq + 1
                        h_used[hci] = pool0 + hcq + 1
                    h_pe = {}
                    for hcq in range(DC):
                        hci = ti * DC + hcq

                        @block.tensor
                        def _(tt_, hcq=hcq, t=t, ui=ui, ldw12=ldw12,
                              ldu_t=ldu_t, hci=hci):
                            tt_.wait_ge(e.ldw.h, ldw12)
                            tt_.wait_ge(e.ldu.h, ldu_t)
                            if hci >= 2:
                                tt_.wait_ge(e.exs.h, silu_t[hci - 2])
                                tt_.wait_ge(e.gat.h, s2g_t[hci - 2])
                            for dc in range(DC):
                                mm = tt_.matmul(pH1[hcq % 2][:],
                                                w12[ui % 2][:, dc, hcq * 128:(hcq + 1) * 128],
                                                u2t[t % 2][:, dc, :],
                                                start=(dc == 0), stop=(dc == DC - 1))
                            e.pes.inc(mm, 1)
                            for dc in range(DC):
                                mm = tt_.matmul(pH2[hcq % 2][:],
                                                w12[ui % 2][:, dc, 1024 + hcq * 128:1024 + (hcq + 1) * 128],
                                                u2t[t % 2][:, dc, :],
                                                start=(dc == 0), stop=(dc == DC - 1))
                            e.pes.inc(mm, 1)
                        h_pe[hcq] = e.pes.n

                    @block.scalar
                    def _(sc, ti=ti, ex=ex, hf=hf, h_pe=h_pe):
                        for hcq in range(DC):
                            hci = ti * DC + hcq
                            sc.wait_ge(e.pes.h, h_pe[hcq] - 1)
                            if hci >= 2:
                                sc.wait_ge(e.pool.h, h_used[hci - 2])
                            sc.activation(
                                s1b[hcq % 2][:], pH1[hcq % 2][:], AF.Silu,
                                bias=e.b1c.ap()[:, ex, hf * 8 + hcq:hf * 8 + hcq + 1],
                                scale=1.0).then_inc(e.exs.h, 1)
                            e.exs.n += 1
                    assert e.exs.n == exs0 + DC

                    for hcq in range(DC):
                        hci = ti * DC + hcq

                        @block.vector
                        def _(v, hcq=hcq, ex=ex, hf=hf, hci=hci,
                              cbs_done=cbs_done, h_pe=h_pe):
                            v.wait_ge(e.pes.h, h_pe[hcq])
                            v.wait_ge(e.act.h, cbs_done)
                            if hci >= 2:
                                v.wait_ge(e.pool.h, h_used[hci - 2])
                            e.gat.inc(v.scalar_tensor_tensor(
                                s2b[hcq % 2][:], pH2[hcq % 2][:],
                                e.b2c.ap()[:, ex, hf * 8 + hcq:hf * 8 + hcq + 1],
                                cbs[:], ALU.add, ALU.mult), 1)

                        @block.gpsimd
                        def _(g, hcq=hcq, ti=ti, hci=hci):
                            g.wait_ge(e.exs.h, silu_t[hci])
                            g.wait_ge(e.gat.h, s2g_t[hci])
                            if ti >= 1 and hcq == 0:
                                g.wait_ge(e.pes.h, hsb_used.get(ti - 1, 0))
                            e.pool.inc(g.tensor_tensor(hsb[:, hcq, :],
                                                       s1b[hcq % 2][:],
                                                       s2b[hcq % 2][:], ALU.mult), 1)
                    assert e.gat.n == gat0 + DC
                    assert e.pool.n == pool0 + DC
                    h_all = e.pool.n

                    for dcq in range(DC):
                        yi = ti * DC + dcq

                        @block.tensor
                        def _(tt_, dcq=dcq, ti=ti, ui=ui, ex=ex, t=t,
                              h_all=h_all, yi=yi, ldw3=ldw3):
                            tt_.wait_ge(e.pool.h, h_all)
                            tt_.wait_ge(e.ldw.h, ldw3)
                            if yi >= 2:
                                tt_.wait_ge(e.act.h, y_evac[yi - 2])
                            for hc in range(DC):
                                tt_.matmul(pY[dcq % 2][:],
                                           w3s[ui % 2][:, hc, dcq * 128:(dcq + 1) * 128],
                                           hsb[:, hc, :],
                                           start=(hc == 0), stop=False)
                            rb = (ex % 3) * 32
                            mm = tt_.matmul(
                                pY[dcq % 2][:],
                                b3u[ui % 2][rb:rb + 1, dcq * 128:(dcq + 1) * 128],
                                e.combR[ex][:, t * TT:(t + 1) * TT],
                                start=False, stop=True)
                            e.pes.inc(mm, 1)
                        y_pe = e.pes.n

                        @block.vector
                        def _(v, dcq=dcq, y_pe=y_pe, yi=yi):
                            v.wait_ge(e.pes.h, y_pe)
                            if yi >= 2:
                                v.wait_ge(e.acc.h, y_acc.get(yi - 2, 0))
                            e.act.inc(v.tensor_copy(ybf[dcq % 2][:],
                                                    pY[dcq % 2][:]), 1)
                            y_evac[yi] = e.act.n

                        @block.gpsimd
                        def _(g, dcq=dcq, t=t, yi=yi):
                            g.wait_ge(e.act.h, y_evac[yi])
                            g.wait_ge(e.stw.h, base_done)
                            e.acc.inc(g.dma_start(
                                out=e.dtile(x_out, t)[:, dcq:dcq + 1, :],
                                in_=ybf[dcq % 2][:],
                                accum_op=ALU.add), 16)
                            y_acc[yi] = e.acc.n
                    hsb_used[ti] = e.pes.n
                    u2_used[ti] = e.pes.n
                w12_used[ui] = e.pes.n
                w3_used[ui] = e.pes.n
        ST.pop_all().close()


def _emit_mla(block, e, x_in, x_out):
    nc = e.nc
    with ExitStack() as SM:
        enm = SM.enter_context
        for rn in ("r_a", "r_b"):
            setattr(e, rn, enm(nc.sbuf_tensor(_un("b_") + rn, [1, NTOK], BF16))[:])
        with ExitStack() as SS:
            ens = SS.enter_context
            for rn in ("r_m", "rs1", "rs2", "rs3"):
                setattr(e, rn, ens(nc.sbuf_tensor(_un("b_") + rn, [1, NTOK], F32))[:])
            e.r_ms = ens(nc.sbuf_tensor(_un("b_r_ms"), [1, NTOK], BF16))[:]
            _emit_stats(block, e, x_in, with_mean=True)
            _emit_ln_rows(block, e, compose_rms=False)
        for s_ in range(BPC):
            for hg in range(2):
                _emit_mla_seq(block, e, x_in, x_out, s_, hg)
        _drain_phase(block, e)
        SM.pop_all().close()


def _emit_mla_seq(block, e, x_in, x_out, s_, hg):
    _barrier(block, e)
    nc = e.nc
    wk = e.wk
    t0 = s_ * NTS
    row_ready = e.row.n
    NHH = 8
    with ExitStack() as ST:
        en = ST.enter_context
        qc = en(nc.sbuf_tensor(_un("l_qc"), [128, 4, S], BF16))
        kc = en(nc.sbuf_tensor(_un("l_kc"), [128, 4, S], BF16))
        qr = en(nc.sbuf_tensor(_un("l_qr"), [128, 4, S], BF16))
        krm = en(nc.sbuf_tensor(_un("l_krm"), [64, S], BF16))
        vtm = en(nc.sbuf_tensor(_un("l_vtm"), [128, 16, 8, 65], BF16))
        nbcm = en(nc.sbuf_tensor(_un("l_nbcm"), [128, 8], F32))

        with ExitStack() as SP:
            enp = SP.enter_context
            dqw = enp(nc.sbuf_tensor(_un("l_dqw"), [128, DC, 256], BF16))
            dkw = enp(nc.sbuf_tensor(_un("l_dkw"), [128, DC, 256], BF16))
            krw = enp(nc.sbuf_tensor(_un("l_krw"), [128, DC, 32], BF16))
            qrw = enp(nc.sbuf_tensor(_un("l_qrw"), [128, 2, 256], BF16))
            cm2 = enp(nc.sbuf_tensor(_un("l_cm2"), [64, S], BF16))
            sm2 = enp(nc.sbuf_tensor(_un("l_sm2"), [64, S], BF16))
            cq = enp(nc.sbuf_tensor(_un("l_cq"), [128, 2, S], BF16))
            ckv = enp(nc.sbuf_tensor(_un("l_ckv"), [128, 2, S], BF16))
            xs = enp(nc.sbuf_tensor(_un("l_xs"), [128, DC, TT], F32))
            t2b = enp(nc.sbuf_tensor(_un("l_t2b"), [128, DC, TT], BF16))
            rt = [enp(nc.sbuf_tensor(_un(f"l_rt{i}"), [128, TT], F32)) for i in range(2)]
            krr = enp(nc.sbuf_tensor(_un("l_krr"), [32, TT], BF16))
            wuc = [enp(nc.sbuf_tensor(_un(f"l_wuc{i}"), [128, 2, 128], BF16)) for i in range(2)]
            wvv = enp(nc.sbuf_tensor(_un("l_wvv"), [128, 2, 512], BF16))
            sqb2 = [enp(nc.sbuf_tensor(_un(f"l_sqb2{i}"), [128, TT], BF16)) for i in range(2)]
            qc2s = enp(nc.sbuf_tensor(_un("l_qc2s"), [1, 32], F32))
            kc2s = enp(nc.sbuf_tensor(_un("l_kc2s"), [1, 32], F32))
            qr2s = enp(nc.sbuf_tensor(_un("l_qr2s"), [1, 32], F32))
            kr2s = enp(nc.sbuf_tensor(_un("l_kr2s"), [1, 4], F32))
            qm2m = enp(nc.sbuf_tensor(_un("l_qm2m"), [1, 8], F32))
            km2m = enp(nc.sbuf_tensor(_un("l_km2m"), [1, 8], F32))
            bpm = enp(nc.sbuf_tensor(_un("l_bpm"), [1, 8], F32))
            negbm = enp(nc.sbuf_tensor(_un("l_negbm"), [1, 8], F32))
            kr2m = enp(nc.sbuf_tensor(_un("l_kr2m"), [1, 1], F32))
            pAB = enp(nc.psum_tensor(_un("l_pAB"), [128, TT], F32))
            pBB = enp(nc.psum_tensor(_un("l_pBB"), [128, TT], F32))
            pcq = [enp(nc.psum_tensor(_un(f"l_pcq{i}"), [128, TT], F32)) for i in range(2)]
            pkr = enp(nc.psum_tensor(_un("l_pkr"), [32, TT], F32))
            par = enp(nc.psum_tensor(_un("l_par"), [64, TT], F32))
            pqr = enp(nc.psum_tensor(_un("l_pqr"), [128, TT], F32))
            pvm = enp(nc.psum_tensor(_un("l_pvm"), [128, TT], F32))

            @block.gpsimd
            def _(g, hg=hg):
                g.wait_ge(e.ldc.h, e.ldc_done)
                g.wait_ge(e.ldg.h, e.ldg_done)
                for src, dst in [
                    (wk["m_W_DQ"].rearrange("(p2 p) f -> p p2 f", p=128), dqw[:]),
                    (wk["m_W_DKV"].rearrange("(p2 p) f -> p p2 f", p=128), dkw[:]),
                    (wk["m_W_KR"].rearrange("(p2 p) f -> p p2 f", p=128), krw[:]),
                    (wk["m_W_QR"][:, hg * 256:(hg + 1) * 256]
                     .rearrange("(p2 p) f -> p p2 f", p=128), qrw[:]),
                    (wk["cm2"], cm2[:]),
                    (wk["sm2"], sm2[:]),
                    (wk["m_W_UV"][:, hg * 512:(hg + 1) * 512]
                     .rearrange("(p2 p) f -> p p2 f", p=128), wvv[:]),
                ]:
                    e.ldw.inc(g.dma_start(out=dst, in_=src), 16)
            ldw_done = e.ldw.n

            @block.vector
            def _(v):
                v.wait_ge(e.ldc.h, e.ldc_done)
                v.wait_ge(e.ldg.h, e.ldg_done)
                e.dve.inc(v.memset(vtm.ap()[:, :, :, 64:65], 1.0), 1)
            vones = e.dve.n

            xs_free = {}
            t2_done = {}
            t2_used = {}
            cq_evac = {}
            kr_rope_prev = 0
            for ts in range(NTS):
                t = t0 + ts
                sl = slice(t * TT, (t + 1) * TT)
                ssl = slice(ts * TT, (ts + 1) * TT)
                ldx_t = e.ldx.n + 16

                @block.sync
                def _(sy, ts=ts, t=t):
                    if ts >= 1:
                        sy.wait_ge(e.dve.h, xs_free[ts - 1])
                    e.ldx.inc(sy.dma_start(out=xs[:], in_=e.dtile(x_in, t)), 16)

                @block.tensor
                def _(tt_, ts=ts, sl=sl):
                    tt_.wait_ge(e.row.h, row_ready)
                    if ts >= 1:
                        tt_.wait_ge(e.dve.h, t2_done[ts - 1])
                    tt_.matmul(pAB[:], e.onesB[:], e.r_a[:, sl], start=True, stop=True)
                    e.pes.inc(tt_.matmul(pBB[:], e.onesB[:], e.r_b[:, sl],
                                         start=True, stop=True), 1)
                ab_pe = e.pes.n

                @block.vector
                def _(v, ts=ts, ldx_t=ldx_t, ab_pe=ab_pe):
                    v.wait_ge(e.ldx.h, ldx_t)
                    v.wait_ge(e.pes.h, ab_pe)
                    if ts >= 2:
                        v.wait_ge(e.pes.h, t2_used[ts - 2])
                    for dc in range(DC):
                        e.dve.inc(v.tensor_tensor(rt[dc % 2][:], xs[:, dc, :],
                                                  pAB[:], ALU.mult), 1)
                        v.wait_ge(e.dve.h, e.dve.n)
                        e.dve.inc(v.tensor_tensor(t2b[:, dc, :], rt[dc % 2][:],
                                                  pBB[:], ALU.add), 1)
                xs_free[ts] = e.dve.n
                t2_done[ts] = e.dve.n
                t2d = t2_done[ts]

                for j in range(5):
                    aj = ts * 5 + j

                    @block.tensor
                    def _(tt_, j=j, ts=ts, t2d=t2d, aj=aj,
                          kr_rope_prev=kr_rope_prev):
                        tt_.wait_ge(e.dve.h, t2d)
                        tt_.wait_ge(e.ldw.h, ldw_done)
                        if j < 4 and aj >= 2:
                            tt_.wait_ge(e.act.h, cq_evac.get(aj - 2, 0))
                        if j == 4 and ts >= 1:
                            tt_.wait_ge(e.act.h, cq_evac[(ts - 1) * 5 + 4])
                            tt_.wait_ge(e.dve.h, kr_rope_prev)
                        if j < 2:
                            w, lc = dqw, j
                        elif j < 4:
                            w, lc = dkw, j - 2
                        if j < 4:
                            for dc in range(DC):
                                mm = tt_.matmul(pcq[j % 2][:],
                                                w[:, dc, lc * 128:(lc + 1) * 128],
                                                t2b[:, dc, :],
                                                start=(dc == 0), stop=(dc == DC - 1))
                        else:
                            for dc in range(DC):
                                mm = tt_.matmul(pkr[:], krw[:, dc, :],
                                                t2b[:, dc, :],
                                                start=(dc == 0), stop=(dc == DC - 1))
                        e.pes.inc(mm, 1)
                    j_pe = e.pes.n

                    @block.vector
                    def _(v, j=j, ts=ts, ssl=ssl, j_pe=j_pe, aj=aj,
                          kr_rope_prev=kr_rope_prev):
                        v.wait_ge(e.pes.h, j_pe)
                        if j == 4 and ts >= 1:
                            v.wait_ge(e.dve.h, kr_rope_prev)
                        if j < 2:
                            dst = cq[:, j, ssl]
                            srcp = pcq[j % 2][:]
                        elif j < 4:
                            dst = ckv[:, j - 2, ssl]
                            srcp = pcq[j % 2][:]
                        else:
                            dst = krr[:]
                            srcp = pkr[:]
                        e.act.inc(v.tensor_copy(dst, srcp), 1)
                        cq_evac[aj] = e.act.n
                t2_used[ts] = e.pes.n
                krr_done = cq_evac[ts * 5 + 4]

                @block.tensor
                def _(tt_, krr_done=krr_done, kr_rope_prev=kr_rope_prev):
                    tt_.wait_ge(e.act.h, krr_done)
                    tt_.wait_ge(e.dve.h, kr_rope_prev)
                    e.pes.inc(tt_.matmul(par[0:32, :], e.a1T[:], krr[:],
                                         start=True, stop=True), 1)
                akr_pe = e.pes.n

                @block.vector
                def _(v, ssl=ssl, akr_pe=akr_pe, krr_done=krr_done):
                    v.wait_ge(e.pes.h, akr_pe)
                    v.wait_ge(e.act.h, krr_done)
                    e.dve.inc(v.tensor_tensor(rt[0][0:32, :], krr[:],
                                              cm2[0:32, ssl], ALU.mult), 1)
                    e.dve.inc(v.tensor_tensor(rt[1][0:32, :], par[0:32, :],
                                              sm2[0:32, ssl], ALU.mult), 1)
                    v.wait_ge(e.dve.h, e.dve.n)
                    e.dve.inc(v.tensor_tensor(krm[0:32, ssl], rt[0][0:32, :],
                                              rt[1][0:32, :], ALU.add), 1)
                    e.dve.inc(v.tensor_tensor(krm[32:64, ssl], rt[0][0:32, :],
                                              rt[1][0:32, :], ALU.add), 1)
                kr_rope_prev = e.dve.n

            cq_all = e.act.n
            kr_all = e.dve.n

            nrm_pe2 = {}
            nrm_red2 = {}
            pend = []

            def norm_chunk(ni, src, P, halves, guards):
                @block.gpsimd
                def _(g, ni=ni, src=src, P=P, guards=guards):
                    for (sh, sv) in guards:
                        g.wait_ge(sh, sv)
                    if ni >= 2:
                        g.wait_ge(e.pes.h, nrm_pe2[ni - 2])
                    e.pool.inc(g.tensor_tensor(sqb2[ni % 2][0:P, :], src, src,
                                               ALU.mult), 1)
                sq_t = e.pool.n

                def emit(ni=ni, halves=halves, sq_t=sq_t):
                    p = pAB if ni % 2 == 0 else pBB

                    @block.tensor
                    def _(tt_):
                        tt_.wait_ge(e.pool.h, sq_t)
                        if ni >= 2:
                            tt_.wait_ge(e.dve.h, nrm_red2[ni - 2])
                        for (b, kext, arr, slot) in halves:
                            mm = tt_.matmul(p[b:b + 1, :],
                                            e.onesC1[b:b + kext, 0:1],
                                            sqb2[ni % 2][b:b + kext, :],
                                            start=True, stop=True)
                        e.pes.inc(mm, 1)
                    nrm_pe2[ni] = e.pes.n

                    @block.vector
                    def _(v):
                        v.wait_ge(e.pes.h, nrm_pe2[ni])
                        for (b, kext, arr, slot) in halves:
                            e.dve.inc(v.reduce_max(arr[0:1, slot:slot + 1],
                                                   p[b:b + 1, :], axis=AX), 1)
                    nrm_red2[ni] = e.dve.n
                pend.append(emit)
                if len(pend) > 1:
                    pend.pop(0)()

            qc_evac = {}
            v_evac = {}
            wuc_used = {}
            qr_rope = kr_all
            for j in range(8):
                @block.gpsimd
                def _(g, j=j, hg=hg):
                    if j >= 2:
                        g.wait_ge(e.pes.h, wuc_used[j - 2])
                    src = wk["m_W_UQ"] if j < 4 else wk["m_W_UK"]
                    hcj = hg * 4 + (j % 4)
                    e.ldw.inc(g.dma_start(
                        out=wuc[j % 2][:],
                        in_=src[:, hcj * 128:(hcj + 1) * 128]
                        .rearrange("(lc p) f -> p lc f", p=128)), 16)
                wl = e.ldw.n
                for ts in range(NTS):
                    bj = j * 4 + ts
                    ssl = slice(ts * TT, (ts + 1) * TT)

                    @block.tensor
                    def _(tt_, j=j, ts=ts, ssl=ssl, cq_all=cq_all, bj=bj, wl=wl):
                        tt_.wait_ge(e.act.h, cq_all)
                        tt_.wait_ge(e.ldw.h, wl)
                        if bj >= 2:
                            tt_.wait_ge(e.act.h, qc_evac[bj - 2])
                        src = cq if j < 4 else ckv
                        for lc in range(2):
                            mm = tt_.matmul(pcq[bj % 2][:],
                                            wuc[j % 2][:, lc, :],
                                            src[:, lc, ssl],
                                            start=(lc == 0), stop=(lc == 1))
                        e.pes.inc(mm, 1)
                        if ts == NTS - 1:
                            wuc_used[j] = e.pes.n
                    j_pe = e.pes.n

                    @block.vector
                    def _(v, j=j, ssl=ssl, j_pe=j_pe, bj=bj):
                        v.wait_ge(e.pes.h, j_pe)
                        dst = qc if j < 4 else kc
                        e.act.inc(v.tensor_copy(dst[:, j % 4, ssl], pcq[bj % 2][:]), 1)
                        qc_evac[bj] = e.act.n

                    arr = qc2s if j < 4 else kc2s
                    h0 = (j % 4) * 2
                    norm_chunk(bj, (qc if j < 4 else kc)[:, j % 4, ssl], 128,
                               [(0, 64, arr, h0 * 4 + ts),
                                (64, 64, arr, (h0 + 1) * 4 + ts)],
                               [(e.act.h, qc_evac[bj])])

            for ts in range(NTS):
                ssl = slice(ts * TT, (ts + 1) * TT)
                for pc in range(2):
                    apc = ts * 2 + pc

                    @block.tensor
                    def _(tt_, pc=pc, ssl=ssl, cq_all=cq_all, apc=apc):
                        tt_.wait_ge(e.act.h, cq_all)
                        tt_.wait_ge(e.dve.h, qr_rope)
                        for lc in range(2):
                            mm = tt_.matmul(pqr[:], qrw[:, lc, pc * 128:(pc + 1) * 128],
                                            cq[:, lc, ssl], start=(lc == 0), stop=(lc == 1))
                        e.pes.inc(mm, 1)
                    qr_pe = e.pes.n

                    @block.vector
                    def _(v, pc=pc, ssl=ssl, qr_pe=qr_pe):
                        v.wait_ge(e.pes.h, qr_pe)
                        v.tensor_copy(qr[0:64, 2 * pc, ssl], pqr[0:64, :])
                        v.wait_ge(e.dve.h, e.dve.n)
                        e.act.inc(v.tensor_copy(qr[0:64, 2 * pc + 1, ssl],
                                                pqr[64:128, :]), 1)
                    qre = e.act.n
                    for hh in (2 * pc, 2 * pc + 1):
                        @block.tensor
                        def _(tt_, hh=hh, ssl=ssl, qre=qre):
                            tt_.wait_ge(e.act.h, qre)
                            tt_.wait_ge(e.dve.h, qr_rope)
                            e.pes.inc(tt_.matmul(par[:], e.a2T[:],
                                                 qr.ap()[0:64, hh, ssl],
                                                 start=True, stop=True), 1)
                        ar_pe = e.pes.n

                        @block.vector
                        def _(v, hh=hh, ssl=ssl, ar_pe=ar_pe):
                            v.wait_ge(e.pes.h, ar_pe)
                            e.dve.inc(v.tensor_tensor(rt[0][0:64, :],
                                                      qr.ap()[0:64, hh, ssl],
                                                      cm2[:, ssl], ALU.mult), 1)
                            e.dve.inc(v.tensor_tensor(rt[1][0:64, :], par[:],
                                                      sm2[:, ssl], ALU.mult), 1)
                            v.wait_ge(e.dve.h, e.dve.n)
                            e.dve.inc(v.tensor_tensor(qr.ap()[0:64, hh, ssl],
                                                      rt[0][0:64, :],
                                                      rt[1][0:64, :], ALU.add), 1)
                        qr_rope = e.dve.n
                        norm_chunk(32 + ts * 4 + hh, qr.ap()[0:64, hh, ssl], 64,
                                   [(0, 32, qr2s, (2 * hh) * 4 + ts),
                                    (32, 32, qr2s, (2 * hh + 1) * 4 + ts)],
                                   [(e.dve.h, qr_rope)])
                for st_ in range(4):
                    ai = ts * 4 + st_

                    @block.tensor
                    def _(tt_, st_=st_, ts=ts, ai=ai, cq_all=cq_all,
                          ldw_done=ldw_done):
                        tt_.wait_ge(e.act.h, cq_all)
                        tt_.wait_ge(e.ldw.h, ldw_done)
                        if ai >= 1:
                            tt_.wait_ge(e.act.h, v_evac[ai - 1])
                        for lc in range(2):
                            mm = tt_.matmul(
                                pvm[:],
                                ckv[:, lc, ts * TT + st_ * 128:ts * TT + (st_ + 1) * 128],
                                wvv[:, lc, :],
                                start=(lc == 0), stop=(lc == 1))
                        e.pes.inc(mm, 1)
                    v_pe = e.pes.n

                    @block.vector
                    def _(v, st_=st_, ts=ts, ai=ai, v_pe=v_pe, vones=vones):
                        v.wait_ge(e.pes.h, v_pe)
                        v.wait_ge(e.dve.h, vones)
                        kt = ts * 4 + st_
                        e.act.inc(v.tensor_copy(
                            vtm.ap()[:, kt, 0:8, 0:64], pvm[:]), 1)
                        v_evac[ai] = e.act.n

            for ts in range(NTS):
                norm_chunk(48 + ts, krm[0:32, ts * TT:(ts + 1) * TT], 32,
                           [(0, 32, kr2s, ts)], [(e.dve.h, kr_all)])
            while pend:
                pend.pop(0)()
            nrm_all = e.dve.n

            @block.vector
            def _(v, nrm_all=nrm_all):
                v.wait_ge(e.dve.h, nrm_all)
                for lh in range(8):
                    e.dve.inc(v.reduce_max(qm2m[0:1, lh:lh + 1],
                                           qc2s[0:1, lh * 4:lh * 4 + 4], axis=AX), 1)
                    e.dve.inc(v.reduce_max(km2m[0:1, lh:lh + 1],
                                           kc2s[0:1, lh * 4:lh * 4 + 4], axis=AX), 1)
                    e.dve.inc(v.reduce_max(bpm[0:1, lh:lh + 1],
                                           qr2s[0:1, lh * 4:lh * 4 + 4], axis=AX), 1)
                e.dve.inc(v.reduce_max(kr2m[:], kr2s[:], axis=AX), 1)
                v.wait_ge(e.dve.h, e.dve.n)
                e.dve.inc(v.tensor_tensor(qm2m[:], qm2m[:], bpm[:], ALU.add), 1)
                e.dve.inc(v.tensor_scalar(km2m[:], km2m[:], kr2m[0:1, 0:1],
                                          None, ALU.add), 1)
                v.wait_ge(e.dve.h, e.dve.n)
                e.dve.inc(v.tensor_tensor(bpm[:], qm2m[:], km2m[:], ALU.mult), 1)
            bqk_t = e.dve.n

            @block.scalar
            def _(sc, bqk_t=bqk_t):
                sc.wait_ge(e.dve.h, bqk_t)
                e.sq.inc(sc.activation(bpm[:], bpm[:], AF.Sqrt,
                                       bias=0.0, scale=1.0), 1)
            sqrt_t = e.sq.n

            @block.vector
            def _(v, sqrt_t=sqrt_t):
                v.wait_ge(e.sq.h, sqrt_t)
                e.dve.inc(v.tensor_scalar(negbm[:], bpm[:], 80.0, -SC_MLA,
                                          ALU.min, ALU.mult), 1)
            nbm_t = e.dve.n
            act_now = e.act.n

            @block.tensor
            def _(tt_, nbm_t=nbm_t, act_now=act_now):
                tt_.wait_ge(e.dve.h, nbm_t)
                tt_.wait_ge(e.act.h, act_now)
                for lh in range(8):
                    mm = tt_.matmul(pqr[:, lh:lh + 1], e.onesF[:],
                                    negbm[0:1, lh:lh + 1], start=True, stop=True)
                e.pes.inc(mm, 1)
            nbp_t = e.pes.n

            @block.vector
            def _(v, nbp_t=nbp_t):
                v.wait_ge(e.pes.h, nbp_t)
                e.dve.inc(v.tensor_copy(nbcm[:], pqr[:, 0:8]), 1)
            e.nbcm_ready = e.dve.n

            proj_pe = e.pes.n
            proj_act = e.act.n
            proj_dve = e.dve.n
            SP.pop_all().close()

        _emit_mla_attention(block, e, x_in, x_out, s_, hg, qc, kc, qr, krm, vtm,
                            nbcm, proj_pe, proj_act, proj_dve)
        ST.pop_all().close()


def _emit_mla_attention(block, e, x_in, x_out, s_, hg, qc, kc, qr, krm, vtm,
                        nbcm, proj_pe, proj_act, proj_dve):
    _barrier(block, e)
    nc = e.nc
    wk = e.wk
    t0 = s_ * NTS
    NHH = 8
    with ExitStack() as ST:
        en = ST.enter_context
        wow = en(nc.sbuf_tensor(_un("l_wow"), [128, 4, D], BF16))
        mask4 = en(nc.sbuf_tensor(_un("l_mask4"), [128, 4, 512], BF16))
        attf = en(nc.sbuf_tensor(_un("l_attf"), [128, 4, TT], BF16))
        psb = [en(nc.sbuf_tensor(_un(f"l_psb{i}"), [128, TT], BF16)) for i in range(4)]
        recb = en(nc.sbuf_tensor(_un("l_recb"), [1, TT], BF16))
        rbs = en(nc.sbuf_tensor(_un("l_rbs"), [64, TT], BF16))
        axs = en(nc.sbuf_tensor(_un("l_axs"), [128, DC, TT], F32))
        x2b = en(nc.sbuf_tensor(_un("l_x2b"), [128, DC, TT], F32))
        pSt = [en(nc.psum_tensor(_un(f"b_pSt{i}"), [128, TT], F32)) for i in range(4)]
        pPV = en(nc.psum_tensor(_un("b_pPV"), [65, TT], F32))
        pMi = en(nc.psum_tensor(_un("b_pMi"), [128, TT], F32))

        @block.gpsimd
        def _(g, hg=hg):
            e.ldw.inc(g.dma_start(
                out=wow[:],
                in_=wk["m_W_O"][hg * 512:(hg + 1) * 512]
                .rearrange("(hc p) f -> p hc f", p=128)), 16)
            e.ldw.inc(g.dma_start(
                out=mask4.ap().rearrange("p r f -> p (r f)"), in_=wk["mask4"]), 16)
        wow_done = e.ldw.n

        pv_free = ("dve", proj_dve)
        pmi_guard = ("act", proj_act)
        attf_free = ("pes", proj_pe)
        xs_guard = ("dve", proj_dve)
        st_guard = ("dve", proj_dve)
        x2_guard = ("stw", e.stw.n) if hg == 0 else ("acc", e.acc.n)
        base_stw = e.stw.n
        recb_prev = proj_dve
        rbs_guard = proj_dve
        st_t = {}
        clip_t = {}
        pv_t = {}
        kabs = 0
        exp_jobs = []
        exs_base = e.exs.n

        def semh(name):
            return getattr(e, name).h

        for q5 in range(NTS):
            qsl = slice(q5 * TT, (q5 + 1) * TT)
            nkb = 4 * (q5 + 1)
            for lh in range(NHH):
                hc = lh // 2
                hb = (lh % 2) * 64
                hr = lh // 2
                hrb = (lh % 2) * 32
                ka0 = kabs

                g0, g1 = st_guard
                gp0, gp1 = pv_free

                @block.tensor
                def _(tt_, lh=lh, nkb=nkb, ka0=ka0, hb=hb, hc=hc, hr=hr,
                      hrb=hrb, qsl=qsl, g0=g0, g1=g1, gp0=gp0, gp1=gp1,
                      first=(q5 == 0 and lh == 0)):
                    if first:
                        tt_.wait_ge(e.act.h, proj_act)
                    tt_.wait_ge(semh(g0), g1)

                    def pv(i):
                        ka = ka0 + i
                        tt_.wait_ge(e.exs.h, exs_base + ka + 1)
                        if i == 0:
                            tt_.wait_ge(semh(gp0), gp1)
                        mm = tt_.matmul(pPV[:], vtm.ap()[:, i, lh, :],
                                        psb[ka % 4][:],
                                        start=(i == 0), stop=(i == nkb - 1))
                        e.pes.inc(mm, 1)
                        pv_t[ka] = e.pes.n

                    for i in range(nkb):
                        ka = ka0 + i
                        kt = slice(i * 128, (i + 1) * 128)
                        if ka >= 4:
                            tt_.wait_ge(e.exs.h, exs_base + ka - 3)
                        tt_.matmul(pSt[ka % 4][:], kc.ap()[hb:hb + 64, hc, kt],
                                   qc.ap()[hb:hb + 64, hc, qsl],
                                   start=True, stop=False)
                        mm = tt_.matmul(pSt[ka % 4][:], krm[hrb:hrb + 32, kt],
                                        qr.ap()[hrb:hrb + 32, hr, qsl],
                                        start=False, stop=True)
                        e.pes.inc(mm, 1)
                        st_t[ka] = e.pes.n
                        if i >= 2:
                            pv(i - 2)
                    pv(nkb - 2)
                    pv(nkb - 1)
                pv_all = e.pes.n

                @block.vector
                def _(v, nkb=nkb, ka0=ka0, q5=q5):
                    v.wait_ge(e.ldw.h, wow_done)
                    for i in range(nkb):
                        ka = ka0 + i
                        v.wait_ge(e.pes.h, st_t[ka])
                        e.dve.inc(v.tensor_scalar(pSt[ka % 4][:], pSt[ka % 4][:],
                                                  80.0, -80.0, ALU.min, ALU.max), 1)
                        r = i - 4 * q5
                        if 0 <= r < 4:
                            v.wait_ge(e.dve.h, e.dve.n)
                            e.dve.inc(v.tensor_tensor(pSt[ka % 4][:],
                                                      pSt[ka % 4][:],
                                                      mask4[:, r, :], ALU.add), 1)
                        clip_t[ka] = e.dve.n

                exp_jobs.append((lh, ka0, nkb))
                if lh == NHH - 1:
                    @block.scalar
                    def _(sc, jobs=tuple(exp_jobs)):
                        sc.wait_ge(e.dve.h, e.nbcm_ready)
                        for (jlh, jka0, jnkb) in jobs:
                            for i in range(jnkb):
                                ka = jka0 + i
                                sc.wait_ge(e.dve.h, clip_t[ka])
                                if ka >= 4:
                                    sc.wait_ge(e.pes.h, pv_t[ka - 4])
                                sc.activation(psb[ka % 4][:], pSt[ka % 4][:],
                                              AF.Exp, bias=nbcm[:, jlh:jlh + 1],
                                              scale=SC_MLA).then_inc(e.exs.h, 1)
                                e.exs.n += 1
                    exp_jobs = []
                kabs += nkb

                @block.vector
                def _(v, pv_all=pv_all, recb_prev=recb_prev):
                    v.wait_ge(e.pes.h, pv_all)
                    v.wait_ge(e.dve.h, recb_prev)
                    e.dve.inc(v.tensor_copy(recb[:], pPV[64:65, :]), 1)
                recb_done = e.dve.n
                recb_prev = recb_done
                g0, g1 = pmi_guard

                @block.tensor
                def _(tt_, recb_done=recb_done, g0=g0, g1=g1):
                    tt_.wait_ge(e.dve.h, recb_done)
                    tt_.wait_ge(semh(g0), g1)
                    e.pes.inc(tt_.matmul(pMi[0:64, :], e.onesB[0:1, 0:64], recb[:],
                                         start=True, stop=True), 1)
                rb_pe = e.pes.n
                g0, g1 = attf_free

                @block.vector
                def _(v, lh=lh, rb_pe=rb_pe, rbs_guard=rbs_guard, g0=g0, g1=g1):
                    v.wait_ge(e.pes.h, rb_pe)
                    v.wait_ge(e.dve.h, rbs_guard)
                    with e.nc.allow_low_precision(reason="softmax recip bf16"):
                        e.dve.inc(v.reciprocal(rbs[:], pMi[0:64, :]), 1)
                    v.wait_ge(e.dve.h, e.dve.n)
                    if lh == 0:
                        v.wait_ge(semh(g0), g1)
                    hc2 = lh // 2
                    hb2 = (lh % 2) * 64
                    e.dve.inc(v.tensor_tensor(attf.ap()[hb2:hb2 + 64, hc2, :],
                                              pPV[0:64, :], rbs[:], ALU.mult), 1)
                pmi_guard = ("dve", e.dve.n)
                pv_free = ("dve", e.dve.n)
                rbs_guard = e.dve.n

            t = t0 + q5
            attf_all = e.dve.n
            g0, g1 = xs_guard
            if hg == 0:
                ldx_t = e.ldx2.n + 16

                @block.sync
                def _(sy, t=t, g0=g0, g1=g1):
                    sy.wait_ge(semh(g0), g1)
                    e.ldx2.inc(sy.dma_start(out=axs[:], in_=e.dtile(x_in, t)), 16)

            gx0, gx1 = x2_guard
            wo_pe = {}
            for dc in range(DC):
                @block.tensor
                def _(tt_, dc=dc, attf_all=attf_all):
                    if dc == 0:
                        tt_.wait_ge(e.dve.h, attf_all)
                        tt_.wait_ge(e.ldw.h, wow_done)
                    if dc >= 4:
                        tt_.wait_ge(e.dve.h, wo_used[dc - 4])
                    for hv in range(4):
                        mm = tt_.matmul(pSt[dc % 4][:],
                                        wow[:, hv, dc * 128:(dc + 1) * 128],
                                        attf.ap()[:, hv, :],
                                        start=(hv == 0), stop=(hv == 3))
                    e.pes.inc(mm, 1)
                if dc == 0:
                    wo_used = {}
                wo_pe[dc] = e.pes.n

                if hg == 0:
                    @block.vector
                    def _(v, dc=dc, ldx_t=ldx_t, gx0=gx0, gx1=gx1):
                        v.wait_ge(e.ldx2.h, ldx_t)
                        v.wait_ge(e.pes.h, wo_pe[dc])
                        if dc == 0:
                            v.wait_ge(semh(gx0), gx1)
                        e.dve.inc(v.tensor_tensor(x2b[:, dc, :], axs[:, dc, :],
                                                  pSt[dc % 4][:], ALU.add), 1)
                        wo_used[dc] = e.dve.n
                else:
                    @block.vector
                    def _(v, dc=dc, gx0=gx0, gx1=gx1):
                        v.wait_ge(e.pes.h, wo_pe[dc])
                        if dc == 0:
                            v.wait_ge(semh(gx0), gx1)
                        e.dve.inc(v.tensor_copy(x2b[:, dc, :],
                                                pSt[dc % 4][:]), 1)
                        wo_used[dc] = e.dve.n
            x2_done = e.dve.n
            attf_free = ("pes", wo_pe[DC - 1])
            xs_guard = ("dve", x2_done)
            st_guard = ("dve", x2_done)

            if hg == 0:
                @block.sync
                def _(sy, t=t, x2_done=x2_done):
                    sy.wait_ge(e.dve.h, x2_done)
                    e.stw.inc(sy.dma_start(out=e.dtile(x_out, t), in_=x2b[:]), 16)
                x2_guard = ("stw", e.stw.n)
            else:
                @block.gpsimd
                def _(g, t=t, x2_done=x2_done, base_stw=base_stw):
                    g.wait_ge(e.dve.h, x2_done)
                    g.wait_ge(e.stw.h, base_stw)
                    e.acc.inc(g.dma_start(out=e.dtile(x_out, t), in_=x2b[:],
                                          accum_op=ALU.add), 16)
                x2_guard = ("acc", e.acc.n)
        ST.pop_all().close()


def _emit_final(block, e):
    @block.gpsimd
    def _(g):
        g.wait_ge(e.stw.h, e.stw.n)
        g.wait_ge(e.acc.h, e.acc.n)

    @block.sync
    def _(sy):
        sy.wait_ge(e.acc.h, e.acc.n)
        sy.wait_ge(e.stw.h, e.stw.n)


def kernel(**inputs):
    C, xc = _prepare(inputs)
    nc = _build()
    in_maps = [dict(C, x_fm=xc[c]) for c in range(N_CORES)]
    res = run_bass_kernel_spmd(nc, in_maps, core_ids=list(range(N_CORES)))
    out = np.empty((16, 2048, 1024), np.float32)
    for c in range(N_CORES):
        out[2 * c:2 * c + 2] = res.results[c]["y_fm"].T.reshape(BPC, S, D)
    return out


# version 05 (final)
# speedup vs baseline: 1.1888x; 1.1888x over previous
import numpy as np
import ml_dtypes
from contextlib import ExitStack
import concourse.bass as bass
import concourse.mybir as mybir
from concourse.bass_utils import run_bass_kernel_spmd

F32 = mybir.dt.float32
BF16 = mybir.dt.bfloat16
AF = mybir.ActivationFunctionType
ALU = mybir.AluOpType
AX = mybir.AxisListType.X

D = 1024
S = 2048
BPC = 2
NTOK = BPC * S
DC = 8
TT = 512
NT = NTOK // TT
NTS = S // TT
EPS1 = 1e-5
EPS2 = 1e-6
NH = 16
SC_IN = float(64 ** -0.5)
SC_MLA = float(1.0 / np.sqrt(np.float32(96.0)))
NE = 8
HID = 2048
NEG = -1e30

PHASES = "ABC"
DEBUG = ()
N_CORES = 8


def _bd(*mats):
    n = sum(m.shape[0] for m in mats)
    m_ = sum(m.shape[1] for m in mats)
    out = np.zeros((n, m_), np.float32)
    r = c = 0
    for m in mats:
        out[r:r + m.shape[0], c:c + m.shape[1]] = m
        r += m.shape[0]; c += m.shape[1]
    return out


def _rope_tables():
    j = np.arange(0, 64, 2, dtype=np.float64) / 64.0
    invf = 1.0 / (10000.0 ** j)
    t = np.arange(S, dtype=np.float64)
    ang = t[None, :] * invf[:, None]
    c64 = np.concatenate([np.cos(ang), np.cos(ang)], axis=0).astype(np.float32)
    s64 = np.concatenate([np.sin(ang), np.sin(ang)], axis=0).astype(np.float32)
    c2i = np.concatenate([c64, c64], axis=0)
    s2i = np.concatenate([s64, s64], axis=0)
    B = np.zeros((64, 64), np.float32)
    for i in range(32):
        B[i, i + 32] = -1.0
        B[i + 32, i] = 1.0
    b2T = np.ascontiguousarray(_bd(B, B).T)
    i2 = np.arange(0, 32, 2, dtype=np.float64) / 32.0
    invm = 10000.0 ** (-i2)
    angm = t[None, :] * invm[:, None]
    c32 = np.repeat(np.cos(angm), 2, axis=0).astype(np.float32)
    s32 = np.repeat(np.sin(angm), 2, axis=0).astype(np.float32)
    cm2 = np.concatenate([c32, c32], axis=0)
    sm2 = np.concatenate([s32, s32], axis=0)
    A = np.zeros((32, 32), np.float32)
    for i in range(16):
        A[2 * i, 2 * i + 1] = -1.0
        A[2 * i + 1, 2 * i] = 1.0
    a2T = np.ascontiguousarray(_bd(A, A).T)
    a1T = np.ascontiguousarray(A.T)
    return c2i, s2i, b2T, cm2, sm2, a2T, a1T


def _prepare(inputs):
    f = np.float32
    C = {}
    def ct(x):
        return np.ascontiguousarray(np.asarray(x, dtype=f))
    C["qkvT"] = ct(np.asarray(inputs["L0_ffn_qkv_w"], f).T)
    C["outT"] = ct(np.asarray(inputs["L0_ffn_out_w"], f).T)
    C["w1T"] = ct(np.asarray(inputs["L0_ffn_w1"], f).transpose(0, 2, 1))
    C["w2T"] = ct(np.asarray(inputs["L0_ffn_w2"], f).transpose(0, 2, 1))
    C["w3T"] = ct(np.asarray(inputs["L0_ffn_w3"], f).transpose(0, 2, 1))
    C["gateT"] = ct(np.asarray(inputs["L0_ffn_gate_w"], f).T)
    for k in ("W_DQ", "W_UQ", "W_QR", "W_DKV", "W_UK", "W_UV", "W_KR", "W_O"):
        C["m_" + k] = ct(inputs["L0_attn_" + k])
    qb = np.asarray(inputs["L0_ffn_qkv_b"], f)
    C["qkvb_c"] = ct(qb.reshape(24, 128).T)
    C["vbr"] = ct(qb[2048:].reshape(1, 1024))
    C["outb_c"] = ct(np.asarray(inputs["L0_ffn_out_b"], f).reshape(8, 128).T)
    C["n1c"] = ct(np.asarray(inputs["L0_ffn_norm1_w"], f).reshape(8, 128).T)
    C["n2c"] = ct(np.asarray(inputs["L0_ffn_norm2_w"], f).reshape(8, 128).T)
    C["n1r"] = ct(np.asarray(inputs["L0_ffn_norm1_w"], f).reshape(1, 1024))
    C["b1c"] = ct(np.asarray(inputs["L0_ffn_b1"], f).reshape(8, 16, 128).transpose(2, 0, 1))
    C["b2c"] = ct(np.asarray(inputs["L0_ffn_b2"], f).reshape(8, 16, 128).transpose(2, 0, 1))
    C["b3r"] = ct(np.asarray(inputs["L0_ffn_b3"], f).reshape(1, 8192))
    c2i, s2i, b2T, cm2, sm2, a2T, a1T = _rope_tables()
    C["c2i"], C["s2i"], C["b2T"] = c2i, s2i, b2T
    C["cm2"], C["sm2"], C["a2T"], C["a1T"] = cm2, sm2, a2T, a1T
    C["identF"] = np.eye(128, dtype=f)
    m4 = np.zeros((4, 128, 512), f)
    tri = np.fromfunction(lambda k, q: (k > q), (128, 128))
    for r in range(4):
        for s_ in range(4):
            if s_ < r:
                m4[r][:, s_ * 128:(s_ + 1) * 128] = NEG
            elif s_ == r:
                m4[r][:, s_ * 128:(s_ + 1) * 128] = np.where(tri, NEG, 0.0)
    C["mask4"] = np.ascontiguousarray(m4.transpose(1, 0, 2).reshape(128, 4 * 512))
    x = np.asarray(inputs["L0_attn_x"], f)
    xc = [np.ascontiguousarray(x[2 * c:2 * c + 2].reshape(NTOK, D).T)
          for c in range(N_CORES)]
    return C, xc


_UID = [0]


def _un(name):
    _UID[0] += 1
    return f"{name}_{_UID[0]}"


class Sem:
    __slots__ = ("h", "n")
    def __init__(self, h):
        self.h = h
        self.n = 0
    def inc(self, inst, k=1):
        inst.then_inc(self.h, k)
        self.n += k
        return self.n


class Env:
    pass


DRAM_SPECS = [
    ("qkvT", [D, 3072]), ("outT", [D, D]),
    ("w1T", [NE, D, HID]), ("w2T", [NE, D, HID]), ("w3T", [NE, HID, D]),
    ("gateT", [D, 8]),
    ("m_W_DQ", [D, 256]), ("m_W_UQ", [256, D]), ("m_W_QR", [256, 512]),
    ("m_W_DKV", [D, 256]), ("m_W_UK", [256, D]), ("m_W_UV", [256, D]),
    ("m_W_KR", [D, 32]), ("m_W_O", [D, D]),
    ("qkvb_c", [128, 24]), ("vbr", [1, 1024]), ("outb_c", [128, 8]),
    ("n1c", [128, 8]), ("n2c", [128, 8]), ("n1r", [1, 1024]),
    ("b1c", [128, 8, 16]), ("b2c", [128, 8, 16]), ("b3r", [1, 8192]),
    ("c2i", [128, S]), ("s2i", [128, S]), ("b2T", [128, 128]),
    ("cm2", [64, S]), ("sm2", [64, S]), ("a2T", [64, 64]), ("a1T", [32, 32]),
    ("identF", [128, 128]), ("mask4", [128, 4 * 512]),
]

ENGS = ("tensor", "vector", "scalar", "gpsimd", "sync")


def _build():
    nc = bass.Bass("TRN2", target_bir_lowering=False)
    e = Env()
    e.nc = nc
    dt = nc.dram_tensor
    e.x0 = dt("x_fm", [D, NTOK], F32, kind="ExternalInput").ap()
    e.wk = {name: dt(name, shape, F32, kind="ExternalInput").ap()
            for name, shape in DRAM_SPECS}

    def scratch(name, dtype):
        kind = "ExternalOutput" if name in DEBUG else "Internal"
        return dt(name, [D, NTOK], dtype, kind=kind).ap()
    e.x1s = scratch("x1s", F32)
    e.x2s = scratch("x2s", F32)
    e.us = scratch("us", F32)
    e.u2s = scratch("u2s", BF16)
    e.u1s = scratch("u1s", BF16) if "u1s" in DEBUG else None
    e.yfm = dt("y_fm", [D, NTOK], F32, kind="ExternalOutput").ap()
    if "rows" in DEBUG:
        e.rows_dbg = dt("rows_dbg", [4, NTOK], BF16, kind="ExternalOutput").ap()
        e.rowsf_dbg = dt("rowsf_dbg", [2, NTOK], F32, kind="ExternalOutput").ap()
    else:
        e.rows_dbg = None
        e.rowsf_dbg = None
    e.qk_dbg = (dt("qk_dbg", [128, 4 * 8 * S], BF16, kind="ExternalOutput").ap()
                if "qk" in DEBUG else None)
    e.attf_dbg = (dt("attf_dbg", [128, 8 * NTOK], BF16, kind="ExternalOutput").ap()
                  if "attf" in DEBUG else None)

    def dtile(ap, t, w=TT):
        return ap.rearrange("(dc p) f -> p dc f", p=128)[:, :, t * w:(t + 1) * w]
    e.dtile = dtile

    with ExitStack() as ST:
        en = ST.enter_context
        for s_ in ("ldc", "ldg", "ldx", "ldx2", "ldw", "sq", "pes", "row", "dve", "act",
                   "pool", "stw", "acc", "bar", "gat", "ldu", "exs"):
            setattr(e, s_, Sem(en(nc.semaphore(s_))))
        sb = nc.sbuf_tensor
        e.identF = en(sb("c_identF", [128, 128], F32))
        e.onesF = en(sb("c_onesF", [1, 128], F32))
        e.onesB = en(sb("c_onesB", [1, 128], BF16))
        e.oneC = en(sb("c_oneC", [128, 1], BF16))
        e.b2T = en(sb("c_b2T", [128, 128], BF16))
        e.a2T = en(sb("c_a2T", [64, 64], BF16))
        e.a1T = en(sb("c_a1T", [32, 32], BF16))
        e.qkvb = en(sb("c_qkvb", [128, 24], F32))
        e.vbr = en(sb("c_vbr", [1, 1024], BF16))
        e.outb = en(sb("c_outb", [128, 8], F32))
        e.n1c = en(sb("c_n1c", [128, 8], F32))
        e.n2c = en(sb("c_n2c", [128, 8], F32))
        e.n1r = en(sb("c_n1r", [1, 1024], BF16))
        e.b1c = en(sb("c_b1c", [128, 8, 16], F32))
        e.b2c = en(sb("c_b2c", [128, 8, 16], F32))
        e.gateT = en(sb("c_gateT", [128, 8, 8], BF16))
        e.onesC1 = en(sb("c_onesC1", [128, 1], BF16))
        e.ceps1 = en(sb("c_eps1", [1, 1], F32))
        e.ceps2 = en(sb("c_eps2", [1, 1], F32))

        with nc.Block() as block:
            _emit_const_loads(block, e)
            out_a = e.x1s if ("B" in PHASES or "C" in PHASES) else e.yfm
            _emit_ffn(block, e, e.x0, out_a)
            if "B" in PHASES:
                _barrier(block, e)
                out_b = e.x2s if "C" in PHASES else e.yfm
                _emit_mla(block, e, out_a, out_b)
            if "C" in PHASES:
                _barrier(block, e)
                _emit_ffn(block, e, e.x2s, e.yfm)
            _emit_final(block, e)
    return nc


def _barrier(block, e):
    bs = e.bar
    caps = [(s_, getattr(e, s_).n) for s_ in
            ("ldc", "ldg", "ldx", "ldx2", "ldw", "ldu", "stw", "acc")]

    @block.gpsimd
    def _(g):
        for s_, n_ in caps:
            g.wait_ge(getattr(e, s_).h, n_)

    @block.sync
    def _(sy):
        for s_, n_ in caps:
            sy.wait_ge(getattr(e, s_).h, n_)
    for name in ENGS:
        def _f(eng, bs=bs):
            bs.inc(eng.nop(), 1)
        getattr(block, name)(_f)
    tgt = bs.n
    for name in ENGS:
        def _f(eng, tgt=tgt, bs=bs):
            eng.wait_ge(bs.h, tgt)
        getattr(block, name)(_f)


def _emit_const_loads(block, e):
    wk = e.wk
    ldc = e.ldc

    @block.sync
    def _(s_):
        for src, dst in [
            (wk["identF"], e.identF[:]), (wk["qkvb_c"], e.qkvb[:]),
            (wk["outb_c"], e.outb[:]), (wk["n1c"], e.n1c[:]), (wk["n2c"], e.n2c[:]),
            (wk["b1c"], e.b1c[:]), (wk["b2c"], e.b2c[:]),
        ]:
            ldc.inc(s_.dma_start(out=dst, in_=src), 16)

    @block.gpsimd
    def _(g):
        for src, dst in [
            (wk["b2T"], e.b2T[:]), (wk["a2T"], e.a2T[:]), (wk["a1T"], e.a1T[:]),
            (wk["n1r"], e.n1r[:]),
            (wk["vbr"], e.vbr[:]),
            (wk["gateT"].rearrange("(dc p) x -> p dc x", p=128), e.gateT[:]),
        ]:
            e.ldg.inc(g.dma_start(out=dst, in_=src), 16)

    @block.vector
    def _(v):
        v.memset(e.onesF[:], 1.0)
        v.memset(e.onesB[:], 1.0)
        v.memset(e.onesC1[:], 1.0)
        v.memset(e.ceps1[:], EPS1)
        e.row.inc(v.memset(e.ceps2[:], EPS2), 1)
        e.dve.inc(v.memset(e.oneC[:], 1.0 / 1024.0), 1)
    e.ldc_done = ldc.n
    e.ldg_done = e.ldg.n
    e.memset_done = e.dve.n


def _row_chain(block, e, ops):
    for eng_name, fn in ops:
        tgt = e.row.n
        def _f(eng, fn=fn, tgt=tgt):
            eng.wait_ge(e.row.h, tgt)
            with e.nc.allow_low_precision(reason="per-token scale rows in bf16"):
                e.row.inc(fn(eng), 1)
        getattr(block, eng_name)(_f)


def _emit_stats(block, e, x_in, with_mean):
    nc = e.nc
    with ExitStack() as ST:
        en = ST.enter_context
        xb = [en(nc.sbuf_tensor(_un(f"st_xb{i}"), [128, DC, TT], BF16)) for i in range(2)]
        sq = [en(nc.sbuf_tensor(_un(f"st_sq{i}"), [128, DC, TT], BF16)) for i in range(2)]
        pm = [en(nc.psum_tensor(_un(f"st_pm{i}"), [1, TT], F32)) for i in range(2)]
        pms = [en(nc.psum_tensor(_un(f"st_pms{i}"), [1, TT], F32)) for i in range(2)]
        b_ldx = e.ldg.n
        b_sq = e.sq.n
        b_pe = e.pes.n
        b_row = e.row.n
        for t in range(NT):
            @block.gpsimd
            def _(g, t=t):
                if t >= 2:
                    g.wait_ge(e.pes.h, b_pe + (t - 1))
                e.ldg.inc(g.dma_start(out=xb[t % 2][:], in_=e.dtile(x_in, t)), 16)

            @block.scalar
            def _(sc, t=t):
                sc.wait_ge(e.ldg.h, b_ldx + 16 * (t + 1))
                if t >= 2:
                    sc.wait_ge(e.pes.h, b_pe + (t - 1))
                e.sq.inc(sc.activation(sq[t % 2].ap(), xb[t % 2].ap(), AF.Square,
                                       bias=0.0, scale=1.0), 1)

            @block.tensor
            def _(tt_, t=t):
                tt_.wait_ge(e.dve.h, e.memset_done)
                tt_.wait_ge(e.sq.h, b_sq + (t + 1))
                if t >= 2:
                    tt_.wait_ge(e.row.h, b_row + 2 * (t - 1))
                if with_mean:
                    for dc in range(DC):
                        tt_.matmul(pm[t % 2][:], e.oneC[:], xb[t % 2][:, dc, :],
                                   start=(dc == 0), stop=(dc == DC - 1))
                for dc in range(DC):
                    mm = tt_.matmul(pms[t % 2][:], e.oneC[:], sq[t % 2][:, dc, :],
                                    start=(dc == 0), stop=(dc == DC - 1))
                e.pes.inc(mm, 1)

            @block.vector
            def _(v, t=t):
                v.wait_ge(e.pes.h, b_pe + (t + 1))
                if with_mean:
                    v.tensor_copy(e.r_m[:, t * TT:(t + 1) * TT], pm[t % 2][:])
                e.row.inc(v.tensor_copy(e.r_ms[:, t * TT:(t + 1) * TT],
                                        pms[t % 2][:]), 2)


def _emit_ln_rows(block, e, compose_rms):
    ops = [
        ("scalar", lambda sc: sc.activation(e.rs1[:], e.r_m[:], AF.Square, bias=0.0, scale=1.0)),
        ("vector", lambda v: v.tensor_tensor(e.rs2[:], e.r_ms[:], e.rs1[:], ALU.subtract)),
        ("scalar", lambda sc: sc.activation(e.rs1[:], e.rs2[:], AF.Sqrt, bias=e.ceps1[0:1, 0:1], scale=1.0)),
        ("vector", lambda v: v.reciprocal(e.r_a[:], e.rs1[:])),
        ("vector", lambda v: v.scalar_tensor_tensor(e.r_b[:], e.r_m[:], -1.0, e.r_a[:], ALU.mult, ALU.mult)),
    ]
    if compose_rms:
        ops += [
            ("scalar", lambda sc: sc.activation(e.rs1[:], e.r_a[:], AF.Square, bias=0.0, scale=1.0)),
            ("vector", lambda v: v.tensor_tensor(e.rs3[:], e.rs2[:], e.rs1[:], ALU.mult)),
            ("scalar", lambda sc: sc.activation(e.rs1[:], e.rs3[:], AF.Sqrt, bias=e.ceps2[0:1, 0:1], scale=1.0)),
            ("vector", lambda v: v.reciprocal(e.rs3[:], e.rs1[:])),
            ("vector", lambda v: v.tensor_tensor(e.r_A[:], e.r_a[:], e.rs3[:], ALU.mult)),
            ("vector", lambda v: v.scalar_tensor_tensor(e.r_B[:], e.r_m[:], -1.0, e.r_A[:], ALU.mult, ALU.mult)),
        ]
    _row_chain(block, e, ops)


def _emit_rms_row(block, e):
    _row_chain(block, e, [
        ("scalar", lambda sc: sc.activation(e.r_B[:], e.r_ms[:], AF.Sqrt, bias=e.ceps2[0:1, 0:1], scale=1.0)),
        ("vector", lambda v: v.reciprocal(e.r_A[:], e.r_B[:])),
    ])


def _emit_ffn(block, e, x_in, x_out):
    nc = e.nc
    with ExitStack() as SF:
        enf = SF.enter_context
        for rn in ("r_A", "r_B", "r_ms", "r_a", "r_b"):
            setattr(e, rn, enf(nc.sbuf_tensor(_un("f_") + rn, [1, NTOK], BF16))[:])

        with ExitStack() as SS:
            ens = SS.enter_context
            for rn in ("r_m", "rs1", "rs2", "rs3"):
                setattr(e, rn, ens(nc.sbuf_tensor(_un("f_") + rn, [1, NTOK], F32))[:])
            _emit_stats(block, e, x_in, with_mean=True)
            _emit_ln_rows(block, e, compose_rms=True)
            if e.rows_dbg is not None:
                rt_ = e.row.n

                @block.sync
                def _(sy, rt_=rt_):
                    sy.wait_ge(e.row.h, rt_)
                    for i, r in enumerate((e.r_A, e.r_B, e.r_a, e.r_b)):
                        e.stw.inc(sy.dma_start(out=e.rows_dbg[i:i + 1, :], in_=r), 16)

                @block.gpsimd
                def _(g, rt_=rt_):
                    g.wait_ge(e.row.h, rt_)
                    for i, r in enumerate((e.r_m, e.r_ms)):
                        e.stw.inc(g.dma_start(out=e.rowsf_dbg[i:i + 1, :], in_=r), 16)
        for s_ in range(BPC):
            _emit_inner_seq(block, e, x_in, s_)
        _emit_rms_row(block, e)
        with ExitStack() as SC:
            enc = SC.enter_context
            cT = [enc(nc.sbuf_tensor(_un(f"f_cT{i}"), [97, NTOK], BF16)) for i in range(3)]
            e.combR = [cT[i // 3][(i % 3) * 32:(i % 3) * 32 + 1, :] for i in range(NE)]
            base_done = _emit_u2_gate_base(block, e, x_in, x_out)
            _emit_moe(block, e, x_out, base_done)
            _drain_phase(block, e)
        SF.pop_all().close()


def _drain_phase(block, e):
    _barrier(block, e)


def _emit_inner_seq(block, e, x_in, s_):
    _barrier(block, e)
    nc = e.nc
    wk = e.wk
    t0 = s_ * NTS
    row_ready = e.row.n
    with ExitStack() as ST:
        en = ST.enter_context
        qrope = en(nc.sbuf_tensor(_un("i_qrope"), [128, 8, S], BF16))
        krope = en(nc.sbuf_tensor(_un("i_krope"), [128, 8, S], BF16))
        vtm = en(nc.sbuf_tensor(_un("i_vtm"), [128, 16, 16, 65], BF16))
        nbc = en(nc.sbuf_tensor(_un("i_nbc"), [128, 16], F32))

        with ExitStack() as SP:
            enp = SP.enter_context
            u1f = enp(nc.sbuf_tensor(_un("i_u1f"), [128, DC, S], BF16))
            qkraw = [enp(nc.sbuf_tensor(_un(f"i_qkraw{i}"), [128, TT], BF16)) for i in range(2)]
            rt = [enp(nc.sbuf_tensor(_un(f"i_rt{i}"), [128, TT], F32)) for i in range(2)]
            pGB = enp(nc.psum_tensor(_un("i_pGB"), [128, TT], F32))
            pBW = [enp(nc.psum_tensor(_un(f"i_pBW{i}"), [128, TT], F32)) for i in range(1)]
            pqk = [enp(nc.psum_tensor(_un(f"i_pqk{i}"), [128, TT], F32)) for i in range(2)]
            pbq = [enp(nc.psum_tensor(_un(f"i_pbq{i}"), [128, TT], F32)) for i in range(2)]
            pvv = [enp(nc.psum_tensor(_un(f"i_pvv{i}"), [128, TT], F32)) for i in range(2)]


            @block.vector
            def _(v):
                v.wait_ge(e.ldc.h, e.ldc_done)
                v.wait_ge(e.ldg.h, e.ldg_done)
                e.dve.inc(v.memset(vtm.ap()[:, :, :, 64:65], 1.0), 1)
            vones = e.dve.n

            SX = ExitStack()
            xs = SX.enter_context(nc.sbuf_tensor(_un("i_xs"), [128, DC, TT], F32))
            xs_free = {}
            u1_done = {}
            for ts in range(NTS):
                t = t0 + ts
                sl = slice(t * TT, (t + 1) * TT)
                ssl = slice(ts * TT, (ts + 1) * TT)
                ldx_t = e.ldx.n + 16

                @block.sync
                def _(sy, ts=ts, t=t):
                    if ts >= 1:
                        sy.wait_ge(e.dve.h, xs_free[ts - 1])
                    e.ldx.inc(sy.dma_start(out=xs[:], in_=e.dtile(x_in, t)), 16)

                @block.tensor
                def _(tt_, ts=ts, sl=sl):
                    tt_.wait_ge(e.row.h, row_ready)
                    if ts >= 1:
                        tt_.wait_ge(e.dve.h, u1_done[ts - 1])
                    e.pes.inc(tt_.matmul(pGB[:], e.onesB[:], e.r_A[:, sl],
                                         start=True, stop=True), 1)
                for dc in range(DC):
                    @block.tensor
                    def _(tt_, dc=dc, ts=ts, sl=sl):
                        if dc >= 1:
                            tt_.wait_ge(e.dve.h, bw_used[dc - 1])
                        elif ts >= 1:
                            tt_.wait_ge(e.dve.h, u1_done[ts - 1])
                        e.pes.inc(tt_.matmul(pBW[0][:],
                                             e.n1r[0:1, dc * 128:(dc + 1) * 128],
                                             e.r_B[:, sl], start=True, stop=True), 1)
                    if dc == 0:
                        bw_used = {}
                    bw_pe = e.pes.n

                    @block.vector
                    def _(v, dc=dc, ts=ts, ssl=ssl, bw_pe=bw_pe, ldx_t=ldx_t):
                        v.wait_ge(e.pes.h, bw_pe)
                        v.wait_ge(e.ldx.h, ldx_t)
                        e.dve.inc(v.tensor_tensor(rt[dc % 2][:], xs[:, dc, :],
                                                  pGB[:], ALU.mult), 1)
                        v.wait_ge(e.dve.h, e.dve.n)
                        e.dve.inc(v.scalar_tensor_tensor(
                            u1f[:, dc, ssl], rt[dc % 2][:], e.n1c[:, dc:dc + 1],
                            pBW[0][:], ALU.mult, ALU.add), 1)
                        bw_used[dc] = e.dve.n
                xs_free[ts] = e.dve.n
                u1_done[ts] = e.dve.n
            u1_all = e.dve.n
            SX.pop_all().close()

            if e.u1s is not None:
                @block.sync
                def _(sy, u1_all=u1_all):
                    sy.wait_ge(e.dve.h, u1_all)
                    for ts in range(NTS):
                        e.stw.inc(sy.dma_start(
                            out=e.dtile(e.u1s, t0 + ts),
                            in_=u1f[:, :, ts * TT:(ts + 1) * TT]), 16)

            with ExitStack() as SV:
                wv = SV.enter_context(nc.sbuf_tensor(_un("i_wv"), [128, DC, 512], BF16))
                tbl_done = e.ldw.n
                v_evac = {}
                vh_used = {}
                for hh in range(2):
                    @block.gpsimd
                    def _(g, hh=hh, u1_all=u1_all):
                        g.wait_ge(e.ldc.h, e.ldc_done)
                        g.wait_ge(e.ldg.h, e.ldg_done)
                        g.wait_ge(e.dve.h, u1_all)
                        if hh >= 1:
                            g.wait_ge(e.pes.h, vh_used[hh - 1])
                        e.ldw.inc(g.dma_start(
                            out=wv[:],
                            in_=wk["qkvT"][:, 2048 + hh * 512:2048 + (hh + 1) * 512]
                            .rearrange("(dc p) f -> p dc f", p=128)), 16)
                    wvl = e.ldw.n
                    for ts in range(NTS):
                        for st_ in range(4):
                            ai = hh * 16 + ts * 4 + st_

                            @block.tensor
                            def _(tt_, st_=st_, hh=hh, ts=ts, ai=ai,
                                  u1_all=u1_all, wvl=wvl):
                                tt_.wait_ge(e.dve.h, u1_all)
                                tt_.wait_ge(e.ldw.h, wvl)
                                if ai >= 2:
                                    tt_.wait_ge(e.act.h, v_evac[ai - 2])
                                for dc in range(DC):
                                    tt_.matmul(pvv[ai % 2][:],
                                               u1f[:, dc, ts * TT + st_ * 128:ts * TT + (st_ + 1) * 128],
                                               wv[:, dc, :],
                                               start=(dc == 0), stop=False)
                                mm = tt_.matmul(pvv[ai % 2][:], e.onesB[:],
                                                e.vbr[0:1, hh * 512:(hh + 1) * 512],
                                                start=False, stop=True)
                                e.pes.inc(mm, 1)
                                if ts == NTS - 1 and st_ == 3:
                                    vh_used[hh] = e.pes.n
                            pe_v = e.pes.n

                            @block.vector
                            def _(v, st_=st_, hh=hh, ts=ts, ai=ai, pe_v=pe_v,
                                  vones=vones):
                                v.wait_ge(e.pes.h, pe_v)
                                v.wait_ge(e.dve.h, vones)
                                kt = ts * 4 + st_
                                e.act.inc(v.tensor_copy(
                                    vtm.ap()[:, kt, hh * 8:(hh + 1) * 8, 0:64],
                                    pvv[ai % 2][:]), 1)
                                v_evac[ai] = e.act.n

                v_done_pe = e.pes.n
                v_done_act = e.act.n
                SV.pop_all().close()

            with ExitStack() as SQ:
                enq = SQ.enter_context
                c2i = enq(nc.sbuf_tensor(_un("i_c2i"), [128, S], BF16))
                s2i = enq(nc.sbuf_tensor(_un("i_s2i"), [128, S], BF16))
                wc = [enq(nc.sbuf_tensor(_un(f"i_wc{i}"), [128, DC, 128], BF16)) for i in range(2)]
                sqn = enq(nc.sbuf_tensor(_un("i_sqn"), [128, TT], BF16))
                qn2 = enq(nc.sbuf_tensor(_un("i_qn2"), [1, 64], F32))
                kn2s = enq(nc.sbuf_tensor(_un("i_kn2s"), [1, 64], F32))
                qm2 = enq(nc.sbuf_tensor(_un("i_qm2"), [1, 16], F32))
                km2 = enq(nc.sbuf_tensor(_un("i_km2"), [1, 16], F32))
                b2r = enq(nc.sbuf_tensor(_un("i_b2r"), [1, 16], F32))
                negbr = enq(nc.sbuf_tensor(_un("i_negbr"), [1, 16], F32))

                @block.gpsimd
                def _(g):
                    g.wait_ge(e.ldc.h, e.ldc_done)
                    g.wait_ge(e.ldg.h, e.ldg_done)
                    g.wait_ge(e.pes.h, v_done_pe)
                    e.ldw.inc(g.dma_start(out=c2i[:], in_=wk["c2i"]), 16)
                    e.ldw.inc(g.dma_start(out=s2i[:], in_=wk["s2i"]), 16)
                tbl2_done = e.ldw.n

                qk_evac = {}
                bq_pe = {}
                rope_done = {}
                wc_used = {}
                sqn_t = {}
                nrm_pe = {}
                nrm_red = {}
                for c in range(16):
                    @block.gpsimd
                    def _(g, c=c):
                        if c >= 2:
                            g.wait_ge(e.pes.h, wc_used[c - 2])
                        e.ldw.inc(g.dma_start(
                            out=wc[c % 2][:],
                            in_=wk["qkvT"][:, c * 128:(c + 1) * 128]
                            .rearrange("(dc p) f -> p dc f", p=128)), 16)
                    wcl = e.ldw.n
                    for ts in range(NTS):
                        ac = c * 4 + ts
                        ssl = slice(ts * TT, (ts + 1) * TT)

                        @block.tensor
                        def _(tt_, c=c, ts=ts, ssl=ssl, ac=ac, wcl=wcl,
                              u1_all=u1_all):
                            tt_.wait_ge(e.dve.h, u1_all)
                            tt_.wait_ge(e.ldw.h, wcl)
                            if ac >= 2:
                                tt_.wait_ge(e.act.h, qk_evac[ac - 2])
                            for dc in range(DC):
                                mm = tt_.matmul(pqk[ac % 2][:], wc[c % 2][:, dc, :],
                                                u1f[:, dc, ssl],
                                                start=(dc == 0), stop=(dc == DC - 1))
                            e.pes.inc(mm, 1)
                            if ts == NTS - 1:
                                wc_used[c] = e.pes.n
                        qk_pe = e.pes.n

                        @block.vector
                        def _(v, c=c, qk_pe=qk_pe, ac=ac):
                            v.wait_ge(e.pes.h, qk_pe)
                            if ac >= 2:
                                v.wait_ge(e.pes.h, bq_pe[ac - 2])
                                v.wait_ge(e.dve.h, rope_done[ac - 2])
                                v.wait_ge(e.pool.h, sqn_t[ac - 2])
                            e.act.inc(v.tensor_scalar(qkraw[ac % 2][:], pqk[ac % 2][:],
                                                      e.qkvb[:, c:c + 1], None,
                                                      ALU.add), 1)
                            qk_evac[ac] = e.act.n

                        @block.tensor
                        def _(tt_, ac=ac):
                            tt_.wait_ge(e.act.h, qk_evac[ac])
                            if ac >= 2:
                                tt_.wait_ge(e.dve.h, rope_done[ac - 2])
                            e.pes.inc(tt_.matmul(pbq[ac % 2][:], e.b2T[:], qkraw[ac % 2][:],
                                                 start=True, stop=True), 1)
                            bq_pe[ac] = e.pes.n

                        @block.gpsimd
                        def _(g, ssl=ssl, ac=ac, u1_all=u1_all):
                            g.wait_ge(e.act.h, qk_evac[ac])
                            if ac >= 1:
                                g.wait_ge(e.dve.h, rope_done[ac - 1])
                            else:
                                g.wait_ge(e.dve.h, u1_all)
                            e.pool.inc(g.tensor_tensor(rt[0][:], qkraw[ac % 2][:],
                                                       c2i[:, ssl], ALU.mult), 1)
                        rp_t = e.pool.n

                        @block.vector
                        def _(v, c=c, ssl=ssl, ac=ac, rp_t=rp_t):
                            dst = qrope if c < 8 else krope
                            ci = c % 8
                            v.wait_ge(e.pes.h, bq_pe[ac])
                            v.wait_ge(e.pool.h, rp_t)
                            e.dve.inc(v.tensor_tensor(rt[1][:], pbq[ac % 2][:],
                                                      s2i[:, ssl], ALU.mult), 1)
                            v.wait_ge(e.dve.h, e.dve.n)
                            e.dve.inc(v.tensor_tensor(dst[:, ci, ssl], rt[0][:], rt[1][:],
                                                      ALU.add), 1)
                            rope_done[ac] = e.dve.n

                        @block.gpsimd
                        def _(g, ac=ac):
                            g.wait_ge(e.act.h, qk_evac[ac])
                            if ac >= 1:
                                g.wait_ge(e.pes.h, nrm_pe[ac - 1])
                            e.pool.inc(g.tensor_tensor(sqn[:], qkraw[ac % 2][:],
                                                       qkraw[ac % 2][:], ALU.mult), 1)
                        sqn_t[ac] = e.pool.n

                        @block.tensor
                        def _(tt_, ac=ac):
                            tt_.wait_ge(e.pool.h, sqn_t[ac])
                            if ac < 2:
                                tt_.wait_ge(e.act.h, v_done_act)
                            else:
                                tt_.wait_ge(e.dve.h, nrm_red[ac - 2])
                            tt_.matmul(pvv[ac % 2][0:1, :], e.onesC1[0:64, 0:1],
                                       sqn[0:64, :], start=True, stop=True)
                            mm = tt_.matmul(pvv[ac % 2][32:33, :],
                                            e.onesC1[64:128, 0:1],
                                            sqn[64:128, :], start=True, stop=True)
                            e.pes.inc(mm, 1)
                        nrm_pe[ac] = e.pes.n

                        @block.vector
                        def _(v, ac=ac, c=c, ts=ts):
                            v.wait_ge(e.pes.h, nrm_pe[ac])
                            dst = qn2 if c < 8 else kn2s
                            h0 = (c % 8) * 2
                            e.dve.inc(v.reduce_max(
                                dst[0:1, h0 * 4 + ts:h0 * 4 + ts + 1],
                                pvv[ac % 2][0:1, :], axis=AX), 1)
                            e.dve.inc(v.reduce_max(
                                dst[0:1, (h0 + 1) * 4 + ts:(h0 + 1) * 4 + ts + 1],
                                pvv[ac % 2][32:33, :], axis=AX), 1)
                        nrm_red[ac] = e.dve.n

                bnd0 = e.dve.n

                @block.vector
                def _(v, bnd0=bnd0):
                    v.wait_ge(e.dve.h, bnd0)
                    for h in range(16):
                        e.dve.inc(v.reduce_max(qm2[0:1, h:h + 1],
                                               qn2[0:1, h * 4:h * 4 + 4], axis=AX), 1)
                        e.dve.inc(v.reduce_max(km2[0:1, h:h + 1],
                                               kn2s[0:1, h * 4:h * 4 + 4], axis=AX), 1)
                    v.wait_ge(e.dve.h, e.dve.n)
                    e.dve.inc(v.tensor_tensor(b2r[:], qm2[:], km2[:], ALU.mult), 1)
                bq_t = e.dve.n

                @block.scalar
                def _(sc, bq_t=bq_t):
                    sc.wait_ge(e.dve.h, bq_t)
                    e.sq.inc(sc.activation(b2r[:], b2r[:], AF.Sqrt,
                                           bias=0.0, scale=1.0), 1)
                br_t = e.sq.n

                @block.vector
                def _(v, br_t=br_t):
                    v.wait_ge(e.sq.h, br_t)
                    e.dve.inc(v.tensor_scalar(negbr[:], b2r[:], -SC_IN, None,
                                              ALU.mult), 1)
                nb_t = e.dve.n

                @block.tensor
                def _(tt_, nb_t=nb_t):
                    tt_.wait_ge(e.dve.h, nb_t)
                    for h in range(16):
                        mm = tt_.matmul(pGB[:, h:h + 1], e.onesF[:],
                                        negbr[0:1, h:h + 1], start=True, stop=True)
                    e.pes.inc(mm, 1)
                nbp = e.pes.n

                @block.vector
                def _(v, nbp=nbp):
                    v.wait_ge(e.pes.h, nbp)
                    e.dve.inc(v.tensor_copy(nbc[:], pGB[:, 0:16]), 1)
                e.nbc_ready = e.dve.n
                SQ.pop_all().close()
            proj_pe = e.pes.n
            proj_act = e.act.n
            proj_dve = e.dve.n
            SP.pop_all().close()

        if e.qk_dbg is not None:
            @block.sync
            def _(sy, s_=s_, proj_dve=proj_dve):
                sy.wait_ge(e.dve.h, proj_dve)
                off = s_ * 2 * 8 * S
                e.stw.inc(sy.dma_start(
                    out=e.qk_dbg[:, off:off + 8 * S],
                    in_=qrope.ap().rearrange("p c f -> p (c f)")), 16)
                e.stw.inc(sy.dma_start(
                    out=e.qk_dbg[:, off + 8 * S:off + 16 * S],
                    in_=krope.ap().rearrange("p c f -> p (c f)")), 16)

        _emit_inner_attention(block, e, x_in, s_, qrope, krope, vtm, nbc,
                              proj_pe, proj_act, proj_dve)
        ST.pop_all().close()


def _emit_inner_attention(block, e, x_in, s_, qrope, krope, vtm, nbc,
                          proj_pe, proj_act, proj_dve):
    _barrier(block, e)
    nc = e.nc
    wk = e.wk
    t0 = s_ * NTS
    row_ready = e.row.n
    with ExitStack() as ST:
        en = ST.enter_context
        outw = en(nc.sbuf_tensor(_un("i_outw"), [128, DC, D], BF16))
        attf = en(nc.sbuf_tensor(_un("i_attf"), [128, 8, TT], BF16))
        psb = [en(nc.sbuf_tensor(_un(f"i_psb{i}"), [128, 1024], BF16)) for i in range(6)]
        recb = en(nc.sbuf_tensor(_un("i_recb"), [1, TT], BF16))
        rbs = en(nc.sbuf_tensor(_un("i_rbs"), [64, TT], BF16))
        axs = en(nc.sbuf_tensor(_un("i_axs"), [128, DC, TT], F32))
        umb2 = [en(nc.sbuf_tensor(_un(f"i_umb{i}"), [128, TT], F32)) for i in range(2)]
        sqb = [en(nc.sbuf_tensor(_un(f"i_sqb{i}"), [128, TT], BF16)) for i in range(2)]
        sqm_pe = {}
        pSt = [en(nc.psum_tensor(_un(f"a_pSt{i}"), [128, 1024], F32)) for i in range(3)]
        pPV = en(nc.psum_tensor(_un("a_pPV"), [65, TT], F32))
        pMi = en(nc.psum_tensor(_un("a_pMi"), [128, TT], F32))

        @block.gpsimd
        def _(g):
            e.ldw.inc(g.dma_start(
                out=outw[:], in_=wk["outT"].rearrange("(dc p) f -> p dc f", p=128)), 16)
        outw_done = e.ldw.n

        pv_free = ("dve", proj_dve)
        pmi_guard = ("dve", proj_dve)
        attf_free = ("pes", proj_pe)
        xs_guard = ("dve", proj_dve)
        umb_guard = [("stw", e.stw.n), ("sq", e.sq.n)]
        recb_prev = proj_dve
        rbs_guard = proj_dve
        st_extra = ("dve", proj_dve)
        pv_t = {}
        st_t = {}
        kpg = 0
        exs_base = e.exs.n

        def semh(name):
            return getattr(e, name).h

        NPAIR = 8
        exp_jobs = []
        for q5 in range(NTS):
            qsl = slice(q5 * TT, (q5 + 1) * TT)
            for h in range(NH):
                hc = h // 2
                hb = (h % 2) * 64
                kp0 = kpg
                g0, g1 = st_extra

                @block.tensor
                def _(tt_, hb=hb, hc=hc, qsl=qsl, kp0=kp0, g0=g0, g1=g1,
                      first=(q5 == 0 and h == 0)):
                    if first:
                        tt_.wait_ge(e.act.h, proj_act)
                        tt_.wait_ge(e.dve.h, proj_dve)
                    tt_.wait_ge(semh(g0), g1)
                    for kp in range(NPAIR):
                        kk = kp0 + kp
                        if kk >= 3:
                            tt_.wait_ge(e.exs.h, exs_base + kk - 2)
                        for j in (0, 1):
                            kc = kp * 2 + j
                            mm = tt_.matmul(pSt[kk % 3][:, j * 512:(j + 1) * 512],
                                            krope.ap()[hb:hb + 64, hc, kc * 128:(kc + 1) * 128],
                                            qrope.ap()[hb:hb + 64, hc, qsl],
                                            start=True, stop=True)
                        e.pes.inc(mm, 1)
                        st_t[kp0 + kp] = e.pes.n
                st_last = e.pes.n
                for kp in range(NPAIR):
                    pv_t[kp0 + kp] = st_last + kp + 1

                exp_jobs.append((h, kp0))
                if h == NH - 1:
                    @block.scalar
                    def _(sc, jobs=tuple(exp_jobs)):
                        sc.wait_ge(e.dve.h, e.nbc_ready)
                        for (hh, hp0) in jobs:
                            for kp in range(NPAIR):
                                kk = hp0 + kp
                                sc.wait_ge(e.pes.h, st_t[kk])
                                if kk >= 6:
                                    sc.wait_ge(e.pes.h, pv_t[kk - 6])
                                sc.activation(psb[kk % 6][:], pSt[kk % 3][:],
                                              AF.Exp, bias=nbc[:, hh:hh + 1],
                                              scale=SC_IN).then_inc(e.exs.h, 1)
                                e.exs.n += 1
                    exp_jobs = []

                g0, g1 = pv_free

                @block.tensor
                def _(tt_, h=h, kp0=kp0, g0=g0, g1=g1):
                    for kp in range(NPAIR):
                        kk = kp0 + kp
                        tt_.wait_ge(e.exs.h, exs_base + kk + 1)
                        if kp == 0:
                            tt_.wait_ge(semh(g0), g1)
                        mm = tt_.matmul(pPV[:], vtm.ap()[:, kp * 2, h, :],
                                        psb[kk % 6][:, 0:512],
                                        start=(kp == 0), stop=False)
                        mm = tt_.matmul(pPV[:], vtm.ap()[:, kp * 2 + 1, h, :],
                                        psb[kk % 6][:, 512:1024],
                                        start=False, stop=(kp == NPAIR - 1))
                        e.pes.inc(mm, 1)
                        assert e.pes.n == pv_t[kk]
                kpg += NPAIR
                pv_all = e.pes.n

                @block.vector
                def _(v, pv_all=pv_all, recb_prev=recb_prev):
                    v.wait_ge(e.pes.h, pv_all)
                    v.wait_ge(e.dve.h, recb_prev)
                    e.dve.inc(v.tensor_copy(recb[:], pPV[64:65, :]), 1)
                recb_done = e.dve.n
                recb_prev = recb_done
                g0, g1 = pmi_guard

                @block.tensor
                def _(tt_, recb_done=recb_done, g0=g0, g1=g1):
                    tt_.wait_ge(e.dve.h, recb_done)
                    tt_.wait_ge(semh(g0), g1)
                    e.pes.inc(tt_.matmul(pMi[0:64, :], e.onesB[0:1, 0:64], recb[:],
                                         start=True, stop=True), 1)
                rb_pe = e.pes.n

                @block.vector
                def _(v, rb_pe=rb_pe, rbs_guard=rbs_guard, h=h,
                      g0=attf_free[0], g1=attf_free[1]):
                    v.wait_ge(e.pes.h, rb_pe)
                    v.wait_ge(e.dve.h, rbs_guard)
                    with e.nc.allow_low_precision(reason="softmax recip bf16"):
                        e.dve.inc(v.reciprocal(rbs[:], pMi[0:64, :]), 1)
                    v.wait_ge(e.dve.h, e.dve.n)
                    if h == 0:
                        v.wait_ge(semh(g0), g1)
                    hc2 = h // 2
                    hb2 = (h % 2) * 64
                    e.dve.inc(v.tensor_tensor(attf.ap()[hb2:hb2 + 64, hc2, :],
                                              pPV[0:64, :], rbs[:], ALU.mult), 1)
                pmi_guard = ("dve", e.dve.n)
                pv_free = ("dve", e.dve.n)
                rbs_guard = e.dve.n

            t = t0 + q5
            sl = slice(t * TT, (t + 1) * TT)
            attf_all = e.dve.n
            ldx_t = e.ldx2.n + 16
            g0, g1 = xs_guard

            @block.sync
            def _(sy, t=t, g0=g0, g1=g1):
                sy.wait_ge(semh(g0), g1)
                e.ldx2.inc(sy.dma_start(out=axs[:], in_=e.dtile(x_in, t)), 16)

            gm0, gm1 = pmi_guard
            gu0 = [umb_guard[0], umb_guard[1]]
            wo_pe = {}

            @block.tensor
            def _(tt_, attf_all=attf_all, sl=sl, gm0=gm0, gm1=gm1):
                tt_.wait_ge(e.dve.h, attf_all)
                tt_.wait_ge(e.ldw.h, outw_done)
                tt_.wait_ge(e.row.h, row_ready)
                tt_.wait_ge(semh(gm0), gm1)
                e.pes.inc(tt_.matmul(pMi[:], e.onesB[:], e.r_a[:, sl],
                                     start=True, stop=True), 1)
            wo_used = {}
            sq_done = {}
            dma_done = {}
            for dc in range(DC):
                @block.tensor
                def _(tt_, dc=dc, sl=sl):
                    if dc >= 2:
                        tt_.wait_ge(e.dve.h, wo_used[dc - 2])
                    for hv in range(8):
                        tt_.matmul(pSt[dc % 2][:, 0:512],
                                   outw[:, hv, dc * 128:(dc + 1) * 128],
                                   attf.ap()[:, hv, :],
                                   start=(hv == 0), stop=False)
                    mm = tt_.matmul(pSt[dc % 2][:, 0:512], e.onesB[:],
                                    e.r_b[:, sl], start=False, stop=True)
                    e.pes.inc(mm, 1)
                wo_pe[dc] = e.pes.n

                @block.vector
                def _(v, dc=dc, ldx_t=ldx_t, gu0=gu0):
                    v.wait_ge(e.ldx2.h, ldx_t)
                    v.wait_ge(e.pes.h, wo_pe[dc])
                    if dc == 0:
                        for gg0, gg1 in gu0:
                            v.wait_ge(semh(gg0), gg1)
                    if dc >= 2:
                        v.wait_ge(e.stw.h, dma_done[dc - 2])
                        v.wait_ge(e.sq.h, sq_done[dc - 2])
                    e.dve.inc(v.tensor_tensor(umb2[dc % 2][:], axs[:, dc, :],
                                              pMi[:], ALU.mult), 1)
                    v.wait_ge(e.dve.h, e.dve.n)
                    e.dve.inc(v.scalar_tensor_tensor(
                        umb2[dc % 2][:], pSt[dc % 2][:, 0:512], e.outb[:, dc:dc + 1],
                        umb2[dc % 2][:], ALU.add, ALU.add), 1)
                    wo_used[dc] = e.dve.n
                um_d = e.dve.n

                @block.sync
                def _(sy, dc=dc, t=t, um_d=um_d):
                    sy.wait_ge(e.dve.h, um_d)
                    e.stw.inc(sy.dma_start(
                        out=e.dtile(e.us, t)[:, dc:dc + 1, :],
                        in_=umb2[dc % 2][:]), 16)
                dma_done[dc] = e.stw.n

                @block.gpsimd
                def _(g, dc=dc, um_d=um_d):
                    g.wait_ge(e.dve.h, um_d)
                    if dc >= 2:
                        g.wait_ge(e.pes.h, sqm_pe[dc - 2])
                    e.sq.inc(g.tensor_tensor(sqb[dc % 2][:], umb2[dc % 2][:],
                                             umb2[dc % 2][:], ALU.mult), 1)
                sq_done[dc] = e.sq.n

                @block.tensor
                def _(tt_, dc=dc, gpv=pv_free):
                    tt_.wait_ge(e.sq.h, sq_done[dc])
                    if dc == 0:
                        tt_.wait_ge(semh(gpv[0]), gpv[1])
                    mm = tt_.matmul(pPV[0:1, 0:TT], e.oneC[:], sqb[dc % 2][:],
                                    start=(dc == 0), stop=(dc == DC - 1))
                    e.pes.inc(mm, 1)
                sqm_pe[dc] = e.pes.n
            um_done = e.dve.n
            attf_free = ("pes", wo_pe[DC - 1])
            xs_guard = ("dve", um_done)
            st_extra = ("dve", um_done)
            pmi_guard = ("dve", um_done)
            umb_guard = [("stw", dma_done[DC - 1]), ("sq", sq_done[DC - 1])]
            msq_pe = e.pes.n

            @block.vector
            def _(v, t=t, msq_pe=msq_pe):
                v.wait_ge(e.pes.h, msq_pe)
                e.row.inc(v.tensor_copy(e.r_ms[:, t * TT:(t + 1) * TT],
                                        pPV[0:1, 0:TT]), 1)
            pv_free = ("row", e.row.n)
        ST.pop_all().close()


def _emit_u2_gate_base(block, e, x_in, x_out):
    _barrier(block, e)
    nc = e.nc
    row_r = e.row.n
    with ExitStack() as ST:
        en = ST.enter_context
        uss = [en(nc.sbuf_tensor(_un(f"g_uss{i}"), [128, DC, TT], F32)) for i in range(2)]
        xss = [en(nc.sbuf_tensor(_un(f"g_xss{i}"), [128, DC, TT], F32)) for i in range(2)]
        u2b = [en(nc.sbuf_tensor(_un(f"g_u2b{i}"), [128, DC, TT], BF16)) for i in range(2)]
        x1b = [en(nc.sbuf_tensor(_un(f"g_x1b{i}"), [128, DC, TT], F32)) for i in range(2)]
        lgt = en(nc.sbuf_tensor(_un("g_lgt"), [128, 8], F32))
        top = en(nc.sbuf_tensor(_un("g_top"), [128, 8], F32))
        gw = en(nc.sbuf_tensor(_un("g_gw"), [128, 6], F32))
        msk = en(nc.sbuf_tensor(_un("g_msk"), [128, 4, 8], F32))
        logit = en(nc.sbuf_tensor(_un("g_logit"), [8, NTOK], F32))
        pR2 = [en(nc.psum_tensor(_un(f"g_pR2{i}"), [128, TT], F32)) for i in range(2)]
        plg = [en(nc.psum_tensor(_un(f"g_plg{i}"), [8, TT], F32)) for i in range(2)]
        ptr = en(nc.psum_tensor(_un("g_ptr"), [128, 8], F32))
        prow = en(nc.psum_tensor(_un("g_prow"), [1, 128], F32))
        b_ldx = e.ldx.n
        b_ldx2 = e.ldx2.n
        tile_done = {}
        lg_evac = {}
        for t in range(NT):
            @block.sync
            def _(sy, t=t):
                if t >= 2:
                    sy.wait_ge(e.dve.h, tile_done[t - 2])
                e.ldx.inc(sy.dma_start(out=uss[t % 2][:], in_=e.dtile(e.us, t)), 16)
                e.ldx2.inc(sy.dma_start(out=xss[t % 2][:], in_=e.dtile(x_in, t)), 16)

            @block.tensor
            def _(tt_, t=t, row_r=row_r):
                tt_.wait_ge(e.row.h, row_r)
                if t >= 2:
                    tt_.wait_ge(e.dve.h, tile_done[t - 2])
                e.pes.inc(tt_.matmul(pR2[t % 2][:], e.onesB[:],
                                     e.r_A[:, t * TT:(t + 1) * TT],
                                     start=True, stop=True), 1)
            pe_r2 = e.pes.n

            @block.vector
            def _(v, t=t, pe_r2=pe_r2):
                v.wait_ge(e.ldx.h, b_ldx + 16 * (t + 1))
                v.wait_ge(e.ldx2.h, b_ldx2 + 16 * (t + 1))
                v.wait_ge(e.pes.h, pe_r2)
                for dc in range(DC):
                    e.dve.inc(v.scalar_tensor_tensor(
                        u2b[t % 2][:, dc, :], uss[t % 2][:, dc, :],
                        e.n2c[:, dc:dc + 1], pR2[t % 2][:], ALU.mult, ALU.mult), 1)
                    e.dve.inc(v.scalar_tensor_tensor(
                        x1b[t % 2][:, dc, :], uss[t % 2][:, dc, :], 0.5,
                        xss[t % 2][:, dc, :], ALU.mult, ALU.add), 1)
                tile_done[t] = e.dve.n
            td = tile_done[t]

            @block.sync
            def _(sy, t=t, td=td):
                sy.wait_ge(e.dve.h, td)
                e.stw.inc(sy.dma_start(out=e.dtile(e.u2s, t), in_=u2b[t % 2][:]), 16)
                e.stw.inc(sy.dma_start(out=e.dtile(x_out, t), in_=x1b[t % 2][:]), 16)

            @block.tensor
            def _(tt_, t=t, td=td):
                tt_.wait_ge(e.dve.h, td)
                tt_.wait_ge(e.ldc.h, e.ldc_done)
                tt_.wait_ge(e.ldg.h, e.ldg_done)
                if t >= 2:
                    tt_.wait_ge(e.act.h, lg_evac[t - 2])
                for dc in range(DC):
                    mm = tt_.matmul(plg[t % 2][:], e.gateT[:, dc, :],
                                    u2b[t % 2][:, dc, :],
                                    start=(dc == 0), stop=(dc == DC - 1))
                e.pes.inc(mm, 1)
            lg_pe = e.pes.n

            @block.vector
            def _(v, t=t, lg_pe=lg_pe):
                v.wait_ge(e.pes.h, lg_pe)
                e.act.inc(v.tensor_copy(logit[:, t * TT:(t + 1) * TT],
                                        plg[t % 2][:]), 1)
                lg_evac[t] = e.act.n
        lg_all = e.act.n
        base_done = e.stw.n

        row_evac = e.gat.n
        for i in range(NTOK // 128):
            i128 = slice(i * 128, (i + 1) * 128)

            @block.tensor
            def _(tt_, i=i, i128=i128, lg_all=lg_all, row_evac=row_evac):
                tt_.wait_ge(e.act.h, lg_all)
                tt_.wait_ge(e.gat.h, row_evac)
                if i >= 1:
                    tt_.wait_ge(e.dve.h, comb_prev)
                e.pes.inc(tt_.transpose(ptr[:], logit[:, i128],
                                        e.identF[0:8, 0:8]), 1)
            trd = e.pes.n

            @block.vector
            def _(v, trd=trd):
                v.wait_ge(e.pes.h, trd)
                e.dve.inc(v.tensor_copy(lgt[:], ptr[:]), 1)
                v.wait_ge(e.dve.h, e.dve.n)
                e.dve.inc(v.max(out=top[:], in_=lgt[:]), 1)
                v.wait_ge(e.dve.h, e.dve.n)
                e.dve.inc(v.tensor_tensor(gw[:, 0:1], top[:, 1:2], top[:, 0:1],
                                          ALU.subtract), 1)
            gd1 = e.dve.n

            @block.scalar
            def _(sc, gd1=gd1):
                sc.wait_ge(e.dve.h, gd1)
                e.act.inc(sc.activation(gw[:, 1:2], gw[:, 0:1], AF.Exp,
                                        bias=0.0, scale=1.0), 1)
            gd2 = e.act.n

            @block.vector
            def _(v, gd2=gd2):
                v.wait_ge(e.act.h, gd2)
                e.dve.inc(v.tensor_scalar(gw[:, 2:3], gw[:, 1:2], 1.0, None,
                                          ALU.add), 1)
                v.wait_ge(e.dve.h, e.dve.n)
                e.dve.inc(v.reciprocal(gw[:, 3:4], gw[:, 2:3]), 1)
                v.wait_ge(e.dve.h, e.dve.n)
                e.dve.inc(v.tensor_scalar(gw[:, 4:5], gw[:, 3:4], 0.5, None,
                                          ALU.mult), 1)
                v.wait_ge(e.dve.h, e.dve.n)
                e.dve.inc(v.tensor_tensor(gw[:, 5:6], gw[:, 1:2], gw[:, 4:5],
                                          ALU.mult), 1)
                e.dve.inc(v.tensor_scalar(msk[:, 0, :], lgt[:], top[:, 0:1], None,
                                          ALU.is_equal), 1)
                e.dve.inc(v.tensor_scalar(msk[:, 2, :], lgt[:], top[:, 1:2], None,
                                          ALU.is_equal), 1)
                v.wait_ge(e.dve.h, e.dve.n)
                e.dve.inc(v.tensor_scalar(msk[:, 1, :], msk[:, 0, :], -1.0, 1.0,
                                          ALU.mult, ALU.add), 1)
                v.wait_ge(e.dve.h, e.dve.n)
                e.dve.inc(v.tensor_tensor(msk[:, 2, :], msk[:, 2, :],
                                          msk[:, 1, :], ALU.mult), 1)
                v.wait_ge(e.dve.h, e.dve.n)
                e.dve.inc(v.tensor_scalar(msk[:, 3, :], msk[:, 0, :], gw[:, 4:5],
                                          None, ALU.mult), 1)
                v.wait_ge(e.dve.h, e.dve.n)
                e.dve.inc(v.scalar_tensor_tensor(msk[:, 3, :], msk[:, 2, :],
                                                 gw[:, 5:6], msk[:, 3, :],
                                                 ALU.mult, ALU.add), 1)
            comb_done = e.dve.n
            comb_prev = comb_done

            for ex in range(NE):
                @block.tensor
                def _(tt_, ex=ex, comb_done=comb_done, row_evac=row_evac):
                    tt_.wait_ge(e.dve.h, comb_done)
                    tt_.wait_ge(e.gat.h, row_evac)
                    e.pes.inc(tt_.transpose(prow[:], msk.ap()[:, 3, ex:ex + 1],
                                            e.identF[:]), 1)
                tr1 = e.pes.n

                @block.vector
                def _(v, ex=ex, i=i, tr1=tr1):
                    v.wait_ge(e.pes.h, tr1)
                    e.gat.inc(v.tensor_copy(e.combR[ex][:, i * 128:(i + 1) * 128],
                                            prow[:]), 1)
                row_evac = e.gat.n
        ST.pop_all().close()
    return base_done


def _emit_moe(block, e, x_out, base_done):
    _barrier(block, e)
    nc = e.nc
    wk = e.wk
    comb_ready = e.gat.n
    with ExitStack() as ST:
        en = ST.enter_context
        w12 = [en(nc.sbuf_tensor(_un(f"m_w12{i}"), [128, DC, 2048], BF16)) for i in range(2)]
        w3s = [en(nc.sbuf_tensor(_un(f"m_w3s{i}"), [128, DC, 1024], BF16)) for i in range(2)]
        u2t = [en(nc.sbuf_tensor(_un(f"m_u2t{i}"), [128, DC, TT], BF16)) for i in range(2)]
        s1b = [en(nc.sbuf_tensor(_un(f"m_s1b{i}"), [128, TT], BF16)) for i in range(2)]
        s2b = [en(nc.sbuf_tensor(_un(f"m_s2b{i}"), [128, TT], BF16)) for i in range(2)]
        hsb = en(nc.sbuf_tensor(_un("m_hsb"), [128, DC, TT], BF16))
        ybf = [en(nc.sbuf_tensor(_un(f"m_ybf{i}"), [128, TT], F32)) for i in range(2)]
        cbs = en(nc.sbuf_tensor(_un("m_cbs"), [128, TT], BF16))
        b3u = [en(nc.sbuf_tensor(_un(f"m_b3u{i}"), [97, 1024], BF16)) for i in range(2)]
        onesB3 = en(nc.sbuf_tensor(_un("m_onesB3"), [97, 128], BF16))
        pH1 = [en(nc.psum_tensor(_un(f"m_pH1{i}"), [128, TT], F32)) for i in range(2)]
        pH2 = [en(nc.psum_tensor(_un(f"m_pH2{i}"), [128, TT], F32)) for i in range(2)]
        pY = [en(nc.psum_tensor(_un(f"m_pY{i}"), [128, TT], F32)) for i in range(2)]
        pcb = en(nc.psum_tensor(_un("m_pcb"), [128, TT], F32))

        @block.vector
        def _(v):
            e.dve.inc(v.memset(onesB3[:], 1.0), 1)
        ob3_done = e.dve.n

        w12_used = {}
        w3_used = {}
        u2_used = {}
        silu_t = {}
        s2g_t = {}
        h_used = {}
        y_acc = {}
        y_evac = {}
        hsb_used = {}
        cb_guard = ("act", e.act.n)

        def semh(name):
            return getattr(e, name).h

        ldw12_t = {}
        ldw3_t = {}

        def emit_loads(ui):
            ex_, hf_ = divmod(ui, 2)

            @block.gpsimd
            def _(g, ex=ex_, hf=hf_, ui=ui):
                if ui >= 2:
                    g.wait_ge(e.pes.h, w12_used[ui - 2])
                else:
                    g.wait_ge(e.ldc.h, e.ldc_done)
                    g.wait_ge(e.ldg.h, e.ldg_done)
                e.ldw.inc(g.dma_start(
                    out=w12[ui % 2][:, :, 0:1024],
                    in_=wk["w1T"][ex, :, hf * 1024:(hf + 1) * 1024]
                    .rearrange("(dc p) f -> p dc f", p=128)), 16)
                e.ldw.inc(g.dma_start(
                    out=w12[ui % 2][:, :, 1024:2048],
                    in_=wk["w2T"][ex, :, hf * 1024:(hf + 1) * 1024]
                    .rearrange("(dc p) f -> p dc f", p=128)), 16)
            ldw12_t[ui] = e.ldw.n

            @block.gpsimd
            def _(g, ex=ex_, hf=hf_, ui=ui):
                if ui >= 2:
                    g.wait_ge(e.pes.h, w3_used[ui - 2])
                e.ldw.inc(g.dma_start(
                    out=w3s[ui % 2][:],
                    in_=wk["w3T"][ex, hf * 1024:(hf + 1) * 1024, :]
                    .rearrange("(hc p) f -> p hc f", p=128)), 16)
                for rb in (0, 32, 64):
                    e.ldw.inc(g.dma_start(
                        out=b3u[ui % 2][rb:rb + 1, :],
                        in_=wk["b3r"][0:1, ex * 1024:(ex + 1) * 1024]), 16)
            ldw3_t[ui] = e.ldw.n

        ui = -1
        for ex in range(NE):
            for hf in range(2):
                ui += 1
                if ui == 0:
                    emit_loads(0)
                ldw12 = ldw12_t[ui]
                ldw3 = ldw3_t[ui]

                for t in range(NT):
                    ti = ui * NT + t
                    if t == 1 and ui + 1 < NE * 2:
                        emit_loads(ui + 1)

                    @block.sync
                    def _(sy, t=t, ti=ti):
                        if ti >= 2:
                            sy.wait_ge(e.pes.h, u2_used[ti - 2])
                        e.ldu.inc(sy.dma_start(out=u2t[t % 2][:],
                                               in_=e.dtile(e.u2s, t)), 16)
                    ldu_t = e.ldu.n

                    g0, g1 = cb_guard

                    @block.tensor
                    def _(tt_, ex=ex, t=t, g0=g0, g1=g1):
                        tt_.wait_ge(e.gat.h, comb_ready)
                        tt_.wait_ge(semh(g0), g1)
                        tt_.wait_ge(e.dve.h, ob3_done)
                        rb = (ex % 3) * 32
                        e.pes.inc(tt_.matmul(
                            pcb[:], onesB3[rb:rb + 1, 0:128],
                            e.combR[ex][:, t * TT:(t + 1) * TT],
                            start=True, stop=True), 1)
                    cb_pe = e.pes.n

                    @block.vector
                    def _(v, cb_pe=cb_pe):
                        v.wait_ge(e.pes.h, cb_pe)
                        e.act.inc(v.tensor_copy(cbs[:], pcb[:]), 1)
                    cbs_done = e.act.n
                    cb_guard = ("act", cbs_done)

                    exs0 = e.exs.n
                    gat0 = e.gat.n
                    pool0 = e.pool.n
                    for hcq in range(DC):
                        hci = ti * DC + hcq
                        silu_t[hci] = exs0 + hcq + 1
                        s2g_t[hci] = gat0 + hcq + 1
                        h_used[hci] = pool0 + hcq + 1
                    h_pe = {}
                    for hcq in range(DC):
                        hci = ti * DC + hcq

                        @block.tensor
                        def _(tt_, hcq=hcq, t=t, ui=ui, ldw12=ldw12,
                              ldu_t=ldu_t, hci=hci):
                            tt_.wait_ge(e.ldw.h, ldw12)
                            tt_.wait_ge(e.ldu.h, ldu_t)
                            if hci >= 2:
                                tt_.wait_ge(e.exs.h, silu_t[hci - 2])
                                tt_.wait_ge(e.gat.h, s2g_t[hci - 2])
                            for dc in range(DC):
                                mm = tt_.matmul(pH1[hcq % 2][:],
                                                w12[ui % 2][:, dc, hcq * 128:(hcq + 1) * 128],
                                                u2t[t % 2][:, dc, :],
                                                start=(dc == 0), stop=(dc == DC - 1))
                            e.pes.inc(mm, 1)
                            for dc in range(DC):
                                mm = tt_.matmul(pH2[hcq % 2][:],
                                                w12[ui % 2][:, dc, 1024 + hcq * 128:1024 + (hcq + 1) * 128],
                                                u2t[t % 2][:, dc, :],
                                                start=(dc == 0), stop=(dc == DC - 1))
                            e.pes.inc(mm, 1)
                        h_pe[hcq] = e.pes.n

                    @block.scalar
                    def _(sc, ti=ti, ex=ex, hf=hf, h_pe=h_pe):
                        for hcq in range(DC):
                            hci = ti * DC + hcq
                            sc.wait_ge(e.pes.h, h_pe[hcq] - 1)
                            if hci >= 2:
                                sc.wait_ge(e.pool.h, h_used[hci - 2])
                            sc.activation(
                                s1b[hcq % 2][:], pH1[hcq % 2][:], AF.Silu,
                                bias=e.b1c.ap()[:, ex, hf * 8 + hcq:hf * 8 + hcq + 1],
                                scale=1.0).then_inc(e.exs.h, 1)
                            e.exs.n += 1
                    assert e.exs.n == exs0 + DC

                    for hcq in range(DC):
                        hci = ti * DC + hcq

                        @block.vector
                        def _(v, hcq=hcq, ex=ex, hf=hf, hci=hci,
                              cbs_done=cbs_done, h_pe=h_pe):
                            v.wait_ge(e.pes.h, h_pe[hcq])
                            v.wait_ge(e.act.h, cbs_done)
                            if hci >= 2:
                                v.wait_ge(e.pool.h, h_used[hci - 2])
                            e.gat.inc(v.scalar_tensor_tensor(
                                s2b[hcq % 2][:], pH2[hcq % 2][:],
                                e.b2c.ap()[:, ex, hf * 8 + hcq:hf * 8 + hcq + 1],
                                cbs[:], ALU.add, ALU.mult), 1)

                        @block.gpsimd
                        def _(g, hcq=hcq, ti=ti, hci=hci):
                            g.wait_ge(e.exs.h, silu_t[hci])
                            g.wait_ge(e.gat.h, s2g_t[hci])
                            if ti >= 1 and hcq == 0:
                                g.wait_ge(e.pes.h, hsb_used.get(ti - 1, 0))
                            e.pool.inc(g.tensor_tensor(hsb[:, hcq, :],
                                                       s1b[hcq % 2][:],
                                                       s2b[hcq % 2][:], ALU.mult), 1)
                    assert e.gat.n == gat0 + DC
                    assert e.pool.n == pool0 + DC
                    h_all = e.pool.n

                    for dcq in range(DC):
                        yi = ti * DC + dcq

                        @block.tensor
                        def _(tt_, dcq=dcq, ti=ti, ui=ui, ex=ex, t=t,
                              h_all=h_all, yi=yi, ldw3=ldw3):
                            tt_.wait_ge(e.pool.h, h_all)
                            tt_.wait_ge(e.ldw.h, ldw3)
                            if yi >= 2:
                                tt_.wait_ge(e.act.h, y_evac[yi - 2])
                            for hc in range(DC):
                                tt_.matmul(pY[dcq % 2][:],
                                           w3s[ui % 2][:, hc, dcq * 128:(dcq + 1) * 128],
                                           hsb[:, hc, :],
                                           start=(hc == 0), stop=False)
                            rb = (ex % 3) * 32
                            mm = tt_.matmul(
                                pY[dcq % 2][:],
                                b3u[ui % 2][rb:rb + 1, dcq * 128:(dcq + 1) * 128],
                                e.combR[ex][:, t * TT:(t + 1) * TT],
                                start=False, stop=True)
                            e.pes.inc(mm, 1)
                        y_pe = e.pes.n

                        @block.vector
                        def _(v, dcq=dcq, y_pe=y_pe, yi=yi):
                            v.wait_ge(e.pes.h, y_pe)
                            if yi >= 2:
                                v.wait_ge(e.acc.h, y_acc.get(yi - 2, 0))
                            e.act.inc(v.tensor_copy(ybf[dcq % 2][:],
                                                    pY[dcq % 2][:]), 1)
                            y_evac[yi] = e.act.n

                        @block.gpsimd
                        def _(g, dcq=dcq, t=t, yi=yi):
                            g.wait_ge(e.act.h, y_evac[yi])
                            g.wait_ge(e.stw.h, base_done)
                            g.wait_ge(e.acc.h, y_acc.get(yi - NT * DC, 0))
                            e.acc.inc(g.dma_start(
                                out=e.dtile(x_out, t)[:, dcq:dcq + 1, :],
                                in_=ybf[dcq % 2][:],
                                accum_op=ALU.add), 16)
                            y_acc[yi] = e.acc.n
                    hsb_used[ti] = e.pes.n
                    u2_used[ti] = e.pes.n
                w12_used[ui] = e.pes.n
                w3_used[ui] = e.pes.n
        ST.pop_all().close()


def _emit_mla(block, e, x_in, x_out):
    nc = e.nc
    with ExitStack() as SM:
        enm = SM.enter_context
        for rn in ("r_a", "r_b"):
            setattr(e, rn, enm(nc.sbuf_tensor(_un("b_") + rn, [1, NTOK], BF16))[:])
        with ExitStack() as SS:
            ens = SS.enter_context
            for rn in ("r_m", "rs1", "rs2", "rs3"):
                setattr(e, rn, ens(nc.sbuf_tensor(_un("b_") + rn, [1, NTOK], F32))[:])
            e.r_ms = ens(nc.sbuf_tensor(_un("b_r_ms"), [1, NTOK], BF16))[:]
            _emit_stats(block, e, x_in, with_mean=True)
            _emit_ln_rows(block, e, compose_rms=False)
        for s_ in range(BPC):
            for hg in range(2):
                _emit_mla_seq(block, e, x_in, x_out, s_, hg)
        _drain_phase(block, e)
        SM.pop_all().close()


def _emit_mla_seq(block, e, x_in, x_out, s_, hg):
    _barrier(block, e)
    nc = e.nc
    wk = e.wk
    t0 = s_ * NTS
    row_ready = e.row.n
    NHH = 8
    with ExitStack() as ST:
        en = ST.enter_context
        qc = en(nc.sbuf_tensor(_un("l_qc"), [128, 4, S], BF16))
        kc = en(nc.sbuf_tensor(_un("l_kc"), [128, 4, S], BF16))
        qr = en(nc.sbuf_tensor(_un("l_qr"), [128, 4, S], BF16))
        krm = en(nc.sbuf_tensor(_un("l_krm"), [64, S], BF16))
        vtm = en(nc.sbuf_tensor(_un("l_vtm"), [128, 16, 8, 65], BF16))
        nbcm = en(nc.sbuf_tensor(_un("l_nbcm"), [128, 8], F32))

        with ExitStack() as SP:
            enp = SP.enter_context
            dqw = enp(nc.sbuf_tensor(_un("l_dqw"), [128, DC, 256], BF16))
            dkw = enp(nc.sbuf_tensor(_un("l_dkw"), [128, DC, 256], BF16))
            krw = enp(nc.sbuf_tensor(_un("l_krw"), [128, DC, 32], BF16))
            qrw = enp(nc.sbuf_tensor(_un("l_qrw"), [128, 2, 256], BF16))
            cm2 = enp(nc.sbuf_tensor(_un("l_cm2"), [64, S], BF16))
            sm2 = enp(nc.sbuf_tensor(_un("l_sm2"), [64, S], BF16))
            cq = enp(nc.sbuf_tensor(_un("l_cq"), [128, 2, S], BF16))
            ckv = enp(nc.sbuf_tensor(_un("l_ckv"), [128, 2, S], BF16))
            xs = enp(nc.sbuf_tensor(_un("l_xs"), [128, DC, TT], F32))
            t2b = enp(nc.sbuf_tensor(_un("l_t2b"), [128, DC, TT], BF16))
            rt = [enp(nc.sbuf_tensor(_un(f"l_rt{i}"), [128, TT], F32)) for i in range(2)]
            krr = enp(nc.sbuf_tensor(_un("l_krr"), [32, TT], BF16))
            wuc = [enp(nc.sbuf_tensor(_un(f"l_wuc{i}"), [128, 2, 128], BF16)) for i in range(2)]
            wvv = enp(nc.sbuf_tensor(_un("l_wvv"), [128, 2, 512], BF16))
            sqb2 = [enp(nc.sbuf_tensor(_un(f"l_sqb2{i}"), [128, TT], BF16)) for i in range(2)]
            qc2s = enp(nc.sbuf_tensor(_un("l_qc2s"), [1, 32], F32))
            kc2s = enp(nc.sbuf_tensor(_un("l_kc2s"), [1, 32], F32))
            qr2s = enp(nc.sbuf_tensor(_un("l_qr2s"), [1, 32], F32))
            kr2s = enp(nc.sbuf_tensor(_un("l_kr2s"), [1, 4], F32))
            qm2m = enp(nc.sbuf_tensor(_un("l_qm2m"), [1, 8], F32))
            km2m = enp(nc.sbuf_tensor(_un("l_km2m"), [1, 8], F32))
            bpm = enp(nc.sbuf_tensor(_un("l_bpm"), [1, 8], F32))
            negbm = enp(nc.sbuf_tensor(_un("l_negbm"), [1, 8], F32))
            kr2m = enp(nc.sbuf_tensor(_un("l_kr2m"), [1, 1], F32))
            pAB = enp(nc.psum_tensor(_un("l_pAB"), [128, TT], F32))
            pBB = enp(nc.psum_tensor(_un("l_pBB"), [128, TT], F32))
            pcq = [enp(nc.psum_tensor(_un(f"l_pcq{i}"), [128, TT], F32)) for i in range(2)]
            pkr = enp(nc.psum_tensor(_un("l_pkr"), [32, TT], F32))
            par = enp(nc.psum_tensor(_un("l_par"), [64, TT], F32))
            pqr = enp(nc.psum_tensor(_un("l_pqr"), [128, TT], F32))
            pvm = enp(nc.psum_tensor(_un("l_pvm"), [128, TT], F32))

            @block.gpsimd
            def _(g, hg=hg):
                g.wait_ge(e.ldc.h, e.ldc_done)
                g.wait_ge(e.ldg.h, e.ldg_done)
                for src, dst in [
                    (wk["m_W_DQ"].rearrange("(p2 p) f -> p p2 f", p=128), dqw[:]),
                    (wk["m_W_DKV"].rearrange("(p2 p) f -> p p2 f", p=128), dkw[:]),
                    (wk["m_W_KR"].rearrange("(p2 p) f -> p p2 f", p=128), krw[:]),
                    (wk["m_W_QR"][:, hg * 256:(hg + 1) * 256]
                     .rearrange("(p2 p) f -> p p2 f", p=128), qrw[:]),
                    (wk["cm2"], cm2[:]),
                    (wk["sm2"], sm2[:]),
                    (wk["m_W_UV"][:, hg * 512:(hg + 1) * 512]
                     .rearrange("(p2 p) f -> p p2 f", p=128), wvv[:]),
                ]:
                    e.ldw.inc(g.dma_start(out=dst, in_=src), 16)
            ldw_done = e.ldw.n

            @block.vector
            def _(v):
                v.wait_ge(e.ldc.h, e.ldc_done)
                v.wait_ge(e.ldg.h, e.ldg_done)
                e.dve.inc(v.memset(vtm.ap()[:, :, :, 64:65], 1.0), 1)
            vones = e.dve.n

            xs_free = {}
            t2_done = {}
            t2_used = {}
            cq_evac = {}
            kr_rope_prev = 0
            for ts in range(NTS):
                t = t0 + ts
                sl = slice(t * TT, (t + 1) * TT)
                ssl = slice(ts * TT, (ts + 1) * TT)
                ldx_t = e.ldx.n + 16

                @block.sync
                def _(sy, ts=ts, t=t):
                    if ts >= 1:
                        sy.wait_ge(e.dve.h, xs_free[ts - 1])
                    e.ldx.inc(sy.dma_start(out=xs[:], in_=e.dtile(x_in, t)), 16)

                @block.tensor
                def _(tt_, ts=ts, sl=sl):
                    tt_.wait_ge(e.row.h, row_ready)
                    if ts >= 1:
                        tt_.wait_ge(e.dve.h, t2_done[ts - 1])
                    tt_.matmul(pAB[:], e.onesB[:], e.r_a[:, sl], start=True, stop=True)
                    e.pes.inc(tt_.matmul(pBB[:], e.onesB[:], e.r_b[:, sl],
                                         start=True, stop=True), 1)
                ab_pe = e.pes.n

                @block.vector
                def _(v, ts=ts, ldx_t=ldx_t, ab_pe=ab_pe):
                    v.wait_ge(e.ldx.h, ldx_t)
                    v.wait_ge(e.pes.h, ab_pe)
                    if ts >= 2:
                        v.wait_ge(e.pes.h, t2_used[ts - 2])
                    for dc in range(DC):
                        e.dve.inc(v.tensor_tensor(rt[dc % 2][:], xs[:, dc, :],
                                                  pAB[:], ALU.mult), 1)
                        v.wait_ge(e.dve.h, e.dve.n)
                        e.dve.inc(v.tensor_tensor(t2b[:, dc, :], rt[dc % 2][:],
                                                  pBB[:], ALU.add), 1)
                xs_free[ts] = e.dve.n
                t2_done[ts] = e.dve.n
                t2d = t2_done[ts]

                for j in range(5):
                    aj = ts * 5 + j

                    @block.tensor
                    def _(tt_, j=j, ts=ts, t2d=t2d, aj=aj,
                          kr_rope_prev=kr_rope_prev):
                        tt_.wait_ge(e.dve.h, t2d)
                        tt_.wait_ge(e.ldw.h, ldw_done)
                        if j < 4 and aj >= 2:
                            tt_.wait_ge(e.act.h, cq_evac.get(aj - 2, 0))
                        if j == 4 and ts >= 1:
                            tt_.wait_ge(e.act.h, cq_evac[(ts - 1) * 5 + 4])
                            tt_.wait_ge(e.dve.h, kr_rope_prev)
                        if j < 2:
                            w, lc = dqw, j
                        elif j < 4:
                            w, lc = dkw, j - 2
                        if j < 4:
                            for dc in range(DC):
                                mm = tt_.matmul(pcq[j % 2][:],
                                                w[:, dc, lc * 128:(lc + 1) * 128],
                                                t2b[:, dc, :],
                                                start=(dc == 0), stop=(dc == DC - 1))
                        else:
                            for dc in range(DC):
                                mm = tt_.matmul(pkr[:], krw[:, dc, :],
                                                t2b[:, dc, :],
                                                start=(dc == 0), stop=(dc == DC - 1))
                        e.pes.inc(mm, 1)
                    j_pe = e.pes.n

                    @block.vector
                    def _(v, j=j, ts=ts, ssl=ssl, j_pe=j_pe, aj=aj,
                          kr_rope_prev=kr_rope_prev):
                        v.wait_ge(e.pes.h, j_pe)
                        if j == 4 and ts >= 1:
                            v.wait_ge(e.dve.h, kr_rope_prev)
                        if j < 2:
                            dst = cq[:, j, ssl]
                            srcp = pcq[j % 2][:]
                        elif j < 4:
                            dst = ckv[:, j - 2, ssl]
                            srcp = pcq[j % 2][:]
                        else:
                            dst = krr[:]
                            srcp = pkr[:]
                        e.act.inc(v.tensor_copy(dst, srcp), 1)
                        cq_evac[aj] = e.act.n
                t2_used[ts] = e.pes.n
                krr_done = cq_evac[ts * 5 + 4]

                @block.tensor
                def _(tt_, krr_done=krr_done, kr_rope_prev=kr_rope_prev):
                    tt_.wait_ge(e.act.h, krr_done)
                    tt_.wait_ge(e.dve.h, kr_rope_prev)
                    e.pes.inc(tt_.matmul(par[0:32, :], e.a1T[:], krr[:],
                                         start=True, stop=True), 1)
                akr_pe = e.pes.n

                @block.vector
                def _(v, ssl=ssl, akr_pe=akr_pe, krr_done=krr_done):
                    v.wait_ge(e.pes.h, akr_pe)
                    v.wait_ge(e.act.h, krr_done)
                    e.dve.inc(v.tensor_tensor(rt[0][0:32, :], krr[:],
                                              cm2[0:32, ssl], ALU.mult), 1)
                    e.dve.inc(v.tensor_tensor(rt[1][0:32, :], par[0:32, :],
                                              sm2[0:32, ssl], ALU.mult), 1)
                    v.wait_ge(e.dve.h, e.dve.n)
                    e.dve.inc(v.tensor_tensor(krm[0:32, ssl], rt[0][0:32, :],
                                              rt[1][0:32, :], ALU.add), 1)
                    e.dve.inc(v.tensor_tensor(krm[32:64, ssl], rt[0][0:32, :],
                                              rt[1][0:32, :], ALU.add), 1)
                kr_rope_prev = e.dve.n

            cq_all = e.act.n
            kr_all = e.dve.n

            nrm_pe2 = {}
            nrm_red2 = {}
            pend = []

            def norm_chunk(ni, src, P, halves, guards):
                @block.gpsimd
                def _(g, ni=ni, src=src, P=P, guards=guards):
                    for (sh, sv) in guards:
                        g.wait_ge(sh, sv)
                    if ni >= 2:
                        g.wait_ge(e.pes.h, nrm_pe2[ni - 2])
                    e.pool.inc(g.tensor_tensor(sqb2[ni % 2][0:P, :], src, src,
                                               ALU.mult), 1)
                sq_t = e.pool.n

                def emit(ni=ni, halves=halves, sq_t=sq_t):
                    p = pAB if ni % 2 == 0 else pBB

                    @block.tensor
                    def _(tt_):
                        tt_.wait_ge(e.pool.h, sq_t)
                        if ni >= 2:
                            tt_.wait_ge(e.dve.h, nrm_red2[ni - 2])
                        for (b, kext, arr, slot) in halves:
                            mm = tt_.matmul(p[b:b + 1, :],
                                            e.onesC1[b:b + kext, 0:1],
                                            sqb2[ni % 2][b:b + kext, :],
                                            start=True, stop=True)
                        e.pes.inc(mm, 1)
                    nrm_pe2[ni] = e.pes.n

                    @block.vector
                    def _(v):
                        v.wait_ge(e.pes.h, nrm_pe2[ni])
                        for (b, kext, arr, slot) in halves:
                            e.dve.inc(v.reduce_max(arr[0:1, slot:slot + 1],
                                                   p[b:b + 1, :], axis=AX), 1)
                    nrm_red2[ni] = e.dve.n
                pend.append(emit)
                if len(pend) > 1:
                    pend.pop(0)()

            qc_evac = {}
            v_evac = {}
            wuc_used = {}
            qr_rope = kr_all
            for j in range(8):
                @block.gpsimd
                def _(g, j=j, hg=hg):
                    if j >= 2:
                        g.wait_ge(e.pes.h, wuc_used[j - 2])
                    src = wk["m_W_UQ"] if j < 4 else wk["m_W_UK"]
                    hcj = hg * 4 + (j % 4)
                    e.ldw.inc(g.dma_start(
                        out=wuc[j % 2][:],
                        in_=src[:, hcj * 128:(hcj + 1) * 128]
                        .rearrange("(lc p) f -> p lc f", p=128)), 16)
                wl = e.ldw.n
                for ts in range(NTS):
                    bj = j * 4 + ts
                    ssl = slice(ts * TT, (ts + 1) * TT)

                    @block.tensor
                    def _(tt_, j=j, ts=ts, ssl=ssl, cq_all=cq_all, bj=bj, wl=wl):
                        tt_.wait_ge(e.act.h, cq_all)
                        tt_.wait_ge(e.ldw.h, wl)
                        if bj >= 2:
                            tt_.wait_ge(e.act.h, qc_evac[bj - 2])
                        src = cq if j < 4 else ckv
                        for lc in range(2):
                            mm = tt_.matmul(pcq[bj % 2][:],
                                            wuc[j % 2][:, lc, :],
                                            src[:, lc, ssl],
                                            start=(lc == 0), stop=(lc == 1))
                        e.pes.inc(mm, 1)
                        if ts == NTS - 1:
                            wuc_used[j] = e.pes.n
                    j_pe = e.pes.n

                    @block.vector
                    def _(v, j=j, ssl=ssl, j_pe=j_pe, bj=bj):
                        v.wait_ge(e.pes.h, j_pe)
                        dst = qc if j < 4 else kc
                        e.act.inc(v.tensor_copy(dst[:, j % 4, ssl], pcq[bj % 2][:]), 1)
                        qc_evac[bj] = e.act.n

                    arr = qc2s if j < 4 else kc2s
                    h0 = (j % 4) * 2
                    norm_chunk(bj, (qc if j < 4 else kc)[:, j % 4, ssl], 128,
                               [(0, 64, arr, h0 * 4 + ts),
                                (64, 64, arr, (h0 + 1) * 4 + ts)],
                               [(e.act.h, qc_evac[bj])])

            for ts in range(NTS):
                ssl = slice(ts * TT, (ts + 1) * TT)
                for pc in range(2):
                    apc = ts * 2 + pc

                    @block.tensor
                    def _(tt_, pc=pc, ssl=ssl, cq_all=cq_all, apc=apc):
                        tt_.wait_ge(e.act.h, cq_all)
                        tt_.wait_ge(e.dve.h, qr_rope)
                        for lc in range(2):
                            mm = tt_.matmul(pqr[:], qrw[:, lc, pc * 128:(pc + 1) * 128],
                                            cq[:, lc, ssl], start=(lc == 0), stop=(lc == 1))
                        e.pes.inc(mm, 1)
                    qr_pe = e.pes.n

                    @block.vector
                    def _(v, pc=pc, ssl=ssl, qr_pe=qr_pe):
                        v.wait_ge(e.pes.h, qr_pe)
                        v.tensor_copy(qr[0:64, 2 * pc, ssl], pqr[0:64, :])
                        v.wait_ge(e.dve.h, e.dve.n)
                        e.act.inc(v.tensor_copy(qr[0:64, 2 * pc + 1, ssl],
                                                pqr[64:128, :]), 1)
                    qre = e.act.n
                    for hh in (2 * pc, 2 * pc + 1):
                        @block.tensor
                        def _(tt_, hh=hh, ssl=ssl, qre=qre):
                            tt_.wait_ge(e.act.h, qre)
                            tt_.wait_ge(e.dve.h, qr_rope)
                            e.pes.inc(tt_.matmul(par[:], e.a2T[:],
                                                 qr.ap()[0:64, hh, ssl],
                                                 start=True, stop=True), 1)
                        ar_pe = e.pes.n

                        @block.vector
                        def _(v, hh=hh, ssl=ssl, ar_pe=ar_pe):
                            v.wait_ge(e.pes.h, ar_pe)
                            e.dve.inc(v.tensor_tensor(rt[0][0:64, :],
                                                      qr.ap()[0:64, hh, ssl],
                                                      cm2[:, ssl], ALU.mult), 1)
                            e.dve.inc(v.tensor_tensor(rt[1][0:64, :], par[:],
                                                      sm2[:, ssl], ALU.mult), 1)
                            v.wait_ge(e.dve.h, e.dve.n)
                            e.dve.inc(v.tensor_tensor(qr.ap()[0:64, hh, ssl],
                                                      rt[0][0:64, :],
                                                      rt[1][0:64, :], ALU.add), 1)
                        qr_rope = e.dve.n
                        norm_chunk(32 + ts * 4 + hh, qr.ap()[0:64, hh, ssl], 64,
                                   [(0, 32, qr2s, (2 * hh) * 4 + ts),
                                    (32, 32, qr2s, (2 * hh + 1) * 4 + ts)],
                                   [(e.dve.h, qr_rope)])
                for st_ in range(4):
                    ai = ts * 4 + st_

                    @block.tensor
                    def _(tt_, st_=st_, ts=ts, ai=ai, cq_all=cq_all,
                          ldw_done=ldw_done):
                        tt_.wait_ge(e.act.h, cq_all)
                        tt_.wait_ge(e.ldw.h, ldw_done)
                        if ai >= 1:
                            tt_.wait_ge(e.act.h, v_evac[ai - 1])
                        for lc in range(2):
                            mm = tt_.matmul(
                                pvm[:],
                                ckv[:, lc, ts * TT + st_ * 128:ts * TT + (st_ + 1) * 128],
                                wvv[:, lc, :],
                                start=(lc == 0), stop=(lc == 1))
                        e.pes.inc(mm, 1)
                    v_pe = e.pes.n

                    @block.vector
                    def _(v, st_=st_, ts=ts, ai=ai, v_pe=v_pe, vones=vones):
                        v.wait_ge(e.pes.h, v_pe)
                        v.wait_ge(e.dve.h, vones)
                        kt = ts * 4 + st_
                        e.act.inc(v.tensor_copy(
                            vtm.ap()[:, kt, 0:8, 0:64], pvm[:]), 1)
                        v_evac[ai] = e.act.n

            for ts in range(NTS):
                norm_chunk(48 + ts, krm[0:32, ts * TT:(ts + 1) * TT], 32,
                           [(0, 32, kr2s, ts)], [(e.dve.h, kr_all)])
            while pend:
                pend.pop(0)()
            nrm_all = e.dve.n

            @block.vector
            def _(v, nrm_all=nrm_all):
                v.wait_ge(e.dve.h, nrm_all)
                for lh in range(8):
                    e.dve.inc(v.reduce_max(qm2m[0:1, lh:lh + 1],
                                           qc2s[0:1, lh * 4:lh * 4 + 4], axis=AX), 1)
                    e.dve.inc(v.reduce_max(km2m[0:1, lh:lh + 1],
                                           kc2s[0:1, lh * 4:lh * 4 + 4], axis=AX), 1)
                    e.dve.inc(v.reduce_max(bpm[0:1, lh:lh + 1],
                                           qr2s[0:1, lh * 4:lh * 4 + 4], axis=AX), 1)
                e.dve.inc(v.reduce_max(kr2m[:], kr2s[:], axis=AX), 1)
                v.wait_ge(e.dve.h, e.dve.n)
                e.dve.inc(v.tensor_tensor(qm2m[:], qm2m[:], bpm[:], ALU.add), 1)
                e.dve.inc(v.tensor_scalar(km2m[:], km2m[:], kr2m[0:1, 0:1],
                                          None, ALU.add), 1)
                v.wait_ge(e.dve.h, e.dve.n)
                e.dve.inc(v.tensor_tensor(bpm[:], qm2m[:], km2m[:], ALU.mult), 1)
            bqk_t = e.dve.n

            @block.scalar
            def _(sc, bqk_t=bqk_t):
                sc.wait_ge(e.dve.h, bqk_t)
                e.sq.inc(sc.activation(bpm[:], bpm[:], AF.Sqrt,
                                       bias=0.0, scale=1.0), 1)
            sqrt_t = e.sq.n

            @block.vector
            def _(v, sqrt_t=sqrt_t):
                v.wait_ge(e.sq.h, sqrt_t)
                e.dve.inc(v.tensor_scalar(negbm[:], bpm[:], 80.0, -SC_MLA,
                                          ALU.min, ALU.mult), 1)
            nbm_t = e.dve.n
            act_now = e.act.n

            @block.tensor
            def _(tt_, nbm_t=nbm_t, act_now=act_now):
                tt_.wait_ge(e.dve.h, nbm_t)
                tt_.wait_ge(e.act.h, act_now)
                for lh in range(8):
                    mm = tt_.matmul(pqr[:, lh:lh + 1], e.onesF[:],
                                    negbm[0:1, lh:lh + 1], start=True, stop=True)
                e.pes.inc(mm, 1)
            nbp_t = e.pes.n

            @block.vector
            def _(v, nbp_t=nbp_t):
                v.wait_ge(e.pes.h, nbp_t)
                e.dve.inc(v.tensor_copy(nbcm[:], pqr[:, 0:8]), 1)
            e.nbcm_ready = e.dve.n

            proj_pe = e.pes.n
            proj_act = e.act.n
            proj_dve = e.dve.n
            SP.pop_all().close()

        _emit_mla_attention(block, e, x_in, x_out, s_, hg, qc, kc, qr, krm, vtm,
                            nbcm, proj_pe, proj_act, proj_dve)
        ST.pop_all().close()


def _emit_mla_attention(block, e, x_in, x_out, s_, hg, qc, kc, qr, krm, vtm,
                        nbcm, proj_pe, proj_act, proj_dve):
    _barrier(block, e)
    nc = e.nc
    wk = e.wk
    t0 = s_ * NTS
    NHH = 8
    with ExitStack() as ST:
        en = ST.enter_context
        wow = en(nc.sbuf_tensor(_un("l_wow"), [128, 4, D], BF16))
        mask4 = en(nc.sbuf_tensor(_un("l_mask4"), [128, 4, 512], BF16))
        attf = en(nc.sbuf_tensor(_un("l_attf"), [128, 4, TT], BF16))
        psb = [en(nc.sbuf_tensor(_un(f"l_psb{i}"), [128, TT], BF16)) for i in range(4)]
        recb = en(nc.sbuf_tensor(_un("l_recb"), [1, TT], BF16))
        rbs = en(nc.sbuf_tensor(_un("l_rbs"), [64, TT], BF16))
        axs = en(nc.sbuf_tensor(_un("l_axs"), [128, DC, TT], F32))
        x2b = en(nc.sbuf_tensor(_un("l_x2b"), [128, DC, TT], F32))
        pSt = [en(nc.psum_tensor(_un(f"b_pSt{i}"), [128, TT], F32)) for i in range(4)]
        pPV = en(nc.psum_tensor(_un("b_pPV"), [65, TT], F32))
        pMi = en(nc.psum_tensor(_un("b_pMi"), [128, TT], F32))

        @block.gpsimd
        def _(g, hg=hg):
            e.ldw.inc(g.dma_start(
                out=wow[:],
                in_=wk["m_W_O"][hg * 512:(hg + 1) * 512]
                .rearrange("(hc p) f -> p hc f", p=128)), 16)
            e.ldw.inc(g.dma_start(
                out=mask4.ap().rearrange("p r f -> p (r f)"), in_=wk["mask4"]), 16)
        wow_done = e.ldw.n

        pv_free = ("dve", proj_dve)
        pmi_guard = ("act", proj_act)
        attf_free = ("pes", proj_pe)
        xs_guard = ("dve", proj_dve)
        st_guard = ("dve", proj_dve)
        x2_guard = ("stw", e.stw.n) if hg == 0 else ("acc", e.acc.n)
        base_stw = e.stw.n
        recb_prev = proj_dve
        rbs_guard = proj_dve
        st_t = {}
        clip_t = {}
        pv_t = {}
        kabs = 0
        exp_jobs = []
        exs_base = e.exs.n

        def semh(name):
            return getattr(e, name).h

        for q5 in range(NTS):
            qsl = slice(q5 * TT, (q5 + 1) * TT)
            nkb = 4 * (q5 + 1)
            for lh in range(NHH):
                hc = lh // 2
                hb = (lh % 2) * 64
                hr = lh // 2
                hrb = (lh % 2) * 32
                ka0 = kabs

                g0, g1 = st_guard
                gp0, gp1 = pv_free

                @block.tensor
                def _(tt_, lh=lh, nkb=nkb, ka0=ka0, hb=hb, hc=hc, hr=hr,
                      hrb=hrb, qsl=qsl, g0=g0, g1=g1, gp0=gp0, gp1=gp1,
                      first=(q5 == 0 and lh == 0)):
                    if first:
                        tt_.wait_ge(e.act.h, proj_act)
                    tt_.wait_ge(semh(g0), g1)

                    def pv(i):
                        ka = ka0 + i
                        tt_.wait_ge(e.exs.h, exs_base + ka + 1)
                        if i == 0:
                            tt_.wait_ge(semh(gp0), gp1)
                        mm = tt_.matmul(pPV[:], vtm.ap()[:, i, lh, :],
                                        psb[ka % 4][:],
                                        start=(i == 0), stop=(i == nkb - 1))
                        e.pes.inc(mm, 1)
                        pv_t[ka] = e.pes.n

                    for i in range(nkb):
                        ka = ka0 + i
                        kt = slice(i * 128, (i + 1) * 128)
                        if ka >= 4:
                            tt_.wait_ge(e.exs.h, exs_base + ka - 3)
                        tt_.matmul(pSt[ka % 4][:], kc.ap()[hb:hb + 64, hc, kt],
                                   qc.ap()[hb:hb + 64, hc, qsl],
                                   start=True, stop=False)
                        mm = tt_.matmul(pSt[ka % 4][:], krm[hrb:hrb + 32, kt],
                                        qr.ap()[hrb:hrb + 32, hr, qsl],
                                        start=False, stop=True)
                        e.pes.inc(mm, 1)
                        st_t[ka] = e.pes.n
                        if i >= 2:
                            pv(i - 2)
                    pv(nkb - 2)
                    pv(nkb - 1)
                pv_all = e.pes.n

                @block.vector
                def _(v, nkb=nkb, ka0=ka0, q5=q5):
                    v.wait_ge(e.ldw.h, wow_done)
                    for i in range(nkb):
                        ka = ka0 + i
                        v.wait_ge(e.pes.h, st_t[ka])
                        e.dve.inc(v.tensor_scalar(pSt[ka % 4][:], pSt[ka % 4][:],
                                                  80.0, -80.0, ALU.min, ALU.max), 1)
                        r = i - 4 * q5
                        if 0 <= r < 4:
                            v.wait_ge(e.dve.h, e.dve.n)
                            e.dve.inc(v.tensor_tensor(pSt[ka % 4][:],
                                                      pSt[ka % 4][:],
                                                      mask4[:, r, :], ALU.add), 1)
                        clip_t[ka] = e.dve.n

                exp_jobs.append((lh, ka0, nkb))
                if lh == NHH - 1:
                    @block.scalar
                    def _(sc, jobs=tuple(exp_jobs)):
                        sc.wait_ge(e.dve.h, e.nbcm_ready)
                        for (jlh, jka0, jnkb) in jobs:
                            for i in range(jnkb):
                                ka = jka0 + i
                                sc.wait_ge(e.dve.h, clip_t[ka])
                                if ka >= 4:
                                    sc.wait_ge(e.pes.h, pv_t[ka - 4])
                                sc.activation(psb[ka % 4][:], pSt[ka % 4][:],
                                              AF.Exp, bias=nbcm[:, jlh:jlh + 1],
                                              scale=SC_MLA).then_inc(e.exs.h, 1)
                                e.exs.n += 1
                    exp_jobs = []
                kabs += nkb

                @block.vector
                def _(v, pv_all=pv_all, recb_prev=recb_prev):
                    v.wait_ge(e.pes.h, pv_all)
                    v.wait_ge(e.dve.h, recb_prev)
                    e.dve.inc(v.tensor_copy(recb[:], pPV[64:65, :]), 1)
                recb_done = e.dve.n
                recb_prev = recb_done
                g0, g1 = pmi_guard

                @block.tensor
                def _(tt_, recb_done=recb_done, g0=g0, g1=g1):
                    tt_.wait_ge(e.dve.h, recb_done)
                    tt_.wait_ge(semh(g0), g1)
                    e.pes.inc(tt_.matmul(pMi[0:64, :], e.onesB[0:1, 0:64], recb[:],
                                         start=True, stop=True), 1)
                rb_pe = e.pes.n
                g0, g1 = attf_free

                @block.vector
                def _(v, lh=lh, rb_pe=rb_pe, rbs_guard=rbs_guard, g0=g0, g1=g1):
                    v.wait_ge(e.pes.h, rb_pe)
                    v.wait_ge(e.dve.h, rbs_guard)
                    with e.nc.allow_low_precision(reason="softmax recip bf16"):
                        e.dve.inc(v.reciprocal(rbs[:], pMi[0:64, :]), 1)
                    v.wait_ge(e.dve.h, e.dve.n)
                    if lh == 0:
                        v.wait_ge(semh(g0), g1)
                    hc2 = lh // 2
                    hb2 = (lh % 2) * 64
                    e.dve.inc(v.tensor_tensor(attf.ap()[hb2:hb2 + 64, hc2, :],
                                              pPV[0:64, :], rbs[:], ALU.mult), 1)
                pmi_guard = ("dve", e.dve.n)
                pv_free = ("dve", e.dve.n)
                rbs_guard = e.dve.n

            t = t0 + q5
            attf_all = e.dve.n
            g0, g1 = xs_guard
            if hg == 0:
                ldx_t = e.ldx2.n + 16

                @block.sync
                def _(sy, t=t, g0=g0, g1=g1):
                    sy.wait_ge(semh(g0), g1)
                    e.ldx2.inc(sy.dma_start(out=axs[:], in_=e.dtile(x_in, t)), 16)

            gx0, gx1 = x2_guard
            wo_pe = {}
            for dc in range(DC):
                @block.tensor
                def _(tt_, dc=dc, attf_all=attf_all):
                    if dc == 0:
                        tt_.wait_ge(e.dve.h, attf_all)
                        tt_.wait_ge(e.ldw.h, wow_done)
                    if dc >= 4:
                        tt_.wait_ge(e.dve.h, wo_used[dc - 4])
                    for hv in range(4):
                        mm = tt_.matmul(pSt[dc % 4][:],
                                        wow[:, hv, dc * 128:(dc + 1) * 128],
                                        attf.ap()[:, hv, :],
                                        start=(hv == 0), stop=(hv == 3))
                    e.pes.inc(mm, 1)
                if dc == 0:
                    wo_used = {}
                wo_pe[dc] = e.pes.n

                if hg == 0:
                    @block.vector
                    def _(v, dc=dc, ldx_t=ldx_t, gx0=gx0, gx1=gx1):
                        v.wait_ge(e.ldx2.h, ldx_t)
                        v.wait_ge(e.pes.h, wo_pe[dc])
                        if dc == 0:
                            v.wait_ge(semh(gx0), gx1)
                        e.dve.inc(v.tensor_tensor(x2b[:, dc, :], axs[:, dc, :],
                                                  pSt[dc % 4][:], ALU.add), 1)
                        wo_used[dc] = e.dve.n
                else:
                    @block.vector
                    def _(v, dc=dc, gx0=gx0, gx1=gx1):
                        v.wait_ge(e.pes.h, wo_pe[dc])
                        if dc == 0:
                            v.wait_ge(semh(gx0), gx1)
                        e.dve.inc(v.tensor_copy(x2b[:, dc, :],
                                                pSt[dc % 4][:]), 1)
                        wo_used[dc] = e.dve.n
            x2_done = e.dve.n
            attf_free = ("pes", wo_pe[DC - 1])
            xs_guard = ("dve", x2_done)
            st_guard = ("dve", x2_done)

            if hg == 0:
                @block.sync
                def _(sy, t=t, x2_done=x2_done):
                    sy.wait_ge(e.dve.h, x2_done)
                    e.stw.inc(sy.dma_start(out=e.dtile(x_out, t), in_=x2b[:]), 16)
                x2_guard = ("stw", e.stw.n)
            else:
                @block.gpsimd
                def _(g, t=t, x2_done=x2_done, base_stw=base_stw):
                    g.wait_ge(e.dve.h, x2_done)
                    g.wait_ge(e.stw.h, base_stw)
                    e.acc.inc(g.dma_start(out=e.dtile(x_out, t), in_=x2b[:],
                                          accum_op=ALU.add), 16)
                x2_guard = ("acc", e.acc.n)
        ST.pop_all().close()


def _emit_final(block, e):
    @block.gpsimd
    def _(g):
        g.wait_ge(e.stw.h, e.stw.n)
        g.wait_ge(e.acc.h, e.acc.n)

    @block.sync
    def _(sy):
        sy.wait_ge(e.acc.h, e.acc.n)
        sy.wait_ge(e.stw.h, e.stw.n)


def kernel(**inputs):
    C, xc = _prepare(inputs)
    nc = _build()
    in_maps = [dict(C, x_fm=xc[c]) for c in range(N_CORES)]
    res = run_bass_kernel_spmd(nc, in_maps, core_ids=list(range(N_CORES)))
    out = np.empty((16, 2048, 1024), np.float32)
    for c in range(N_CORES):
        out[2 * c:2 * c + 2] = res.results[c]["y_fm"].T.reshape(BPC, S, D)
    return out


# version 06
# speedup vs baseline: 1.2066x; 1.0149x over previous
import numpy as np
import ml_dtypes
from contextlib import ExitStack
import concourse.bass as bass
import concourse.mybir as mybir
from concourse.bass_utils import run_bass_kernel_spmd

F32 = mybir.dt.float32
BF16 = mybir.dt.bfloat16
AF = mybir.ActivationFunctionType
ALU = mybir.AluOpType
AX = mybir.AxisListType.X

D = 1024
S = 2048
BPC = 2
NTOK = BPC * S
DC = 8
TT = 512
NT = NTOK // TT
NTS = S // TT
EPS1 = 1e-5
EPS2 = 1e-6
NH = 16
SC_IN = float(64 ** -0.5)
SC_MLA = float(1.0 / np.sqrt(np.float32(96.0)))
NE = 8
HID = 2048
NEG = -1e30

PHASES = "ABC"
DEBUG = ()
N_CORES = 8


def _bd(*mats):
    n = sum(m.shape[0] for m in mats)
    m_ = sum(m.shape[1] for m in mats)
    out = np.zeros((n, m_), np.float32)
    r = c = 0
    for m in mats:
        out[r:r + m.shape[0], c:c + m.shape[1]] = m
        r += m.shape[0]; c += m.shape[1]
    return out


def _rope_tables():
    j = np.arange(0, 64, 2, dtype=np.float64) / 64.0
    invf = 1.0 / (10000.0 ** j)
    t = np.arange(S, dtype=np.float64)
    ang = t[None, :] * invf[:, None]
    c64 = np.concatenate([np.cos(ang), np.cos(ang)], axis=0).astype(np.float32)
    s64 = np.concatenate([np.sin(ang), np.sin(ang)], axis=0).astype(np.float32)
    c2i = np.concatenate([c64, c64], axis=0)
    s2i = np.concatenate([s64, s64], axis=0)
    B = np.zeros((64, 64), np.float32)
    for i in range(32):
        B[i, i + 32] = -1.0
        B[i + 32, i] = 1.0
    b2T = np.ascontiguousarray(_bd(B, B).T)
    i2 = np.arange(0, 32, 2, dtype=np.float64) / 32.0
    invm = 10000.0 ** (-i2)
    angm = t[None, :] * invm[:, None]
    c32 = np.repeat(np.cos(angm), 2, axis=0).astype(np.float32)
    s32 = np.repeat(np.sin(angm), 2, axis=0).astype(np.float32)
    cm2 = np.concatenate([c32, c32], axis=0)
    sm2 = np.concatenate([s32, s32], axis=0)
    A = np.zeros((32, 32), np.float32)
    for i in range(16):
        A[2 * i, 2 * i + 1] = -1.0
        A[2 * i + 1, 2 * i] = 1.0
    a2T = np.ascontiguousarray(_bd(A, A).T)
    a1T = np.ascontiguousarray(A.T)
    return c2i, s2i, b2T, cm2, sm2, a2T, a1T


def _prepare(inputs):
    f = np.float32
    C = {}
    def ct(x):
        return np.ascontiguousarray(np.asarray(x, dtype=f))
    C["qkvT"] = ct(np.asarray(inputs["L0_ffn_qkv_w"], f).T)
    C["outT"] = ct(np.asarray(inputs["L0_ffn_out_w"], f).T)
    C["w1T"] = ct(np.asarray(inputs["L0_ffn_w1"], f).transpose(0, 2, 1))
    C["w2T"] = ct(np.asarray(inputs["L0_ffn_w2"], f).transpose(0, 2, 1))
    C["w3T"] = ct(np.asarray(inputs["L0_ffn_w3"], f).transpose(0, 2, 1))
    C["gateT"] = ct(np.asarray(inputs["L0_ffn_gate_w"], f).T)
    for k in ("W_DQ", "W_UQ", "W_QR", "W_DKV", "W_UK", "W_UV", "W_KR", "W_O"):
        C["m_" + k] = ct(inputs["L0_attn_" + k])
    qb = np.asarray(inputs["L0_ffn_qkv_b"], f)
    C["qkvb_c"] = ct(qb.reshape(24, 128).T)
    C["vbr"] = ct(qb[2048:].reshape(1, 1024))
    C["outb_c"] = ct(np.asarray(inputs["L0_ffn_out_b"], f).reshape(8, 128).T)
    C["n1c"] = ct(np.asarray(inputs["L0_ffn_norm1_w"], f).reshape(8, 128).T)
    C["n2c"] = ct(np.asarray(inputs["L0_ffn_norm2_w"], f).reshape(8, 128).T)
    C["n1r"] = ct(np.asarray(inputs["L0_ffn_norm1_w"], f).reshape(1, 1024))
    C["b1c"] = ct(np.asarray(inputs["L0_ffn_b1"], f).reshape(8, 16, 128).transpose(2, 0, 1))
    C["b2c"] = ct(np.asarray(inputs["L0_ffn_b2"], f).reshape(8, 16, 128).transpose(2, 0, 1))
    C["b3r"] = ct(np.asarray(inputs["L0_ffn_b3"], f).reshape(1, 8192))
    c2i, s2i, b2T, cm2, sm2, a2T, a1T = _rope_tables()
    C["c2i"], C["s2i"], C["b2T"] = c2i, s2i, b2T
    C["cm2"], C["sm2"], C["a2T"], C["a1T"] = cm2, sm2, a2T, a1T
    C["identF"] = np.eye(128, dtype=f)
    m4 = np.zeros((4, 128, 512), f)
    tri = np.fromfunction(lambda k, q: (k > q), (128, 128))
    for r in range(4):
        for s_ in range(4):
            if s_ < r:
                m4[r][:, s_ * 128:(s_ + 1) * 128] = NEG
            elif s_ == r:
                m4[r][:, s_ * 128:(s_ + 1) * 128] = np.where(tri, NEG, 0.0)
    C["mask4"] = np.ascontiguousarray(m4.transpose(1, 0, 2).reshape(128, 4 * 512))
    x = np.asarray(inputs["L0_attn_x"], f)
    xc = [np.ascontiguousarray(x[2 * c:2 * c + 2].reshape(NTOK, D).T)
          for c in range(N_CORES)]
    return C, xc


_UID = [0]


def _un(name):
    _UID[0] += 1
    return f"{name}_{_UID[0]}"


class Sem:
    __slots__ = ("h", "n")
    def __init__(self, h):
        self.h = h
        self.n = 0
    def inc(self, inst, k=1):
        inst.then_inc(self.h, k)
        self.n += k
        return self.n


class Env:
    pass


DRAM_SPECS = [
    ("qkvT", [D, 3072]), ("outT", [D, D]),
    ("w1T", [NE, D, HID]), ("w2T", [NE, D, HID]), ("w3T", [NE, HID, D]),
    ("gateT", [D, 8]),
    ("m_W_DQ", [D, 256]), ("m_W_UQ", [256, D]), ("m_W_QR", [256, 512]),
    ("m_W_DKV", [D, 256]), ("m_W_UK", [256, D]), ("m_W_UV", [256, D]),
    ("m_W_KR", [D, 32]), ("m_W_O", [D, D]),
    ("qkvb_c", [128, 24]), ("vbr", [1, 1024]), ("outb_c", [128, 8]),
    ("n1c", [128, 8]), ("n2c", [128, 8]), ("n1r", [1, 1024]),
    ("b1c", [128, 8, 16]), ("b2c", [128, 8, 16]), ("b3r", [1, 8192]),
    ("c2i", [128, S]), ("s2i", [128, S]), ("b2T", [128, 128]),
    ("cm2", [64, S]), ("sm2", [64, S]), ("a2T", [64, 64]), ("a1T", [32, 32]),
    ("identF", [128, 128]), ("mask4", [128, 4 * 512]),
]

ENGS = ("tensor", "vector", "scalar", "gpsimd", "sync")


def _build():
    nc = bass.Bass("TRN2", target_bir_lowering=False)
    e = Env()
    e.nc = nc
    dt = nc.dram_tensor
    e.x0 = dt("x_fm", [D, NTOK], F32, kind="ExternalInput").ap()
    e.wk = {name: dt(name, shape, F32, kind="ExternalInput").ap()
            for name, shape in DRAM_SPECS}

    def scratch(name, dtype):
        kind = "ExternalOutput" if name in DEBUG else "Internal"
        return dt(name, [D, NTOK], dtype, kind=kind).ap()
    e.x1s = scratch("x1s", F32)
    e.x2s = scratch("x2s", F32)
    e.us = scratch("us", F32)
    e.u2s = scratch("u2s", BF16)
    e.u1s = scratch("u1s", BF16) if "u1s" in DEBUG else None
    e.yfm = dt("y_fm", [D, NTOK], F32, kind="ExternalOutput").ap()
    if "rows" in DEBUG:
        e.rows_dbg = dt("rows_dbg", [4, NTOK], BF16, kind="ExternalOutput").ap()
        e.rowsf_dbg = dt("rowsf_dbg", [2, NTOK], F32, kind="ExternalOutput").ap()
    else:
        e.rows_dbg = None
        e.rowsf_dbg = None
    e.qk_dbg = (dt("qk_dbg", [128, 4 * 8 * S], BF16, kind="ExternalOutput").ap()
                if "qk" in DEBUG else None)
    e.attf_dbg = (dt("attf_dbg", [128, 8 * NTOK], BF16, kind="ExternalOutput").ap()
                  if "attf" in DEBUG else None)

    def dtile(ap, t, w=TT):
        return ap.rearrange("(dc p) f -> p dc f", p=128)[:, :, t * w:(t + 1) * w]
    e.dtile = dtile

    with ExitStack() as ST:
        en = ST.enter_context
        for s_ in ("ldc", "ldg", "ldx", "ldx2", "ldw", "sq", "pes", "row", "dve", "act",
                   "pool", "stw", "acc", "bar", "gat", "ldu", "exs"):
            setattr(e, s_, Sem(en(nc.semaphore(s_))))
        sb = nc.sbuf_tensor
        e.identF = en(sb("c_identF", [128, 128], F32))
        e.onesF = en(sb("c_onesF", [1, 128], F32))
        e.onesB = en(sb("c_onesB", [1, 128], BF16))
        e.oneC = en(sb("c_oneC", [128, 1], BF16))
        e.b2T = en(sb("c_b2T", [128, 128], BF16))
        e.a2T = en(sb("c_a2T", [64, 64], BF16))
        e.a1T = en(sb("c_a1T", [32, 32], BF16))
        e.qkvb = en(sb("c_qkvb", [128, 24], F32))
        e.vbr = en(sb("c_vbr", [1, 1024], BF16))
        e.outb = en(sb("c_outb", [128, 8], F32))
        e.n1c = en(sb("c_n1c", [128, 8], F32))
        e.n2c = en(sb("c_n2c", [128, 8], F32))
        e.n1r = en(sb("c_n1r", [1, 1024], BF16))
        e.b1c = en(sb("c_b1c", [128, 8, 16], F32))
        e.b2c = en(sb("c_b2c", [128, 8, 16], F32))
        e.gateT = en(sb("c_gateT", [128, 8, 8], BF16))
        e.onesC1 = en(sb("c_onesC1", [128, 1], BF16))
        e.ceps1 = en(sb("c_eps1", [1, 1], F32))
        e.ceps2 = en(sb("c_eps2", [1, 1], F32))

        with nc.Block() as block:
            _emit_const_loads(block, e)
            out_a = e.x1s if ("B" in PHASES or "C" in PHASES) else e.yfm
            _emit_ffn(block, e, e.x0, out_a)
            if "B" in PHASES:
                _barrier(block, e)
                out_b = e.x2s if "C" in PHASES else e.yfm
                _emit_mla(block, e, out_a, out_b)
            if "C" in PHASES:
                _barrier(block, e)
                _emit_ffn(block, e, e.x2s, e.yfm)
            _emit_final(block, e)
    return nc


def _barrier(block, e):
    bs = e.bar
    caps = [(s_, getattr(e, s_).n) for s_ in
            ("ldc", "ldg", "ldx", "ldx2", "ldw", "ldu", "stw", "acc")]

    @block.gpsimd
    def _(g):
        for s_, n_ in caps:
            g.wait_ge(getattr(e, s_).h, n_)

    @block.sync
    def _(sy):
        for s_, n_ in caps:
            sy.wait_ge(getattr(e, s_).h, n_)
    for name in ENGS:
        def _f(eng, bs=bs):
            bs.inc(eng.nop(), 1)
        getattr(block, name)(_f)
    tgt = bs.n
    for name in ENGS:
        def _f(eng, tgt=tgt, bs=bs):
            eng.wait_ge(bs.h, tgt)
        getattr(block, name)(_f)


def _emit_const_loads(block, e):
    wk = e.wk
    ldc = e.ldc

    @block.sync
    def _(s_):
        for src, dst in [
            (wk["identF"], e.identF[:]), (wk["qkvb_c"], e.qkvb[:]),
            (wk["outb_c"], e.outb[:]), (wk["n1c"], e.n1c[:]), (wk["n2c"], e.n2c[:]),
            (wk["b1c"], e.b1c[:]), (wk["b2c"], e.b2c[:]),
        ]:
            ldc.inc(s_.dma_start(out=dst, in_=src), 16)

    @block.gpsimd
    def _(g):
        for src, dst in [
            (wk["b2T"], e.b2T[:]), (wk["a2T"], e.a2T[:]), (wk["a1T"], e.a1T[:]),
            (wk["n1r"], e.n1r[:]),
            (wk["vbr"], e.vbr[:]),
            (wk["gateT"].rearrange("(dc p) x -> p dc x", p=128), e.gateT[:]),
        ]:
            e.ldg.inc(g.dma_start(out=dst, in_=src), 16)

    @block.vector
    def _(v):
        v.memset(e.onesF[:], 1.0)
        v.memset(e.onesB[:], 1.0)
        v.memset(e.onesC1[:], 1.0)
        v.memset(e.ceps1[:], EPS1)
        e.row.inc(v.memset(e.ceps2[:], EPS2), 1)
        e.dve.inc(v.memset(e.oneC[:], 1.0 / 1024.0), 1)
    e.ldc_done = ldc.n
    e.ldg_done = e.ldg.n
    e.memset_done = e.dve.n


def _row_chain(block, e, ops):
    for eng_name, fn in ops:
        tgt = e.row.n
        def _f(eng, fn=fn, tgt=tgt):
            eng.wait_ge(e.row.h, tgt)
            with e.nc.allow_low_precision(reason="per-token scale rows in bf16"):
                e.row.inc(fn(eng), 1)
        getattr(block, eng_name)(_f)


def _emit_stats(block, e, x_in, with_mean):
    nc = e.nc
    with ExitStack() as ST:
        en = ST.enter_context
        xb = [en(nc.sbuf_tensor(_un(f"st_xb{i}"), [128, DC, TT], BF16)) for i in range(2)]
        sq = [en(nc.sbuf_tensor(_un(f"st_sq{i}"), [128, DC, TT], BF16)) for i in range(2)]
        pm = [en(nc.psum_tensor(_un(f"st_pm{i}"), [1, TT], F32)) for i in range(2)]
        pms = [en(nc.psum_tensor(_un(f"st_pms{i}"), [1, TT], F32)) for i in range(2)]
        b_ldx = e.ldg.n
        b_sq = e.sq.n
        b_pe = e.pes.n
        b_row = e.row.n
        for t in range(NT):
            @block.gpsimd
            def _(g, t=t):
                if t >= 2:
                    g.wait_ge(e.pes.h, b_pe + (t - 1))
                e.ldg.inc(g.dma_start(out=xb[t % 2][:], in_=e.dtile(x_in, t)), 16)

            @block.scalar
            def _(sc, t=t):
                sc.wait_ge(e.ldg.h, b_ldx + 16 * (t + 1))
                if t >= 2:
                    sc.wait_ge(e.pes.h, b_pe + (t - 1))
                e.sq.inc(sc.activation(sq[t % 2].ap(), xb[t % 2].ap(), AF.Square,
                                       bias=0.0, scale=1.0), 1)

            @block.tensor
            def _(tt_, t=t):
                tt_.wait_ge(e.dve.h, e.memset_done)
                tt_.wait_ge(e.sq.h, b_sq + (t + 1))
                if t >= 2:
                    tt_.wait_ge(e.row.h, b_row + 2 * (t - 1))
                if with_mean:
                    for dc in range(DC):
                        tt_.matmul(pm[t % 2][:], e.oneC[:], xb[t % 2][:, dc, :],
                                   start=(dc == 0), stop=(dc == DC - 1))
                for dc in range(DC):
                    mm = tt_.matmul(pms[t % 2][:], e.oneC[:], sq[t % 2][:, dc, :],
                                    start=(dc == 0), stop=(dc == DC - 1))
                e.pes.inc(mm, 1)

            @block.vector
            def _(v, t=t):
                v.wait_ge(e.pes.h, b_pe + (t + 1))
                if with_mean:
                    v.tensor_copy(e.r_m[:, t * TT:(t + 1) * TT], pm[t % 2][:])
                e.row.inc(v.tensor_copy(e.r_ms[:, t * TT:(t + 1) * TT],
                                        pms[t % 2][:]), 2)


def _emit_ln_rows(block, e, compose_rms):
    ops = [
        ("scalar", lambda sc: sc.activation(e.rs1[:], e.r_m[:], AF.Square, bias=0.0, scale=1.0)),
        ("vector", lambda v: v.tensor_tensor(e.rs2[:], e.r_ms[:], e.rs1[:], ALU.subtract)),
        ("scalar", lambda sc: sc.activation(e.rs1[:], e.rs2[:], AF.Sqrt, bias=e.ceps1[0:1, 0:1], scale=1.0)),
        ("vector", lambda v: v.reciprocal(e.r_a[:], e.rs1[:])),
        ("vector", lambda v: v.scalar_tensor_tensor(e.r_b[:], e.r_m[:], -1.0, e.r_a[:], ALU.mult, ALU.mult)),
    ]
    if compose_rms:
        ops += [
            ("scalar", lambda sc: sc.activation(e.rs1[:], e.r_a[:], AF.Square, bias=0.0, scale=1.0)),
            ("vector", lambda v: v.tensor_tensor(e.rs3[:], e.rs2[:], e.rs1[:], ALU.mult)),
            ("scalar", lambda sc: sc.activation(e.rs1[:], e.rs3[:], AF.Sqrt, bias=e.ceps2[0:1, 0:1], scale=1.0)),
            ("vector", lambda v: v.reciprocal(e.rs3[:], e.rs1[:])),
            ("vector", lambda v: v.tensor_tensor(e.r_A[:], e.r_a[:], e.rs3[:], ALU.mult)),
            ("vector", lambda v: v.scalar_tensor_tensor(e.r_B[:], e.r_m[:], -1.0, e.r_A[:], ALU.mult, ALU.mult)),
        ]
    _row_chain(block, e, ops)


def _emit_rms_row(block, e):
    _row_chain(block, e, [
        ("scalar", lambda sc: sc.activation(e.r_B[:], e.r_ms[:], AF.Sqrt, bias=e.ceps2[0:1, 0:1], scale=1.0)),
        ("vector", lambda v: v.reciprocal(e.r_A[:], e.r_B[:])),
    ])


def _emit_ffn(block, e, x_in, x_out):
    nc = e.nc
    with ExitStack() as SF:
        enf = SF.enter_context
        for rn in ("r_A", "r_B", "r_ms", "r_a", "r_b"):
            setattr(e, rn, enf(nc.sbuf_tensor(_un("f_") + rn, [1, NTOK], BF16))[:])

        with ExitStack() as SS:
            ens = SS.enter_context
            for rn in ("r_m", "rs1", "rs2", "rs3"):
                setattr(e, rn, ens(nc.sbuf_tensor(_un("f_") + rn, [1, NTOK], F32))[:])
            _emit_stats(block, e, x_in, with_mean=True)
            _emit_ln_rows(block, e, compose_rms=True)
            if e.rows_dbg is not None:
                rt_ = e.row.n

                @block.sync
                def _(sy, rt_=rt_):
                    sy.wait_ge(e.row.h, rt_)
                    for i, r in enumerate((e.r_A, e.r_B, e.r_a, e.r_b)):
                        e.stw.inc(sy.dma_start(out=e.rows_dbg[i:i + 1, :], in_=r), 16)

                @block.gpsimd
                def _(g, rt_=rt_):
                    g.wait_ge(e.row.h, rt_)
                    for i, r in enumerate((e.r_m, e.r_ms)):
                        e.stw.inc(g.dma_start(out=e.rowsf_dbg[i:i + 1, :], in_=r), 16)
        for s_ in range(BPC):
            _emit_inner_seq(block, e, x_in, s_)
        _emit_rms_row(block, e)
        with ExitStack() as SC:
            enc = SC.enter_context
            cT = [enc(nc.sbuf_tensor(_un(f"f_cT{i}"), [97, NTOK], BF16)) for i in range(3)]
            e.combR = [cT[i // 3][(i % 3) * 32:(i % 3) * 32 + 1, :] for i in range(NE)]
            base_done = _emit_u2_gate_base(block, e, x_in, x_out)
            _emit_moe(block, e, x_out, base_done)
            _drain_phase(block, e)
        SF.pop_all().close()


def _drain_phase(block, e):
    _barrier(block, e)


def _emit_inner_seq(block, e, x_in, s_):
    _barrier(block, e)
    nc = e.nc
    wk = e.wk
    t0 = s_ * NTS
    row_ready = e.row.n
    with ExitStack() as ST:
        en = ST.enter_context
        qrope = en(nc.sbuf_tensor(_un("i_qrope"), [128, 8, S], BF16))
        krope = en(nc.sbuf_tensor(_un("i_krope"), [128, 8, S], BF16))
        vtm = en(nc.sbuf_tensor(_un("i_vtm"), [128, 16, 16, 65], BF16))
        nbc = en(nc.sbuf_tensor(_un("i_nbc"), [128, 16], F32))

        with ExitStack() as SP:
            enp = SP.enter_context
            u1f = enp(nc.sbuf_tensor(_un("i_u1f"), [128, DC, S], BF16))
            qkraw = [enp(nc.sbuf_tensor(_un(f"i_qkraw{i}"), [128, TT], BF16)) for i in range(2)]
            rt = [enp(nc.sbuf_tensor(_un(f"i_rt{i}"), [128, TT], F32)) for i in range(2)]
            pGB = enp(nc.psum_tensor(_un("i_pGB"), [128, TT], F32))
            pBW = [enp(nc.psum_tensor(_un(f"i_pBW{i}"), [128, TT], F32)) for i in range(1)]
            pqk = [enp(nc.psum_tensor(_un(f"i_pqk{i}"), [128, TT], F32)) for i in range(2)]
            pbq = [enp(nc.psum_tensor(_un(f"i_pbq{i}"), [128, TT], F32)) for i in range(2)]
            pvv = [enp(nc.psum_tensor(_un(f"i_pvv{i}"), [128, TT], F32)) for i in range(2)]


            @block.vector
            def _(v):
                v.wait_ge(e.ldc.h, e.ldc_done)
                v.wait_ge(e.ldg.h, e.ldg_done)
                e.dve.inc(v.memset(vtm.ap()[:, :, :, 64:65], 1.0), 1)
            vones = e.dve.n

            SX = ExitStack()
            xs = SX.enter_context(nc.sbuf_tensor(_un("i_xs"), [128, DC, TT], F32))
            xs_free = {}
            u1_done = {}
            for ts in range(NTS):
                t = t0 + ts
                sl = slice(t * TT, (t + 1) * TT)
                ssl = slice(ts * TT, (ts + 1) * TT)
                ldx_t = e.ldx.n + 16

                @block.sync
                def _(sy, ts=ts, t=t):
                    if ts >= 1:
                        sy.wait_ge(e.dve.h, xs_free[ts - 1])
                    e.ldx.inc(sy.dma_start(out=xs[:], in_=e.dtile(x_in, t)), 16)

                @block.tensor
                def _(tt_, ts=ts, sl=sl):
                    tt_.wait_ge(e.row.h, row_ready)
                    if ts >= 1:
                        tt_.wait_ge(e.dve.h, u1_done[ts - 1])
                    e.pes.inc(tt_.matmul(pGB[:], e.onesB[:], e.r_A[:, sl],
                                         start=True, stop=True), 1)
                for dc in range(DC):
                    @block.tensor
                    def _(tt_, dc=dc, ts=ts, sl=sl):
                        if dc >= 1:
                            tt_.wait_ge(e.dve.h, bw_used[dc - 1])
                        elif ts >= 1:
                            tt_.wait_ge(e.dve.h, u1_done[ts - 1])
                        e.pes.inc(tt_.matmul(pBW[0][:],
                                             e.n1r[0:1, dc * 128:(dc + 1) * 128],
                                             e.r_B[:, sl], start=True, stop=True), 1)
                    if dc == 0:
                        bw_used = {}
                    bw_pe = e.pes.n

                    @block.vector
                    def _(v, dc=dc, ts=ts, ssl=ssl, bw_pe=bw_pe, ldx_t=ldx_t):
                        v.wait_ge(e.pes.h, bw_pe)
                        v.wait_ge(e.ldx.h, ldx_t)
                        e.dve.inc(v.tensor_tensor(rt[dc % 2][:], xs[:, dc, :],
                                                  pGB[:], ALU.mult), 1)
                        v.wait_ge(e.dve.h, e.dve.n)
                        e.dve.inc(v.scalar_tensor_tensor(
                            u1f[:, dc, ssl], rt[dc % 2][:], e.n1c[:, dc:dc + 1],
                            pBW[0][:], ALU.mult, ALU.add), 1)
                        bw_used[dc] = e.dve.n
                xs_free[ts] = e.dve.n
                u1_done[ts] = e.dve.n
            u1_all = e.dve.n
            SX.pop_all().close()

            if e.u1s is not None:
                @block.sync
                def _(sy, u1_all=u1_all):
                    sy.wait_ge(e.dve.h, u1_all)
                    for ts in range(NTS):
                        e.stw.inc(sy.dma_start(
                            out=e.dtile(e.u1s, t0 + ts),
                            in_=u1f[:, :, ts * TT:(ts + 1) * TT]), 16)

            with ExitStack() as SV:
                wv = SV.enter_context(nc.sbuf_tensor(_un("i_wv"), [128, DC, 512], BF16))
                tbl_done = e.ldw.n
                v_evac = {}
                vh_used = {}
                for hh in range(2):
                    @block.gpsimd
                    def _(g, hh=hh, u1_all=u1_all):
                        g.wait_ge(e.ldc.h, e.ldc_done)
                        g.wait_ge(e.ldg.h, e.ldg_done)
                        g.wait_ge(e.dve.h, u1_all)
                        if hh >= 1:
                            g.wait_ge(e.pes.h, vh_used[hh - 1])
                        e.ldw.inc(g.dma_start(
                            out=wv[:],
                            in_=wk["qkvT"][:, 2048 + hh * 512:2048 + (hh + 1) * 512]
                            .rearrange("(dc p) f -> p dc f", p=128)), 16)
                    wvl = e.ldw.n
                    for ts in range(NTS):
                        for st_ in range(4):
                            ai = hh * 16 + ts * 4 + st_

                            @block.tensor
                            def _(tt_, st_=st_, hh=hh, ts=ts, ai=ai,
                                  u1_all=u1_all, wvl=wvl):
                                tt_.wait_ge(e.dve.h, u1_all)
                                tt_.wait_ge(e.ldw.h, wvl)
                                if ai >= 2:
                                    tt_.wait_ge(e.act.h, v_evac[ai - 2])
                                for dc in range(DC):
                                    tt_.matmul(pvv[ai % 2][:],
                                               u1f[:, dc, ts * TT + st_ * 128:ts * TT + (st_ + 1) * 128],
                                               wv[:, dc, :],
                                               start=(dc == 0), stop=False)
                                mm = tt_.matmul(pvv[ai % 2][:], e.onesB[:],
                                                e.vbr[0:1, hh * 512:(hh + 1) * 512],
                                                start=False, stop=True)
                                e.pes.inc(mm, 1)
                                if ts == NTS - 1 and st_ == 3:
                                    vh_used[hh] = e.pes.n
                            pe_v = e.pes.n

                            @block.vector
                            def _(v, st_=st_, hh=hh, ts=ts, ai=ai, pe_v=pe_v,
                                  vones=vones):
                                v.wait_ge(e.pes.h, pe_v)
                                v.wait_ge(e.dve.h, vones)
                                kt = ts * 4 + st_
                                e.act.inc(v.tensor_copy(
                                    vtm.ap()[:, kt, hh * 8:(hh + 1) * 8, 0:64],
                                    pvv[ai % 2][:]), 1)
                                v_evac[ai] = e.act.n

                v_done_pe = e.pes.n
                v_done_act = e.act.n
                SV.pop_all().close()

            with ExitStack() as SQ:
                enq = SQ.enter_context
                c2i = enq(nc.sbuf_tensor(_un("i_c2i"), [128, S], BF16))
                s2i = enq(nc.sbuf_tensor(_un("i_s2i"), [128, S], BF16))
                wc = [enq(nc.sbuf_tensor(_un(f"i_wc{i}"), [128, DC, 128], BF16)) for i in range(2)]
                sqn = enq(nc.sbuf_tensor(_un("i_sqn"), [128, TT], BF16))
                qn2 = enq(nc.sbuf_tensor(_un("i_qn2"), [1, 64], F32))
                kn2s = enq(nc.sbuf_tensor(_un("i_kn2s"), [1, 64], F32))
                qm2 = enq(nc.sbuf_tensor(_un("i_qm2"), [1, 16], F32))
                km2 = enq(nc.sbuf_tensor(_un("i_km2"), [1, 16], F32))
                b2r = enq(nc.sbuf_tensor(_un("i_b2r"), [1, 16], F32))
                negbr = enq(nc.sbuf_tensor(_un("i_negbr"), [1, 16], F32))

                @block.gpsimd
                def _(g):
                    g.wait_ge(e.ldc.h, e.ldc_done)
                    g.wait_ge(e.ldg.h, e.ldg_done)
                    g.wait_ge(e.pes.h, v_done_pe)
                    e.ldw.inc(g.dma_start(out=c2i[:], in_=wk["c2i"]), 16)
                    e.ldw.inc(g.dma_start(out=s2i[:], in_=wk["s2i"]), 16)
                tbl2_done = e.ldw.n

                qk_evac = {}
                bq_pe = {}
                rope_done = {}
                wc_used = {}
                sqn_t = {}
                nrm_pe = {}
                nrm_red = {}
                for c in range(16):
                    @block.gpsimd
                    def _(g, c=c):
                        if c >= 2:
                            g.wait_ge(e.pes.h, wc_used[c - 2])
                        e.ldw.inc(g.dma_start(
                            out=wc[c % 2][:],
                            in_=wk["qkvT"][:, c * 128:(c + 1) * 128]
                            .rearrange("(dc p) f -> p dc f", p=128)), 16)
                    wcl = e.ldw.n
                    for ts in range(NTS):
                        ac = c * 4 + ts
                        ssl = slice(ts * TT, (ts + 1) * TT)

                        @block.tensor
                        def _(tt_, c=c, ts=ts, ssl=ssl, ac=ac, wcl=wcl,
                              u1_all=u1_all):
                            tt_.wait_ge(e.dve.h, u1_all)
                            tt_.wait_ge(e.ldw.h, wcl)
                            if ac >= 2:
                                tt_.wait_ge(e.act.h, qk_evac[ac - 2])
                            for dc in range(DC):
                                mm = tt_.matmul(pqk[ac % 2][:], wc[c % 2][:, dc, :],
                                                u1f[:, dc, ssl],
                                                start=(dc == 0), stop=(dc == DC - 1))
                            e.pes.inc(mm, 1)
                            if ts == NTS - 1:
                                wc_used[c] = e.pes.n
                        qk_pe = e.pes.n

                        @block.vector
                        def _(v, c=c, qk_pe=qk_pe, ac=ac):
                            v.wait_ge(e.pes.h, qk_pe)
                            if ac >= 2:
                                v.wait_ge(e.pes.h, bq_pe[ac - 2])
                                v.wait_ge(e.dve.h, rope_done[ac - 2])
                                v.wait_ge(e.pool.h, sqn_t[ac - 2])
                            e.act.inc(v.tensor_scalar(qkraw[ac % 2][:], pqk[ac % 2][:],
                                                      e.qkvb[:, c:c + 1], None,
                                                      ALU.add), 1)
                            qk_evac[ac] = e.act.n

                        @block.tensor
                        def _(tt_, ac=ac):
                            tt_.wait_ge(e.act.h, qk_evac[ac])
                            if ac >= 2:
                                tt_.wait_ge(e.dve.h, rope_done[ac - 2])
                            e.pes.inc(tt_.matmul(pbq[ac % 2][:], e.b2T[:], qkraw[ac % 2][:],
                                                 start=True, stop=True), 1)
                            bq_pe[ac] = e.pes.n

                        @block.gpsimd
                        def _(g, ssl=ssl, ac=ac, u1_all=u1_all):
                            g.wait_ge(e.act.h, qk_evac[ac])
                            if ac >= 1:
                                g.wait_ge(e.dve.h, rope_done[ac - 1])
                            else:
                                g.wait_ge(e.dve.h, u1_all)
                            e.pool.inc(g.tensor_tensor(rt[0][:], qkraw[ac % 2][:],
                                                       c2i[:, ssl], ALU.mult), 1)
                        rp_t = e.pool.n

                        @block.vector
                        def _(v, c=c, ssl=ssl, ac=ac, rp_t=rp_t):
                            dst = qrope if c < 8 else krope
                            ci = c % 8
                            v.wait_ge(e.pes.h, bq_pe[ac])
                            v.wait_ge(e.pool.h, rp_t)
                            e.dve.inc(v.tensor_tensor(rt[1][:], pbq[ac % 2][:],
                                                      s2i[:, ssl], ALU.mult), 1)
                            v.wait_ge(e.dve.h, e.dve.n)
                            e.dve.inc(v.tensor_tensor(dst[:, ci, ssl], rt[0][:], rt[1][:],
                                                      ALU.add), 1)
                            rope_done[ac] = e.dve.n

                        @block.gpsimd
                        def _(g, ac=ac):
                            g.wait_ge(e.act.h, qk_evac[ac])
                            if ac >= 1:
                                g.wait_ge(e.pes.h, nrm_pe[ac - 1])
                            e.pool.inc(g.tensor_tensor(sqn[:], qkraw[ac % 2][:],
                                                       qkraw[ac % 2][:], ALU.mult), 1)
                        sqn_t[ac] = e.pool.n

                        @block.tensor
                        def _(tt_, ac=ac):
                            tt_.wait_ge(e.pool.h, sqn_t[ac])
                            if ac < 2:
                                tt_.wait_ge(e.act.h, v_done_act)
                            else:
                                tt_.wait_ge(e.dve.h, nrm_red[ac - 2])
                            tt_.matmul(pvv[ac % 2][0:1, :], e.onesC1[0:64, 0:1],
                                       sqn[0:64, :], start=True, stop=True)
                            mm = tt_.matmul(pvv[ac % 2][32:33, :],
                                            e.onesC1[64:128, 0:1],
                                            sqn[64:128, :], start=True, stop=True)
                            e.pes.inc(mm, 1)
                        nrm_pe[ac] = e.pes.n

                        @block.vector
                        def _(v, ac=ac, c=c, ts=ts):
                            v.wait_ge(e.pes.h, nrm_pe[ac])
                            dst = qn2 if c < 8 else kn2s
                            h0 = (c % 8) * 2
                            e.dve.inc(v.reduce_max(
                                dst[0:1, h0 * 4 + ts:h0 * 4 + ts + 1],
                                pvv[ac % 2][0:1, :], axis=AX), 1)
                            e.dve.inc(v.reduce_max(
                                dst[0:1, (h0 + 1) * 4 + ts:(h0 + 1) * 4 + ts + 1],
                                pvv[ac % 2][32:33, :], axis=AX), 1)
                        nrm_red[ac] = e.dve.n

                bnd0 = e.dve.n

                @block.vector
                def _(v, bnd0=bnd0):
                    v.wait_ge(e.dve.h, bnd0)
                    for h in range(16):
                        e.dve.inc(v.reduce_max(qm2[0:1, h:h + 1],
                                               qn2[0:1, h * 4:h * 4 + 4], axis=AX), 1)
                        e.dve.inc(v.reduce_max(km2[0:1, h:h + 1],
                                               kn2s[0:1, h * 4:h * 4 + 4], axis=AX), 1)
                    v.wait_ge(e.dve.h, e.dve.n)
                    e.dve.inc(v.tensor_tensor(b2r[:], qm2[:], km2[:], ALU.mult), 1)
                bq_t = e.dve.n

                @block.scalar
                def _(sc, bq_t=bq_t):
                    sc.wait_ge(e.dve.h, bq_t)
                    e.sq.inc(sc.activation(b2r[:], b2r[:], AF.Sqrt,
                                           bias=0.0, scale=1.0), 1)
                br_t = e.sq.n

                @block.vector
                def _(v, br_t=br_t):
                    v.wait_ge(e.sq.h, br_t)
                    e.dve.inc(v.tensor_scalar(negbr[:], b2r[:], -SC_IN, None,
                                              ALU.mult), 1)
                nb_t = e.dve.n

                @block.tensor
                def _(tt_, nb_t=nb_t):
                    tt_.wait_ge(e.dve.h, nb_t)
                    for h in range(16):
                        mm = tt_.matmul(pGB[:, h:h + 1], e.onesF[:],
                                        negbr[0:1, h:h + 1], start=True, stop=True)
                    e.pes.inc(mm, 1)
                nbp = e.pes.n

                @block.vector
                def _(v, nbp=nbp):
                    v.wait_ge(e.pes.h, nbp)
                    e.dve.inc(v.tensor_copy(nbc[:], pGB[:, 0:16]), 1)
                e.nbc_ready = e.dve.n
                SQ.pop_all().close()
            proj_pe = e.pes.n
            proj_act = e.act.n
            proj_dve = e.dve.n
            SP.pop_all().close()

        if e.qk_dbg is not None:
            @block.sync
            def _(sy, s_=s_, proj_dve=proj_dve):
                sy.wait_ge(e.dve.h, proj_dve)
                off = s_ * 2 * 8 * S
                e.stw.inc(sy.dma_start(
                    out=e.qk_dbg[:, off:off + 8 * S],
                    in_=qrope.ap().rearrange("p c f -> p (c f)")), 16)
                e.stw.inc(sy.dma_start(
                    out=e.qk_dbg[:, off + 8 * S:off + 16 * S],
                    in_=krope.ap().rearrange("p c f -> p (c f)")), 16)

        _emit_inner_attention(block, e, x_in, s_, qrope, krope, vtm, nbc,
                              proj_pe, proj_act, proj_dve)
        ST.pop_all().close()


def _emit_inner_attention(block, e, x_in, s_, qrope, krope, vtm, nbc,
                          proj_pe, proj_act, proj_dve):
    _barrier(block, e)
    nc = e.nc
    wk = e.wk
    t0 = s_ * NTS
    row_ready = e.row.n
    with ExitStack() as ST:
        en = ST.enter_context
        outw = en(nc.sbuf_tensor(_un("i_outw"), [128, DC, D], BF16))
        attf = en(nc.sbuf_tensor(_un("i_attf"), [128, 8, TT], BF16))
        psb = [en(nc.sbuf_tensor(_un(f"i_psb{i}"), [128, 1024], BF16)) for i in range(6)]
        recb = en(nc.sbuf_tensor(_un("i_recb"), [1, TT], BF16))
        rbs = en(nc.sbuf_tensor(_un("i_rbs"), [64, TT], BF16))
        axs = en(nc.sbuf_tensor(_un("i_axs"), [128, DC, TT], F32))
        umb2 = [en(nc.sbuf_tensor(_un(f"i_umb{i}"), [128, TT], F32)) for i in range(2)]
        sqb = [en(nc.sbuf_tensor(_un(f"i_sqb{i}"), [128, TT], BF16)) for i in range(2)]
        sqm_pe = {}
        pSt = [en(nc.psum_tensor(_un(f"a_pSt{i}"), [128, 1024], F32)) for i in range(3)]
        pPV = en(nc.psum_tensor(_un("a_pPV"), [65, TT], F32))
        pMi = en(nc.psum_tensor(_un("a_pMi"), [128, TT], F32))

        @block.gpsimd
        def _(g):
            e.ldw.inc(g.dma_start(
                out=outw[:], in_=wk["outT"].rearrange("(dc p) f -> p dc f", p=128)), 16)
        outw_done = e.ldw.n

        pv_free = ("dve", proj_dve)
        pmi_guard = ("dve", proj_dve)
        attf_free = ("pes", proj_pe)
        xs_guard = ("dve", proj_dve)
        umb_guard = [("stw", e.stw.n), ("sq", e.sq.n)]
        recb_prev = proj_dve
        rbs_guard = proj_dve
        st_extra = ("dve", proj_dve)
        pv_t = {}
        st_t = {}
        kpg = 0
        exs_base = e.exs.n

        def semh(name):
            return getattr(e, name).h

        NPAIR = 8
        exp_jobs = []
        for q5 in range(NTS):
            qsl = slice(q5 * TT, (q5 + 1) * TT)
            for h in range(NH):
                hc = h // 2
                hb = (h % 2) * 64
                kp0 = kpg
                g0, g1 = st_extra

                @block.tensor
                def _(tt_, hb=hb, hc=hc, qsl=qsl, kp0=kp0, g0=g0, g1=g1,
                      first=(q5 == 0 and h == 0)):
                    if first:
                        tt_.wait_ge(e.act.h, proj_act)
                        tt_.wait_ge(e.dve.h, proj_dve)
                    tt_.wait_ge(semh(g0), g1)
                    for kp in range(NPAIR):
                        kk = kp0 + kp
                        if kk >= 3:
                            tt_.wait_ge(e.exs.h, exs_base + kk - 2)
                        for j in (0, 1):
                            kc = kp * 2 + j
                            mm = tt_.matmul(pSt[kk % 3][:, j * 512:(j + 1) * 512],
                                            krope.ap()[hb:hb + 64, hc, kc * 128:(kc + 1) * 128],
                                            qrope.ap()[hb:hb + 64, hc, qsl],
                                            start=True, stop=True)
                        e.pes.inc(mm, 1)
                        st_t[kp0 + kp] = e.pes.n
                st_last = e.pes.n
                for kp in range(NPAIR):
                    pv_t[kp0 + kp] = st_last + kp + 1

                exp_jobs.append((h, kp0))
                if h == NH - 1:
                    @block.scalar
                    def _(sc, jobs=tuple(exp_jobs)):
                        sc.wait_ge(e.dve.h, e.nbc_ready)
                        for (hh, hp0) in jobs:
                            for kp in range(NPAIR):
                                kk = hp0 + kp
                                sc.wait_ge(e.pes.h, st_t[kk])
                                if kk >= 6:
                                    sc.wait_ge(e.pes.h, pv_t[kk - 6])
                                sc.activation(psb[kk % 6][:], pSt[kk % 3][:],
                                              AF.Exp, bias=nbc[:, hh:hh + 1],
                                              scale=SC_IN).then_inc(e.exs.h, 1)
                                e.exs.n += 1
                    exp_jobs = []

                g0, g1 = pv_free

                @block.tensor
                def _(tt_, h=h, kp0=kp0, g0=g0, g1=g1):
                    for kp in range(NPAIR):
                        kk = kp0 + kp
                        tt_.wait_ge(e.exs.h, exs_base + kk + 1)
                        if kp == 0:
                            tt_.wait_ge(semh(g0), g1)
                        mm = tt_.matmul(pPV[:], vtm.ap()[:, kp * 2, h, :],
                                        psb[kk % 6][:, 0:512],
                                        start=(kp == 0), stop=False)
                        mm = tt_.matmul(pPV[:], vtm.ap()[:, kp * 2 + 1, h, :],
                                        psb[kk % 6][:, 512:1024],
                                        start=False, stop=(kp == NPAIR - 1))
                        e.pes.inc(mm, 1)
                        assert e.pes.n == pv_t[kk]
                kpg += NPAIR
                pv_all = e.pes.n

                @block.vector
                def _(v, pv_all=pv_all, recb_prev=recb_prev):
                    v.wait_ge(e.pes.h, pv_all)
                    v.wait_ge(e.dve.h, recb_prev)
                    e.dve.inc(v.tensor_copy(recb[:], pPV[64:65, :]), 1)
                recb_done = e.dve.n
                recb_prev = recb_done
                g0, g1 = pmi_guard

                @block.tensor
                def _(tt_, recb_done=recb_done, g0=g0, g1=g1):
                    tt_.wait_ge(e.dve.h, recb_done)
                    tt_.wait_ge(semh(g0), g1)
                    e.pes.inc(tt_.matmul(pMi[0:64, :], e.onesB[0:1, 0:64], recb[:],
                                         start=True, stop=True), 1)
                rb_pe = e.pes.n

                @block.vector
                def _(v, rb_pe=rb_pe, rbs_guard=rbs_guard, h=h,
                      g0=attf_free[0], g1=attf_free[1]):
                    v.wait_ge(e.pes.h, rb_pe)
                    v.wait_ge(e.dve.h, rbs_guard)
                    with e.nc.allow_low_precision(reason="softmax recip bf16"):
                        e.dve.inc(v.reciprocal(rbs[:], pMi[0:64, :]), 1)
                    v.wait_ge(e.dve.h, e.dve.n)
                    if h == 0:
                        v.wait_ge(semh(g0), g1)
                    hc2 = h // 2
                    hb2 = (h % 2) * 64
                    e.dve.inc(v.tensor_tensor(attf.ap()[hb2:hb2 + 64, hc2, :],
                                              pPV[0:64, :], rbs[:], ALU.mult), 1)
                pmi_guard = ("dve", e.dve.n)
                pv_free = ("dve", e.dve.n)
                rbs_guard = e.dve.n

            t = t0 + q5
            sl = slice(t * TT, (t + 1) * TT)
            attf_all = e.dve.n
            ldx_t = e.ldx2.n + 16
            g0, g1 = xs_guard

            @block.sync
            def _(sy, t=t, g0=g0, g1=g1):
                sy.wait_ge(semh(g0), g1)
                e.ldx2.inc(sy.dma_start(out=axs[:], in_=e.dtile(x_in, t)), 16)

            gm0, gm1 = pmi_guard
            gu0 = [umb_guard[0], umb_guard[1]]
            wo_pe = {}

            @block.tensor
            def _(tt_, attf_all=attf_all, sl=sl, gm0=gm0, gm1=gm1):
                tt_.wait_ge(e.dve.h, attf_all)
                tt_.wait_ge(e.ldw.h, outw_done)
                tt_.wait_ge(e.row.h, row_ready)
                tt_.wait_ge(semh(gm0), gm1)
                e.pes.inc(tt_.matmul(pMi[:], e.onesB[:], e.r_a[:, sl],
                                     start=True, stop=True), 1)
            wo_used = {}
            sq_done = {}
            dma_done = {}
            for dc in range(DC):
                @block.tensor
                def _(tt_, dc=dc, sl=sl):
                    if dc >= 2:
                        tt_.wait_ge(e.dve.h, wo_used[dc - 2])
                    for hv in range(8):
                        tt_.matmul(pSt[dc % 2][:, 0:512],
                                   outw[:, hv, dc * 128:(dc + 1) * 128],
                                   attf.ap()[:, hv, :],
                                   start=(hv == 0), stop=False)
                    mm = tt_.matmul(pSt[dc % 2][:, 0:512], e.onesB[:],
                                    e.r_b[:, sl], start=False, stop=True)
                    e.pes.inc(mm, 1)
                wo_pe[dc] = e.pes.n

                @block.vector
                def _(v, dc=dc, ldx_t=ldx_t, gu0=gu0):
                    v.wait_ge(e.ldx2.h, ldx_t)
                    v.wait_ge(e.pes.h, wo_pe[dc])
                    if dc == 0:
                        for gg0, gg1 in gu0:
                            v.wait_ge(semh(gg0), gg1)
                    if dc >= 2:
                        v.wait_ge(e.stw.h, dma_done[dc - 2])
                        v.wait_ge(e.sq.h, sq_done[dc - 2])
                    e.dve.inc(v.tensor_tensor(umb2[dc % 2][:], axs[:, dc, :],
                                              pMi[:], ALU.mult), 1)
                    v.wait_ge(e.dve.h, e.dve.n)
                    e.dve.inc(v.scalar_tensor_tensor(
                        umb2[dc % 2][:], pSt[dc % 2][:, 0:512], e.outb[:, dc:dc + 1],
                        umb2[dc % 2][:], ALU.add, ALU.add), 1)
                    wo_used[dc] = e.dve.n
                um_d = e.dve.n

                @block.sync
                def _(sy, dc=dc, t=t, um_d=um_d):
                    sy.wait_ge(e.dve.h, um_d)
                    e.stw.inc(sy.dma_start(
                        out=e.dtile(e.us, t)[:, dc:dc + 1, :],
                        in_=umb2[dc % 2][:]), 16)
                dma_done[dc] = e.stw.n

                @block.gpsimd
                def _(g, dc=dc, um_d=um_d):
                    g.wait_ge(e.dve.h, um_d)
                    if dc >= 2:
                        g.wait_ge(e.pes.h, sqm_pe[dc - 2])
                    e.sq.inc(g.tensor_tensor(sqb[dc % 2][:], umb2[dc % 2][:],
                                             umb2[dc % 2][:], ALU.mult), 1)
                sq_done[dc] = e.sq.n

                @block.tensor
                def _(tt_, dc=dc, gpv=pv_free):
                    tt_.wait_ge(e.sq.h, sq_done[dc])
                    if dc == 0:
                        tt_.wait_ge(semh(gpv[0]), gpv[1])
                    mm = tt_.matmul(pPV[0:1, 0:TT], e.oneC[:], sqb[dc % 2][:],
                                    start=(dc == 0), stop=(dc == DC - 1))
                    e.pes.inc(mm, 1)
                sqm_pe[dc] = e.pes.n
            um_done = e.dve.n
            attf_free = ("pes", wo_pe[DC - 1])
            xs_guard = ("dve", um_done)
            st_extra = ("dve", um_done)
            pmi_guard = ("dve", um_done)
            umb_guard = [("stw", dma_done[DC - 1]), ("sq", sq_done[DC - 1])]
            msq_pe = e.pes.n

            @block.vector
            def _(v, t=t, msq_pe=msq_pe):
                v.wait_ge(e.pes.h, msq_pe)
                e.row.inc(v.tensor_copy(e.r_ms[:, t * TT:(t + 1) * TT],
                                        pPV[0:1, 0:TT]), 1)
            pv_free = ("row", e.row.n)
        ST.pop_all().close()


def _emit_u2_gate_base(block, e, x_in, x_out):
    _barrier(block, e)
    nc = e.nc
    row_r = e.row.n
    with ExitStack() as ST:
        en = ST.enter_context
        uss = [en(nc.sbuf_tensor(_un(f"g_uss{i}"), [128, DC, TT], F32)) for i in range(2)]
        xss = [en(nc.sbuf_tensor(_un(f"g_xss{i}"), [128, DC, TT], F32)) for i in range(2)]
        u2b = [en(nc.sbuf_tensor(_un(f"g_u2b{i}"), [128, DC, TT], BF16)) for i in range(2)]
        x1b = [en(nc.sbuf_tensor(_un(f"g_x1b{i}"), [128, DC, TT], F32)) for i in range(2)]
        lgt = en(nc.sbuf_tensor(_un("g_lgt"), [128, 8], F32))
        top = en(nc.sbuf_tensor(_un("g_top"), [128, 8], F32))
        gw = en(nc.sbuf_tensor(_un("g_gw"), [128, 6], F32))
        msk = en(nc.sbuf_tensor(_un("g_msk"), [128, 4, 8], F32))
        logit = en(nc.sbuf_tensor(_un("g_logit"), [8, NTOK], F32))
        pR2 = [en(nc.psum_tensor(_un(f"g_pR2{i}"), [128, TT], F32)) for i in range(2)]
        plg = [en(nc.psum_tensor(_un(f"g_plg{i}"), [8, TT], F32)) for i in range(2)]
        ptr = en(nc.psum_tensor(_un("g_ptr"), [128, 8], F32))
        prow = en(nc.psum_tensor(_un("g_prow"), [1, 128], F32))
        b_ldx = e.ldx.n
        b_ldx2 = e.ldx2.n
        tile_done = {}
        lg_evac = {}
        for t in range(NT):
            @block.sync
            def _(sy, t=t):
                if t >= 2:
                    sy.wait_ge(e.dve.h, tile_done[t - 2])
                e.ldx.inc(sy.dma_start(out=uss[t % 2][:], in_=e.dtile(e.us, t)), 16)
                e.ldx2.inc(sy.dma_start(out=xss[t % 2][:], in_=e.dtile(x_in, t)), 16)

            @block.tensor
            def _(tt_, t=t, row_r=row_r):
                tt_.wait_ge(e.row.h, row_r)
                if t >= 2:
                    tt_.wait_ge(e.dve.h, tile_done[t - 2])
                e.pes.inc(tt_.matmul(pR2[t % 2][:], e.onesB[:],
                                     e.r_A[:, t * TT:(t + 1) * TT],
                                     start=True, stop=True), 1)
            pe_r2 = e.pes.n

            @block.vector
            def _(v, t=t, pe_r2=pe_r2):
                v.wait_ge(e.ldx.h, b_ldx + 16 * (t + 1))
                v.wait_ge(e.ldx2.h, b_ldx2 + 16 * (t + 1))
                v.wait_ge(e.pes.h, pe_r2)
                for dc in range(DC):
                    e.dve.inc(v.scalar_tensor_tensor(
                        u2b[t % 2][:, dc, :], uss[t % 2][:, dc, :],
                        e.n2c[:, dc:dc + 1], pR2[t % 2][:], ALU.mult, ALU.mult), 1)
                    e.dve.inc(v.scalar_tensor_tensor(
                        x1b[t % 2][:, dc, :], uss[t % 2][:, dc, :], 0.5,
                        xss[t % 2][:, dc, :], ALU.mult, ALU.add), 1)
                tile_done[t] = e.dve.n
            td = tile_done[t]

            @block.sync
            def _(sy, t=t, td=td):
                sy.wait_ge(e.dve.h, td)
                e.stw.inc(sy.dma_start(out=e.dtile(e.u2s, t), in_=u2b[t % 2][:]), 16)
                e.stw.inc(sy.dma_start(out=e.dtile(x_out, t), in_=x1b[t % 2][:]), 16)

            @block.tensor
            def _(tt_, t=t, td=td):
                tt_.wait_ge(e.dve.h, td)
                tt_.wait_ge(e.ldc.h, e.ldc_done)
                tt_.wait_ge(e.ldg.h, e.ldg_done)
                if t >= 2:
                    tt_.wait_ge(e.act.h, lg_evac[t - 2])
                for dc in range(DC):
                    mm = tt_.matmul(plg[t % 2][:], e.gateT[:, dc, :],
                                    u2b[t % 2][:, dc, :],
                                    start=(dc == 0), stop=(dc == DC - 1))
                e.pes.inc(mm, 1)
            lg_pe = e.pes.n

            @block.vector
            def _(v, t=t, lg_pe=lg_pe):
                v.wait_ge(e.pes.h, lg_pe)
                e.act.inc(v.tensor_copy(logit[:, t * TT:(t + 1) * TT],
                                        plg[t % 2][:]), 1)
                lg_evac[t] = e.act.n
        lg_all = e.act.n
        base_done = e.stw.n

        row_evac = e.gat.n
        for i in range(NTOK // 128):
            i128 = slice(i * 128, (i + 1) * 128)

            @block.tensor
            def _(tt_, i=i, i128=i128, lg_all=lg_all, row_evac=row_evac):
                tt_.wait_ge(e.act.h, lg_all)
                tt_.wait_ge(e.gat.h, row_evac)
                if i >= 1:
                    tt_.wait_ge(e.dve.h, comb_prev)
                e.pes.inc(tt_.transpose(ptr[:], logit[:, i128],
                                        e.identF[0:8, 0:8]), 1)
            trd = e.pes.n

            @block.vector
            def _(v, trd=trd):
                v.wait_ge(e.pes.h, trd)
                e.dve.inc(v.tensor_copy(lgt[:], ptr[:]), 1)
                v.wait_ge(e.dve.h, e.dve.n)
                e.dve.inc(v.max(out=top[:], in_=lgt[:]), 1)
                v.wait_ge(e.dve.h, e.dve.n)
                e.dve.inc(v.tensor_tensor(gw[:, 0:1], top[:, 1:2], top[:, 0:1],
                                          ALU.subtract), 1)
            gd1 = e.dve.n

            @block.scalar
            def _(sc, gd1=gd1):
                sc.wait_ge(e.dve.h, gd1)
                e.act.inc(sc.activation(gw[:, 1:2], gw[:, 0:1], AF.Exp,
                                        bias=0.0, scale=1.0), 1)
            gd2 = e.act.n

            @block.vector
            def _(v, gd2=gd2):
                v.wait_ge(e.act.h, gd2)
                e.dve.inc(v.tensor_scalar(gw[:, 2:3], gw[:, 1:2], 1.0, None,
                                          ALU.add), 1)
                v.wait_ge(e.dve.h, e.dve.n)
                e.dve.inc(v.reciprocal(gw[:, 3:4], gw[:, 2:3]), 1)
                v.wait_ge(e.dve.h, e.dve.n)
                e.dve.inc(v.tensor_scalar(gw[:, 4:5], gw[:, 3:4], 0.5, None,
                                          ALU.mult), 1)
                v.wait_ge(e.dve.h, e.dve.n)
                e.dve.inc(v.tensor_tensor(gw[:, 5:6], gw[:, 1:2], gw[:, 4:5],
                                          ALU.mult), 1)
                e.dve.inc(v.tensor_scalar(msk[:, 0, :], lgt[:], top[:, 0:1], None,
                                          ALU.is_equal), 1)
                e.dve.inc(v.tensor_scalar(msk[:, 2, :], lgt[:], top[:, 1:2], None,
                                          ALU.is_equal), 1)
                v.wait_ge(e.dve.h, e.dve.n)
                e.dve.inc(v.tensor_scalar(msk[:, 1, :], msk[:, 0, :], -1.0, 1.0,
                                          ALU.mult, ALU.add), 1)
                v.wait_ge(e.dve.h, e.dve.n)
                e.dve.inc(v.tensor_tensor(msk[:, 2, :], msk[:, 2, :],
                                          msk[:, 1, :], ALU.mult), 1)
                v.wait_ge(e.dve.h, e.dve.n)
                e.dve.inc(v.tensor_scalar(msk[:, 3, :], msk[:, 0, :], gw[:, 4:5],
                                          None, ALU.mult), 1)
                v.wait_ge(e.dve.h, e.dve.n)
                e.dve.inc(v.scalar_tensor_tensor(msk[:, 3, :], msk[:, 2, :],
                                                 gw[:, 5:6], msk[:, 3, :],
                                                 ALU.mult, ALU.add), 1)
            comb_done = e.dve.n
            comb_prev = comb_done

            for ex in range(NE):
                @block.tensor
                def _(tt_, ex=ex, comb_done=comb_done, row_evac=row_evac):
                    tt_.wait_ge(e.dve.h, comb_done)
                    tt_.wait_ge(e.gat.h, row_evac)
                    e.pes.inc(tt_.transpose(prow[:], msk.ap()[:, 3, ex:ex + 1],
                                            e.identF[:]), 1)
                tr1 = e.pes.n

                @block.vector
                def _(v, ex=ex, i=i, tr1=tr1):
                    v.wait_ge(e.pes.h, tr1)
                    e.gat.inc(v.tensor_copy(e.combR[ex][:, i * 128:(i + 1) * 128],
                                            prow[:]), 1)
                row_evac = e.gat.n
        ST.pop_all().close()
    return base_done


def _emit_moe(block, e, x_out, base_done):
    _barrier(block, e)
    nc = e.nc
    wk = e.wk
    comb_ready = e.gat.n
    with ExitStack() as ST:
        en = ST.enter_context
        w12 = [en(nc.sbuf_tensor(_un(f"m_w12{i}"), [128, DC, 2048], BF16)) for i in range(2)]
        w3s = [en(nc.sbuf_tensor(_un(f"m_w3s{i}"), [128, DC, 1024], BF16)) for i in range(2)]
        u2t = [en(nc.sbuf_tensor(_un(f"m_u2t{i}"), [128, DC, TT], BF16)) for i in range(2)]
        s1b = [en(nc.sbuf_tensor(_un(f"m_s1b{i}"), [128, TT], BF16)) for i in range(2)]
        s2b = [en(nc.sbuf_tensor(_un(f"m_s2b{i}"), [128, TT], BF16)) for i in range(2)]
        hsb = en(nc.sbuf_tensor(_un("m_hsb"), [128, DC, TT], BF16))
        ybf = [en(nc.sbuf_tensor(_un(f"m_ybf{i}"), [128, TT], F32)) for i in range(2)]
        cbs = en(nc.sbuf_tensor(_un("m_cbs"), [128, TT], BF16))
        b3u = [en(nc.sbuf_tensor(_un(f"m_b3u{i}"), [97, 1024], BF16)) for i in range(2)]
        onesB3 = en(nc.sbuf_tensor(_un("m_onesB3"), [97, 128], BF16))
        pH1 = [en(nc.psum_tensor(_un(f"m_pH1{i}"), [128, TT], F32)) for i in range(2)]
        pH2 = [en(nc.psum_tensor(_un(f"m_pH2{i}"), [128, TT], F32)) for i in range(2)]
        pY = [en(nc.psum_tensor(_un(f"m_pY{i}"), [128, TT], F32)) for i in range(2)]
        pcb = en(nc.psum_tensor(_un("m_pcb"), [128, TT], F32))

        @block.vector
        def _(v):
            e.dve.inc(v.memset(onesB3[:], 1.0), 1)
        ob3_done = e.dve.n

        w12_used = {}
        w3_used = {}
        u2_used = {}
        silu_t = {}
        s2g_t = {}
        h_used = {}
        y_acc = {}
        y_evac = {}
        hsb_used = {}
        cb_guard = ("act", e.act.n)

        def semh(name):
            return getattr(e, name).h

        ldw12_t = {}
        ldw3_t = {}

        def emit_loads(ui):
            ex_, hf_ = divmod(ui, 2)

            @block.gpsimd
            def _(g, ex=ex_, hf=hf_, ui=ui):
                if ui >= 2:
                    g.wait_ge(e.pes.h, w12_used[ui - 2])
                else:
                    g.wait_ge(e.ldc.h, e.ldc_done)
                    g.wait_ge(e.ldg.h, e.ldg_done)
                e.ldw.inc(g.dma_start(
                    out=w12[ui % 2][:, :, 0:1024],
                    in_=wk["w1T"][ex, :, hf * 1024:(hf + 1) * 1024]
                    .rearrange("(dc p) f -> p dc f", p=128)), 16)
                e.ldw.inc(g.dma_start(
                    out=w12[ui % 2][:, :, 1024:2048],
                    in_=wk["w2T"][ex, :, hf * 1024:(hf + 1) * 1024]
                    .rearrange("(dc p) f -> p dc f", p=128)), 16)
            ldw12_t[ui] = e.ldw.n

            @block.gpsimd
            def _(g, ex=ex_, hf=hf_, ui=ui):
                if ui >= 2:
                    g.wait_ge(e.pes.h, w3_used[ui - 2])
                e.ldw.inc(g.dma_start(
                    out=w3s[ui % 2][:],
                    in_=wk["w3T"][ex, hf * 1024:(hf + 1) * 1024, :]
                    .rearrange("(hc p) f -> p hc f", p=128)), 16)
                for rb in (0, 32, 64):
                    e.ldw.inc(g.dma_start(
                        out=b3u[ui % 2][rb:rb + 1, :],
                        in_=wk["b3r"][0:1, ex * 1024:(ex + 1) * 1024]), 16)
            ldw3_t[ui] = e.ldw.n

        ui = -1
        for ex in range(NE):
            for hf in range(2):
                ui += 1
                if ui == 0:
                    emit_loads(0)
                ldw12 = ldw12_t[ui]
                ldw3 = ldw3_t[ui]

                for t in range(NT):
                    ti = ui * NT + t
                    if t == 1 and ui + 1 < NE * 2:
                        emit_loads(ui + 1)

                    @block.sync
                    def _(sy, t=t, ti=ti):
                        if ti >= 2:
                            sy.wait_ge(e.pes.h, u2_used[ti - 2])
                        e.ldu.inc(sy.dma_start(out=u2t[t % 2][:],
                                               in_=e.dtile(e.u2s, t)), 16)
                    ldu_t = e.ldu.n

                    g0, g1 = cb_guard

                    @block.tensor
                    def _(tt_, ex=ex, t=t, g0=g0, g1=g1):
                        tt_.wait_ge(e.gat.h, comb_ready)
                        tt_.wait_ge(semh(g0), g1)
                        tt_.wait_ge(e.dve.h, ob3_done)
                        rb = (ex % 3) * 32
                        e.pes.inc(tt_.matmul(
                            pcb[:], onesB3[rb:rb + 1, 0:128],
                            e.combR[ex][:, t * TT:(t + 1) * TT],
                            start=True, stop=True), 1)
                    cb_pe = e.pes.n

                    @block.vector
                    def _(v, cb_pe=cb_pe):
                        v.wait_ge(e.pes.h, cb_pe)
                        e.act.inc(v.tensor_copy(cbs[:], pcb[:]), 1)
                    cbs_done = e.act.n
                    cb_guard = ("act", cbs_done)

                    exs0 = e.exs.n
                    gat0 = e.gat.n
                    pool0 = e.pool.n
                    for hcq in range(DC):
                        hci = ti * DC + hcq
                        silu_t[hci] = exs0 + hcq + 1
                        s2g_t[hci] = gat0 + hcq + 1
                        h_used[hci] = pool0 + hcq + 1
                    h_pe = {}
                    for hcq in range(DC):
                        hci = ti * DC + hcq

                        @block.tensor
                        def _(tt_, hcq=hcq, t=t, ui=ui, ldw12=ldw12,
                              ldu_t=ldu_t, hci=hci):
                            tt_.wait_ge(e.ldw.h, ldw12)
                            tt_.wait_ge(e.ldu.h, ldu_t)
                            if hci >= 2:
                                tt_.wait_ge(e.exs.h, silu_t[hci - 2])
                                tt_.wait_ge(e.gat.h, s2g_t[hci - 2])
                            for dc in range(DC):
                                mm = tt_.matmul(pH1[hcq % 2][:],
                                                w12[ui % 2][:, dc, hcq * 128:(hcq + 1) * 128],
                                                u2t[t % 2][:, dc, :],
                                                start=(dc == 0), stop=(dc == DC - 1))
                            e.pes.inc(mm, 1)
                            for dc in range(DC):
                                mm = tt_.matmul(pH2[hcq % 2][:],
                                                w12[ui % 2][:, dc, 1024 + hcq * 128:1024 + (hcq + 1) * 128],
                                                u2t[t % 2][:, dc, :],
                                                start=(dc == 0), stop=(dc == DC - 1))
                            e.pes.inc(mm, 1)
                        h_pe[hcq] = e.pes.n

                    @block.scalar
                    def _(sc, ti=ti, ex=ex, hf=hf, h_pe=h_pe):
                        for hcq in range(DC):
                            hci = ti * DC + hcq
                            sc.wait_ge(e.pes.h, h_pe[hcq] - 1)
                            if hci >= 2:
                                sc.wait_ge(e.pool.h, h_used[hci - 2])
                            sc.activation(
                                s1b[hcq % 2][:], pH1[hcq % 2][:], AF.Silu,
                                bias=e.b1c.ap()[:, ex, hf * 8 + hcq:hf * 8 + hcq + 1],
                                scale=1.0).then_inc(e.exs.h, 1)
                            e.exs.n += 1
                    assert e.exs.n == exs0 + DC

                    for hcq in range(DC):
                        hci = ti * DC + hcq

                        @block.vector
                        def _(v, hcq=hcq, ex=ex, hf=hf, hci=hci,
                              cbs_done=cbs_done, h_pe=h_pe):
                            v.wait_ge(e.pes.h, h_pe[hcq])
                            v.wait_ge(e.act.h, cbs_done)
                            if hci >= 2:
                                v.wait_ge(e.pool.h, h_used[hci - 2])
                            e.gat.inc(v.scalar_tensor_tensor(
                                s2b[hcq % 2][:], pH2[hcq % 2][:],
                                e.b2c.ap()[:, ex, hf * 8 + hcq:hf * 8 + hcq + 1],
                                cbs[:], ALU.add, ALU.mult), 1)

                        @block.gpsimd
                        def _(g, hcq=hcq, ti=ti, hci=hci):
                            g.wait_ge(e.exs.h, silu_t[hci])
                            g.wait_ge(e.gat.h, s2g_t[hci])
                            if ti >= 1 and hcq == 0:
                                g.wait_ge(e.pes.h, hsb_used.get(ti - 1, 0))
                            e.pool.inc(g.tensor_tensor(hsb[:, hcq, :],
                                                       s1b[hcq % 2][:],
                                                       s2b[hcq % 2][:], ALU.mult), 1)
                    assert e.gat.n == gat0 + DC
                    assert e.pool.n == pool0 + DC
                    h_all = e.pool.n

                    for dcq in range(DC):
                        yi = ti * DC + dcq

                        @block.tensor
                        def _(tt_, dcq=dcq, ti=ti, ui=ui, ex=ex, t=t,
                              h_all=h_all, yi=yi, ldw3=ldw3):
                            if dcq >= 1:
                                tt_.wait_ge(e.pool.h, h_all)
                            tt_.wait_ge(e.ldw.h, ldw3)
                            if yi >= 2:
                                tt_.wait_ge(e.act.h, y_evac[yi - 2])
                            for hc in range(DC):
                                if dcq == 0:
                                    tt_.wait_ge(e.pool.h, h_used[ti * DC + hc])
                                tt_.matmul(pY[dcq % 2][:],
                                           w3s[ui % 2][:, hc, dcq * 128:(dcq + 1) * 128],
                                           hsb[:, hc, :],
                                           start=(hc == 0), stop=False)
                            rb = (ex % 3) * 32
                            mm = tt_.matmul(
                                pY[dcq % 2][:],
                                b3u[ui % 2][rb:rb + 1, dcq * 128:(dcq + 1) * 128],
                                e.combR[ex][:, t * TT:(t + 1) * TT],
                                start=False, stop=True)
                            e.pes.inc(mm, 1)
                        y_pe = e.pes.n

                        @block.vector
                        def _(v, dcq=dcq, y_pe=y_pe, yi=yi):
                            v.wait_ge(e.pes.h, y_pe)
                            if yi >= 2:
                                v.wait_ge(e.acc.h, y_acc.get(yi - 2, 0))
                            e.act.inc(v.tensor_copy(ybf[dcq % 2][:],
                                                    pY[dcq % 2][:]), 1)
                            y_evac[yi] = e.act.n

                        @block.gpsimd
                        def _(g, dcq=dcq, t=t, yi=yi):
                            g.wait_ge(e.act.h, y_evac[yi])
                            g.wait_ge(e.stw.h, base_done)
                            g.wait_ge(e.acc.h, y_acc.get(yi - NT * DC, 0))
                            e.acc.inc(g.dma_start(
                                out=e.dtile(x_out, t)[:, dcq:dcq + 1, :],
                                in_=ybf[dcq % 2][:],
                                accum_op=ALU.add), 16)
                            y_acc[yi] = e.acc.n
                    hsb_used[ti] = e.pes.n
                    u2_used[ti] = e.pes.n
                w12_used[ui] = e.pes.n
                w3_used[ui] = e.pes.n
        ST.pop_all().close()


def _emit_mla(block, e, x_in, x_out):
    nc = e.nc
    with ExitStack() as SM:
        enm = SM.enter_context
        for rn in ("r_a", "r_b"):
            setattr(e, rn, enm(nc.sbuf_tensor(_un("b_") + rn, [1, NTOK], BF16))[:])
        with ExitStack() as SS:
            ens = SS.enter_context
            for rn in ("r_m", "rs1", "rs2", "rs3"):
                setattr(e, rn, ens(nc.sbuf_tensor(_un("b_") + rn, [1, NTOK], F32))[:])
            e.r_ms = ens(nc.sbuf_tensor(_un("b_r_ms"), [1, NTOK], BF16))[:]
            _emit_stats(block, e, x_in, with_mean=True)
            _emit_ln_rows(block, e, compose_rms=False)
        for s_ in range(BPC):
            for hg in range(2):
                _emit_mla_seq(block, e, x_in, x_out, s_, hg)
        _drain_phase(block, e)
        SM.pop_all().close()


def _emit_mla_seq(block, e, x_in, x_out, s_, hg):
    _barrier(block, e)
    nc = e.nc
    wk = e.wk
    t0 = s_ * NTS
    row_ready = e.row.n
    NHH = 8
    with ExitStack() as ST:
        en = ST.enter_context
        qc = en(nc.sbuf_tensor(_un("l_qc"), [128, 4, S], BF16))
        kc = en(nc.sbuf_tensor(_un("l_kc"), [128, 4, S], BF16))
        qr = en(nc.sbuf_tensor(_un("l_qr"), [128, 4, S], BF16))
        krm = en(nc.sbuf_tensor(_un("l_krm"), [64, S], BF16))
        vtm = en(nc.sbuf_tensor(_un("l_vtm"), [128, 16, 8, 65], BF16))
        nbcm = en(nc.sbuf_tensor(_un("l_nbcm"), [128, 8], F32))

        with ExitStack() as SP:
            enp = SP.enter_context
            dqw = enp(nc.sbuf_tensor(_un("l_dqw"), [128, DC, 256], BF16))
            dkw = enp(nc.sbuf_tensor(_un("l_dkw"), [128, DC, 256], BF16))
            krw = enp(nc.sbuf_tensor(_un("l_krw"), [128, DC, 32], BF16))
            qrw = enp(nc.sbuf_tensor(_un("l_qrw"), [128, 2, 256], BF16))
            cm2 = enp(nc.sbuf_tensor(_un("l_cm2"), [64, S], BF16))
            sm2 = enp(nc.sbuf_tensor(_un("l_sm2"), [64, S], BF16))
            cq = enp(nc.sbuf_tensor(_un("l_cq"), [128, 2, S], BF16))
            ckv = enp(nc.sbuf_tensor(_un("l_ckv"), [128, 2, S], BF16))
            xs = enp(nc.sbuf_tensor(_un("l_xs"), [128, DC, TT], F32))
            t2b = enp(nc.sbuf_tensor(_un("l_t2b"), [128, DC, TT], BF16))
            rt = [enp(nc.sbuf_tensor(_un(f"l_rt{i}"), [128, TT], F32)) for i in range(2)]
            krr = enp(nc.sbuf_tensor(_un("l_krr"), [32, TT], BF16))
            wuc = [enp(nc.sbuf_tensor(_un(f"l_wuc{i}"), [128, 2, 128], BF16)) for i in range(2)]
            wvv = enp(nc.sbuf_tensor(_un("l_wvv"), [128, 2, 512], BF16))
            sqb2 = [enp(nc.sbuf_tensor(_un(f"l_sqb2{i}"), [128, TT], BF16)) for i in range(2)]
            qc2s = enp(nc.sbuf_tensor(_un("l_qc2s"), [1, 32], F32))
            kc2s = enp(nc.sbuf_tensor(_un("l_kc2s"), [1, 32], F32))
            qr2s = enp(nc.sbuf_tensor(_un("l_qr2s"), [1, 32], F32))
            kr2s = enp(nc.sbuf_tensor(_un("l_kr2s"), [1, 4], F32))
            qm2m = enp(nc.sbuf_tensor(_un("l_qm2m"), [1, 8], F32))
            km2m = enp(nc.sbuf_tensor(_un("l_km2m"), [1, 8], F32))
            bpm = enp(nc.sbuf_tensor(_un("l_bpm"), [1, 8], F32))
            negbm = enp(nc.sbuf_tensor(_un("l_negbm"), [1, 8], F32))
            kr2m = enp(nc.sbuf_tensor(_un("l_kr2m"), [1, 1], F32))
            pAB = enp(nc.psum_tensor(_un("l_pAB"), [128, TT], F32))
            pBB = enp(nc.psum_tensor(_un("l_pBB"), [128, TT], F32))
            pcq = [enp(nc.psum_tensor(_un(f"l_pcq{i}"), [128, TT], F32)) for i in range(2)]
            pkr = enp(nc.psum_tensor(_un("l_pkr"), [32, TT], F32))
            par = enp(nc.psum_tensor(_un("l_par"), [64, TT], F32))
            pqr = enp(nc.psum_tensor(_un("l_pqr"), [128, TT], F32))
            pvm = enp(nc.psum_tensor(_un("l_pvm"), [128, TT], F32))

            @block.gpsimd
            def _(g, hg=hg):
                g.wait_ge(e.ldc.h, e.ldc_done)
                g.wait_ge(e.ldg.h, e.ldg_done)
                for src, dst in [
                    (wk["m_W_DQ"].rearrange("(p2 p) f -> p p2 f", p=128), dqw[:]),
                    (wk["m_W_DKV"].rearrange("(p2 p) f -> p p2 f", p=128), dkw[:]),
                    (wk["m_W_KR"].rearrange("(p2 p) f -> p p2 f", p=128), krw[:]),
                    (wk["m_W_QR"][:, hg * 256:(hg + 1) * 256]
                     .rearrange("(p2 p) f -> p p2 f", p=128), qrw[:]),
                    (wk["cm2"], cm2[:]),
                    (wk["sm2"], sm2[:]),
                    (wk["m_W_UV"][:, hg * 512:(hg + 1) * 512]
                     .rearrange("(p2 p) f -> p p2 f", p=128), wvv[:]),
                ]:
                    e.ldw.inc(g.dma_start(out=dst, in_=src), 16)
            ldw_done = e.ldw.n

            @block.vector
            def _(v):
                v.wait_ge(e.ldc.h, e.ldc_done)
                v.wait_ge(e.ldg.h, e.ldg_done)
                e.dve.inc(v.memset(vtm.ap()[:, :, :, 64:65], 1.0), 1)
            vones = e.dve.n

            xs_free = {}
            t2_done = {}
            t2_used = {}
            cq_evac = {}
            kr_rope_prev = 0
            for ts in range(NTS):
                t = t0 + ts
                sl = slice(t * TT, (t + 1) * TT)
                ssl = slice(ts * TT, (ts + 1) * TT)
                ldx_t = e.ldx.n + 16

                @block.sync
                def _(sy, ts=ts, t=t):
                    if ts >= 1:
                        sy.wait_ge(e.dve.h, xs_free[ts - 1])
                    e.ldx.inc(sy.dma_start(out=xs[:], in_=e.dtile(x_in, t)), 16)

                @block.tensor
                def _(tt_, ts=ts, sl=sl):
                    tt_.wait_ge(e.row.h, row_ready)
                    if ts >= 1:
                        tt_.wait_ge(e.dve.h, t2_done[ts - 1])
                    tt_.matmul(pAB[:], e.onesB[:], e.r_a[:, sl], start=True, stop=True)
                    e.pes.inc(tt_.matmul(pBB[:], e.onesB[:], e.r_b[:, sl],
                                         start=True, stop=True), 1)
                ab_pe = e.pes.n

                @block.vector
                def _(v, ts=ts, ldx_t=ldx_t, ab_pe=ab_pe):
                    v.wait_ge(e.ldx.h, ldx_t)
                    v.wait_ge(e.pes.h, ab_pe)
                    if ts >= 2:
                        v.wait_ge(e.pes.h, t2_used[ts - 2])
                    for dc in range(DC):
                        e.dve.inc(v.tensor_tensor(rt[dc % 2][:], xs[:, dc, :],
                                                  pAB[:], ALU.mult), 1)
                        v.wait_ge(e.dve.h, e.dve.n)
                        e.dve.inc(v.tensor_tensor(t2b[:, dc, :], rt[dc % 2][:],
                                                  pBB[:], ALU.add), 1)
                xs_free[ts] = e.dve.n
                t2_done[ts] = e.dve.n
                t2d = t2_done[ts]

                for j in range(5):
                    aj = ts * 5 + j

                    @block.tensor
                    def _(tt_, j=j, ts=ts, t2d=t2d, aj=aj,
                          kr_rope_prev=kr_rope_prev):
                        tt_.wait_ge(e.dve.h, t2d)
                        tt_.wait_ge(e.ldw.h, ldw_done)
                        if j < 4 and aj >= 2:
                            tt_.wait_ge(e.act.h, cq_evac.get(aj - 2, 0))
                        if j == 4 and ts >= 1:
                            tt_.wait_ge(e.act.h, cq_evac[(ts - 1) * 5 + 4])
                            tt_.wait_ge(e.dve.h, kr_rope_prev)
                        if j < 2:
                            w, lc = dqw, j
                        elif j < 4:
                            w, lc = dkw, j - 2
                        if j < 4:
                            for dc in range(DC):
                                mm = tt_.matmul(pcq[j % 2][:],
                                                w[:, dc, lc * 128:(lc + 1) * 128],
                                                t2b[:, dc, :],
                                                start=(dc == 0), stop=(dc == DC - 1))
                        else:
                            for dc in range(DC):
                                mm = tt_.matmul(pkr[:], krw[:, dc, :],
                                                t2b[:, dc, :],
                                                start=(dc == 0), stop=(dc == DC - 1))
                        e.pes.inc(mm, 1)
                    j_pe = e.pes.n

                    @block.vector
                    def _(v, j=j, ts=ts, ssl=ssl, j_pe=j_pe, aj=aj,
                          kr_rope_prev=kr_rope_prev):
                        v.wait_ge(e.pes.h, j_pe)
                        if j == 4 and ts >= 1:
                            v.wait_ge(e.dve.h, kr_rope_prev)
                        if j < 2:
                            dst = cq[:, j, ssl]
                            srcp = pcq[j % 2][:]
                        elif j < 4:
                            dst = ckv[:, j - 2, ssl]
                            srcp = pcq[j % 2][:]
                        else:
                            dst = krr[:]
                            srcp = pkr[:]
                        e.act.inc(v.tensor_copy(dst, srcp), 1)
                        cq_evac[aj] = e.act.n
                t2_used[ts] = e.pes.n
                krr_done = cq_evac[ts * 5 + 4]

                @block.tensor
                def _(tt_, krr_done=krr_done, kr_rope_prev=kr_rope_prev):
                    tt_.wait_ge(e.act.h, krr_done)
                    tt_.wait_ge(e.dve.h, kr_rope_prev)
                    e.pes.inc(tt_.matmul(par[0:32, :], e.a1T[:], krr[:],
                                         start=True, stop=True), 1)
                akr_pe = e.pes.n

                @block.vector
                def _(v, ssl=ssl, akr_pe=akr_pe, krr_done=krr_done):
                    v.wait_ge(e.pes.h, akr_pe)
                    v.wait_ge(e.act.h, krr_done)
                    e.dve.inc(v.tensor_tensor(rt[0][0:32, :], krr[:],
                                              cm2[0:32, ssl], ALU.mult), 1)
                    e.dve.inc(v.tensor_tensor(rt[1][0:32, :], par[0:32, :],
                                              sm2[0:32, ssl], ALU.mult), 1)
                    v.wait_ge(e.dve.h, e.dve.n)
                    e.dve.inc(v.tensor_tensor(krm[0:32, ssl], rt[0][0:32, :],
                                              rt[1][0:32, :], ALU.add), 1)
                    e.dve.inc(v.tensor_tensor(krm[32:64, ssl], rt[0][0:32, :],
                                              rt[1][0:32, :], ALU.add), 1)
                kr_rope_prev = e.dve.n

            cq_all = e.act.n
            kr_all = e.dve.n

            nrm_pe2 = {}
            nrm_red2 = {}
            pend = []

            def norm_chunk(ni, src, P, halves, guards):
                @block.gpsimd
                def _(g, ni=ni, src=src, P=P, guards=guards):
                    for (sh, sv) in guards:
                        g.wait_ge(sh, sv)
                    if ni >= 2:
                        g.wait_ge(e.pes.h, nrm_pe2[ni - 2])
                    e.pool.inc(g.tensor_tensor(sqb2[ni % 2][0:P, :], src, src,
                                               ALU.mult), 1)
                sq_t = e.pool.n

                def emit(ni=ni, halves=halves, sq_t=sq_t):
                    p = pAB if ni % 2 == 0 else pBB

                    @block.tensor
                    def _(tt_):
                        tt_.wait_ge(e.pool.h, sq_t)
                        if ni >= 2:
                            tt_.wait_ge(e.dve.h, nrm_red2[ni - 2])
                        for (b, kext, arr, slot) in halves:
                            mm = tt_.matmul(p[b:b + 1, :],
                                            e.onesC1[b:b + kext, 0:1],
                                            sqb2[ni % 2][b:b + kext, :],
                                            start=True, stop=True)
                        e.pes.inc(mm, 1)
                    nrm_pe2[ni] = e.pes.n

                    @block.vector
                    def _(v):
                        v.wait_ge(e.pes.h, nrm_pe2[ni])
                        for (b, kext, arr, slot) in halves:
                            e.dve.inc(v.reduce_max(arr[0:1, slot:slot + 1],
                                                   p[b:b + 1, :], axis=AX), 1)
                    nrm_red2[ni] = e.dve.n
                pend.append(emit)
                if len(pend) > 1:
                    pend.pop(0)()

            qc_evac = {}
            v_evac = {}
            wuc_used = {}
            qr_rope = kr_all
            for j in range(8):
                @block.gpsimd
                def _(g, j=j, hg=hg):
                    if j >= 2:
                        g.wait_ge(e.pes.h, wuc_used[j - 2])
                    src = wk["m_W_UQ"] if j < 4 else wk["m_W_UK"]
                    hcj = hg * 4 + (j % 4)
                    e.ldw.inc(g.dma_start(
                        out=wuc[j % 2][:],
                        in_=src[:, hcj * 128:(hcj + 1) * 128]
                        .rearrange("(lc p) f -> p lc f", p=128)), 16)
                wl = e.ldw.n
                for ts in range(NTS):
                    bj = j * 4 + ts
                    ssl = slice(ts * TT, (ts + 1) * TT)

                    @block.tensor
                    def _(tt_, j=j, ts=ts, ssl=ssl, cq_all=cq_all, bj=bj, wl=wl):
                        tt_.wait_ge(e.act.h, cq_all)
                        tt_.wait_ge(e.ldw.h, wl)
                        if bj >= 2:
                            tt_.wait_ge(e.act.h, qc_evac[bj - 2])
                        src = cq if j < 4 else ckv
                        for lc in range(2):
                            mm = tt_.matmul(pcq[bj % 2][:],
                                            wuc[j % 2][:, lc, :],
                                            src[:, lc, ssl],
                                            start=(lc == 0), stop=(lc == 1))
                        e.pes.inc(mm, 1)
                        if ts == NTS - 1:
                            wuc_used[j] = e.pes.n
                    j_pe = e.pes.n

                    @block.vector
                    def _(v, j=j, ssl=ssl, j_pe=j_pe, bj=bj):
                        v.wait_ge(e.pes.h, j_pe)
                        dst = qc if j < 4 else kc
                        e.act.inc(v.tensor_copy(dst[:, j % 4, ssl], pcq[bj % 2][:]), 1)
                        qc_evac[bj] = e.act.n

                    arr = qc2s if j < 4 else kc2s
                    h0 = (j % 4) * 2
                    norm_chunk(bj, (qc if j < 4 else kc)[:, j % 4, ssl], 128,
                               [(0, 64, arr, h0 * 4 + ts),
                                (64, 64, arr, (h0 + 1) * 4 + ts)],
                               [(e.act.h, qc_evac[bj])])

            for ts in range(NTS):
                ssl = slice(ts * TT, (ts + 1) * TT)
                for pc in range(2):
                    apc = ts * 2 + pc

                    @block.tensor
                    def _(tt_, pc=pc, ssl=ssl, cq_all=cq_all, apc=apc):
                        tt_.wait_ge(e.act.h, cq_all)
                        tt_.wait_ge(e.dve.h, qr_rope)
                        for lc in range(2):
                            mm = tt_.matmul(pqr[:], qrw[:, lc, pc * 128:(pc + 1) * 128],
                                            cq[:, lc, ssl], start=(lc == 0), stop=(lc == 1))
                        e.pes.inc(mm, 1)
                    qr_pe = e.pes.n

                    @block.vector
                    def _(v, pc=pc, ssl=ssl, qr_pe=qr_pe):
                        v.wait_ge(e.pes.h, qr_pe)
                        v.tensor_copy(qr[0:64, 2 * pc, ssl], pqr[0:64, :])
                        v.wait_ge(e.dve.h, e.dve.n)
                        e.act.inc(v.tensor_copy(qr[0:64, 2 * pc + 1, ssl],
                                                pqr[64:128, :]), 1)
                    qre = e.act.n
                    for hh in (2 * pc, 2 * pc + 1):
                        @block.tensor
                        def _(tt_, hh=hh, ssl=ssl, qre=qre):
                            tt_.wait_ge(e.act.h, qre)
                            tt_.wait_ge(e.dve.h, qr_rope)
                            e.pes.inc(tt_.matmul(par[:], e.a2T[:],
                                                 qr.ap()[0:64, hh, ssl],
                                                 start=True, stop=True), 1)
                        ar_pe = e.pes.n

                        @block.vector
                        def _(v, hh=hh, ssl=ssl, ar_pe=ar_pe):
                            v.wait_ge(e.pes.h, ar_pe)
                            e.dve.inc(v.tensor_tensor(rt[0][0:64, :],
                                                      qr.ap()[0:64, hh, ssl],
                                                      cm2[:, ssl], ALU.mult), 1)
                            e.dve.inc(v.tensor_tensor(rt[1][0:64, :], par[:],
                                                      sm2[:, ssl], ALU.mult), 1)
                            v.wait_ge(e.dve.h, e.dve.n)
                            e.dve.inc(v.tensor_tensor(qr.ap()[0:64, hh, ssl],
                                                      rt[0][0:64, :],
                                                      rt[1][0:64, :], ALU.add), 1)
                        qr_rope = e.dve.n
                        norm_chunk(32 + ts * 4 + hh, qr.ap()[0:64, hh, ssl], 64,
                                   [(0, 32, qr2s, (2 * hh) * 4 + ts),
                                    (32, 32, qr2s, (2 * hh + 1) * 4 + ts)],
                                   [(e.dve.h, qr_rope)])
                for st_ in range(4):
                    ai = ts * 4 + st_

                    @block.tensor
                    def _(tt_, st_=st_, ts=ts, ai=ai, cq_all=cq_all,
                          ldw_done=ldw_done):
                        tt_.wait_ge(e.act.h, cq_all)
                        tt_.wait_ge(e.ldw.h, ldw_done)
                        if ai >= 1:
                            tt_.wait_ge(e.act.h, v_evac[ai - 1])
                        for lc in range(2):
                            mm = tt_.matmul(
                                pvm[:],
                                ckv[:, lc, ts * TT + st_ * 128:ts * TT + (st_ + 1) * 128],
                                wvv[:, lc, :],
                                start=(lc == 0), stop=(lc == 1))
                        e.pes.inc(mm, 1)
                    v_pe = e.pes.n

                    @block.vector
                    def _(v, st_=st_, ts=ts, ai=ai, v_pe=v_pe, vones=vones):
                        v.wait_ge(e.pes.h, v_pe)
                        v.wait_ge(e.dve.h, vones)
                        kt = ts * 4 + st_
                        e.act.inc(v.tensor_copy(
                            vtm.ap()[:, kt, 0:8, 0:64], pvm[:]), 1)
                        v_evac[ai] = e.act.n

            for ts in range(NTS):
                norm_chunk(48 + ts, krm[0:32, ts * TT:(ts + 1) * TT], 32,
                           [(0, 32, kr2s, ts)], [(e.dve.h, kr_all)])
            while pend:
                pend.pop(0)()
            nrm_all = e.dve.n

            @block.vector
            def _(v, nrm_all=nrm_all):
                v.wait_ge(e.dve.h, nrm_all)
                for lh in range(8):
                    e.dve.inc(v.reduce_max(qm2m[0:1, lh:lh + 1],
                                           qc2s[0:1, lh * 4:lh * 4 + 4], axis=AX), 1)
                    e.dve.inc(v.reduce_max(km2m[0:1, lh:lh + 1],
                                           kc2s[0:1, lh * 4:lh * 4 + 4], axis=AX), 1)
                    e.dve.inc(v.reduce_max(bpm[0:1, lh:lh + 1],
                                           qr2s[0:1, lh * 4:lh * 4 + 4], axis=AX), 1)
                e.dve.inc(v.reduce_max(kr2m[:], kr2s[:], axis=AX), 1)
                v.wait_ge(e.dve.h, e.dve.n)
                e.dve.inc(v.tensor_tensor(qm2m[:], qm2m[:], bpm[:], ALU.add), 1)
                e.dve.inc(v.tensor_scalar(km2m[:], km2m[:], kr2m[0:1, 0:1],
                                          None, ALU.add), 1)
                v.wait_ge(e.dve.h, e.dve.n)
                e.dve.inc(v.tensor_tensor(bpm[:], qm2m[:], km2m[:], ALU.mult), 1)
            bqk_t = e.dve.n

            @block.scalar
            def _(sc, bqk_t=bqk_t):
                sc.wait_ge(e.dve.h, bqk_t)
                e.sq.inc(sc.activation(bpm[:], bpm[:], AF.Sqrt,
                                       bias=0.0, scale=1.0), 1)
            sqrt_t = e.sq.n

            @block.vector
            def _(v, sqrt_t=sqrt_t):
                v.wait_ge(e.sq.h, sqrt_t)
                e.dve.inc(v.tensor_scalar(negbm[:], bpm[:], 80.0, -SC_MLA,
                                          ALU.min, ALU.mult), 1)
            nbm_t = e.dve.n
            act_now = e.act.n

            @block.tensor
            def _(tt_, nbm_t=nbm_t, act_now=act_now):
                tt_.wait_ge(e.dve.h, nbm_t)
                tt_.wait_ge(e.act.h, act_now)
                for lh in range(8):
                    mm = tt_.matmul(pqr[:, lh:lh + 1], e.onesF[:],
                                    negbm[0:1, lh:lh + 1], start=True, stop=True)
                e.pes.inc(mm, 1)
            nbp_t = e.pes.n

            @block.vector
            def _(v, nbp_t=nbp_t):
                v.wait_ge(e.pes.h, nbp_t)
                e.dve.inc(v.tensor_copy(nbcm[:], pqr[:, 0:8]), 1)
            e.nbcm_ready = e.dve.n

            proj_pe = e.pes.n
            proj_act = e.act.n
            proj_dve = e.dve.n
            SP.pop_all().close()

        _emit_mla_attention(block, e, x_in, x_out, s_, hg, qc, kc, qr, krm, vtm,
                            nbcm, proj_pe, proj_act, proj_dve)
        ST.pop_all().close()


def _emit_mla_attention(block, e, x_in, x_out, s_, hg, qc, kc, qr, krm, vtm,
                        nbcm, proj_pe, proj_act, proj_dve):
    _barrier(block, e)
    nc = e.nc
    wk = e.wk
    t0 = s_ * NTS
    NHH = 8
    with ExitStack() as ST:
        en = ST.enter_context
        wow = en(nc.sbuf_tensor(_un("l_wow"), [128, 4, D], BF16))
        mask4 = en(nc.sbuf_tensor(_un("l_mask4"), [128, 4, 512], BF16))
        attf = en(nc.sbuf_tensor(_un("l_attf"), [128, 4, TT], BF16))
        psb = [en(nc.sbuf_tensor(_un(f"l_psb{i}"), [128, TT], BF16)) for i in range(4)]
        recb = en(nc.sbuf_tensor(_un("l_recb"), [1, TT], BF16))
        rbs = en(nc.sbuf_tensor(_un("l_rbs"), [64, TT], BF16))
        axs = en(nc.sbuf_tensor(_un("l_axs"), [128, DC, TT], F32))
        x2b = en(nc.sbuf_tensor(_un("l_x2b"), [128, DC, TT], F32))
        pSt = [en(nc.psum_tensor(_un(f"b_pSt{i}"), [128, TT], F32)) for i in range(4)]
        pPV = en(nc.psum_tensor(_un("b_pPV"), [65, TT], F32))
        pMi = en(nc.psum_tensor(_un("b_pMi"), [128, TT], F32))

        @block.gpsimd
        def _(g, hg=hg):
            e.ldw.inc(g.dma_start(
                out=wow[:],
                in_=wk["m_W_O"][hg * 512:(hg + 1) * 512]
                .rearrange("(hc p) f -> p hc f", p=128)), 16)
            e.ldw.inc(g.dma_start(
                out=mask4.ap().rearrange("p r f -> p (r f)"), in_=wk["mask4"]), 16)
        wow_done = e.ldw.n

        pv_free = ("dve", proj_dve)
        pmi_guard = ("act", proj_act)
        attf_free = ("pes", proj_pe)
        xs_guard = ("dve", proj_dve)
        st_guard = ("dve", proj_dve)
        x2_guard = ("stw", e.stw.n) if hg == 0 else ("acc", e.acc.n)
        base_stw = e.stw.n
        recb_prev = proj_dve
        rbs_guard = proj_dve
        st_t = {}
        clip_t = {}
        pv_t = {}
        kabs = 0
        exp_jobs = []
        exs_base = e.exs.n

        def semh(name):
            return getattr(e, name).h

        for q5 in range(NTS):
            qsl = slice(q5 * TT, (q5 + 1) * TT)
            nkb = 4 * (q5 + 1)
            for lh in range(NHH):
                hc = lh // 2
                hb = (lh % 2) * 64
                hr = lh // 2
                hrb = (lh % 2) * 32
                ka0 = kabs

                g0, g1 = st_guard
                gp0, gp1 = pv_free

                @block.tensor
                def _(tt_, lh=lh, nkb=nkb, ka0=ka0, hb=hb, hc=hc, hr=hr,
                      hrb=hrb, qsl=qsl, g0=g0, g1=g1, gp0=gp0, gp1=gp1,
                      first=(q5 == 0 and lh == 0)):
                    if first:
                        tt_.wait_ge(e.act.h, proj_act)
                    tt_.wait_ge(semh(g0), g1)

                    def pv(i):
                        ka = ka0 + i
                        tt_.wait_ge(e.exs.h, exs_base + ka + 1)
                        if i == 0:
                            tt_.wait_ge(semh(gp0), gp1)
                        mm = tt_.matmul(pPV[:], vtm.ap()[:, i, lh, :],
                                        psb[ka % 4][:],
                                        start=(i == 0), stop=(i == nkb - 1))
                        e.pes.inc(mm, 1)
                        pv_t[ka] = e.pes.n

                    for i in range(nkb):
                        ka = ka0 + i
                        kt = slice(i * 128, (i + 1) * 128)
                        if ka >= 4:
                            tt_.wait_ge(e.exs.h, exs_base + ka - 3)
                        tt_.matmul(pSt[ka % 4][:], kc.ap()[hb:hb + 64, hc, kt],
                                   qc.ap()[hb:hb + 64, hc, qsl],
                                   start=True, stop=False)
                        mm = tt_.matmul(pSt[ka % 4][:], krm[hrb:hrb + 32, kt],
                                        qr.ap()[hrb:hrb + 32, hr, qsl],
                                        start=False, stop=True)
                        e.pes.inc(mm, 1)
                        st_t[ka] = e.pes.n
                        if i >= 2:
                            pv(i - 2)
                    pv(nkb - 2)
                    pv(nkb - 1)
                pv_all = e.pes.n

                @block.vector
                def _(v, nkb=nkb, ka0=ka0, q5=q5):
                    v.wait_ge(e.ldw.h, wow_done)
                    for i in range(nkb):
                        ka = ka0 + i
                        v.wait_ge(e.pes.h, st_t[ka])
                        e.dve.inc(v.tensor_scalar(pSt[ka % 4][:], pSt[ka % 4][:],
                                                  80.0, -80.0, ALU.min, ALU.max), 1)
                        r = i - 4 * q5
                        if 0 <= r < 4:
                            v.wait_ge(e.dve.h, e.dve.n)
                            e.dve.inc(v.tensor_tensor(pSt[ka % 4][:],
                                                      pSt[ka % 4][:],
                                                      mask4[:, r, :], ALU.add), 1)
                        clip_t[ka] = e.dve.n

                exp_jobs.append((lh, ka0, nkb))
                if lh == NHH - 1:
                    @block.scalar
                    def _(sc, jobs=tuple(exp_jobs)):
                        sc.wait_ge(e.dve.h, e.nbcm_ready)
                        for (jlh, jka0, jnkb) in jobs:
                            for i in range(jnkb):
                                ka = jka0 + i
                                sc.wait_ge(e.dve.h, clip_t[ka])
                                if ka >= 4:
                                    sc.wait_ge(e.pes.h, pv_t[ka - 4])
                                sc.activation(psb[ka % 4][:], pSt[ka % 4][:],
                                              AF.Exp, bias=nbcm[:, jlh:jlh + 1],
                                              scale=SC_MLA).then_inc(e.exs.h, 1)
                                e.exs.n += 1
                    exp_jobs = []
                kabs += nkb

                @block.vector
                def _(v, pv_all=pv_all, recb_prev=recb_prev):
                    v.wait_ge(e.pes.h, pv_all)
                    v.wait_ge(e.dve.h, recb_prev)
                    e.dve.inc(v.tensor_copy(recb[:], pPV[64:65, :]), 1)
                recb_done = e.dve.n
                recb_prev = recb_done
                g0, g1 = pmi_guard

                @block.tensor
                def _(tt_, recb_done=recb_done, g0=g0, g1=g1):
                    tt_.wait_ge(e.dve.h, recb_done)
                    tt_.wait_ge(semh(g0), g1)
                    e.pes.inc(tt_.matmul(pMi[0:64, :], e.onesB[0:1, 0:64], recb[:],
                                         start=True, stop=True), 1)
                rb_pe = e.pes.n
                g0, g1 = attf_free

                @block.vector
                def _(v, lh=lh, rb_pe=rb_pe, rbs_guard=rbs_guard, g0=g0, g1=g1):
                    v.wait_ge(e.pes.h, rb_pe)
                    v.wait_ge(e.dve.h, rbs_guard)
                    with e.nc.allow_low_precision(reason="softmax recip bf16"):
                        e.dve.inc(v.reciprocal(rbs[:], pMi[0:64, :]), 1)
                    v.wait_ge(e.dve.h, e.dve.n)
                    if lh == 0:
                        v.wait_ge(semh(g0), g1)
                    hc2 = lh // 2
                    hb2 = (lh % 2) * 64
                    e.dve.inc(v.tensor_tensor(attf.ap()[hb2:hb2 + 64, hc2, :],
                                              pPV[0:64, :], rbs[:], ALU.mult), 1)
                pmi_guard = ("dve", e.dve.n)
                pv_free = ("dve", e.dve.n)
                rbs_guard = e.dve.n

            t = t0 + q5
            attf_all = e.dve.n
            g0, g1 = xs_guard
            if hg == 0:
                ldx_t = e.ldx2.n + 16

                @block.sync
                def _(sy, t=t, g0=g0, g1=g1):
                    sy.wait_ge(semh(g0), g1)
                    e.ldx2.inc(sy.dma_start(out=axs[:], in_=e.dtile(x_in, t)), 16)

            gx0, gx1 = x2_guard
            wo_pe = {}
            for dc in range(DC):
                @block.tensor
                def _(tt_, dc=dc, attf_all=attf_all):
                    if dc == 0:
                        tt_.wait_ge(e.dve.h, attf_all)
                        tt_.wait_ge(e.ldw.h, wow_done)
                    if dc >= 4:
                        tt_.wait_ge(e.dve.h, wo_used[dc - 4])
                    for hv in range(4):
                        mm = tt_.matmul(pSt[dc % 4][:],
                                        wow[:, hv, dc * 128:(dc + 1) * 128],
                                        attf.ap()[:, hv, :],
                                        start=(hv == 0), stop=(hv == 3))
                    e.pes.inc(mm, 1)
                if dc == 0:
                    wo_used = {}
                wo_pe[dc] = e.pes.n

                if hg == 0:
                    @block.vector
                    def _(v, dc=dc, ldx_t=ldx_t, gx0=gx0, gx1=gx1):
                        v.wait_ge(e.ldx2.h, ldx_t)
                        v.wait_ge(e.pes.h, wo_pe[dc])
                        if dc == 0:
                            v.wait_ge(semh(gx0), gx1)
                        e.dve.inc(v.tensor_tensor(x2b[:, dc, :], axs[:, dc, :],
                                                  pSt[dc % 4][:], ALU.add), 1)
                        wo_used[dc] = e.dve.n
                else:
                    @block.vector
                    def _(v, dc=dc, gx0=gx0, gx1=gx1):
                        v.wait_ge(e.pes.h, wo_pe[dc])
                        if dc == 0:
                            v.wait_ge(semh(gx0), gx1)
                        e.dve.inc(v.tensor_copy(x2b[:, dc, :],
                                                pSt[dc % 4][:]), 1)
                        wo_used[dc] = e.dve.n
            x2_done = e.dve.n
            attf_free = ("pes", wo_pe[DC - 1])
            xs_guard = ("dve", x2_done)
            st_guard = ("dve", x2_done)

            if hg == 0:
                @block.sync
                def _(sy, t=t, x2_done=x2_done):
                    sy.wait_ge(e.dve.h, x2_done)
                    e.stw.inc(sy.dma_start(out=e.dtile(x_out, t), in_=x2b[:]), 16)
                x2_guard = ("stw", e.stw.n)
            else:
                @block.gpsimd
                def _(g, t=t, x2_done=x2_done, base_stw=base_stw):
                    g.wait_ge(e.dve.h, x2_done)
                    g.wait_ge(e.stw.h, base_stw)
                    e.acc.inc(g.dma_start(out=e.dtile(x_out, t), in_=x2b[:],
                                          accum_op=ALU.add), 16)
                x2_guard = ("acc", e.acc.n)
        ST.pop_all().close()


def _emit_final(block, e):
    @block.gpsimd
    def _(g):
        g.wait_ge(e.stw.h, e.stw.n)
        g.wait_ge(e.acc.h, e.acc.n)

    @block.sync
    def _(sy):
        sy.wait_ge(e.acc.h, e.acc.n)
        sy.wait_ge(e.stw.h, e.stw.n)


def kernel(**inputs):
    C, xc = _prepare(inputs)
    nc = _build()
    in_maps = [dict(C, x_fm=xc[c]) for c in range(N_CORES)]
    res = run_bass_kernel_spmd(nc, in_maps, core_ids=list(range(N_CORES)))
    out = np.empty((16, 2048, 1024), np.float32)
    for c in range(N_CORES):
        out[2 * c:2 * c + 2] = res.results[c]["y_fm"].T.reshape(BPC, S, D)
    return out


# version 07
# speedup vs baseline: 1.2159x; 1.0077x over previous
import numpy as np
import ml_dtypes
from contextlib import ExitStack
import concourse.bass as bass
import concourse.mybir as mybir
from concourse.bass_utils import run_bass_kernel_spmd

F32 = mybir.dt.float32
BF16 = mybir.dt.bfloat16
AF = mybir.ActivationFunctionType
ALU = mybir.AluOpType
AX = mybir.AxisListType.X

D = 1024
S = 2048
BPC = 2
NTOK = BPC * S
DC = 8
TT = 512
NT = NTOK // TT
NTS = S // TT
EPS1 = 1e-5
EPS2 = 1e-6
NH = 16
SC_IN = float(64 ** -0.5)
SC_MLA = float(1.0 / np.sqrt(np.float32(96.0)))
NE = 8
HID = 2048
NEG = -1e30

PHASES = "ABC"
DEBUG = ()
N_CORES = 8


def _bd(*mats):
    n = sum(m.shape[0] for m in mats)
    m_ = sum(m.shape[1] for m in mats)
    out = np.zeros((n, m_), np.float32)
    r = c = 0
    for m in mats:
        out[r:r + m.shape[0], c:c + m.shape[1]] = m
        r += m.shape[0]; c += m.shape[1]
    return out


def _rope_tables():
    j = np.arange(0, 64, 2, dtype=np.float64) / 64.0
    invf = 1.0 / (10000.0 ** j)
    t = np.arange(S, dtype=np.float64)
    ang = t[None, :] * invf[:, None]
    c64 = np.concatenate([np.cos(ang), np.cos(ang)], axis=0).astype(np.float32)
    s64 = np.concatenate([np.sin(ang), np.sin(ang)], axis=0).astype(np.float32)
    c2i = np.concatenate([c64, c64], axis=0)
    s2i = np.concatenate([s64, s64], axis=0)
    B = np.zeros((64, 64), np.float32)
    for i in range(32):
        B[i, i + 32] = -1.0
        B[i + 32, i] = 1.0
    b2T = np.ascontiguousarray(_bd(B, B).T)
    i2 = np.arange(0, 32, 2, dtype=np.float64) / 32.0
    invm = 10000.0 ** (-i2)
    angm = t[None, :] * invm[:, None]
    c32 = np.repeat(np.cos(angm), 2, axis=0).astype(np.float32)
    s32 = np.repeat(np.sin(angm), 2, axis=0).astype(np.float32)
    cm2 = np.concatenate([c32, c32], axis=0)
    sm2 = np.concatenate([s32, s32], axis=0)
    A = np.zeros((32, 32), np.float32)
    for i in range(16):
        A[2 * i, 2 * i + 1] = -1.0
        A[2 * i + 1, 2 * i] = 1.0
    a2T = np.ascontiguousarray(_bd(A, A).T)
    a1T = np.ascontiguousarray(A.T)
    return c2i, s2i, b2T, cm2, sm2, a2T, a1T


def _prepare(inputs):
    f = np.float32
    C = {}
    def ct(x):
        return np.ascontiguousarray(np.asarray(x, dtype=f))
    C["qkvT"] = ct(np.asarray(inputs["L0_ffn_qkv_w"], f).T)
    C["outT"] = ct(np.asarray(inputs["L0_ffn_out_w"], f).T)
    C["w1T"] = ct(np.asarray(inputs["L0_ffn_w1"], f).transpose(0, 2, 1))
    C["w2T"] = ct(np.asarray(inputs["L0_ffn_w2"], f).transpose(0, 2, 1))
    C["w3T"] = ct(np.asarray(inputs["L0_ffn_w3"], f).transpose(0, 2, 1))
    C["gateT"] = ct(np.asarray(inputs["L0_ffn_gate_w"], f).T)
    for k in ("W_DQ", "W_UQ", "W_QR", "W_DKV", "W_UK", "W_UV", "W_KR", "W_O"):
        C["m_" + k] = ct(inputs["L0_attn_" + k])
    qb = np.asarray(inputs["L0_ffn_qkv_b"], f)
    C["qkvb_c"] = ct(qb.reshape(24, 128).T)
    C["vbr"] = ct(qb[2048:].reshape(1, 1024))
    C["outb_c"] = ct(np.asarray(inputs["L0_ffn_out_b"], f).reshape(8, 128).T)
    C["n1c"] = ct(np.asarray(inputs["L0_ffn_norm1_w"], f).reshape(8, 128).T)
    C["n2c"] = ct(np.asarray(inputs["L0_ffn_norm2_w"], f).reshape(8, 128).T)
    C["n1r"] = ct(np.asarray(inputs["L0_ffn_norm1_w"], f).reshape(1, 1024))
    C["b1c"] = ct(np.asarray(inputs["L0_ffn_b1"], f).reshape(8, 16, 128).transpose(2, 0, 1))
    C["b2c"] = ct(np.asarray(inputs["L0_ffn_b2"], f).reshape(8, 16, 128).transpose(2, 0, 1))
    C["b3r"] = ct(np.asarray(inputs["L0_ffn_b3"], f).reshape(1, 8192))
    c2i, s2i, b2T, cm2, sm2, a2T, a1T = _rope_tables()
    C["c2i"], C["s2i"], C["b2T"] = c2i, s2i, b2T
    C["cm2"], C["sm2"], C["a2T"], C["a1T"] = cm2, sm2, a2T, a1T
    C["identF"] = np.eye(128, dtype=f)
    m4 = np.zeros((4, 128, 512), f)
    tri = np.fromfunction(lambda k, q: (k > q), (128, 128))
    for r in range(4):
        for s_ in range(4):
            if s_ < r:
                m4[r][:, s_ * 128:(s_ + 1) * 128] = NEG
            elif s_ == r:
                m4[r][:, s_ * 128:(s_ + 1) * 128] = np.where(tri, NEG, 0.0)
    C["mask4"] = np.ascontiguousarray(m4.transpose(1, 0, 2).reshape(128, 4 * 512))
    x = np.asarray(inputs["L0_attn_x"], f)
    xc = [np.ascontiguousarray(x[2 * c:2 * c + 2].reshape(NTOK, D).T)
          for c in range(N_CORES)]
    return C, xc


_UID = [0]


def _un(name):
    _UID[0] += 1
    return f"{name}_{_UID[0]}"


class Sem:
    __slots__ = ("h", "n")
    def __init__(self, h):
        self.h = h
        self.n = 0
    def inc(self, inst, k=1):
        inst.then_inc(self.h, k)
        self.n += k
        return self.n


class Env:
    pass


DRAM_SPECS = [
    ("qkvT", [D, 3072]), ("outT", [D, D]),
    ("w1T", [NE, D, HID]), ("w2T", [NE, D, HID]), ("w3T", [NE, HID, D]),
    ("gateT", [D, 8]),
    ("m_W_DQ", [D, 256]), ("m_W_UQ", [256, D]), ("m_W_QR", [256, 512]),
    ("m_W_DKV", [D, 256]), ("m_W_UK", [256, D]), ("m_W_UV", [256, D]),
    ("m_W_KR", [D, 32]), ("m_W_O", [D, D]),
    ("qkvb_c", [128, 24]), ("vbr", [1, 1024]), ("outb_c", [128, 8]),
    ("n1c", [128, 8]), ("n2c", [128, 8]), ("n1r", [1, 1024]),
    ("b1c", [128, 8, 16]), ("b2c", [128, 8, 16]), ("b3r", [1, 8192]),
    ("c2i", [128, S]), ("s2i", [128, S]), ("b2T", [128, 128]),
    ("cm2", [64, S]), ("sm2", [64, S]), ("a2T", [64, 64]), ("a1T", [32, 32]),
    ("identF", [128, 128]), ("mask4", [128, 4 * 512]),
]

ENGS = ("tensor", "vector", "scalar", "gpsimd", "sync")


def _build():
    nc = bass.Bass("TRN2", target_bir_lowering=False)
    e = Env()
    e.nc = nc
    dt = nc.dram_tensor
    e.x0 = dt("x_fm", [D, NTOK], F32, kind="ExternalInput").ap()
    e.wk = {name: dt(name, shape, F32, kind="ExternalInput").ap()
            for name, shape in DRAM_SPECS}

    def scratch(name, dtype):
        kind = "ExternalOutput" if name in DEBUG else "Internal"
        return dt(name, [D, NTOK], dtype, kind=kind).ap()
    e.x1s = scratch("x1s", F32)
    e.x2s = scratch("x2s", F32)
    e.us = scratch("us", F32)
    e.u2s = scratch("u2s", BF16)
    e.u1s = scratch("u1s", BF16) if "u1s" in DEBUG else None
    e.mrs = dt("mrs", [1, 2 * NTOK], BF16, kind="Internal").ap()
    e.yfm = dt("y_fm", [D, NTOK], F32, kind="ExternalOutput").ap()
    if "rows" in DEBUG:
        e.rows_dbg = dt("rows_dbg", [4, NTOK], BF16, kind="ExternalOutput").ap()
        e.rowsf_dbg = dt("rowsf_dbg", [2, NTOK], F32, kind="ExternalOutput").ap()
    else:
        e.rows_dbg = None
        e.rowsf_dbg = None
    e.qk_dbg = (dt("qk_dbg", [128, 4 * 8 * S], BF16, kind="ExternalOutput").ap()
                if "qk" in DEBUG else None)
    e.attf_dbg = (dt("attf_dbg", [128, 8 * NTOK], BF16, kind="ExternalOutput").ap()
                  if "attf" in DEBUG else None)

    def dtile(ap, t, w=TT):
        return ap.rearrange("(dc p) f -> p dc f", p=128)[:, :, t * w:(t + 1) * w]
    e.dtile = dtile

    with ExitStack() as ST:
        en = ST.enter_context
        for s_ in ("ldc", "ldg", "ldx", "ldx2", "ldw", "sq", "pes", "row", "dve", "act",
                   "pool", "stw", "acc", "bar", "gat", "ldu", "exs"):
            setattr(e, s_, Sem(en(nc.semaphore(s_))))
        sb = nc.sbuf_tensor
        e.identF = en(sb("c_identF", [128, 128], F32))
        e.onesF = en(sb("c_onesF", [1, 128], F32))
        e.onesB = en(sb("c_onesB", [1, 128], BF16))
        e.oneC = en(sb("c_oneC", [128, 1], BF16))
        e.b2T = en(sb("c_b2T", [128, 128], BF16))
        e.a2T = en(sb("c_a2T", [64, 64], BF16))
        e.a1T = en(sb("c_a1T", [32, 32], BF16))
        e.qkvb = en(sb("c_qkvb", [128, 24], F32))
        e.vbr = en(sb("c_vbr", [1, 1024], BF16))
        e.outb = en(sb("c_outb", [128, 8], F32))
        e.n1c = en(sb("c_n1c", [128, 8], F32))
        e.n2c = en(sb("c_n2c", [128, 8], F32))
        e.n1r = en(sb("c_n1r", [1, 1024], BF16))
        e.b1c = en(sb("c_b1c", [128, 8, 16], F32))
        e.b2c = en(sb("c_b2c", [128, 8, 16], F32))
        e.gateT = en(sb("c_gateT", [128, 8, 8], BF16))
        e.onesC1 = en(sb("c_onesC1", [128, 1], BF16))
        e.ceps1 = en(sb("c_eps1", [1, 1], F32))
        e.ceps2 = en(sb("c_eps2", [1, 1], F32))

        with nc.Block() as block:
            _emit_const_loads(block, e)
            out_a = e.x1s if ("B" in PHASES or "C" in PHASES) else e.yfm
            _emit_ffn(block, e, e.x0, out_a)
            if "B" in PHASES:
                _barrier(block, e)
                out_b = e.x2s if "C" in PHASES else e.yfm
                _emit_mla(block, e, out_a, out_b)
            if "C" in PHASES:
                _barrier(block, e)
                _emit_ffn(block, e, e.x2s, e.yfm)
            _emit_final(block, e)
    return nc


def _barrier(block, e):
    bs = e.bar
    caps = [(s_, getattr(e, s_).n) for s_ in
            ("ldc", "ldg", "ldx", "ldx2", "ldw", "ldu", "stw", "acc")]

    @block.gpsimd
    def _(g):
        for s_, n_ in caps:
            g.wait_ge(getattr(e, s_).h, n_)

    @block.sync
    def _(sy):
        for s_, n_ in caps:
            sy.wait_ge(getattr(e, s_).h, n_)
    for name in ENGS:
        def _f(eng, bs=bs):
            bs.inc(eng.nop(), 1)
        getattr(block, name)(_f)
    tgt = bs.n
    for name in ENGS:
        def _f(eng, tgt=tgt, bs=bs):
            eng.wait_ge(bs.h, tgt)
        getattr(block, name)(_f)


def _emit_const_loads(block, e):
    wk = e.wk
    ldc = e.ldc

    @block.sync
    def _(s_):
        for src, dst in [
            (wk["identF"], e.identF[:]), (wk["qkvb_c"], e.qkvb[:]),
            (wk["outb_c"], e.outb[:]), (wk["n1c"], e.n1c[:]), (wk["n2c"], e.n2c[:]),
            (wk["b1c"], e.b1c[:]), (wk["b2c"], e.b2c[:]),
        ]:
            ldc.inc(s_.dma_start(out=dst, in_=src), 16)

    @block.gpsimd
    def _(g):
        for src, dst in [
            (wk["b2T"], e.b2T[:]), (wk["a2T"], e.a2T[:]), (wk["a1T"], e.a1T[:]),
            (wk["n1r"], e.n1r[:]),
            (wk["vbr"], e.vbr[:]),
            (wk["gateT"].rearrange("(dc p) x -> p dc x", p=128), e.gateT[:]),
        ]:
            e.ldg.inc(g.dma_start(out=dst, in_=src), 16)

    @block.vector
    def _(v):
        v.memset(e.onesF[:], 1.0)
        v.memset(e.onesB[:], 1.0)
        v.memset(e.onesC1[:], 1.0)
        v.memset(e.ceps1[:], EPS1)
        e.row.inc(v.memset(e.ceps2[:], EPS2), 1)
        e.dve.inc(v.memset(e.oneC[:], 1.0 / 1024.0), 1)
    e.ldc_done = ldc.n
    e.ldg_done = e.ldg.n
    e.memset_done = e.dve.n


def _row_chain(block, e, ops):
    for eng_name, fn in ops:
        tgt = e.row.n
        def _f(eng, fn=fn, tgt=tgt):
            eng.wait_ge(e.row.h, tgt)
            with e.nc.allow_low_precision(reason="per-token scale rows in bf16"):
                e.row.inc(fn(eng), 1)
        getattr(block, eng_name)(_f)


def _emit_stats(block, e, x_in, with_mean):
    nc = e.nc
    with ExitStack() as ST:
        en = ST.enter_context
        xb = [en(nc.sbuf_tensor(_un(f"st_xb{i}"), [128, DC, TT], BF16)) for i in range(2)]
        sq = [en(nc.sbuf_tensor(_un(f"st_sq{i}"), [128, DC, TT], BF16)) for i in range(2)]
        pm = [en(nc.psum_tensor(_un(f"st_pm{i}"), [1, TT], F32)) for i in range(2)]
        pms = [en(nc.psum_tensor(_un(f"st_pms{i}"), [1, TT], F32)) for i in range(2)]
        b_ldx = e.ldg.n
        b_sq = e.sq.n
        b_pe = e.pes.n
        b_row = e.row.n
        for t in range(NT):
            @block.gpsimd
            def _(g, t=t):
                if t >= 2:
                    g.wait_ge(e.pes.h, b_pe + (t - 1))
                e.ldg.inc(g.dma_start(out=xb[t % 2][:], in_=e.dtile(x_in, t)), 16)

            @block.scalar
            def _(sc, t=t):
                sc.wait_ge(e.ldg.h, b_ldx + 16 * (t + 1))
                if t >= 2:
                    sc.wait_ge(e.pes.h, b_pe + (t - 1))
                e.sq.inc(sc.activation(sq[t % 2].ap(), xb[t % 2].ap(), AF.Square,
                                       bias=0.0, scale=1.0), 1)

            @block.tensor
            def _(tt_, t=t):
                tt_.wait_ge(e.dve.h, e.memset_done)
                tt_.wait_ge(e.sq.h, b_sq + (t + 1))
                if t >= 2:
                    tt_.wait_ge(e.row.h, b_row + 2 * (t - 1))
                if with_mean:
                    for dc in range(DC):
                        tt_.matmul(pm[t % 2][:], e.oneC[:], xb[t % 2][:, dc, :],
                                   start=(dc == 0), stop=(dc == DC - 1))
                for dc in range(DC):
                    mm = tt_.matmul(pms[t % 2][:], e.oneC[:], sq[t % 2][:, dc, :],
                                    start=(dc == 0), stop=(dc == DC - 1))
                e.pes.inc(mm, 1)

            @block.vector
            def _(v, t=t):
                v.wait_ge(e.pes.h, b_pe + (t + 1))
                if with_mean:
                    v.tensor_copy(e.r_m[:, t * TT:(t + 1) * TT], pm[t % 2][:])
                e.row.inc(v.tensor_copy(e.r_ms[:, t * TT:(t + 1) * TT],
                                        pms[t % 2][:]), 2)


def _emit_ln_rows(block, e, compose_rms):
    ops = [
        ("scalar", lambda sc: sc.activation(e.rs1[:], e.r_m[:], AF.Square, bias=0.0, scale=1.0)),
        ("vector", lambda v: v.tensor_tensor(e.rs2[:], e.r_ms[:], e.rs1[:], ALU.subtract)),
        ("scalar", lambda sc: sc.activation(e.rs1[:], e.rs2[:], AF.Sqrt, bias=e.ceps1[0:1, 0:1], scale=1.0)),
        ("vector", lambda v: v.reciprocal(e.r_a[:], e.rs1[:])),
        ("vector", lambda v: v.scalar_tensor_tensor(e.r_b[:], e.r_m[:], -1.0, e.r_a[:], ALU.mult, ALU.mult)),
    ]
    if compose_rms:
        ops += [
            ("scalar", lambda sc: sc.activation(e.rs1[:], e.r_a[:], AF.Square, bias=0.0, scale=1.0)),
            ("vector", lambda v: v.tensor_tensor(e.rs3[:], e.rs2[:], e.rs1[:], ALU.mult)),
            ("scalar", lambda sc: sc.activation(e.rs1[:], e.rs3[:], AF.Sqrt, bias=e.ceps2[0:1, 0:1], scale=1.0)),
            ("vector", lambda v: v.reciprocal(e.rs3[:], e.rs1[:])),
            ("vector", lambda v: v.tensor_tensor(e.r_A[:], e.r_a[:], e.rs3[:], ALU.mult)),
            ("vector", lambda v: v.scalar_tensor_tensor(e.r_B[:], e.r_m[:], -1.0, e.r_A[:], ALU.mult, ALU.mult)),
        ]
    _row_chain(block, e, ops)


def _emit_rms_row(block, e):
    _row_chain(block, e, [
        ("scalar", lambda sc: sc.activation(e.r_B[:], e.r_ms[:], AF.Sqrt, bias=e.ceps2[0:1, 0:1], scale=1.0)),
        ("vector", lambda v: v.reciprocal(e.r_A[:], e.r_B[:])),
    ])


def _emit_ffn(block, e, x_in, x_out):
    nc = e.nc
    with ExitStack() as SF:
        enf = SF.enter_context
        for rn in ("r_A", "r_B", "r_ms", "r_a", "r_b"):
            setattr(e, rn, enf(nc.sbuf_tensor(_un("f_") + rn, [1, NTOK], BF16))[:])

        with ExitStack() as SS:
            ens = SS.enter_context
            for rn in ("r_m", "rs1", "rs2", "rs3"):
                setattr(e, rn, ens(nc.sbuf_tensor(_un("f_") + rn, [1, NTOK], F32))[:])
            _emit_stats(block, e, x_in, with_mean=True)
            _emit_ln_rows(block, e, compose_rms=True)
            if e.rows_dbg is not None:
                rt_ = e.row.n

                @block.sync
                def _(sy, rt_=rt_):
                    sy.wait_ge(e.row.h, rt_)
                    for i, r in enumerate((e.r_A, e.r_B, e.r_a, e.r_b)):
                        e.stw.inc(sy.dma_start(out=e.rows_dbg[i:i + 1, :], in_=r), 16)

                @block.gpsimd
                def _(g, rt_=rt_):
                    g.wait_ge(e.row.h, rt_)
                    for i, r in enumerate((e.r_m, e.r_ms)):
                        e.stw.inc(g.dma_start(out=e.rowsf_dbg[i:i + 1, :], in_=r), 16)
        for s_ in range(BPC):
            _emit_inner_seq(block, e, x_in, s_)
        _emit_rms_row(block, e)
        with ExitStack() as SC:
            enc = SC.enter_context
            cT = [enc(nc.sbuf_tensor(_un(f"f_cT{i}"), [97, NTOK], BF16)) for i in range(3)]
            e.combR = [cT[i // 3][(i % 3) * 32:(i % 3) * 32 + 1, :] for i in range(NE)]
            base_done = _emit_u2_gate_base(block, e, x_in, x_out)
            _emit_moe(block, e, x_out, base_done)
            _drain_phase(block, e)
        SF.pop_all().close()


def _drain_phase(block, e):
    _barrier(block, e)


def _emit_inner_seq(block, e, x_in, s_):
    _barrier(block, e)
    nc = e.nc
    wk = e.wk
    t0 = s_ * NTS
    row_ready = e.row.n
    with ExitStack() as ST:
        en = ST.enter_context
        qrope = en(nc.sbuf_tensor(_un("i_qrope"), [128, 8, S], BF16))
        krope = en(nc.sbuf_tensor(_un("i_krope"), [128, 8, S], BF16))
        vtm = en(nc.sbuf_tensor(_un("i_vtm"), [128, 16, 16, 65], BF16))
        nbc = en(nc.sbuf_tensor(_un("i_nbc"), [128, 16], F32))

        with ExitStack() as SP:
            enp = SP.enter_context
            u1f = enp(nc.sbuf_tensor(_un("i_u1f"), [128, DC, S], BF16))
            qkraw = [enp(nc.sbuf_tensor(_un(f"i_qkraw{i}"), [128, TT], BF16)) for i in range(2)]
            rt = [enp(nc.sbuf_tensor(_un(f"i_rt{i}"), [128, TT], F32)) for i in range(2)]
            pGB = enp(nc.psum_tensor(_un("i_pGB"), [128, TT], F32))
            pBW = [enp(nc.psum_tensor(_un(f"i_pBW{i}"), [128, TT], F32)) for i in range(1)]
            pqk = [enp(nc.psum_tensor(_un(f"i_pqk{i}"), [128, TT], F32)) for i in range(2)]
            pbq = [enp(nc.psum_tensor(_un(f"i_pbq{i}"), [128, TT], F32)) for i in range(2)]
            pvv = [enp(nc.psum_tensor(_un(f"i_pvv{i}"), [128, TT], F32)) for i in range(2)]


            @block.vector
            def _(v):
                v.wait_ge(e.ldc.h, e.ldc_done)
                v.wait_ge(e.ldg.h, e.ldg_done)
                e.dve.inc(v.memset(vtm.ap()[:, :, :, 64:65], 1.0), 1)
            vones = e.dve.n

            SX = ExitStack()
            xs = SX.enter_context(nc.sbuf_tensor(_un("i_xs"), [128, DC, TT], F32))
            xs_free = {}
            u1_done = {}
            for ts in range(NTS):
                t = t0 + ts
                sl = slice(t * TT, (t + 1) * TT)
                ssl = slice(ts * TT, (ts + 1) * TT)
                ldx_t = e.ldx.n + 16

                @block.sync
                def _(sy, ts=ts, t=t):
                    if ts >= 1:
                        sy.wait_ge(e.dve.h, xs_free[ts - 1])
                    e.ldx.inc(sy.dma_start(out=xs[:], in_=e.dtile(x_in, t)), 16)

                @block.tensor
                def _(tt_, ts=ts, sl=sl):
                    tt_.wait_ge(e.row.h, row_ready)
                    if ts >= 1:
                        tt_.wait_ge(e.dve.h, u1_done[ts - 1])
                    e.pes.inc(tt_.matmul(pGB[:], e.onesB[:], e.r_A[:, sl],
                                         start=True, stop=True), 1)
                for dc in range(DC):
                    @block.tensor
                    def _(tt_, dc=dc, ts=ts, sl=sl):
                        if dc >= 1:
                            tt_.wait_ge(e.dve.h, bw_used[dc - 1])
                        elif ts >= 1:
                            tt_.wait_ge(e.dve.h, u1_done[ts - 1])
                        e.pes.inc(tt_.matmul(pBW[0][:],
                                             e.n1r[0:1, dc * 128:(dc + 1) * 128],
                                             e.r_B[:, sl], start=True, stop=True), 1)
                    if dc == 0:
                        bw_used = {}
                    bw_pe = e.pes.n

                    @block.vector
                    def _(v, dc=dc, ts=ts, ssl=ssl, bw_pe=bw_pe, ldx_t=ldx_t):
                        v.wait_ge(e.pes.h, bw_pe)
                        v.wait_ge(e.ldx.h, ldx_t)
                        e.dve.inc(v.tensor_tensor(rt[dc % 2][:], xs[:, dc, :],
                                                  pGB[:], ALU.mult), 1)
                        v.wait_ge(e.dve.h, e.dve.n)
                        e.dve.inc(v.scalar_tensor_tensor(
                            u1f[:, dc, ssl], rt[dc % 2][:], e.n1c[:, dc:dc + 1],
                            pBW[0][:], ALU.mult, ALU.add), 1)
                        bw_used[dc] = e.dve.n
                xs_free[ts] = e.dve.n
                u1_done[ts] = e.dve.n
            u1_all = e.dve.n
            SX.pop_all().close()

            if e.u1s is not None:
                @block.sync
                def _(sy, u1_all=u1_all):
                    sy.wait_ge(e.dve.h, u1_all)
                    for ts in range(NTS):
                        e.stw.inc(sy.dma_start(
                            out=e.dtile(e.u1s, t0 + ts),
                            in_=u1f[:, :, ts * TT:(ts + 1) * TT]), 16)

            with ExitStack() as SV:
                wv = SV.enter_context(nc.sbuf_tensor(_un("i_wv"), [128, DC, 512], BF16))
                tbl_done = e.ldw.n
                v_evac = {}
                vh_used = {}
                for hh in range(2):
                    @block.gpsimd
                    def _(g, hh=hh, u1_all=u1_all):
                        g.wait_ge(e.ldc.h, e.ldc_done)
                        g.wait_ge(e.ldg.h, e.ldg_done)
                        g.wait_ge(e.dve.h, u1_all)
                        if hh >= 1:
                            g.wait_ge(e.pes.h, vh_used[hh - 1])
                        e.ldw.inc(g.dma_start(
                            out=wv[:],
                            in_=wk["qkvT"][:, 2048 + hh * 512:2048 + (hh + 1) * 512]
                            .rearrange("(dc p) f -> p dc f", p=128)), 16)
                    wvl = e.ldw.n
                    for ts in range(NTS):
                        for st_ in range(4):
                            ai = hh * 16 + ts * 4 + st_

                            @block.tensor
                            def _(tt_, st_=st_, hh=hh, ts=ts, ai=ai,
                                  u1_all=u1_all, wvl=wvl):
                                tt_.wait_ge(e.dve.h, u1_all)
                                tt_.wait_ge(e.ldw.h, wvl)
                                if ai >= 2:
                                    tt_.wait_ge(e.act.h, v_evac[ai - 2])
                                for dc in range(DC):
                                    tt_.matmul(pvv[ai % 2][:],
                                               u1f[:, dc, ts * TT + st_ * 128:ts * TT + (st_ + 1) * 128],
                                               wv[:, dc, :],
                                               start=(dc == 0), stop=False)
                                mm = tt_.matmul(pvv[ai % 2][:], e.onesB[:],
                                                e.vbr[0:1, hh * 512:(hh + 1) * 512],
                                                start=False, stop=True)
                                e.pes.inc(mm, 1)
                                if ts == NTS - 1 and st_ == 3:
                                    vh_used[hh] = e.pes.n
                            pe_v = e.pes.n

                            @block.vector
                            def _(v, st_=st_, hh=hh, ts=ts, ai=ai, pe_v=pe_v,
                                  vones=vones):
                                v.wait_ge(e.pes.h, pe_v)
                                v.wait_ge(e.dve.h, vones)
                                kt = ts * 4 + st_
                                e.act.inc(v.tensor_copy(
                                    vtm.ap()[:, kt, hh * 8:(hh + 1) * 8, 0:64],
                                    pvv[ai % 2][:]), 1)
                                v_evac[ai] = e.act.n

                v_done_pe = e.pes.n
                v_done_act = e.act.n
                SV.pop_all().close()

            with ExitStack() as SQ:
                enq = SQ.enter_context
                c2i = enq(nc.sbuf_tensor(_un("i_c2i"), [128, S], BF16))
                s2i = enq(nc.sbuf_tensor(_un("i_s2i"), [128, S], BF16))
                wc = [enq(nc.sbuf_tensor(_un(f"i_wc{i}"), [128, DC, 128], BF16)) for i in range(2)]
                sqn = enq(nc.sbuf_tensor(_un("i_sqn"), [128, TT], BF16))
                qn2 = enq(nc.sbuf_tensor(_un("i_qn2"), [1, 64], F32))
                kn2s = enq(nc.sbuf_tensor(_un("i_kn2s"), [1, 64], F32))
                qm2 = enq(nc.sbuf_tensor(_un("i_qm2"), [1, 16], F32))
                km2 = enq(nc.sbuf_tensor(_un("i_km2"), [1, 16], F32))
                b2r = enq(nc.sbuf_tensor(_un("i_b2r"), [1, 16], F32))
                negbr = enq(nc.sbuf_tensor(_un("i_negbr"), [1, 16], F32))

                @block.gpsimd
                def _(g):
                    g.wait_ge(e.ldc.h, e.ldc_done)
                    g.wait_ge(e.ldg.h, e.ldg_done)
                    g.wait_ge(e.pes.h, v_done_pe)
                    e.ldw.inc(g.dma_start(out=c2i[:], in_=wk["c2i"]), 16)
                    e.ldw.inc(g.dma_start(out=s2i[:], in_=wk["s2i"]), 16)
                tbl2_done = e.ldw.n

                qk_evac = {}
                bq_pe = {}
                rope_done = {}
                wc_used = {}
                sqn_t = {}
                nrm_pe = {}
                nrm_red = {}
                for c in range(16):
                    @block.gpsimd
                    def _(g, c=c):
                        if c >= 2:
                            g.wait_ge(e.pes.h, wc_used[c - 2])
                        e.ldw.inc(g.dma_start(
                            out=wc[c % 2][:],
                            in_=wk["qkvT"][:, c * 128:(c + 1) * 128]
                            .rearrange("(dc p) f -> p dc f", p=128)), 16)
                    wcl = e.ldw.n
                    for ts in range(NTS):
                        ac = c * 4 + ts
                        ssl = slice(ts * TT, (ts + 1) * TT)

                        @block.tensor
                        def _(tt_, c=c, ts=ts, ssl=ssl, ac=ac, wcl=wcl,
                              u1_all=u1_all):
                            tt_.wait_ge(e.dve.h, u1_all)
                            tt_.wait_ge(e.ldw.h, wcl)
                            if ac >= 2:
                                tt_.wait_ge(e.act.h, qk_evac[ac - 2])
                            for dc in range(DC):
                                mm = tt_.matmul(pqk[ac % 2][:], wc[c % 2][:, dc, :],
                                                u1f[:, dc, ssl],
                                                start=(dc == 0), stop=(dc == DC - 1))
                            e.pes.inc(mm, 1)
                            if ts == NTS - 1:
                                wc_used[c] = e.pes.n
                        qk_pe = e.pes.n

                        @block.vector
                        def _(v, c=c, qk_pe=qk_pe, ac=ac):
                            v.wait_ge(e.pes.h, qk_pe)
                            if ac >= 2:
                                v.wait_ge(e.pes.h, bq_pe[ac - 2])
                                v.wait_ge(e.dve.h, rope_done[ac - 2])
                                v.wait_ge(e.pool.h, sqn_t[ac - 2])
                            e.act.inc(v.tensor_scalar(qkraw[ac % 2][:], pqk[ac % 2][:],
                                                      e.qkvb[:, c:c + 1], None,
                                                      ALU.add), 1)
                            qk_evac[ac] = e.act.n

                        @block.tensor
                        def _(tt_, ac=ac):
                            tt_.wait_ge(e.act.h, qk_evac[ac])
                            if ac >= 2:
                                tt_.wait_ge(e.dve.h, rope_done[ac - 2])
                            e.pes.inc(tt_.matmul(pbq[ac % 2][:], e.b2T[:], qkraw[ac % 2][:],
                                                 start=True, stop=True), 1)
                            bq_pe[ac] = e.pes.n

                        @block.gpsimd
                        def _(g, ssl=ssl, ac=ac, u1_all=u1_all):
                            g.wait_ge(e.act.h, qk_evac[ac])
                            if ac >= 1:
                                g.wait_ge(e.dve.h, rope_done[ac - 1])
                            else:
                                g.wait_ge(e.dve.h, u1_all)
                            e.pool.inc(g.tensor_tensor(rt[0][:], qkraw[ac % 2][:],
                                                       c2i[:, ssl], ALU.mult), 1)
                        rp_t = e.pool.n

                        @block.vector
                        def _(v, c=c, ssl=ssl, ac=ac, rp_t=rp_t):
                            dst = qrope if c < 8 else krope
                            ci = c % 8
                            v.wait_ge(e.pes.h, bq_pe[ac])
                            v.wait_ge(e.pool.h, rp_t)
                            e.dve.inc(v.tensor_tensor(rt[1][:], pbq[ac % 2][:],
                                                      s2i[:, ssl], ALU.mult), 1)
                            v.wait_ge(e.dve.h, e.dve.n)
                            e.dve.inc(v.tensor_tensor(dst[:, ci, ssl], rt[0][:], rt[1][:],
                                                      ALU.add), 1)
                            rope_done[ac] = e.dve.n

                        @block.gpsimd
                        def _(g, ac=ac):
                            g.wait_ge(e.act.h, qk_evac[ac])
                            if ac >= 1:
                                g.wait_ge(e.pes.h, nrm_pe[ac - 1])
                            e.pool.inc(g.tensor_tensor(sqn[:], qkraw[ac % 2][:],
                                                       qkraw[ac % 2][:], ALU.mult), 1)
                        sqn_t[ac] = e.pool.n

                        @block.tensor
                        def _(tt_, ac=ac):
                            tt_.wait_ge(e.pool.h, sqn_t[ac])
                            if ac < 2:
                                tt_.wait_ge(e.act.h, v_done_act)
                            else:
                                tt_.wait_ge(e.dve.h, nrm_red[ac - 2])
                            tt_.matmul(pvv[ac % 2][0:1, :], e.onesC1[0:64, 0:1],
                                       sqn[0:64, :], start=True, stop=True)
                            mm = tt_.matmul(pvv[ac % 2][32:33, :],
                                            e.onesC1[64:128, 0:1],
                                            sqn[64:128, :], start=True, stop=True)
                            e.pes.inc(mm, 1)
                        nrm_pe[ac] = e.pes.n

                        @block.vector
                        def _(v, ac=ac, c=c, ts=ts):
                            v.wait_ge(e.pes.h, nrm_pe[ac])
                            dst = qn2 if c < 8 else kn2s
                            h0 = (c % 8) * 2
                            e.dve.inc(v.reduce_max(
                                dst[0:1, h0 * 4 + ts:h0 * 4 + ts + 1],
                                pvv[ac % 2][0:1, :], axis=AX), 1)
                            e.dve.inc(v.reduce_max(
                                dst[0:1, (h0 + 1) * 4 + ts:(h0 + 1) * 4 + ts + 1],
                                pvv[ac % 2][32:33, :], axis=AX), 1)
                        nrm_red[ac] = e.dve.n

                bnd0 = e.dve.n

                @block.vector
                def _(v, bnd0=bnd0):
                    v.wait_ge(e.dve.h, bnd0)
                    for h in range(16):
                        e.dve.inc(v.reduce_max(qm2[0:1, h:h + 1],
                                               qn2[0:1, h * 4:h * 4 + 4], axis=AX), 1)
                        e.dve.inc(v.reduce_max(km2[0:1, h:h + 1],
                                               kn2s[0:1, h * 4:h * 4 + 4], axis=AX), 1)
                    v.wait_ge(e.dve.h, e.dve.n)
                    e.dve.inc(v.tensor_tensor(b2r[:], qm2[:], km2[:], ALU.mult), 1)
                bq_t = e.dve.n

                @block.scalar
                def _(sc, bq_t=bq_t):
                    sc.wait_ge(e.dve.h, bq_t)
                    e.sq.inc(sc.activation(b2r[:], b2r[:], AF.Sqrt,
                                           bias=0.0, scale=1.0), 1)
                br_t = e.sq.n

                @block.vector
                def _(v, br_t=br_t):
                    v.wait_ge(e.sq.h, br_t)
                    e.dve.inc(v.tensor_scalar(negbr[:], b2r[:], -SC_IN, None,
                                              ALU.mult), 1)
                nb_t = e.dve.n

                @block.tensor
                def _(tt_, nb_t=nb_t):
                    tt_.wait_ge(e.dve.h, nb_t)
                    for h in range(16):
                        mm = tt_.matmul(pGB[:, h:h + 1], e.onesF[:],
                                        negbr[0:1, h:h + 1], start=True, stop=True)
                    e.pes.inc(mm, 1)
                nbp = e.pes.n

                @block.vector
                def _(v, nbp=nbp):
                    v.wait_ge(e.pes.h, nbp)
                    e.dve.inc(v.tensor_copy(nbc[:], pGB[:, 0:16]), 1)
                e.nbc_ready = e.dve.n
                SQ.pop_all().close()
            proj_pe = e.pes.n
            proj_act = e.act.n
            proj_dve = e.dve.n
            SP.pop_all().close()

        if e.qk_dbg is not None:
            @block.sync
            def _(sy, s_=s_, proj_dve=proj_dve):
                sy.wait_ge(e.dve.h, proj_dve)
                off = s_ * 2 * 8 * S
                e.stw.inc(sy.dma_start(
                    out=e.qk_dbg[:, off:off + 8 * S],
                    in_=qrope.ap().rearrange("p c f -> p (c f)")), 16)
                e.stw.inc(sy.dma_start(
                    out=e.qk_dbg[:, off + 8 * S:off + 16 * S],
                    in_=krope.ap().rearrange("p c f -> p (c f)")), 16)

        _emit_inner_attention(block, e, x_in, s_, qrope, krope, vtm, nbc,
                              proj_pe, proj_act, proj_dve)
        ST.pop_all().close()


def _emit_inner_attention(block, e, x_in, s_, qrope, krope, vtm, nbc,
                          proj_pe, proj_act, proj_dve):
    _barrier(block, e)
    nc = e.nc
    wk = e.wk
    t0 = s_ * NTS
    row_ready = e.row.n
    with ExitStack() as ST:
        en = ST.enter_context
        outw = en(nc.sbuf_tensor(_un("i_outw"), [128, DC, D], BF16))
        attf = en(nc.sbuf_tensor(_un("i_attf"), [128, 8, TT], BF16))
        psb = [en(nc.sbuf_tensor(_un(f"i_psb{i}"), [128, 1024], BF16)) for i in range(6)]
        recb = en(nc.sbuf_tensor(_un("i_recb"), [1, TT], BF16))
        rbs = en(nc.sbuf_tensor(_un("i_rbs"), [64, TT], BF16))
        axs = en(nc.sbuf_tensor(_un("i_axs"), [128, DC, TT], F32))
        umb2 = [en(nc.sbuf_tensor(_un(f"i_umb{i}"), [128, TT], F32)) for i in range(2)]
        sqb = [en(nc.sbuf_tensor(_un(f"i_sqb{i}"), [128, TT], BF16)) for i in range(2)]
        sqm_pe = {}
        pSt = [en(nc.psum_tensor(_un(f"a_pSt{i}"), [128, 1024], F32)) for i in range(3)]
        pPV = en(nc.psum_tensor(_un("a_pPV"), [65, TT], F32))
        pMi = en(nc.psum_tensor(_un("a_pMi"), [128, TT], F32))

        @block.gpsimd
        def _(g):
            e.ldw.inc(g.dma_start(
                out=outw[:], in_=wk["outT"].rearrange("(dc p) f -> p dc f", p=128)), 16)
        outw_done = e.ldw.n

        pv_free = ("dve", proj_dve)
        pmi_guard = ("dve", proj_dve)
        attf_free = ("pes", proj_pe)
        xs_guard = ("dve", proj_dve)
        umb_guard = [("stw", e.stw.n), ("sq", e.sq.n)]
        recb_prev = proj_dve
        rbs_guard = proj_dve
        st_extra = ("dve", proj_dve)
        pv_t = {}
        st_t = {}
        kpg = 0
        exs_base = e.exs.n

        def semh(name):
            return getattr(e, name).h

        NPAIR = 8
        exp_jobs = []
        for q5 in range(NTS):
            qsl = slice(q5 * TT, (q5 + 1) * TT)
            for h in range(NH):
                hc = h // 2
                hb = (h % 2) * 64
                kp0 = kpg
                g0, g1 = st_extra

                @block.tensor
                def _(tt_, hb=hb, hc=hc, qsl=qsl, kp0=kp0, g0=g0, g1=g1,
                      first=(q5 == 0 and h == 0)):
                    if first:
                        tt_.wait_ge(e.act.h, proj_act)
                        tt_.wait_ge(e.dve.h, proj_dve)
                    tt_.wait_ge(semh(g0), g1)
                    for kp in range(NPAIR):
                        kk = kp0 + kp
                        if kk >= 3:
                            tt_.wait_ge(e.exs.h, exs_base + kk - 2)
                        for j in (0, 1):
                            kc = kp * 2 + j
                            mm = tt_.matmul(pSt[kk % 3][:, j * 512:(j + 1) * 512],
                                            krope.ap()[hb:hb + 64, hc, kc * 128:(kc + 1) * 128],
                                            qrope.ap()[hb:hb + 64, hc, qsl],
                                            start=True, stop=True)
                        e.pes.inc(mm, 1)
                        st_t[kp0 + kp] = e.pes.n
                st_last = e.pes.n
                for kp in range(NPAIR):
                    pv_t[kp0 + kp] = st_last + kp + 1

                exp_jobs.append((h, kp0))
                if h == NH - 1:
                    @block.scalar
                    def _(sc, jobs=tuple(exp_jobs)):
                        sc.wait_ge(e.dve.h, e.nbc_ready)
                        for (hh, hp0) in jobs:
                            for kp in range(NPAIR):
                                kk = hp0 + kp
                                sc.wait_ge(e.pes.h, st_t[kk])
                                if kk >= 6:
                                    sc.wait_ge(e.pes.h, pv_t[kk - 6])
                                sc.activation(psb[kk % 6][:], pSt[kk % 3][:],
                                              AF.Exp, bias=nbc[:, hh:hh + 1],
                                              scale=SC_IN).then_inc(e.exs.h, 1)
                                e.exs.n += 1
                    exp_jobs = []

                g0, g1 = pv_free

                @block.tensor
                def _(tt_, h=h, kp0=kp0, g0=g0, g1=g1):
                    for kp in range(NPAIR):
                        kk = kp0 + kp
                        tt_.wait_ge(e.exs.h, exs_base + kk + 1)
                        if kp == 0:
                            tt_.wait_ge(semh(g0), g1)
                        mm = tt_.matmul(pPV[:], vtm.ap()[:, kp * 2, h, :],
                                        psb[kk % 6][:, 0:512],
                                        start=(kp == 0), stop=False)
                        mm = tt_.matmul(pPV[:], vtm.ap()[:, kp * 2 + 1, h, :],
                                        psb[kk % 6][:, 512:1024],
                                        start=False, stop=(kp == NPAIR - 1))
                        e.pes.inc(mm, 1)
                        assert e.pes.n == pv_t[kk]
                kpg += NPAIR
                pv_all = e.pes.n

                @block.vector
                def _(v, pv_all=pv_all, recb_prev=recb_prev):
                    v.wait_ge(e.pes.h, pv_all)
                    v.wait_ge(e.dve.h, recb_prev)
                    e.dve.inc(v.tensor_copy(recb[:], pPV[64:65, :]), 1)
                recb_done = e.dve.n
                recb_prev = recb_done
                g0, g1 = pmi_guard

                @block.tensor
                def _(tt_, recb_done=recb_done, g0=g0, g1=g1):
                    tt_.wait_ge(e.dve.h, recb_done)
                    tt_.wait_ge(semh(g0), g1)
                    e.pes.inc(tt_.matmul(pMi[0:64, :], e.onesB[0:1, 0:64], recb[:],
                                         start=True, stop=True), 1)
                rb_pe = e.pes.n

                @block.vector
                def _(v, rb_pe=rb_pe, rbs_guard=rbs_guard, h=h,
                      g0=attf_free[0], g1=attf_free[1]):
                    v.wait_ge(e.pes.h, rb_pe)
                    v.wait_ge(e.dve.h, rbs_guard)
                    with e.nc.allow_low_precision(reason="softmax recip bf16"):
                        e.dve.inc(v.reciprocal(rbs[:], pMi[0:64, :]), 1)
                    v.wait_ge(e.dve.h, e.dve.n)
                    if h == 0:
                        v.wait_ge(semh(g0), g1)
                    hc2 = h // 2
                    hb2 = (h % 2) * 64
                    e.dve.inc(v.tensor_tensor(attf.ap()[hb2:hb2 + 64, hc2, :],
                                              pPV[0:64, :], rbs[:], ALU.mult), 1)
                pmi_guard = ("dve", e.dve.n)
                pv_free = ("dve", e.dve.n)
                rbs_guard = e.dve.n

            t = t0 + q5
            sl = slice(t * TT, (t + 1) * TT)
            attf_all = e.dve.n
            ldx_t = e.ldx2.n + 16
            g0, g1 = xs_guard

            @block.sync
            def _(sy, t=t, g0=g0, g1=g1):
                sy.wait_ge(semh(g0), g1)
                e.ldx2.inc(sy.dma_start(out=axs[:], in_=e.dtile(x_in, t)), 16)

            gm0, gm1 = pmi_guard
            gu0 = [umb_guard[0], umb_guard[1]]
            wo_pe = {}

            @block.tensor
            def _(tt_, attf_all=attf_all, sl=sl, gm0=gm0, gm1=gm1):
                tt_.wait_ge(e.dve.h, attf_all)
                tt_.wait_ge(e.ldw.h, outw_done)
                tt_.wait_ge(e.row.h, row_ready)
                tt_.wait_ge(semh(gm0), gm1)
                e.pes.inc(tt_.matmul(pMi[:], e.onesB[:], e.r_a[:, sl],
                                     start=True, stop=True), 1)
            wo_used = {}
            sq_done = {}
            dma_done = {}
            for dc in range(DC):
                @block.tensor
                def _(tt_, dc=dc, sl=sl):
                    if dc >= 2:
                        tt_.wait_ge(e.dve.h, wo_used[dc - 2])
                    for hv in range(8):
                        tt_.matmul(pSt[dc % 2][:, 0:512],
                                   outw[:, hv, dc * 128:(dc + 1) * 128],
                                   attf.ap()[:, hv, :],
                                   start=(hv == 0), stop=False)
                    mm = tt_.matmul(pSt[dc % 2][:, 0:512], e.onesB[:],
                                    e.r_b[:, sl], start=False, stop=True)
                    e.pes.inc(mm, 1)
                wo_pe[dc] = e.pes.n

                @block.vector
                def _(v, dc=dc, ldx_t=ldx_t, gu0=gu0):
                    v.wait_ge(e.ldx2.h, ldx_t)
                    v.wait_ge(e.pes.h, wo_pe[dc])
                    if dc == 0:
                        for gg0, gg1 in gu0:
                            v.wait_ge(semh(gg0), gg1)
                    if dc >= 2:
                        v.wait_ge(e.stw.h, dma_done[dc - 2])
                        v.wait_ge(e.sq.h, sq_done[dc - 2])
                    e.dve.inc(v.tensor_tensor(umb2[dc % 2][:], axs[:, dc, :],
                                              pMi[:], ALU.mult), 1)
                    v.wait_ge(e.dve.h, e.dve.n)
                    e.dve.inc(v.scalar_tensor_tensor(
                        umb2[dc % 2][:], pSt[dc % 2][:, 0:512], e.outb[:, dc:dc + 1],
                        umb2[dc % 2][:], ALU.add, ALU.add), 1)
                    wo_used[dc] = e.dve.n
                um_d = e.dve.n

                @block.sync
                def _(sy, dc=dc, t=t, um_d=um_d):
                    sy.wait_ge(e.dve.h, um_d)
                    e.stw.inc(sy.dma_start(
                        out=e.dtile(e.us, t)[:, dc:dc + 1, :],
                        in_=umb2[dc % 2][:]), 16)
                dma_done[dc] = e.stw.n

                @block.gpsimd
                def _(g, dc=dc, um_d=um_d):
                    g.wait_ge(e.dve.h, um_d)
                    if dc >= 2:
                        g.wait_ge(e.pes.h, sqm_pe[dc - 2])
                    e.sq.inc(g.tensor_tensor(sqb[dc % 2][:], umb2[dc % 2][:],
                                             umb2[dc % 2][:], ALU.mult), 1)
                sq_done[dc] = e.sq.n

                @block.tensor
                def _(tt_, dc=dc, gpv=pv_free):
                    tt_.wait_ge(e.sq.h, sq_done[dc])
                    if dc == 0:
                        tt_.wait_ge(semh(gpv[0]), gpv[1])
                    mm = tt_.matmul(pPV[0:1, 0:TT], e.oneC[:], sqb[dc % 2][:],
                                    start=(dc == 0), stop=(dc == DC - 1))
                    e.pes.inc(mm, 1)
                sqm_pe[dc] = e.pes.n
            um_done = e.dve.n
            attf_free = ("pes", wo_pe[DC - 1])
            xs_guard = ("dve", um_done)
            st_extra = ("dve", um_done)
            pmi_guard = ("dve", um_done)
            umb_guard = [("stw", dma_done[DC - 1]), ("sq", sq_done[DC - 1])]
            msq_pe = e.pes.n

            @block.vector
            def _(v, t=t, msq_pe=msq_pe):
                v.wait_ge(e.pes.h, msq_pe)
                e.row.inc(v.tensor_copy(e.r_ms[:, t * TT:(t + 1) * TT],
                                        pPV[0:1, 0:TT]), 1)
            pv_free = ("row", e.row.n)
        ST.pop_all().close()


def _emit_u2_gate_base(block, e, x_in, x_out):
    _barrier(block, e)
    nc = e.nc
    row_r = e.row.n
    with ExitStack() as ST:
        en = ST.enter_context
        uss = [en(nc.sbuf_tensor(_un(f"g_uss{i}"), [128, DC, TT], F32)) for i in range(2)]
        xss = [en(nc.sbuf_tensor(_un(f"g_xss{i}"), [128, DC, TT], F32)) for i in range(2)]
        u2b = [en(nc.sbuf_tensor(_un(f"g_u2b{i}"), [128, DC, TT], BF16)) for i in range(2)]
        x1b = [en(nc.sbuf_tensor(_un(f"g_x1b{i}"), [128, DC, TT], F32)) for i in range(2)]
        lgt = en(nc.sbuf_tensor(_un("g_lgt"), [128, 8], F32))
        top = en(nc.sbuf_tensor(_un("g_top"), [128, 8], F32))
        gw = en(nc.sbuf_tensor(_un("g_gw"), [128, 6], F32))
        msk = en(nc.sbuf_tensor(_un("g_msk"), [128, 4, 8], F32))
        logit = en(nc.sbuf_tensor(_un("g_logit"), [8, NTOK], F32))
        pR2 = [en(nc.psum_tensor(_un(f"g_pR2{i}"), [128, TT], F32)) for i in range(2)]
        plg = [en(nc.psum_tensor(_un(f"g_plg{i}"), [8, TT], F32)) for i in range(2)]
        ptr = en(nc.psum_tensor(_un("g_ptr"), [128, 8], F32))
        prow = en(nc.psum_tensor(_un("g_prow"), [1, 128], F32))
        b_ldx = e.ldx.n
        b_ldx2 = e.ldx2.n
        tile_done = {}
        lg_evac = {}
        for t in range(NT):
            @block.sync
            def _(sy, t=t):
                if t >= 2:
                    sy.wait_ge(e.dve.h, tile_done[t - 2])
                e.ldx.inc(sy.dma_start(out=uss[t % 2][:], in_=e.dtile(e.us, t)), 16)
                e.ldx2.inc(sy.dma_start(out=xss[t % 2][:], in_=e.dtile(x_in, t)), 16)

            @block.tensor
            def _(tt_, t=t, row_r=row_r):
                tt_.wait_ge(e.row.h, row_r)
                if t >= 2:
                    tt_.wait_ge(e.dve.h, tile_done[t - 2])
                e.pes.inc(tt_.matmul(pR2[t % 2][:], e.onesB[:],
                                     e.r_A[:, t * TT:(t + 1) * TT],
                                     start=True, stop=True), 1)
            pe_r2 = e.pes.n

            @block.vector
            def _(v, t=t, pe_r2=pe_r2):
                v.wait_ge(e.ldx.h, b_ldx + 16 * (t + 1))
                v.wait_ge(e.ldx2.h, b_ldx2 + 16 * (t + 1))
                v.wait_ge(e.pes.h, pe_r2)
                for dc in range(DC):
                    e.dve.inc(v.scalar_tensor_tensor(
                        u2b[t % 2][:, dc, :], uss[t % 2][:, dc, :],
                        e.n2c[:, dc:dc + 1], pR2[t % 2][:], ALU.mult, ALU.mult), 1)
                    e.dve.inc(v.scalar_tensor_tensor(
                        x1b[t % 2][:, dc, :], uss[t % 2][:, dc, :], 0.5,
                        xss[t % 2][:, dc, :], ALU.mult, ALU.add), 1)
                tile_done[t] = e.dve.n
            td = tile_done[t]

            @block.sync
            def _(sy, t=t, td=td):
                sy.wait_ge(e.dve.h, td)
                e.stw.inc(sy.dma_start(out=e.dtile(e.u2s, t), in_=u2b[t % 2][:]), 16)
                e.stw.inc(sy.dma_start(out=e.dtile(x_out, t), in_=x1b[t % 2][:]), 16)

            @block.tensor
            def _(tt_, t=t, td=td):
                tt_.wait_ge(e.dve.h, td)
                tt_.wait_ge(e.ldc.h, e.ldc_done)
                tt_.wait_ge(e.ldg.h, e.ldg_done)
                if t >= 2:
                    tt_.wait_ge(e.act.h, lg_evac[t - 2])
                for dc in range(DC):
                    mm = tt_.matmul(plg[t % 2][:], e.gateT[:, dc, :],
                                    u2b[t % 2][:, dc, :],
                                    start=(dc == 0), stop=(dc == DC - 1))
                e.pes.inc(mm, 1)
            lg_pe = e.pes.n

            @block.vector
            def _(v, t=t, lg_pe=lg_pe):
                v.wait_ge(e.pes.h, lg_pe)
                e.act.inc(v.tensor_copy(logit[:, t * TT:(t + 1) * TT],
                                        plg[t % 2][:]), 1)
                lg_evac[t] = e.act.n
        lg_all = e.act.n
        base_done = e.stw.n

        row_evac = e.gat.n
        for i in range(NTOK // 128):
            i128 = slice(i * 128, (i + 1) * 128)

            @block.tensor
            def _(tt_, i=i, i128=i128, lg_all=lg_all, row_evac=row_evac):
                tt_.wait_ge(e.act.h, lg_all)
                tt_.wait_ge(e.gat.h, row_evac)
                if i >= 1:
                    tt_.wait_ge(e.dve.h, comb_prev)
                e.pes.inc(tt_.transpose(ptr[:], logit[:, i128],
                                        e.identF[0:8, 0:8]), 1)
            trd = e.pes.n

            @block.vector
            def _(v, trd=trd):
                v.wait_ge(e.pes.h, trd)
                e.dve.inc(v.tensor_copy(lgt[:], ptr[:]), 1)
                v.wait_ge(e.dve.h, e.dve.n)
                e.dve.inc(v.max(out=top[:], in_=lgt[:]), 1)
                v.wait_ge(e.dve.h, e.dve.n)
                e.dve.inc(v.tensor_tensor(gw[:, 0:1], top[:, 1:2], top[:, 0:1],
                                          ALU.subtract), 1)
            gd1 = e.dve.n

            @block.scalar
            def _(sc, gd1=gd1):
                sc.wait_ge(e.dve.h, gd1)
                e.act.inc(sc.activation(gw[:, 1:2], gw[:, 0:1], AF.Exp,
                                        bias=0.0, scale=1.0), 1)
            gd2 = e.act.n

            @block.vector
            def _(v, gd2=gd2):
                v.wait_ge(e.act.h, gd2)
                e.dve.inc(v.tensor_scalar(gw[:, 2:3], gw[:, 1:2], 1.0, None,
                                          ALU.add), 1)
                v.wait_ge(e.dve.h, e.dve.n)
                e.dve.inc(v.reciprocal(gw[:, 3:4], gw[:, 2:3]), 1)
                v.wait_ge(e.dve.h, e.dve.n)
                e.dve.inc(v.tensor_scalar(gw[:, 4:5], gw[:, 3:4], 0.5, None,
                                          ALU.mult), 1)
                v.wait_ge(e.dve.h, e.dve.n)
                e.dve.inc(v.tensor_tensor(gw[:, 5:6], gw[:, 1:2], gw[:, 4:5],
                                          ALU.mult), 1)
                e.dve.inc(v.tensor_scalar(msk[:, 0, :], lgt[:], top[:, 0:1], None,
                                          ALU.is_equal), 1)
                e.dve.inc(v.tensor_scalar(msk[:, 2, :], lgt[:], top[:, 1:2], None,
                                          ALU.is_equal), 1)
                v.wait_ge(e.dve.h, e.dve.n)
                e.dve.inc(v.tensor_scalar(msk[:, 1, :], msk[:, 0, :], -1.0, 1.0,
                                          ALU.mult, ALU.add), 1)
                v.wait_ge(e.dve.h, e.dve.n)
                e.dve.inc(v.tensor_tensor(msk[:, 2, :], msk[:, 2, :],
                                          msk[:, 1, :], ALU.mult), 1)
                v.wait_ge(e.dve.h, e.dve.n)
                e.dve.inc(v.tensor_scalar(msk[:, 3, :], msk[:, 0, :], gw[:, 4:5],
                                          None, ALU.mult), 1)
                v.wait_ge(e.dve.h, e.dve.n)
                e.dve.inc(v.scalar_tensor_tensor(msk[:, 3, :], msk[:, 2, :],
                                                 gw[:, 5:6], msk[:, 3, :],
                                                 ALU.mult, ALU.add), 1)
            comb_done = e.dve.n
            comb_prev = comb_done

            for ex in range(NE):
                @block.tensor
                def _(tt_, ex=ex, comb_done=comb_done, row_evac=row_evac):
                    tt_.wait_ge(e.dve.h, comb_done)
                    tt_.wait_ge(e.gat.h, row_evac)
                    e.pes.inc(tt_.transpose(prow[:], msk.ap()[:, 3, ex:ex + 1],
                                            e.identF[:]), 1)
                tr1 = e.pes.n

                @block.vector
                def _(v, ex=ex, i=i, tr1=tr1):
                    v.wait_ge(e.pes.h, tr1)
                    e.gat.inc(v.tensor_copy(e.combR[ex][:, i * 128:(i + 1) * 128],
                                            prow[:]), 1)
                row_evac = e.gat.n
        ST.pop_all().close()
    return base_done


def _emit_moe(block, e, x_out, base_done):
    _barrier(block, e)
    nc = e.nc
    wk = e.wk
    comb_ready = e.gat.n
    with ExitStack() as ST:
        en = ST.enter_context
        w12 = [en(nc.sbuf_tensor(_un(f"m_w12{i}"), [128, DC, 2048], BF16)) for i in range(2)]
        w3s = [en(nc.sbuf_tensor(_un(f"m_w3s{i}"), [128, DC, 1024], BF16)) for i in range(2)]
        u2t = [en(nc.sbuf_tensor(_un(f"m_u2t{i}"), [128, DC, TT], BF16)) for i in range(2)]
        s1b = [en(nc.sbuf_tensor(_un(f"m_s1b{i}"), [128, TT], BF16)) for i in range(2)]
        s2b = [en(nc.sbuf_tensor(_un(f"m_s2b{i}"), [128, TT], BF16)) for i in range(2)]
        hsb = en(nc.sbuf_tensor(_un("m_hsb"), [128, DC, TT], BF16))
        ybf = [en(nc.sbuf_tensor(_un(f"m_ybf{i}"), [128, TT], F32)) for i in range(2)]
        cbs = en(nc.sbuf_tensor(_un("m_cbs"), [128, TT], BF16))
        b3u = [en(nc.sbuf_tensor(_un(f"m_b3u{i}"), [97, 1024], BF16)) for i in range(2)]
        onesB3 = en(nc.sbuf_tensor(_un("m_onesB3"), [97, 128], BF16))
        pH1 = [en(nc.psum_tensor(_un(f"m_pH1{i}"), [128, TT], F32)) for i in range(2)]
        pH2 = [en(nc.psum_tensor(_un(f"m_pH2{i}"), [128, TT], F32)) for i in range(2)]
        pY = [en(nc.psum_tensor(_un(f"m_pY{i}"), [128, TT], F32)) for i in range(2)]
        pcb = en(nc.psum_tensor(_un("m_pcb"), [128, TT], F32))

        @block.vector
        def _(v):
            e.dve.inc(v.memset(onesB3[:], 1.0), 1)
        ob3_done = e.dve.n

        w12_used = {}
        w3_used = {}
        u2_used = {}
        silu_t = {}
        s2g_t = {}
        h_used = {}
        y_acc = {}
        y_evac = {}
        hsb_used = {}
        cb_guard = ("act", e.act.n)

        def semh(name):
            return getattr(e, name).h

        ldw12_t = {}
        ldw3_t = {}

        def emit_loads(ui):
            ex_, hf_ = divmod(ui, 2)

            @block.gpsimd
            def _(g, ex=ex_, hf=hf_, ui=ui):
                if ui >= 2:
                    g.wait_ge(e.pes.h, w12_used[ui - 2])
                else:
                    g.wait_ge(e.ldc.h, e.ldc_done)
                    g.wait_ge(e.ldg.h, e.ldg_done)
                e.ldw.inc(g.dma_start(
                    out=w12[ui % 2][:, :, 0:1024],
                    in_=wk["w1T"][ex, :, hf * 1024:(hf + 1) * 1024]
                    .rearrange("(dc p) f -> p dc f", p=128)), 16)
                e.ldw.inc(g.dma_start(
                    out=w12[ui % 2][:, :, 1024:2048],
                    in_=wk["w2T"][ex, :, hf * 1024:(hf + 1) * 1024]
                    .rearrange("(dc p) f -> p dc f", p=128)), 16)
            ldw12_t[ui] = e.ldw.n

            @block.gpsimd
            def _(g, ex=ex_, hf=hf_, ui=ui):
                if ui >= 2:
                    g.wait_ge(e.pes.h, w3_used[ui - 2])
                e.ldw.inc(g.dma_start(
                    out=w3s[ui % 2][:],
                    in_=wk["w3T"][ex, hf * 1024:(hf + 1) * 1024, :]
                    .rearrange("(hc p) f -> p hc f", p=128)), 16)
                for rb in (0, 32, 64):
                    e.ldw.inc(g.dma_start(
                        out=b3u[ui % 2][rb:rb + 1, :],
                        in_=wk["b3r"][0:1, ex * 1024:(ex + 1) * 1024]), 16)
            ldw3_t[ui] = e.ldw.n

        ui = -1
        for ex in range(NE):
            for hf in range(2):
                ui += 1
                if ui == 0:
                    emit_loads(0)
                ldw12 = ldw12_t[ui]
                ldw3 = ldw3_t[ui]

                for t in range(NT):
                    ti = ui * NT + t
                    if t == 1 and ui + 1 < NE * 2:
                        emit_loads(ui + 1)

                    @block.sync
                    def _(sy, t=t, ti=ti):
                        if ti >= 2:
                            sy.wait_ge(e.pes.h, u2_used[ti - 2])
                        e.ldu.inc(sy.dma_start(out=u2t[t % 2][:],
                                               in_=e.dtile(e.u2s, t)), 16)
                    ldu_t = e.ldu.n

                    g0, g1 = cb_guard

                    @block.tensor
                    def _(tt_, ex=ex, t=t, g0=g0, g1=g1):
                        tt_.wait_ge(e.gat.h, comb_ready)
                        tt_.wait_ge(semh(g0), g1)
                        tt_.wait_ge(e.dve.h, ob3_done)
                        rb = (ex % 3) * 32
                        e.pes.inc(tt_.matmul(
                            pcb[:], onesB3[rb:rb + 1, 0:128],
                            e.combR[ex][:, t * TT:(t + 1) * TT],
                            start=True, stop=True), 1)
                    cb_pe = e.pes.n

                    @block.vector
                    def _(v, cb_pe=cb_pe):
                        v.wait_ge(e.pes.h, cb_pe)
                        e.act.inc(v.tensor_copy(cbs[:], pcb[:]), 1)
                    cbs_done = e.act.n
                    cb_guard = ("act", cbs_done)

                    exs0 = e.exs.n
                    gat0 = e.gat.n
                    pool0 = e.pool.n
                    for hcq in range(DC):
                        hci = ti * DC + hcq
                        silu_t[hci] = exs0 + hcq + 1
                        s2g_t[hci] = gat0 + hcq + 1
                        h_used[hci] = pool0 + hcq + 1
                    h_pe = {}
                    for hcq in range(DC):
                        hci = ti * DC + hcq

                        @block.tensor
                        def _(tt_, hcq=hcq, t=t, ui=ui, ldw12=ldw12,
                              ldu_t=ldu_t, hci=hci):
                            tt_.wait_ge(e.ldw.h, ldw12)
                            tt_.wait_ge(e.ldu.h, ldu_t)
                            if hci >= 2:
                                tt_.wait_ge(e.exs.h, silu_t[hci - 2])
                                tt_.wait_ge(e.gat.h, s2g_t[hci - 2])
                            for dc in range(DC):
                                mm = tt_.matmul(pH1[hcq % 2][:],
                                                w12[ui % 2][:, dc, hcq * 128:(hcq + 1) * 128],
                                                u2t[t % 2][:, dc, :],
                                                start=(dc == 0), stop=(dc == DC - 1))
                            e.pes.inc(mm, 1)
                            for dc in range(DC):
                                mm = tt_.matmul(pH2[hcq % 2][:],
                                                w12[ui % 2][:, dc, 1024 + hcq * 128:1024 + (hcq + 1) * 128],
                                                u2t[t % 2][:, dc, :],
                                                start=(dc == 0), stop=(dc == DC - 1))
                            e.pes.inc(mm, 1)
                        h_pe[hcq] = e.pes.n

                    @block.scalar
                    def _(sc, ti=ti, ex=ex, hf=hf, h_pe=h_pe):
                        for hcq in range(DC):
                            hci = ti * DC + hcq
                            sc.wait_ge(e.pes.h, h_pe[hcq] - 1)
                            if hci >= 2:
                                sc.wait_ge(e.pool.h, h_used[hci - 2])
                            sc.activation(
                                s1b[hcq % 2][:], pH1[hcq % 2][:], AF.Silu,
                                bias=e.b1c.ap()[:, ex, hf * 8 + hcq:hf * 8 + hcq + 1],
                                scale=1.0).then_inc(e.exs.h, 1)
                            e.exs.n += 1
                    assert e.exs.n == exs0 + DC

                    for hcq in range(DC):
                        hci = ti * DC + hcq

                        @block.vector
                        def _(v, hcq=hcq, ex=ex, hf=hf, hci=hci,
                              cbs_done=cbs_done, h_pe=h_pe):
                            v.wait_ge(e.pes.h, h_pe[hcq])
                            v.wait_ge(e.act.h, cbs_done)
                            if hci >= 2:
                                v.wait_ge(e.pool.h, h_used[hci - 2])
                            e.gat.inc(v.scalar_tensor_tensor(
                                s2b[hcq % 2][:], pH2[hcq % 2][:],
                                e.b2c.ap()[:, ex, hf * 8 + hcq:hf * 8 + hcq + 1],
                                cbs[:], ALU.add, ALU.mult), 1)

                        @block.gpsimd
                        def _(g, hcq=hcq, ti=ti, hci=hci):
                            g.wait_ge(e.exs.h, silu_t[hci])
                            g.wait_ge(e.gat.h, s2g_t[hci])
                            if ti >= 1 and hcq == 0:
                                g.wait_ge(e.pes.h, hsb_used.get(ti - 1, 0))
                            e.pool.inc(g.tensor_tensor(hsb[:, hcq, :],
                                                       s1b[hcq % 2][:],
                                                       s2b[hcq % 2][:], ALU.mult), 1)
                    assert e.gat.n == gat0 + DC
                    assert e.pool.n == pool0 + DC
                    h_all = e.pool.n

                    for dcq in range(DC):
                        yi = ti * DC + dcq

                        @block.tensor
                        def _(tt_, dcq=dcq, ti=ti, ui=ui, ex=ex, t=t,
                              h_all=h_all, yi=yi, ldw3=ldw3):
                            tt_.wait_ge(e.pool.h, h_all)
                            tt_.wait_ge(e.ldw.h, ldw3)
                            if yi >= 2:
                                tt_.wait_ge(e.act.h, y_evac[yi - 2])
                            for hc in range(DC):
                                tt_.matmul(pY[dcq % 2][:],
                                           w3s[ui % 2][:, hc, dcq * 128:(dcq + 1) * 128],
                                           hsb[:, hc, :],
                                           start=(hc == 0), stop=False)
                            rb = (ex % 3) * 32
                            mm = tt_.matmul(
                                pY[dcq % 2][:],
                                b3u[ui % 2][rb:rb + 1, dcq * 128:(dcq + 1) * 128],
                                e.combR[ex][:, t * TT:(t + 1) * TT],
                                start=False, stop=True)
                            e.pes.inc(mm, 1)
                        y_pe = e.pes.n

                        @block.vector
                        def _(v, dcq=dcq, y_pe=y_pe, yi=yi):
                            v.wait_ge(e.pes.h, y_pe)
                            if yi >= 2:
                                v.wait_ge(e.acc.h, y_acc.get(yi - 2, 0))
                            e.act.inc(v.tensor_copy(ybf[dcq % 2][:],
                                                    pY[dcq % 2][:]), 1)
                            y_evac[yi] = e.act.n

                        @block.gpsimd
                        def _(g, dcq=dcq, t=t, yi=yi):
                            g.wait_ge(e.act.h, y_evac[yi])
                            g.wait_ge(e.stw.h, base_done)
                            g.wait_ge(e.acc.h, y_acc.get(yi - NT * DC, 0))
                            e.acc.inc(g.dma_start(
                                out=e.dtile(x_out, t)[:, dcq:dcq + 1, :],
                                in_=ybf[dcq % 2][:],
                                accum_op=ALU.add), 16)
                            y_acc[yi] = e.acc.n
                    hsb_used[ti] = e.pes.n
                    u2_used[ti] = e.pes.n
                w12_used[ui] = e.pes.n
                w3_used[ui] = e.pes.n
        ST.pop_all().close()


def _emit_mla(block, e, x_in, x_out):
    nc = e.nc
    with ExitStack() as SM:
        with ExitStack() as SS:
            ens = SS.enter_context
            for rn in ("r_a", "r_b"):
                setattr(e, rn, ens(nc.sbuf_tensor(_un("b_") + rn, [1, NTOK], BF16))[:])
            for rn in ("r_m", "rs1", "rs2", "rs3"):
                setattr(e, rn, ens(nc.sbuf_tensor(_un("b_") + rn, [1, NTOK], F32))[:])
            e.r_ms = ens(nc.sbuf_tensor(_un("b_r_ms"), [1, NTOK], BF16))[:]
            _emit_stats(block, e, x_in, with_mean=True)
            _emit_ln_rows(block, e, compose_rms=False)

            @block.sync
            def _(sy):
                sy.wait_ge(e.row.h, e.row.n)
                e.stw.inc(sy.dma_start(out=e.mrs[0:1, 0:NTOK], in_=e.r_a), 16)
                e.stw.inc(sy.dma_start(out=e.mrs[0:1, NTOK:2 * NTOK],
                                       in_=e.r_b), 16)
            SS.pop_all().close()

        for s_ in range(BPC):
            _emit_mla_seq(block, e, x_in, x_out, s_)
        _drain_phase(block, e)
        SM.pop_all().close()


def _emit_mla_seq(block, e, x_in, x_out, s_):
    _barrier(block, e)
    nc = e.nc
    wk = e.wk
    t0 = s_ * NTS
    if s_ == 0:
        @block.sync
        def _(sy):
            e.stw.inc(sy.dma_start(out=x_out, in_=x_in), 16)
        e.x2copy_done = e.stw.n
    with ExitStack() as ST:
        en = ST.enter_context
        qc = en(nc.sbuf_tensor(_un("l_qc"), [128, 8, S], BF16))
        kc = en(nc.sbuf_tensor(_un("l_kc"), [128, 8, S], BF16))
        qr = en(nc.sbuf_tensor(_un("l_qr"), [128, 8, S], BF16))
        krm = en(nc.sbuf_tensor(_un("l_krm"), [64, S], BF16))
        vtm = en(nc.sbuf_tensor(_un("l_vtm"), [128, 16, 16, 65], BF16))
        nbcm = en(nc.sbuf_tensor(_un("l_nbcm"), [128, 16], F32))

        with ExitStack() as SP:
            enp = SP.enter_context
            dqw = enp(nc.sbuf_tensor(_un("l_dqw"), [128, DC, 256], BF16))
            dkw = enp(nc.sbuf_tensor(_un("l_dkw"), [128, DC, 256], BF16))
            krw = enp(nc.sbuf_tensor(_un("l_krw"), [128, DC, 32], BF16))
            cm2 = enp(nc.sbuf_tensor(_un("l_cm2"), [64, S], BF16))
            sm2 = enp(nc.sbuf_tensor(_un("l_sm2"), [64, S], BF16))
            cq = enp(nc.sbuf_tensor(_un("l_cq"), [128, 2, S], BF16))
            ckv = enp(nc.sbuf_tensor(_un("l_ckv"), [128, 2, S], BF16))
            rt = [enp(nc.sbuf_tensor(_un(f"l_rt{i}"), [128, TT], F32)) for i in range(2)]
            krr = enp(nc.sbuf_tensor(_un("l_krr"), [32, TT], BF16))
            wsrb = enp(nc.sbuf_tensor(_un("l_wsrb"), [1, 576], BF16))
            abb = enp(nc.sbuf_tensor(_un("l_abb"), [128, TT], BF16))

            @block.gpsimd
            def _(g):
                g.wait_ge(e.ldc.h, e.ldc_done)
                g.wait_ge(e.ldg.h, e.ldg_done)
                for src, dst in [
                    (wk["m_W_DQ"].rearrange("(p2 p) f -> p p2 f", p=128), dqw[:]),
                    (wk["m_W_DKV"].rearrange("(p2 p) f -> p p2 f", p=128), dkw[:]),
                    (wk["m_W_KR"].rearrange("(p2 p) f -> p p2 f", p=128), krw[:]),
                    (wk["cm2"], cm2[:]),
                    (wk["sm2"], sm2[:]),
                ]:
                    e.ldw.inc(g.dma_start(out=dst, in_=src), 16)
            ldw_done = e.ldw.n

            @block.vector
            def _(v):
                v.wait_ge(e.ldc.h, e.ldc_done)
                v.wait_ge(e.ldg.h, e.ldg_done)
                e.dve.inc(v.memset(vtm.ap()[:, :, :, 64:65], 1.0), 1)
            vones = e.dve.n

            with ExitStack() as SX:
                enx = SX.enter_context
                xb = [enx(nc.sbuf_tensor(_un(f"l_xb{i}"), [128, DC, TT], BF16))
                      for i in range(2)]
                rab = [enx(nc.sbuf_tensor(_un(f"l_rab{i}"), [1, 1024], BF16))
                       for i in range(2)]
                pAB = enx(nc.psum_tensor(_un("l_pAB"), [128, TT], F32))
                pbw = [enx(nc.psum_tensor(_un(f"l_pbw{i}"), [128, TT], F32))
                       for i in range(2)]
                pcq = [enx(nc.psum_tensor(_un(f"l_pcq{i}"), [128, TT], F32))
                       for i in range(2)]
                pkr = enx(nc.psum_tensor(_un("l_pkr"), [32, TT], F32))
                par = enx(nc.psum_tensor(_un("l_par"), [64, TT], F32))

                @block.tensor
                def _(tt_, ldw_done=ldw_done):
                    tt_.wait_ge(e.ldw.h, ldw_done)
                    for wi in range(2):
                        w = dqw if wi == 0 else dkw
                        for dc in range(DC):
                            tt_.matmul(pbw[0][0:1, wi * 256:wi * 256 + 256],
                                       e.onesC1[:, 0:1], w[:, dc, :],
                                       start=(dc == 0), stop=(dc == DC - 1))
                    for dc in range(DC):
                        mm = tt_.matmul(pbw[1][0:1, 0:32], e.onesC1[:, 0:1],
                                        krw[:, dc, :],
                                        start=(dc == 0), stop=(dc == DC - 1))
                    e.pes.inc(mm, 1)
                ws_pe = e.pes.n

                @block.vector
                def _(v, ws_pe=ws_pe):
                    v.wait_ge(e.pes.h, ws_pe)
                    e.dve.inc(v.tensor_copy(wsrb[0:1, 0:512],
                                            pbw[0][0:1, 0:512]), 1)
                    e.dve.inc(v.tensor_copy(wsrb[0:1, 512:544],
                                            pbw[1][0:1, 0:32]), 1)
                ws_done = e.dve.n

                xb_used = {}
                cq_done = {}
                kr_rope_prev = 0
                for ts in range(NTS):
                    t = t0 + ts
                    sl = slice(t * TT, (t + 1) * TT)
                    ssl = slice(ts * TT, (ts + 1) * TT)

                    @block.gpsimd
                    def _(g, ts=ts, t=t, ldw_done=ldw_done):
                        if ts >= 2:
                            g.wait_ge(e.pes.h, xb_used[ts - 2])
                        else:
                            g.wait_ge(e.ldw.h, ldw_done)
                        e.ldw.inc(g.dma_start(out=xb[ts % 2][:],
                                              in_=e.dtile(x_in, t)), 16)
                    ldx_t = e.ldw.n

                    @block.sync
                    def _(sy, ts=ts, t=t):
                        if ts >= 2:
                            sy.wait_ge(e.pes.h, xb_used[ts - 2])
                        e.ldx.inc(sy.dma_start(
                            out=rab[ts % 2][0:1, 0:TT],
                            in_=e.mrs[0:1, t * TT:(t + 1) * TT]), 16)
                        e.ldx.inc(sy.dma_start(
                            out=rab[ts % 2][0:1, TT:2 * TT],
                            in_=e.mrs[0:1, NTOK + t * TT:NTOK + (t + 1) * TT]), 16)
                    ldr_t = e.ldx.n

                    @block.tensor
                    def _(tt_, ts=ts, ldr_t=ldr_t):
                        tt_.wait_ge(e.ldx.h, ldr_t)
                        if ts >= 1:
                            tt_.wait_ge(e.dve.h, cq_done[ts * 5 - 1])
                        else:
                            tt_.wait_ge(e.dve.h, ws_done)
                        e.pes.inc(tt_.matmul(pAB[:], e.onesB[:],
                                             rab[ts % 2][0:1, 0:TT],
                                             start=True, stop=True), 1)
                    ab_pe = e.pes.n

                    @block.vector
                    def _(v, ab_pe=ab_pe):
                        v.wait_ge(e.pes.h, ab_pe)
                        e.dve.inc(v.tensor_copy(abb[:], pAB[:]), 1)
                    ab_done = e.dve.n

                    for j in range(5):
                        aj = ts * 5 + j

                        @block.tensor
                        def _(tt_, j=j, ts=ts, sl=sl, aj=aj, ldx_t=ldx_t,
                              ldw_done=ldw_done):
                            tt_.wait_ge(e.ldw.h, ldx_t)
                            tt_.wait_ge(e.ldw.h, ldw_done)
                            if aj >= 2:
                                tt_.wait_ge(e.dve.h, cq_done[aj - 2])
                            if j < 4:
                                w = dqw if j < 2 else dkw
                                lc = j % 2
                                for dc in range(DC):
                                    tt_.matmul(pcq[aj % 2][:],
                                               w[:, dc, lc * 128:(lc + 1) * 128],
                                               xb[ts % 2][:, dc, :],
                                               start=(dc == 0), stop=(dc == DC - 1))
                                wo = (0 if j < 2 else 256) + lc * 128
                                mm = tt_.matmul(pbw[aj % 2][:],
                                                wsrb[0:1, wo:wo + 128],
                                                rab[ts % 2][0:1, TT:2 * TT],
                                                start=True, stop=True)
                            else:
                                for dc in range(DC):
                                    tt_.matmul(pkr[:], krw[:, dc, :],
                                               xb[ts % 2][:, dc, :],
                                               start=(dc == 0), stop=(dc == DC - 1))
                                mm = tt_.matmul(pbw[aj % 2][0:32, :],
                                                wsrb[0:1, 512:544],
                                                rab[ts % 2][0:1, TT:2 * TT],
                                                start=True, stop=True)
                            e.pes.inc(mm, 1)
                        j_pe = e.pes.n

                        @block.vector
                        def _(v, j=j, ts=ts, ssl=ssl, j_pe=j_pe, aj=aj,
                              ab_done=ab_done, kr_rope_prev=kr_rope_prev):
                            v.wait_ge(e.pes.h, j_pe)
                            v.wait_ge(e.dve.h, ab_done)
                            if j == 4 and ts >= 1:
                                v.wait_ge(e.dve.h, kr_rope_prev)
                            if j < 4:
                                dst = (cq if j < 2 else ckv)[:, j % 2, ssl]
                                e.dve.inc(v.tensor_tensor(rt[aj % 2][:],
                                                          pcq[aj % 2][:],
                                                          abb[:], ALU.mult), 1)
                                v.wait_ge(e.dve.h, e.dve.n)
                                e.dve.inc(v.tensor_tensor(dst, rt[aj % 2][:],
                                                          pbw[aj % 2][:],
                                                          ALU.add), 1)
                            else:
                                e.dve.inc(v.tensor_tensor(rt[aj % 2][0:32, :],
                                                          pkr[:],
                                                          abb[0:32, :],
                                                          ALU.mult), 1)
                                v.wait_ge(e.dve.h, e.dve.n)
                                e.dve.inc(v.tensor_tensor(krr[:],
                                                          rt[aj % 2][0:32, :],
                                                          pbw[aj % 2][0:32, :],
                                                          ALU.add), 1)
                            cq_done[aj] = e.dve.n
                    xb_used[ts] = e.pes.n
                    krr_done = cq_done[ts * 5 + 4]

                    @block.tensor
                    def _(tt_, krr_done=krr_done, kr_rope_prev=kr_rope_prev):
                        tt_.wait_ge(e.dve.h, krr_done)
                        tt_.wait_ge(e.dve.h, kr_rope_prev)
                        e.pes.inc(tt_.matmul(par[0:32, :], e.a1T[:], krr[:],
                                             start=True, stop=True), 1)
                    akr_pe = e.pes.n

                    @block.vector
                    def _(v, ssl=ssl, akr_pe=akr_pe, krr_done=krr_done):
                        v.wait_ge(e.pes.h, akr_pe)
                        e.dve.inc(v.tensor_tensor(rt[0][0:32, :], krr[:],
                                                  cm2[0:32, ssl], ALU.mult), 1)
                        e.dve.inc(v.tensor_tensor(rt[1][0:32, :], par[0:32, :],
                                                  sm2[0:32, ssl], ALU.mult), 1)
                        v.wait_ge(e.dve.h, e.dve.n)
                        e.dve.inc(v.tensor_tensor(krm[0:32, ssl], rt[0][0:32, :],
                                                  rt[1][0:32, :], ALU.add), 1)
                        e.dve.inc(v.tensor_tensor(krm[32:64, ssl], rt[0][0:32, :],
                                                  rt[1][0:32, :], ALU.add), 1)
                    kr_rope_prev = e.dve.n

                cq_all = e.dve.n
                kr_all = e.dve.n
                proj1_pe = e.pes.n
                SX.pop_all().close()

            with ExitStack() as S2:
                en2 = S2.enter_context
                qrw = en2(nc.sbuf_tensor(_un("l_qrw"), [128, 2, 512], BF16))
                wuc = [en2(nc.sbuf_tensor(_un(f"l_wuc{i}"), [128, 2, 128], BF16))
                       for i in range(2)]
                wvv = en2(nc.sbuf_tensor(_un("l_wvv"), [128, 2, 1024], BF16))
                sqb2 = [en2(nc.sbuf_tensor(_un(f"l_sqb2{i}"), [128, TT], BF16))
                        for i in range(2)]
                qc2s = en2(nc.sbuf_tensor(_un("l_qc2s"), [1, 64], F32))
                kc2s = en2(nc.sbuf_tensor(_un("l_kc2s"), [1, 64], F32))
                qr2s = en2(nc.sbuf_tensor(_un("l_qr2s"), [1, 64], F32))
                kr2s = en2(nc.sbuf_tensor(_un("l_kr2s"), [1, 4], F32))
                qm2m = en2(nc.sbuf_tensor(_un("l_qm2m"), [1, 16], F32))
                km2m = en2(nc.sbuf_tensor(_un("l_km2m"), [1, 16], F32))
                bpm = en2(nc.sbuf_tensor(_un("l_bpm"), [1, 16], F32))
                negbm = en2(nc.sbuf_tensor(_un("l_negbm"), [1, 16], F32))
                kr2m = en2(nc.sbuf_tensor(_un("l_kr2m"), [1, 1], F32))
                pAB2 = en2(nc.psum_tensor(_un("l_pAB2"), [128, TT], F32))
                pBB2 = en2(nc.psum_tensor(_un("l_pBB2"), [128, TT], F32))
                pcq2 = [en2(nc.psum_tensor(_un(f"l_pcq2{i}"), [128, TT], F32))
                        for i in range(2)]
                pqr2 = en2(nc.psum_tensor(_un("l_pqr2"), [128, TT], F32))
                par2 = en2(nc.psum_tensor(_un("l_par2"), [64, TT], F32))
                pvm2 = en2(nc.psum_tensor(_un("l_pvm2"), [128, TT], F32))

                @block.gpsimd
                def _(g, proj1_pe=proj1_pe, cq_all=cq_all):
                    g.wait_ge(e.pes.h, proj1_pe)
                    g.wait_ge(e.dve.h, cq_all)
                    e.ldw.inc(g.dma_start(
                        out=qrw[:],
                        in_=wk["m_W_QR"].rearrange("(p2 p) f -> p p2 f", p=128)), 16)
                    e.ldw.inc(g.dma_start(
                        out=wvv[:],
                        in_=wk["m_W_UV"].rearrange("(p2 p) f -> p p2 f", p=128)), 16)
                qv_done = e.ldw.n

                nrm_pe2 = {}
                nrm_red2 = {}
                pend = []

                def norm_chunk(ni, src, P, halves, guards):
                    @block.gpsimd
                    def _(g, ni=ni, src=src, P=P, guards=guards):
                        for (sh, sv) in guards:
                            g.wait_ge(sh, sv)
                        if ni >= 2:
                            g.wait_ge(e.pes.h, nrm_pe2[ni - 2])
                        e.pool.inc(g.tensor_tensor(sqb2[ni % 2][0:P, :], src, src,
                                                   ALU.mult), 1)
                    sq_t = e.pool.n

                    def emit(ni=ni, halves=halves, sq_t=sq_t):
                        p = pAB2 if ni % 2 == 0 else pBB2

                        @block.tensor
                        def _(tt_):
                            tt_.wait_ge(e.pool.h, sq_t)
                            if ni >= 2:
                                tt_.wait_ge(e.dve.h, nrm_red2[ni - 2])
                            for (b, kext, arr, slot) in halves:
                                mm = tt_.matmul(p[b:b + 1, :],
                                                e.onesC1[b:b + kext, 0:1],
                                                sqb2[ni % 2][b:b + kext, :],
                                                start=True, stop=True)
                            e.pes.inc(mm, 1)
                        nrm_pe2[ni] = e.pes.n

                        @block.vector
                        def _(v):
                            v.wait_ge(e.pes.h, nrm_pe2[ni])
                            for (b, kext, arr, slot) in halves:
                                e.dve.inc(v.reduce_max(arr[0:1, slot:slot + 1],
                                                       p[b:b + 1, :], axis=AX), 1)
                        nrm_red2[ni] = e.dve.n
                    pend.append(emit)
                    if len(pend) > 1:
                        pend.pop(0)()

                qc_evac = {}
                v_evac = {}
                wuc_used = {}
                qr_rope = kr_all
                for j in range(16):
                    @block.gpsimd
                    def _(g, j=j, proj1_pe=proj1_pe):
                        if j >= 2:
                            g.wait_ge(e.pes.h, wuc_used[j - 2])
                        else:
                            g.wait_ge(e.pes.h, proj1_pe)
                        src = wk["m_W_UQ"] if j < 8 else wk["m_W_UK"]
                        hcj = j % 8
                        e.ldw.inc(g.dma_start(
                            out=wuc[j % 2][:],
                            in_=src[:, hcj * 128:(hcj + 1) * 128]
                            .rearrange("(lc p) f -> p lc f", p=128)), 16)
                    wl = e.ldw.n
                    for ts in range(NTS):
                        bj = j * 4 + ts
                        ssl = slice(ts * TT, (ts + 1) * TT)

                        @block.tensor
                        def _(tt_, j=j, ts=ts, ssl=ssl, cq_all=cq_all, bj=bj,
                              wl=wl):
                            tt_.wait_ge(e.dve.h, cq_all)
                            tt_.wait_ge(e.ldw.h, wl)
                            if bj >= 2:
                                tt_.wait_ge(e.act.h, qc_evac[bj - 2])
                            src = cq if j < 8 else ckv
                            for lc in range(2):
                                mm = tt_.matmul(pcq2[bj % 2][:],
                                                wuc[j % 2][:, lc, :],
                                                src[:, lc, ssl],
                                                start=(lc == 0), stop=(lc == 1))
                            e.pes.inc(mm, 1)
                            if ts == NTS - 1:
                                wuc_used[j] = e.pes.n
                        j_pe = e.pes.n

                        @block.vector
                        def _(v, j=j, ssl=ssl, j_pe=j_pe, bj=bj):
                            v.wait_ge(e.pes.h, j_pe)
                            dst = qc if j < 8 else kc
                            e.act.inc(v.tensor_copy(dst[:, j % 8, ssl],
                                                    pcq2[bj % 2][:]), 1)
                            qc_evac[bj] = e.act.n

                        arr = qc2s if j < 8 else kc2s
                        h0 = (j % 8) * 2
                        norm_chunk(bj, (qc if j < 8 else kc)[:, j % 8, ssl], 128,
                                   [(0, 64, arr, h0 * 4 + ts),
                                    (64, 64, arr, (h0 + 1) * 4 + ts)],
                                   [(e.act.h, qc_evac[bj])])

                for ts in range(NTS):
                    ssl = slice(ts * TT, (ts + 1) * TT)
                    for pc in range(4):
                        @block.tensor
                        def _(tt_, pc=pc, ssl=ssl, cq_all=cq_all,
                              qv_done=qv_done):
                            tt_.wait_ge(e.dve.h, cq_all)
                            tt_.wait_ge(e.ldw.h, qv_done)
                            tt_.wait_ge(e.dve.h, qr_rope)
                            for lc in range(2):
                                mm = tt_.matmul(pqr2[:],
                                                qrw[:, lc, pc * 128:(pc + 1) * 128],
                                                cq[:, lc, ssl],
                                                start=(lc == 0), stop=(lc == 1))
                            e.pes.inc(mm, 1)
                        qr_pe = e.pes.n

                        @block.vector
                        def _(v, pc=pc, ssl=ssl, qr_pe=qr_pe):
                            v.wait_ge(e.pes.h, qr_pe)
                            v.tensor_copy(qr[0:64, 2 * pc, ssl], pqr2[0:64, :])
                            v.wait_ge(e.dve.h, e.dve.n)
                            e.act.inc(v.tensor_copy(qr[0:64, 2 * pc + 1, ssl],
                                                    pqr2[64:128, :]), 1)
                        qre = e.act.n
                        for hh in (2 * pc, 2 * pc + 1):
                            @block.tensor
                            def _(tt_, hh=hh, ssl=ssl, qre=qre):
                                tt_.wait_ge(e.act.h, qre)
                                tt_.wait_ge(e.dve.h, qr_rope)
                                e.pes.inc(tt_.matmul(par2[:], e.a2T[:],
                                                     qr.ap()[0:64, hh, ssl],
                                                     start=True, stop=True), 1)
                            ar_pe = e.pes.n

                            @block.vector
                            def _(v, hh=hh, ssl=ssl, ar_pe=ar_pe):
                                v.wait_ge(e.pes.h, ar_pe)
                                e.dve.inc(v.tensor_tensor(rt[0][0:64, :],
                                                          qr.ap()[0:64, hh, ssl],
                                                          cm2[:, ssl], ALU.mult), 1)
                                e.dve.inc(v.tensor_tensor(rt[1][0:64, :], par2[:],
                                                          sm2[:, ssl], ALU.mult), 1)
                                v.wait_ge(e.dve.h, e.dve.n)
                                e.dve.inc(v.tensor_tensor(qr.ap()[0:64, hh, ssl],
                                                          rt[0][0:64, :],
                                                          rt[1][0:64, :], ALU.add), 1)
                            qr_rope = e.dve.n
                            norm_chunk(64 + ts * 8 + hh, qr.ap()[0:64, hh, ssl], 64,
                                       [(0, 32, qr2s, (2 * hh) * 4 + ts),
                                        (32, 32, qr2s, (2 * hh + 1) * 4 + ts)],
                                       [(e.dve.h, qr_rope)])
                    for st_ in range(4):
                        for vh in range(2):
                            ai = ts * 8 + st_ * 2 + vh

                            @block.tensor
                            def _(tt_, st_=st_, ts=ts, vh=vh, ai=ai,
                                  cq_all=cq_all, qv_done=qv_done):
                                tt_.wait_ge(e.dve.h, cq_all)
                                tt_.wait_ge(e.ldw.h, qv_done)
                                if ai >= 1:
                                    tt_.wait_ge(e.act.h, v_evac[ai - 1])
                                for lc in range(2):
                                    mm = tt_.matmul(
                                        pvm2[:],
                                        ckv[:, lc, ts * TT + st_ * 128:ts * TT + (st_ + 1) * 128],
                                        wvv[:, lc, vh * 512:(vh + 1) * 512],
                                        start=(lc == 0), stop=(lc == 1))
                                e.pes.inc(mm, 1)
                            v_pe = e.pes.n

                            @block.vector
                            def _(v, st_=st_, ts=ts, vh=vh, ai=ai, v_pe=v_pe,
                                  vones=vones):
                                v.wait_ge(e.pes.h, v_pe)
                                v.wait_ge(e.dve.h, vones)
                                kt = ts * 4 + st_
                                e.act.inc(v.tensor_copy(
                                    vtm.ap()[:, kt, vh * 8:(vh + 1) * 8, 0:64],
                                    pvm2[:]), 1)
                                v_evac[ai] = e.act.n

                for ts in range(NTS):
                    norm_chunk(96 + ts, krm[0:32, ts * TT:(ts + 1) * TT], 32,
                               [(0, 32, kr2s, ts)], [(e.dve.h, kr_all)])
                while pend:
                    pend.pop(0)()
                nrm_all = e.dve.n

                @block.vector
                def _(v, nrm_all=nrm_all):
                    v.wait_ge(e.dve.h, nrm_all)
                    for lh in range(16):
                        e.dve.inc(v.reduce_max(qm2m[0:1, lh:lh + 1],
                                               qc2s[0:1, lh * 4:lh * 4 + 4],
                                               axis=AX), 1)
                        e.dve.inc(v.reduce_max(km2m[0:1, lh:lh + 1],
                                               kc2s[0:1, lh * 4:lh * 4 + 4],
                                               axis=AX), 1)
                        e.dve.inc(v.reduce_max(bpm[0:1, lh:lh + 1],
                                               qr2s[0:1, lh * 4:lh * 4 + 4],
                                               axis=AX), 1)
                    e.dve.inc(v.reduce_max(kr2m[:], kr2s[:], axis=AX), 1)
                    v.wait_ge(e.dve.h, e.dve.n)
                    e.dve.inc(v.tensor_tensor(qm2m[:], qm2m[:], bpm[:],
                                              ALU.add), 1)
                    e.dve.inc(v.tensor_scalar(km2m[:], km2m[:], kr2m[0:1, 0:1],
                                              None, ALU.add), 1)
                    v.wait_ge(e.dve.h, e.dve.n)
                    e.dve.inc(v.tensor_tensor(bpm[:], qm2m[:], km2m[:],
                                              ALU.mult), 1)
                bqk_t = e.dve.n

                @block.scalar
                def _(sc, bqk_t=bqk_t):
                    sc.wait_ge(e.dve.h, bqk_t)
                    e.sq.inc(sc.activation(bpm[:], bpm[:], AF.Sqrt,
                                           bias=0.0, scale=1.0), 1)
                sqrt_t = e.sq.n

                @block.vector
                def _(v, sqrt_t=sqrt_t):
                    v.wait_ge(e.sq.h, sqrt_t)
                    e.dve.inc(v.tensor_scalar(negbm[:], bpm[:], 80.0, -SC_MLA,
                                              ALU.min, ALU.mult), 1)
                nbm_t = e.dve.n
                act_now = e.act.n

                @block.tensor
                def _(tt_, nbm_t=nbm_t, act_now=act_now):
                    tt_.wait_ge(e.dve.h, nbm_t)
                    tt_.wait_ge(e.act.h, act_now)
                    for lh in range(16):
                        mm = tt_.matmul(pqr2[:, lh:lh + 1], e.onesF[:],
                                        negbm[0:1, lh:lh + 1],
                                        start=True, stop=True)
                    e.pes.inc(mm, 1)
                nbp_t = e.pes.n

                @block.vector
                def _(v, nbp_t=nbp_t):
                    v.wait_ge(e.pes.h, nbp_t)
                    e.dve.inc(v.tensor_copy(nbcm[:], pqr2[:, 0:16]), 1)
                e.nbcm_ready = e.dve.n

                proj_pe = e.pes.n
                proj_act = e.act.n
                proj_dve = e.dve.n
                S2.pop_all().close()
            SP.pop_all().close()

        _emit_mla_attention(block, e, x_out, s_, qc, kc, qr, krm, vtm,
                            nbcm, proj_pe, proj_act, proj_dve)
        ST.pop_all().close()


def _emit_mla_attention(block, e, x_out, s_, qc, kc, qr, krm, vtm,
                        nbcm, proj_pe, proj_act, proj_dve):
    _barrier(block, e)
    nc = e.nc
    wk = e.wk
    t0 = s_ * NTS
    NHH = 16
    with ExitStack() as ST:
        en = ST.enter_context
        wow = en(nc.sbuf_tensor(_un("l_wow"), [128, 8, D], BF16))
        mask4 = en(nc.sbuf_tensor(_un("l_mask4"), [128, 4, 512], BF16))
        attf = en(nc.sbuf_tensor(_un("l_attf"), [128, 8, TT], BF16))
        psb = [en(nc.sbuf_tensor(_un(f"l_psb{i}"), [128, TT], BF16)) for i in range(4)]
        recb = en(nc.sbuf_tensor(_un("l_recb"), [1, TT], BF16))
        rbs = en(nc.sbuf_tensor(_un("l_rbs"), [64, TT], BF16))
        x2b = en(nc.sbuf_tensor(_un("l_x2b"), [128, DC, TT], F32))
        pSt = [en(nc.psum_tensor(_un(f"b_pSt{i}"), [128, TT], F32)) for i in range(4)]
        pPV = en(nc.psum_tensor(_un("b_pPV"), [65, TT], F32))
        pMi = en(nc.psum_tensor(_un("b_pMi"), [128, TT], F32))

        @block.gpsimd
        def _(g):
            e.ldw.inc(g.dma_start(
                out=wow[:],
                in_=wk["m_W_O"].rearrange("(hc p) f -> p hc f", p=128)), 16)
            e.ldw.inc(g.dma_start(
                out=mask4.ap().rearrange("p r f -> p (r f)"), in_=wk["mask4"]), 16)
        wow_done = e.ldw.n

        pv_free = ("dve", proj_dve)
        pmi_guard = ("act", proj_act)
        attf_free = ("pes", proj_pe)
        st_guard = ("dve", proj_dve)
        x2_guard = ("acc", e.acc.n)
        recb_prev = proj_dve
        rbs_guard = proj_dve
        st_t = {}
        clip_t = {}
        pv_t = {}
        kabs = 0
        exp_jobs = []
        exs_base = e.exs.n

        def semh(name):
            return getattr(e, name).h

        for q5 in range(NTS):
            qsl = slice(q5 * TT, (q5 + 1) * TT)
            nkb = 4 * (q5 + 1)
            for lh in range(NHH):
                hc = lh // 2
                hb = (lh % 2) * 64
                hr = lh // 2
                hrb = (lh % 2) * 32
                ka0 = kabs

                g0, g1 = st_guard
                gp0, gp1 = pv_free

                @block.tensor
                def _(tt_, lh=lh, nkb=nkb, ka0=ka0, hb=hb, hc=hc, hr=hr,
                      hrb=hrb, qsl=qsl, g0=g0, g1=g1, gp0=gp0, gp1=gp1,
                      first=(q5 == 0 and lh == 0)):
                    if first:
                        tt_.wait_ge(e.act.h, proj_act)
                    tt_.wait_ge(semh(g0), g1)

                    def pv(i):
                        ka = ka0 + i
                        tt_.wait_ge(e.exs.h, exs_base + ka + 1)
                        if i == 0:
                            tt_.wait_ge(semh(gp0), gp1)
                        mm = tt_.matmul(pPV[:], vtm.ap()[:, i, lh, :],
                                        psb[ka % 4][:],
                                        start=(i == 0), stop=(i == nkb - 1))
                        e.pes.inc(mm, 1)
                        pv_t[ka] = e.pes.n

                    for i in range(nkb):
                        ka = ka0 + i
                        kt = slice(i * 128, (i + 1) * 128)
                        if ka >= 4:
                            tt_.wait_ge(e.exs.h, exs_base + ka - 3)
                        tt_.matmul(pSt[ka % 4][:], kc.ap()[hb:hb + 64, hc, kt],
                                   qc.ap()[hb:hb + 64, hc, qsl],
                                   start=True, stop=False)
                        mm = tt_.matmul(pSt[ka % 4][:], krm[hrb:hrb + 32, kt],
                                        qr.ap()[hrb:hrb + 32, hr, qsl],
                                        start=False, stop=True)
                        e.pes.inc(mm, 1)
                        st_t[ka] = e.pes.n
                        if i >= 2:
                            pv(i - 2)
                    pv(nkb - 2)
                    pv(nkb - 1)
                pv_all = e.pes.n

                @block.vector
                def _(v, nkb=nkb, ka0=ka0, q5=q5):
                    v.wait_ge(e.ldw.h, wow_done)
                    for i in range(nkb):
                        ka = ka0 + i
                        v.wait_ge(e.pes.h, st_t[ka])
                        e.dve.inc(v.tensor_scalar(pSt[ka % 4][:], pSt[ka % 4][:],
                                                  80.0, -80.0, ALU.min, ALU.max), 1)
                        r = i - 4 * q5
                        if 0 <= r < 4:
                            v.wait_ge(e.dve.h, e.dve.n)
                            e.dve.inc(v.tensor_tensor(pSt[ka % 4][:],
                                                      pSt[ka % 4][:],
                                                      mask4[:, r, :], ALU.add), 1)
                        clip_t[ka] = e.dve.n

                exp_jobs.append((lh, ka0, nkb))
                if lh == NHH - 1:
                    @block.scalar
                    def _(sc, jobs=tuple(exp_jobs)):
                        sc.wait_ge(e.dve.h, e.nbcm_ready)
                        for (jlh, jka0, jnkb) in jobs:
                            for i in range(jnkb):
                                ka = jka0 + i
                                sc.wait_ge(e.dve.h, clip_t[ka])
                                if ka >= 4:
                                    sc.wait_ge(e.pes.h, pv_t[ka - 4])
                                sc.activation(psb[ka % 4][:], pSt[ka % 4][:],
                                              AF.Exp, bias=nbcm[:, jlh:jlh + 1],
                                              scale=SC_MLA).then_inc(e.exs.h, 1)
                                e.exs.n += 1
                    exp_jobs = []
                kabs += nkb

                @block.vector
                def _(v, pv_all=pv_all, recb_prev=recb_prev):
                    v.wait_ge(e.pes.h, pv_all)
                    v.wait_ge(e.dve.h, recb_prev)
                    e.dve.inc(v.tensor_copy(recb[:], pPV[64:65, :]), 1)
                recb_done = e.dve.n
                recb_prev = recb_done
                g0, g1 = pmi_guard

                @block.tensor
                def _(tt_, recb_done=recb_done, g0=g0, g1=g1):
                    tt_.wait_ge(e.dve.h, recb_done)
                    tt_.wait_ge(semh(g0), g1)
                    e.pes.inc(tt_.matmul(pMi[0:64, :], e.onesB[0:1, 0:64], recb[:],
                                         start=True, stop=True), 1)
                rb_pe = e.pes.n
                g0, g1 = attf_free

                @block.vector
                def _(v, lh=lh, rb_pe=rb_pe, rbs_guard=rbs_guard, g0=g0, g1=g1):
                    v.wait_ge(e.pes.h, rb_pe)
                    v.wait_ge(e.dve.h, rbs_guard)
                    with e.nc.allow_low_precision(reason="softmax recip bf16"):
                        e.dve.inc(v.reciprocal(rbs[:], pMi[0:64, :]), 1)
                    v.wait_ge(e.dve.h, e.dve.n)
                    if lh == 0:
                        v.wait_ge(semh(g0), g1)
                    hc2 = lh // 2
                    hb2 = (lh % 2) * 64
                    e.dve.inc(v.tensor_tensor(attf.ap()[hb2:hb2 + 64, hc2, :],
                                              pPV[0:64, :], rbs[:], ALU.mult), 1)
                pmi_guard = ("dve", e.dve.n)
                pv_free = ("dve", e.dve.n)
                rbs_guard = e.dve.n

            t = t0 + q5
            attf_all = e.dve.n
            gx0, gx1 = x2_guard
            wo_pe = {}
            for dc in range(DC):
                @block.tensor
                def _(tt_, dc=dc, attf_all=attf_all):
                    if dc == 0:
                        tt_.wait_ge(e.dve.h, attf_all)
                        tt_.wait_ge(e.ldw.h, wow_done)
                    if dc >= 4:
                        tt_.wait_ge(e.dve.h, wo_used[dc - 4])
                    for hv in range(8):
                        mm = tt_.matmul(pSt[dc % 4][:],
                                        wow[:, hv, dc * 128:(dc + 1) * 128],
                                        attf.ap()[:, hv, :],
                                        start=(hv == 0), stop=(hv == 7))
                    e.pes.inc(mm, 1)
                if dc == 0:
                    wo_used = {}
                wo_pe[dc] = e.pes.n

                @block.vector
                def _(v, dc=dc, gx0=gx0, gx1=gx1):
                    v.wait_ge(e.pes.h, wo_pe[dc])
                    if dc == 0:
                        v.wait_ge(semh(gx0), gx1)
                    e.dve.inc(v.tensor_copy(x2b[:, dc, :],
                                            pSt[dc % 4][:]), 1)
                    wo_used[dc] = e.dve.n
            x2_done = e.dve.n
            attf_free = ("pes", wo_pe[DC - 1])
            st_guard = ("dve", x2_done)

            @block.gpsimd
            def _(g, t=t, x2_done=x2_done):
                g.wait_ge(e.dve.h, x2_done)
                g.wait_ge(e.stw.h, e.x2copy_done)
                e.acc.inc(g.dma_start(out=e.dtile(x_out, t), in_=x2b[:],
                                      accum_op=ALU.add), 16)
            x2_guard = ("acc", e.acc.n)
        ST.pop_all().close()


def _emit_final(block, e):
    @block.gpsimd
    def _(g):
        g.wait_ge(e.stw.h, e.stw.n)
        g.wait_ge(e.acc.h, e.acc.n)

    @block.sync
    def _(sy):
        sy.wait_ge(e.acc.h, e.acc.n)
        sy.wait_ge(e.stw.h, e.stw.n)


def kernel(**inputs):
    C, xc = _prepare(inputs)
    nc = _build()
    in_maps = [dict(C, x_fm=xc[c]) for c in range(N_CORES)]
    res = run_bass_kernel_spmd(nc, in_maps, core_ids=list(range(N_CORES)))
    out = np.empty((16, 2048, 1024), np.float32)
    for c in range(N_CORES):
        out[2 * c:2 * c + 2] = res.results[c]["y_fm"].T.reshape(BPC, S, D)
    return out
